# version 01
# baseline (speedup 1.0000x reference)
import contextlib
import numpy as np
import ml_dtypes
import concourse.bass as bass
import concourse.mybir as mybir
from concourse.bass_utils import run_bass_kernel_spmd

F32 = mybir.dt.float32
BF16 = mybir.dt.bfloat16
AF = mybir.ActivationFunctionType
ALU = mybir.AluOpType
AX = mybir.AxisListType
BF16NP = ml_dtypes.bfloat16

NCORES = 8
B = 4
S = 4096
D = 1024
SL = S // NCORES
NT = B * SL
HALO = 16
NTH = NT + B * HALO
PADB = 640
NEG = -1e30
FCONST_VALS = [1e-5, 0.01, 1.0, 0.0, -1e30, 0.5, 2.0, 3.0]


class L:
    def __init__(self, nc, ctx):
        self.nc = nc
        self.ctx = ctx
        self.counts = {}
        self.last_wait = {}
        self.LD = self._sem("LD")
        self.WD = self._sem("WD")
        self.TD = self._sem("TD")
        self.ED = self._sem("ED")
        self.ST = self._sem("ST")
        self.pm = [ctx.enter_context(nc.psum_tensor(f"pm{i}", [128, 512], F32))
                   for i in range(7)]
        self.pmb = ctx.enter_context(nc.psum_tensor("pmb", [128, 1024], BF16))
        self.bank_free = [0] * 8
        self.bank_ready = [0] * 8
        self._rr = 0
        self.inputs = {}
        self.outputs = {}

    def _sem(self, name):
        return self.ctx.enter_context(self.nc.semaphore(name))

    def inc(self, instr, sem, n=1):
        instr.then_inc(sem, n)
        self.counts[sem] = self.counts.get(sem, 0) + n
        return self.counts[sem]

    def val(self, sem):
        return self.counts.get(sem, 0)

    def wait(self, eng, sem, value=None):
        v = self.val(sem) if value is None else value
        if v <= 0:
            return
        key = (id(eng), sem)
        if self.last_wait.get(key, -1) >= v:
            return
        eng.wait_ge(sem, v)
        self.last_wait[key] = v

    def phase_barrier(self):
        for eng in (self.nc.scalar, self.nc.vector, self.nc.gpsimd, self.nc.sync):
            self.wait(eng, self.TD)
            self.wait(eng, self.ED)
        for eng in (self.nc.tensor, self.nc.scalar, self.nc.vector):
            self.wait(eng, self.LD)
            self.wait(eng, self.WD)

    def bank(self, i):
        return self.pm[i] if i < 7 else self.pmb

    def acquire(self, i):
        self.wait(self.nc.tensor, self.ED, self.bank_free[i])

    def written(self, i):
        self.bank_ready[i] = self.val(self.TD)

    def consume(self, eng, i):
        self.wait(eng, self.TD, self.bank_ready[i])

    def freed(self, i):
        self.bank_free[i] = self.val(self.ED)

    def nb(self):
        i = self._rr % 7
        self._rr += 1
        return i

    def din(self, name, shape, dt):
        ap = self.nc.dram_tensor(name, list(shape), dt, kind="ExternalInput").ap()
        self.inputs[name] = ap
        return ap

    def dout(self, name, shape, dt):
        ap = self.nc.dram_tensor(name, list(shape), dt, kind="ExternalOutput").ap()
        self.outputs[name] = ap
        return ap

    def sbuf(self, name, shape, dt):
        return self.ctx.enter_context(self.nc.sbuf_tensor(name, list(shape), dt))


class FM:
    def __init__(self, sb, rows, C):
        self.sb = sb
        self.rows = rows
        self.C = C
        self.np_ = rows // 128

    def t(self, p, c0=0, w=None):
        w = self.C - c0 if w is None else w
        return self.sb[:, p * self.C + c0: p * self.C + c0 + w]

    def rs(self, p, r0, nr, c0=0, w=None):
        w = self.C - c0 if w is None else w
        return self.sb[r0:r0 + nr, p * self.C + c0: p * self.C + c0 + w]


def alloc_fm(l, name, rows, C, dt):
    sb = l.sbuf(name, [128, (rows // 128) * C], dt)
    return FM(sb, rows, C)


def load_fm(l, fmbuf, hbm_ap, cols=None, cast=False):
    nc = l.nc
    C = cols or fmbuf.C
    eng = nc.gpsimd if cast else nc.sync
    sem = l.WD if cast else l.LD
    for p in range(fmbuf.np_):
        l.inc(eng.dma_start(out=fmbuf.t(p, 0, C),
                            in_=hbm_ap[p * 128:(p + 1) * 128, 0:C]), sem, 16)


def store_fm(l, hbm_ap, fmbuf):
    nc = l.nc
    l.wait(nc.sync, l.ED)
    l.wait(nc.sync, l.TD)
    for p in range(fmbuf.np_):
        l.inc(nc.sync.dma_start(out=hbm_ap[p * 128:(p + 1) * 128, :],
                                in_=fmbuf.t(p)), l.ST, 16)


class WStream:

    def __init__(self, l, name, din, nslots=2, stack=None, ow=128):
        self.l = l
        self.din = din
        self.ow = ow
        self.npt = din // 128
        alloc = (stack.enter_context if stack is not None else
                 (lambda cm: l.ctx.enter_context(cm)))
        self.slots = [alloc(l.nc.sbuf_tensor(f"{name}{i}", [128, self.npt * ow], BF16))
                      for i in range(nslots)]
        self.nslots = nslots
        self.cur = 0
        self.slot_ready = [0] * nslots
        self.slot_user = [-1] * nslots

    def issue(self, hbm_ap, ot):
        l, nc = self.l, self.l.nc
        s = self.cur % self.nslots
        self.cur += 1
        l.wait(nc.gpsimd, l.TD,
               self.slot_user[s] if self.slot_user[s] >= 0 else l.val(l.TD))
        src = hbm_ap[ot]
        l.inc(nc.gpsimd.dma_start(
            out=self.slots[s][:].rearrange("p (a o) -> p a o", o=self.ow),
            in_=src.rearrange("(a p) o -> p a o", p=128)), l.WD, 16)
        self.slot_ready[s] = l.val(l.WD)
        return s

    def chunk(self, s, pt, ow=None):
        ow = ow or self.ow
        return self.slots[s][:, pt * self.ow: pt * self.ow + ow]

    def use(self, s):
        self.slot_user[s] = self.l.val(self.l.TD)

    def wait_ready(self, s):
        self.l.wait(self.l.nc.tensor, self.l.WD, self.slot_ready[s])


def emit_ln(l, x, xn, C=None, g_fm=None, b_fm=None, eps=1e-5, consts=None,
            window=None):
    nc = l.nc
    l.phase_barrier()
    C = C or x.C
    cs = consts
    if window is not None:
        nts = [window]
    else:
        nts = [(i * 512, min(512, C - i * 512)) for i in range((C + 511) // 512)]
    NPT = x.np_
    for (c0, w) in nts:
        rof = 0 if window is not None else c0
        bs, bq = l.nb(), l.nb()
        l.acquire(bs)
        l.acquire(bq)
        slot_use = [0, 0]
        for pt in range(NPT):
            sl = pt % 2
            l.wait(nc.scalar, l.TD, slot_use[sl])
            l.inc(nc.scalar.copy(cs["cast_t"][sl][:, 0:w], x.t(pt, c0, w)), l.ED)
            l.inc(nc.scalar.activation(cs["sq_t"][sl][:, 0:w], x.t(pt, c0, w),
                                       AF.Square), l.ED)
            l.wait(nc.tensor, l.ED)
            nc.tensor.matmul(l.bank(bs)[0:1, 0:w], cs["ones128_b"],
                             cs["cast_t"][sl][:, 0:w],
                             start=(pt == 0), stop=(pt == NPT - 1))
            l.inc(nc.tensor.matmul(l.bank(bq)[0:1, 0:w], cs["ones128_b"],
                                   cs["sq_t"][sl][:, 0:w],
                                   start=(pt == 0), stop=(pt == NPT - 1)), l.TD)
            slot_use[sl] = l.val(l.TD)
        l.written(bs)
        l.written(bq)
        l.consume(nc.scalar, bs)
        l.inc(nc.scalar.copy(cs["sums"][0:1, rof:rof + w], l.bank(bs)[0:1, 0:w]), l.ED)
        l.inc(nc.scalar.copy(cs["sqsums"][0:1, rof:rof + w], l.bank(bq)[0:1, 0:w]), l.ED)
        l.freed(bs)
        l.freed(bq)
        bm, bv = l.nb(), l.nb()
        l.acquire(bm)
        l.acquire(bv)
        l.wait(nc.tensor, l.ED)
        l.inc(nc.tensor.matmul(l.bank(bm)[:, 0:w], cs["ones1_128"],
                               cs["sums"][0:1, rof:rof + w], start=True, stop=True), l.TD)
        l.inc(nc.tensor.matmul(l.bank(bv)[:, 0:w], cs["ones1_128"],
                               cs["sqsums"][0:1, rof:rof + w], start=True, stop=True), l.TD)
        l.written(bm)
        l.written(bv)
        l.consume(nc.scalar, bm)
        l.inc(nc.scalar.activation(cs["t1"][:, 0:w], l.bank(bm)[:, 0:w], AF.Square,
                                   scale=1.0 / D), l.ED)
        l.consume(nc.vector, bv)
        l.wait(nc.vector, l.ED)
        l.inc(nc.vector.scalar_tensor_tensor(
            cs["t2"][:, 0:w], l.bank(bv)[:, 0:w], 1.0 / D, cs["t1"][:, 0:w],
            op0=ALU.mult, op1=ALU.subtract), l.ED)
        l.freed(bv)
        l.wait(nc.scalar, l.ED)
        l.inc(nc.scalar.activation(cs["t2"][:, 0:w], cs["t2"][:, 0:w], AF.Ln,
                                   bias=float(eps)), l.ED)
        l.inc(nc.scalar.activation(cs["rrep"][:, rof:rof + w], cs["t2"][:, 0:w],
                                   AF.Exp, scale=-0.5), l.ED)
        l.wait(nc.vector, l.ED)
        l.inc(nc.vector.scalar_tensor_tensor(
            cs["mrrep"][:, rof:rof + w], l.bank(bm)[:, 0:w], 1.0 / D,
            cs["rrep"][:, rof:rof + w], op0=ALU.mult, op1=ALU.mult), l.ED)
        l.freed(bm)
    for pt in range(NPT):
        for (c0, w) in nts:
            rof = 0 if window is not None else c0
            oc0 = 0 if window is not None else c0
            l.wait(nc.vector, l.ED)
            l.inc(nc.vector.tensor_tensor(xn.t(pt, oc0, w), x.t(pt, c0, w),
                                          cs["rrep"][:, rof:rof + w], op=ALU.mult), l.ED)
            l.inc(nc.vector.tensor_tensor(xn.t(pt, oc0, w), xn.t(pt, oc0, w),
                                          cs["mrrep"][:, rof:rof + w],
                                          op=ALU.subtract), l.ED)
            if g_fm is not None:
                l.wait(nc.scalar, l.ED)
                l.inc(nc.scalar.mul(xn.t(pt, oc0, w), xn.t(pt, oc0, w),
                                    g_fm[:, pt:pt + 1]), l.ED)
                l.inc(nc.scalar.add(xn.t(pt, oc0, w), xn.t(pt, oc0, w),
                                    b_fm[:, pt:pt + 1]), l.ED)


def emit_project(l, whbm, wstream, xn, douts, n_tiles, out_fn, extra=None,
                 rhs_fn=None):
    nc = l.nc
    npt = wstream.npt
    s = wstream.issue(whbm, 0)
    for ot in range(douts):
        s_next = wstream.issue(whbm, ot + 1) if ot + 1 < douts else None
        wstream.wait_ready(s)
        for (c0, w, tag) in n_tiles:
            bi = l.nb()
            l.acquire(bi)
            ps = l.bank(bi)[0:wstream.ow, 0:w]
            l.wait(nc.tensor, l.ED)
            last = None
            for pt in range(npt):
                rhs = rhs_fn(pt, c0, w, tag) if rhs_fn else xn.t(pt, c0, w)
                stop = (pt == npt - 1) and not extra
                last = nc.tensor.matmul(ps, wstream.chunk(s, pt), rhs,
                                        start=(pt == 0), stop=stop)
            if extra:
                for i, (lf, rf) in enumerate(extra):
                    last = nc.tensor.matmul(ps, lf(ot), rf(c0, w, tag),
                                            start=False, stop=(i == len(extra) - 1))
            l.inc(last, l.TD)
            l.written(bi)
            wstream.use(s)
            out_fn(ot, c0, w, tag, ps, bi)
        s = s_next


_LNCTR = [0]


def ln_scratch(l, stack, C):
    nc = l.nc
    _LNCTR[0] += 1
    u = f"_{_LNCTR[0]}"
    cs = {}
    cs["cast_t"] = [stack.enter_context(nc.sbuf_tensor(f"lncast{i}{u}", [128, 512], BF16))
                    for i in range(2)]
    cs["sq_t"] = [stack.enter_context(nc.sbuf_tensor(f"lnsq{i}{u}", [128, 512], BF16))
                  for i in range(2)]
    cs["sums"] = stack.enter_context(nc.sbuf_tensor(f"lnsums{u}", [1, C], BF16))
    cs["sqsums"] = stack.enter_context(nc.sbuf_tensor(f"lnsqsums{u}", [1, C], BF16))
    cs["t1"] = stack.enter_context(nc.sbuf_tensor(f"lnt1{u}", [128, 512], F32))
    cs["t2"] = stack.enter_context(nc.sbuf_tensor(f"lnt2{u}", [128, 512], F32))
    cs["rrep"] = stack.enter_context(nc.sbuf_tensor(f"lnr{u}", [128, C], F32))
    cs["mrrep"] = stack.enter_context(nc.sbuf_tensor(f"lnmr{u}", [128, C], F32))
    return cs


def do_ln(l, x, xn, C=None, g_fm=None, b_fm=None, eps=1e-5, window=None):
    with contextlib.ExitStack() as st:
        cs = ln_scratch(l, st, 512 if window is not None else (C or x.C))
        cs["ones128_b"] = l.c_ones128_b
        cs["ones1_128"] = l.c_ones1_128
        emit_ln(l, x, xn, C=C, g_fm=g_fm, b_fm=b_fm, eps=eps, consts=cs,
                window=window)


MAIN4 = [(b * SL, SL, b) for b in range(B)]
MAIN4H = MAIN4 + [(NT, B * HALO, "h")]


def emit_l0aa(l, x, xn, hb):
    nc = l.nc
    l.phase_barrier()
    with contextlib.ExitStack() as st:
        sb = lambda n, s, d: st.enter_context(nc.sbuf_tensor(n, s, d))
        q = FM(sb("aaq", [128, 8 * NT], BF16), 1024, NT)
        k_pad = FM(sb("aak", [128, 2 * B * PADB], BF16), 256, B * PADB)
        v_pad = FM(sb("aav", [128, 2 * B * PADB], BF16), 256, B * PADB)
        ws = WStream(l, "aaw", 1024, stack=st)
        t_bf = sb("aat", [128, 2 * 512], BF16)
        rr_sb = sb("aarr", [2, 512], BF16)

        def mk_evac(dest, tag_map):
            def fn(ot, c0, w, tag, ps, bi):
                l.consume(nc.scalar, bi)
                dst = tag_map(dest, ot, c0, w, tag)
                l.inc(nc.scalar.copy(dst, ps), l.ED)
                l.freed(bi)
            return fn

        def q_map(dest, ot, c0, w, tag):
            return dest.t(ot, c0, w)
        emit_project(l, hb["aa_wq"], ws, xn, 8, MAIN4, mk_evac(q, q_map))

        def kv_map(dest, ot, c0, w, tag):
            if tag == "h":
                base = dest.sb[:, ot * (B * PADB): (ot + 1) * (B * PADB)]
                return base.rearrange("p (b c) -> p b c", c=PADB)[:, :, 112:128]
            b = tag
            return dest.t(ot, b * PADB + 128, w)

        def kv_evac(dest):
            def fn(ot, c0, w, tag, ps, bi):
                l.consume(nc.scalar, bi)
                if tag == "h":
                    dst = kv_map(dest, ot, c0, w, tag)
                    l.inc(nc.scalar.copy(dst, ps.rearrange("p (b c) -> p b c", c=16)),
                          l.ED)
                else:
                    l.inc(nc.scalar.copy(kv_map(dest, ot, c0, w, tag), ps), l.ED)
                l.freed(bi)
            return fn
        emit_project(l, hb["aa_wk"], ws, xn, 2, MAIN4H, kv_evac(k_pad))
        emit_project(l, hb["aa_wv"], ws, xn, 2, MAIN4H, kv_evac(v_pad))
        for dest in (k_pad, v_pad):
            for ot in range(2):
                ap = dest.sb[:, ot * (B * PADB): (ot + 1) * (B * PADB)].rearrange(
                    "p (b c) -> p b c", c=PADB)[:, :, 0:112]
                l.inc(nc.gpsimd.memset(ap, 0.0), l.WD, 1)

        def rms(dest, rrbuf, tiles, wn_ap):
            for pt in range(dest.np_):
                for (c0, w, tag) in tiles:
                    l.wait(nc.scalar, l.TD)
                    l.wait(nc.scalar, l.ED)
                    l.inc(nc.scalar.activation(t_bf[:, 0:w], dest.t(pt, c0, w),
                                               AF.Square), l.ED)
                    bi = l.nb()
                    l.acquire(bi)
                    l.wait(nc.tensor, l.ED)
                    l.inc(nc.tensor.matmul(l.bank(bi)[0:2, 0:w], l.c_ind128_2,
                                           t_bf[:, 0:w], start=True, stop=True), l.TD)
                    l.written(bi)
                    l.consume(nc.scalar, bi)
                    l.inc(nc.scalar.activation(l.bank(bi)[0:2, 0:w],
                                               l.bank(bi)[0:2, 0:w], AF.Ln,
                                               scale=1.0 / 64, bias=0.01), l.ED)
                    l.inc(nc.scalar.activation(rrbuf[0:2, 0:w],
                                               l.bank(bi)[0:2, 0:w], AF.Exp,
                                               scale=-0.5), l.ED)
                    l.freed(bi)
                    bi2 = l.nb()
                    l.acquire(bi2)
                    l.wait(nc.tensor, l.ED)
                    l.inc(nc.tensor.matmul(
                        l.bank(bi2)[:, 0:w], l.c_ind2_128,
                        rrbuf[0:2, 0:w], start=True, stop=True), l.TD)
                    l.written(bi2)
                    l.consume(nc.vector, bi2)
                    l.wait(nc.vector, l.ED)
                    l.inc(nc.vector.tensor_tensor(dest.t(pt, c0, w), dest.t(pt, c0, w),
                                                  l.bank(bi2)[:, 0:w], op=ALU.mult),
                          l.ED)
                    l.freed(bi2)
                    l.wait(nc.scalar, l.ED)
                    l.inc(nc.scalar.mul(dest.t(pt, c0, w), dest.t(pt, c0, w), wn_ap),
                          l.ED)
        rms(q, rr_sb, MAIN4, hb_ap_qnw(l))
        KTILES = ([(b * PADB + 128, 512, b) for b in range(B)]
                  + [(b * PADB + 112, 16, b) for b in range(B)])
        rms(k_pad, rr_sb, KTILES, hb_ap_knw(l))

        v_tok = sb("aavtok", [128, (B * PADB // 128) * 260], BF16)
        ident = l.c_ident[:]
        NTOK = B * PADB // 128
        for vp in range(2):
            for tt in range(NTOK):
                l.wait(nc.tensor, l.ED)
                l.wait(nc.tensor, l.WD)
                l.acquire(7)
                l.inc(nc.tensor.transpose(l.pmb[0:128, 0:128],
                                          v_pad.t(vp, tt * 128, 128), ident), l.TD)
                l.written(7)
                l.consume(nc.scalar, 7)
                for gg in range(2):
                    g = vp * 2 + gg
                    l.inc(nc.scalar.copy(
                        v_tok[:, tt * 260 + g * 65: tt * 260 + g * 65 + 64],
                        l.pmb[0:128, gg * 64: gg * 64 + 64]), l.ED)
                l.freed(7)
        for g in range(4):
            ap = v_tok[:].rearrange("p (t c) -> p t c", c=260)[:, :, g * 65 + 64:g * 65 + 65]
            l.inc(nc.gpsimd.memset(ap, 1.0), l.WD, 1)

        ctx = FM(xn.sb, 1024, xn.C)
        att_e = sb("aae", [128, 4 * 512], BF16)
        av_sb = sb("aaav", [64, 512], BF16)
        den_sb = sb("aaden", [1, 512], F32)
        dinvf_sb = sb("aadinvf", [1, 512], F32)
        dinv_sb = sb("aadinv", [1, 512], BF16)
        eslot_use = [0] * 4
        esl = 0
        R4 = hb["aa_R4"]
        R4f = hb["aa_R4f"]
        r_sb = sb("aar4", [128, 4 * 2 * 512], F32)
        rf_sb = sb("aar4f", [128, 4 * 512], F32)
        for g in range(4):
            for c in range(2):
                l.inc(nc.sync.dma_start(out=r_sb[:, (g * 2 + c) * 512:(g * 2 + c + 1) * 512],
                                        in_=R4[g, c]), l.LD, 16)
            l.inc(nc.sync.dma_start(out=rf_sb[:, g * 512:(g + 1) * 512],
                                    in_=R4f[g]), l.LD, 16)

        for b in range(B):
            for qt in range(4):
                for g in range(4):
                    bA, bB = l.nb(), l.nb()
                    for cls, bi in ((0, bA), (1, bB)):
                        kt = qt + cls
                        l.acquire(bi)
                        l.wait(nc.tensor, l.ED)
                        l.wait(nc.tensor, l.WD)
                        last = None
                        for j in range(4):
                            h = g + 4 * j
                            lhs = k_pad.rs(g // 2, 64 * (g % 2), 64,
                                           b * PADB + kt * 128, 128)
                            rhs = q.rs(h // 2, 64 * (h % 2), 64,
                                       b * SL + qt * 128, 128)
                            last = nc.tensor.matmul(
                                l.bank(bi)[0:128, j * 128:(j + 1) * 128],
                                lhs, rhs, start=True, stop=True)
                        l.inc(last, l.TD)
                        l.written(bi)
                    for cls, bi in ((0, bA), (1, bB)):
                        l.consume(nc.vector, bi)
                        l.wait(nc.vector, l.TD, eslot_use[esl])
                        rtile = (rf_sb[:, g * 512:(g + 1) * 512]
                                 if (cls == 0 and qt == 0)
                                 else r_sb[:, (g * 2 + cls) * 512:(g * 2 + cls + 1) * 512])
                        l.inc(nc.vector.scalar_tensor_tensor(
                            att_e[:, esl * 512:(esl + 1) * 512],
                            l.bank(bi)[:, 0:512], 1.0 / 8.0, rtile,
                            op0=ALU.mult, op1=ALU.add), l.ED)
                        l.freed(bi)
                        l.wait(nc.scalar, l.ED)
                        l.inc(nc.scalar.activation(
                            att_e[:, esl * 512:(esl + 1) * 512],
                            att_e[:, esl * 512:(esl + 1) * 512], AF.Exp), l.ED)
                        if cls == 0:
                            esl0 = esl
                        esl = (esl + 1) % 4
                    bC = l.nb()
                    l.acquire(bC)
                    l.wait(nc.tensor, l.ED)
                    i1 = nc.tensor.matmul(
                        l.bank(bC)[0:65, 0:512],
                        v_tok[:, (b * 5 + qt) * 260 + g * 65:
                              (b * 5 + qt) * 260 + g * 65 + 65],
                        att_e[:, esl0 * 512:(esl0 + 1) * 512], start=True, stop=False)
                    i2 = nc.tensor.matmul(
                        l.bank(bC)[0:65, 0:512],
                        v_tok[:, (b * 5 + qt + 1) * 260 + g * 65:
                              (b * 5 + qt + 1) * 260 + g * 65 + 65],
                        att_e[:, ((esl0 + 1) % 4) * 512:((esl0 + 1) % 4 + 1) * 512],
                        start=False, stop=True)
                    l.inc(i2, l.TD)
                    l.written(bC)
                    eslot_use[esl0] = l.val(l.TD)
                    eslot_use[(esl0 + 1) % 4] = l.val(l.TD)
                    l.consume(nc.scalar, bC)
                    l.wait(nc.scalar, l.ED)
                    l.inc(nc.scalar.copy(den_sb[:], l.bank(bC)[64:65, 0:512]), l.ED)
                    l.inc(nc.scalar.copy(av_sb[:], l.bank(bC)[0:64, 0:512]), l.ED)
                    l.freed(bC)
                    l.wait(nc.vector, l.ED)
                    l.inc(nc.vector.reciprocal(dinvf_sb[:], den_sb[:]), l.ED)
                    l.wait(nc.scalar, l.ED)
                    l.inc(nc.scalar.copy(dinv_sb[:], dinvf_sb[:]), l.ED)
                    bD = l.nb()
                    l.acquire(bD)
                    l.wait(nc.tensor, l.ED)
                    l.inc(nc.tensor.matmul(l.bank(bD)[0:64, 0:512], l.c_ones1_64,
                                           dinv_sb[:], start=True, stop=True), l.TD)
                    l.written(bD)
                    l.consume(nc.vector, bD)
                    for j in range(4):
                        h = g + 4 * j
                        l.inc(nc.vector.tensor_tensor(
                            ctx.rs(h // 2, 64 * (h % 2), 64, b * SL + qt * 128, 128),
                            av_sb[:, j * 128:(j + 1) * 128],
                            l.bank(bD)[0:64, j * 128:(j + 1) * 128], op=ALU.mult),
                            l.ED)
                    l.freed(bD)

        def wo_evac(ot, c0, w, tag, ps, bi):
            l.consume(nc.vector, bi)
            l.wait(nc.vector, l.ED)
            l.inc(nc.vector.tensor_tensor(x.t(ot, c0, w), x.t(ot, c0, w), ps,
                                          op=ALU.add), l.ED)
            l.freed(bi)
        emit_project(l, hb["aa_wo"], ws, ctx, 8, MAIN4, wo_evac)


def hb_ap_qnw(l):
    return l.c_qnw[:, 0:1]


def hb_ap_knw(l):
    return l.c_knw[:, 0:1]


def emit_l0ffn(l, x, xn, hb, pfx="f0", dbg=None):
    nc = l.nc
    l.phase_barrier()
    with contextlib.ExitStack() as st:
        sb = lambda n, s, d: st.enter_context(nc.sbuf_tensor(n, s, d))
        c_fm = FM(sb(pfx + "c", [128, 8 * 512], F32), 1024, 512)
        cb_fm = FM(sb(pfx + "cb", [128, 8 * 512], BF16), 1024, 512)
        c2_fm = FM(sb(pfx + "c2", [128, 8 * 512], F32), 1024, 512)
        fig = sb(pfx + "fig", [128, 3 * 512], F32)
        tmp = sb(pfx + "tmp", [128, 512], F32)
        h_t = sb(pfx + "h", [128, 512], BF16)
        wsg = [WStream(l, pfx + f"wg{i}", 2048, stack=st) for i in range(3)]
        wso = WStream(l, pfx + "wo", 2048, stack=st)
        for ot in range(8):
            l.inc(nc.scalar.activation(c_fm.t(ot), x.t(0, 0, 512), AF.Identity,
                                       bias=l.c_inc_fm[:, ot:ot + 1], scale=0.0),
                  l.ED)
            l.inc(nc.scalar.copy(cb_fm.t(ot), c_fm.t(ot)), l.ED)
        gates = [("wfuf", AF.Sigmoid, l.c_bf_fm), ("wiui", AF.Sigmoid, l.c_bi_fm),
                 ("wcuc", AF.Tanh, l.c_bc_fm)]
        slots = [wsg[gi].issue(hb[nm], 0) for gi, (nm, _, _) in enumerate(gates)]
        for t in range(B):
            for ot in range(8):
                nslots = []
                for gi, (nm, act, bias) in enumerate(gates):
                    if not (t == B - 1 and ot == 7):
                        nslots.append(wsg[gi].issue(
                            hb[nm], (ot + 1) % 8))
                    else:
                        nslots.append(None)
                for gi, (nm, act, bias) in enumerate(gates):
                    s = slots[gi]
                    wsg[gi].wait_ready(s)
                    if dbg is not None and t == 0 and ot == 0 and gi == 0:
                        l.wait(nc.sync, l.WD)
                        l.inc(nc.sync.dma_start(out=dbg["wslot"],
                                                in_=wsg[gi].slots[s][:]), l.ST, 16)
                        l.wait(nc.sync, l.ED)
                        l.inc(nc.sync.dma_start(out=dbg["xnt"],
                                                in_=xn.t(0, 0, SL)), l.ST, 16)
                    bi = l.nb()
                    l.acquire(bi)
                    l.wait(nc.tensor, l.ED)
                    last = None
                    for pt in range(16):
                        rhs = (xn.t(pt, t * SL, SL) if pt < 8
                               else cb_fm.t(pt - 8))
                        last = nc.tensor.matmul(l.bank(bi)[0:128, 0:SL],
                                                wsg[gi].chunk(s, pt), rhs,
                                                start=(pt == 0), stop=(pt == 15))
                    l.inc(last, l.TD)
                    l.written(bi)
                    wsg[gi].use(s)
                    l.consume(nc.scalar, bi)
                    l.wait(nc.scalar, l.ED)
                    l.inc(nc.scalar.activation(fig[:, gi * 512:(gi + 1) * 512],
                                               l.bank(bi)[:, 0:SL], act,
                                               bias=bias[:, ot:ot + 1]), l.ED)
                    l.freed(bi)
                l.wait(nc.vector, l.ED)
                l.inc(nc.vector.tensor_tensor(tmp[:], fig[:, 512:1024],
                                              fig[:, 1024:1536], op=ALU.mult), l.ED)
                l.inc(nc.vector.tensor_tensor(c2_fm.t(ot), fig[:, 0:512],
                                              c_fm.t(ot), op=ALU.mult), l.ED)
                l.inc(nc.vector.tensor_tensor(c2_fm.t(ot), c2_fm.t(ot), tmp[:],
                                              op=ALU.add), l.ED)
                if dbg is not None and t == 0 and ot == 0:
                    l.wait(nc.sync, l.ED)
                    l.inc(nc.sync.dma_start(out=dbg["fig0"], in_=fig[:]), l.ST, 16)
                slots = nslots
            c_fm, c2_fm = c2_fm, c_fm
            for ot_ in range(8):
                l.wait(nc.scalar, l.ED)
                l.wait(nc.scalar, l.TD)
                l.inc(nc.scalar.copy(cb_fm.t(ot_), c_fm.t(ot_)), l.ED)
            if dbg is not None and t == 0:
                l.wait(nc.sync, l.ED)
                for p_ in range(8):
                    l.inc(nc.sync.dma_start(out=dbg["c1"][p_ * 128:(p_ + 1) * 128, :],
                                            in_=c_fm.t(p_)), l.ST, 16)
        so = wso.issue(hb["wouo"], 0)
        for ot in range(8):
            so_next = wso.issue(hb["wouo"], ot + 1) if ot < 7 else None
            wso.wait_ready(so)
            for t in range(B):
                bi = l.nb()
                l.acquire(bi)
                l.wait(nc.tensor, l.ED)
                last = None
                for pt in range(16):
                    rhs = (xn.t(pt, t * SL, SL) if pt < 8 else cb_fm.t(pt - 8))
                    last = nc.tensor.matmul(l.bank(bi)[0:128, 0:SL],
                                            wso.chunk(so, pt), rhs,
                                            start=(pt == 0), stop=(pt == 15))
                l.inc(last, l.TD)
                l.written(bi)
                wso.use(so)
                l.consume(nc.scalar, bi)
                l.wait(nc.scalar, l.ED)
                l.inc(nc.scalar.activation(h_t[:], l.bank(bi)[:, 0:SL], AF.Tanh,
                                           bias=l.c_bo_fm[:, ot:ot + 1]), l.ED)
                l.freed(bi)
                l.wait(nc.vector, l.ED)
                l.inc(nc.vector.tensor_tensor(x.t(ot, t * SL, SL),
                                              x.t(ot, t * SL, SL), h_t[:],
                                              op=ALU.add), l.ED)
            so = so_next


def emit_project_store(l, whbm, wstream, xn, douts, n_tiles, hbm_out, bias_fm=None,
                       bias_ap=None, act=None, stage=None, stage_n=4,
                       hbm_outs=None):
    nc = l.nc
    ow = wstream.ow
    stage_use = [0] * stage_n
    ctr = [0]

    def fn(ot, c0, w, tag, ps, bi):
        sl = ctr[0] % stage_n
        ctr[0] += 1
        l.consume(nc.scalar, bi)
        l.wait(nc.scalar, l.ST, stage_use[sl])
        bias = bias_fm[:, ot:ot + 1] if bias_fm is not None else bias_ap
        if bias is not None:
            l.inc(nc.scalar.activation(stage[0:ow, sl * 512: sl * 512 + w], ps,
                                       act or AF.Identity, bias=bias), l.ED)
        else:
            l.inc(nc.scalar.activation(stage[0:ow, sl * 512: sl * 512 + w], ps,
                                       act or AF.Copy), l.ED)
        l.freed(bi)
        l.wait(nc.sync, l.ED)
        if hbm_outs is not None:
            dest, orow = hbm_outs[ot // 8], (ot % 8) * ow
        else:
            dest, orow = hbm_out, ot * ow
        l.inc(nc.sync.dma_start(out=dest[orow:orow + ow, c0:c0 + w],
                                in_=stage[0:ow, sl * 512: sl * 512 + w]), l.ST, 16)
        stage_use[sl] = l.val(l.ST)
    emit_project(l, whbm, wstream, xn, douts, n_tiles, fn)


def build_launch1(upto=99):
    nc = bass.Bass("TRN2", target_bir_lowering=False)
    ctx = contextlib.ExitStack()
    with ctx:
        l = L(nc, ctx)
        x_in = l.din("x_in", [1024, NTH], F32)
        hb = {}
        hb["aa_wq"] = l.din("aa_wq", [8, 1024, 128], F32)
        hb["aa_wk"] = l.din("aa_wk", [2, 1024, 128], F32)
        hb["aa_wv"] = l.din("aa_wv", [2, 1024, 128], F32)
        hb["aa_wo"] = l.din("aa_wo", [8, 1024, 128], F32)
        hb["aa_R4"] = l.din("aa_R4", [4, 2, 128, 512], F32)
        hb["aa_R4f"] = l.din("aa_R4f", [4, 128, 512], F32)
        for nm in ("wfuf", "wiui", "wcuc", "wouo"):
            hb[nm] = l.din(nm, [8, 2048, 128], F32)
        hb["ab_wq"] = l.din("ab_wq", [8, 1024, 128], F32)
        hb["ab_wk"] = l.din("ab_wk", [8, 1024, 128], F32)
        hb["ab_wv"] = l.din("ab_wv", [8, 1024, 128], F32)
        consts = l.din("consts1", [128, 32], F32)
        identb = l.din("identb", [128, 128], BF16)
        fconsts = l.din("fconsts", [128, 8], F32)
        onesb = l.din("onesb", [128, 4], BF16)
        xres_o = l.dout("xres", [1024, NT], F32)
        qb_o = l.dout("qb", [1024, NT], BF16)
        kb_o = l.dout("kb", [1024, NT], BF16)
        vb_o = l.dout("vb", [1024, NT], BF16)

        fcs = l.sbuf("fcs", [128, 8], F32)
        l.inc(nc.sync.dma_start(out=fcs[:], in_=fconsts[:]), l.LD, 16)
        for i, v in enumerate(FCONST_VALS):
            nc.const_aps.register(F32, float(v), fcs[:, i:i + 1])
        x = alloc_fm(l, "x", 1024, NTH, F32)
        xn = alloc_fm(l, "xn", 1024, NTH, BF16)
        cst = l.sbuf("cst", [128, 32], F32)
        l.c_ident = l.sbuf("identsb", [128, 128], BF16)
        onesb_sb = l.sbuf("onesbsb", [128, 4], BF16)
        stage = l.sbuf("stage", [128, 4 * 512], BF16)

        load_fm(l, x, x_in)
        l.inc(nc.sync.dma_start(out=cst[:], in_=consts[:]), l.LD, 16)
        l.inc(nc.sync.dma_start(out=l.c_ident[:], in_=identb[:]), l.LD, 16)
        l.inc(nc.sync.dma_start(out=onesb_sb[:], in_=onesb[:]), l.LD, 16)
        l.c_ones128_b = onesb_sb[:, 0:1]
        rowones = l.sbuf("rowones", [1, 128], BF16)
        l.inc(nc.gpsimd.memset(rowones[:], 1.0), l.WD, 1)
        l.c_ones1_128 = rowones[0:1, 0:128]
        l.c_ones1_64 = rowones[0:1, 0:64]
        ind128_2 = l.sbuf("ind1282", [128, 2], BF16)
        l.inc(nc.gpsimd.memset(ind128_2[0:64, 0:1], 1.0), l.WD, 1)
        l.inc(nc.gpsimd.memset(ind128_2[0:64, 1:2], 0.0), l.WD, 1)
        l.inc(nc.gpsimd.memset(ind128_2[64:128, 0:1], 0.0), l.WD, 1)
        l.inc(nc.gpsimd.memset(ind128_2[64:128, 1:2], 1.0), l.WD, 1)
        l.c_ind128_2 = ind128_2[:]
        indrow = l.din("indrow", [2, 128], BF16)
        ind2_128 = l.sbuf("ind2128", [2, 128], BF16)
        l.inc(nc.sync.dma_start(out=ind2_128[:], in_=indrow[:]), l.LD, 16)
        l.c_ind2_128 = ind2_128[:]
        l.c_qnw = cst[:, 0:1]
        l.c_knw = cst[:, 1:2]
        l.c_bf_fm = cst[:, 8:16]
        l.c_bi_fm = cst[:, 16:24]
        l.c_bc_fm = cst[:, 24:32]
        consts2 = l.din("consts2", [128, 40], F32)
        cst2 = l.sbuf("cst2", [128, 40], F32)
        l.inc(nc.sync.dma_start(out=cst2[:], in_=consts2[:]), l.LD, 16)
        l.c_bo_fm = cst2[:, 0:8]
        l.c_inc_fm = cst2[:, 8:16]
        l.c_bqb_fm = cst2[:, 16:24]
        l.c_bkb_fm = cst2[:, 24:32]
        l.c_bvb_fm = cst2[:, 32:40]

        for eng in (nc.tensor, nc.scalar, nc.vector):
            l.wait(eng, l.LD)
            l.wait(eng, l.WD)

        do_ln(l, x, xn, C=NTH)
        if upto >= 1:
            emit_l0aa(l, x, xn, hb)
        if upto >= 2:
            do_ln(l, x, xn, C=NT)
        if upto >= 3:
            emit_l0ffn(l, x, xn, hb, pfx="f0")
        if upto >= 4:
            do_ln(l, x, xn, C=NT)
        ws2 = WStream(l, "abw", 1024)
        if upto >= 4:
            emit_project_store(l, hb["ab_wq"], ws2, xn, 8, MAIN4, qb_o,
                               bias_fm=l.c_bqb_fm, stage=stage)
            emit_project_store(l, hb["ab_wk"], ws2, xn, 8, MAIN4, kb_o,
                               bias_fm=l.c_bkb_fm, stage=stage)
            emit_project_store(l, hb["ab_wv"], ws2, xn, 8, MAIN4, vb_o,
                               bias_fm=l.c_bvb_fm, stage=stage)
        l.wait(nc.sync, l.ED)
        l.wait(nc.sync, l.TD)
        for p in range(8):
            l.inc(nc.sync.dma_start(out=xres_o[p * 128:(p + 1) * 128, :],
                                    in_=x.t(p, 0, NT)), l.ST, 16)
        l.wait(nc.sync, l.ST)
    return nc, l


def _tile_w(w_eff):
    din, dout = w_eff.shape
    return np.ascontiguousarray(
        w_eff.reshape(din, dout // 128, 128).transpose(1, 0, 2))


def _alibi_slopes16():
    import math
    base = 2.0 ** (-2.0 ** (-(math.log2(16) - 3)))
    return (base ** np.arange(1, 17)).astype(np.float32)


def _fmcols(vec):
    return np.ascontiguousarray(vec.reshape(8, 128).T)


def prep_launch1(inp):
    slopes = _alibi_slopes16()
    x = inp["L0_attn_a_x"].astype(np.float32)
    base = {}
    base["aa_wq"] = _tile_w(inp["L0_attn_a_wq"].T)
    base["aa_wk"] = _tile_w(inp["L0_attn_a_wk"].T)
    base["aa_wv"] = _tile_w(inp["L0_attn_a_wv"].T)
    base["aa_wo"] = _tile_w(inp["L0_attn_a_wo"].T)
    for nm, wk, uk in (("wfuf", "L0_ffn_wf", "L0_ffn_uf"),
                       ("wiui", "L0_ffn_wi", "L0_ffn_ui"),
                       ("wcuc", "L0_ffn_wc", "L0_ffn_uc"),
                       ("wouo", "L0_ffn_wo", "L0_ffn_uo")):
        base[nm] = np.ascontiguousarray(np.concatenate(
            [_tile_w(inp[wk].T), _tile_w(inp[uk].T)], axis=1))
    base["ab_wq"] = _tile_w(inp["L0_attn_b_wq"].T)
    base["ab_wk"] = _tile_w(inp["L0_attn_b_wk"].T)
    base["ab_wv"] = _tile_w(inp["L0_attn_b_wv"].T)
    c1 = np.zeros((128, 32), np.float32)
    qnw = inp["L0_attn_a_q_norm_w"]
    knw = inp["L0_attn_a_k_norm_w"]
    c1[:, 0] = np.concatenate([qnw, qnw])
    c1[:, 1] = np.concatenate([knw, knw])
    c1[:, 8:16] = _fmcols(inp["L0_ffn_bf"])
    c1[:, 16:24] = _fmcols(inp["L0_ffn_bi"])
    c1[:, 24:32] = _fmcols(inp["L0_ffn_bc"])
    base["consts1"] = c1
    c2 = np.zeros((128, 40), np.float32)
    c2[:, 0:8] = _fmcols(inp["L0_ffn_bo"])
    c2[:, 8:16] = _fmcols(inp["L0_ffn_in_c"])
    c2[:, 16:24] = _fmcols(inp["L0_attn_b_bq"])
    c2[:, 24:32] = _fmcols(inp["L0_attn_b_bk"])
    c2[:, 32:40] = _fmcols(inp["L0_attn_b_bv"])
    base["consts2"] = c2
    base["identb"] = np.eye(128, dtype=BF16NP)
    ob = np.zeros((128, 4), BF16NP)
    ob[:, 0] = 1
    base["onesb"] = ob
    base["fconsts"] = np.tile(np.array([FCONST_VALS], np.float32), (128, 1))
    ir = np.zeros((2, 128), BF16NP)
    ir[0, 0:64] = 1
    ir[1, 64:128] = 1
    base["indrow"] = ir

    i_idx = np.arange(128)[:, None]
    q_idx = np.arange(128)[None, :]
    R4 = np.zeros((4, 2, 128, 512), np.float32)
    R4f_main = np.zeros((4, 128, 512), np.float32)
    for g in range(4):
        for cls in range(2):
            rel = (i_idx - q_idx) - (128 if cls == 0 else 0)
            ok = (rel <= 0) & (rel >= -HALO)
            for j in range(4):
                h = g + 4 * j
                tile = np.where(ok, slopes[h] * rel, NEG).astype(np.float32)
                R4[g, cls, :, j * 128:(j + 1) * 128] = tile
        R4f_main[g] = R4[g, 0]
    R4f_core0 = np.full((4, 128, 512), NEG, np.float32)

    in_maps = []
    for r in range(NCORES):
        m = dict(base)
        xm = x[:, r * SL:(r + 1) * SL, :].transpose(2, 0, 1).reshape(1024, NT)
        if r == 0:
            xh = np.zeros((1024, B * HALO), np.float32)
        else:
            xh = x[:, r * SL - HALO: r * SL, :].transpose(2, 0, 1).reshape(1024, B * HALO)
        m["x_in"] = np.ascontiguousarray(np.concatenate([xm, xh], axis=1))
        m["aa_R4"] = R4
        m["aa_R4f"] = R4f_core0 if r == 0 else R4f_main
        in_maps.append(m)
    return in_maps


def emit_l0ab(l, x, q_fm, kt_hbm, vt_hbm, scale, qt_w=256):
    nc = l.nc
    l.phase_barrier()
    NKT = S // 128
    with contextlib.ExitStack() as st:
        sb = lambda n, s_, d: st.enter_context(nc.sbuf_tensor(n, s_, d))
        kslot = [sb(f"abk{i}", [128, 8 * 128], BF16) for i in range(2)]
        vslot = [sb(f"abv{i}", [128, 1024], BF16) for i in range(2)]
        esb = sb("abe", [128, 2 * qt_w], BF16)
        avs = sb("abav", [128, qt_w], BF16)
        dens = sb("abden", [1, qt_w], F32)
        dinvf = sb("abdinvf", [1, qt_w], F32)
        dinvb = sb("abdinv", [1, qt_w], BF16)
        slot_user = [-1, -1]

        def issue_kv(idx, b, kt):
            s_ = idx % 2
            l.wait(nc.gpsimd, l.TD,
                   slot_user[s_] if slot_user[s_] >= 0 else l.val(l.TD))
            l.inc(nc.gpsimd.dma_start(
                out=kslot[s_][:].rearrange("p (a o) -> p a o", o=128),
                in_=kt_hbm[b][:, kt * 128:(kt + 1) * 128].rearrange(
                    "(a p) o -> p a o", p=128)), l.WD, 16)
            l.inc(nc.gpsimd.dma_start(
                out=vslot[s_][:], in_=vt_hbm[b][kt * 128:(kt + 1) * 128, :]),
                l.WD, 16)
            return s_, l.val(l.WD)

        eslu = [0, 0]
        nqt = SL // qt_w
        for b in range(B):
            for qt in range(nqt):
                q0 = b * SL + qt * qt_w
                for bk in range(7):
                    l.acquire(bk)
                cur = issue_kv(0, b, 0)
                for kt in range(NKT):
                    nxt = issue_kv(kt + 1, b, kt + 1) if kt + 1 < NKT else None
                    s_, rdy = cur
                    l.wait(nc.tensor, l.WD, rdy)
                    sb_bank = kt % 2
                    l.wait(nc.tensor, l.ED, l.bank_free[sb_bank])
                    last = None
                    for pt in range(8):
                        last = nc.tensor.matmul(
                            l.bank(sb_bank)[0:128, 0:qt_w],
                            kslot[s_][:, pt * 128:(pt + 1) * 128],
                            q_fm.t(pt, q0, qt_w),
                            start=(pt == 0), stop=(pt == 7))
                    l.inc(last, l.TD)
                    l.written(sb_bank)
                    l.consume(nc.scalar, sb_bank)
                    l.wait(nc.scalar, l.TD, eslu[kt % 2])
                    l.inc(nc.scalar.activation(esb[:, (kt % 2) * qt_w:(kt % 2 + 1) * qt_w],
                                               l.bank(sb_bank)[0:128, 0:qt_w],
                                               AF.Exp, scale=scale), l.ED)
                    l.freed(sb_bank)
                    l.wait(nc.tensor, l.ED)
                    nc.tensor.matmul(l.bank(6)[0:1, 0:qt_w], l.c_ones128_b,
                                     esb[:, (kt % 2) * qt_w:(kt % 2 + 1) * qt_w],
                                     start=(kt == 0), stop=(kt == NKT - 1))
                    last = None
                    for c in range(8):
                        last = nc.tensor.matmul(
                            l.bank(2 + c // 2)[0:128, (c % 2) * qt_w:(c % 2 + 1) * qt_w],
                            vslot[s_][:, c * 128:(c + 1) * 128],
                            esb[:, (kt % 2) * qt_w:(kt % 2 + 1) * qt_w],
                            start=(kt == 0), stop=(kt == NKT - 1))
                    l.inc(last, l.TD)
                    slot_user[s_] = l.val(l.TD)
                    eslu[kt % 2] = l.val(l.TD)
                    cur = nxt
                for bk in range(7):
                    l.written(bk)
                l.consume(nc.scalar, 6)
                l.inc(nc.scalar.copy(dens[:], l.bank(6)[0:1, 0:qt_w]), l.ED)
                l.freed(6)
                l.wait(nc.vector, l.ED)
                l.inc(nc.vector.reciprocal(dinvf[:], dens[:]), l.ED)
                l.wait(nc.scalar, l.ED)
                l.inc(nc.scalar.copy(dinvb[:], dinvf[:]), l.ED)
                l.acquire(0)
                l.wait(nc.tensor, l.ED)
                l.inc(nc.tensor.matmul(l.bank(0)[0:128, 0:qt_w], l.c_ones1_128,
                                       dinvb[:], start=True, stop=True), l.TD)
                l.written(0)
                for c in range(8):
                    bk = 2 + c // 2
                    l.consume(nc.scalar, bk)
                    l.wait(nc.scalar, l.ED)
                    l.inc(nc.scalar.copy(
                        avs[:], l.bank(bk)[0:128, (c % 2) * qt_w:(c % 2 + 1) * qt_w]),
                        l.ED)
                    l.consume(nc.vector, 0)
                    l.wait(nc.vector, l.ED)
                    l.inc(nc.vector.tensor_tensor(avs[:], avs[:],
                                                  l.bank(0)[0:128, 0:qt_w],
                                                  op=ALU.mult), l.ED)
                    l.inc(nc.vector.tensor_tensor(x.t(c, q0, qt_w), x.t(c, q0, qt_w),
                                                  avs[:], op=ALU.add), l.ED)
                for bk in range(7):
                    l.freed(bk)


def build_launch2(upto=99):
    nc = bass.Bass("TRN2", target_bir_lowering=False)
    ctx = contextlib.ExitStack()
    with ctx:
        l = L(nc, ctx)
        x_in = l.din("x_in", [1024, NT], F32)
        qb_in = l.din("qb_in", [1024, NT], BF16)
        kt_all = l.din("kt_all", [B, 1024, S], BF16)
        vt_all = l.din("vt_all", [B, S, 1024], BF16)
        hb = {}
        for nm in ("wfuf", "wiui", "wcuc", "wouo"):
            hb[nm] = l.din(nm, [8, 2048, 128], F32)
        hb["wdelta"] = l.din("wdelta", [1, 1024, 16], F32)
        hb["win"] = l.din("win", [1, 1024, 16], F32)
        hb["wgate"] = l.din("wgate", [8, 1024, 128], F32)
        hb["wq1"] = l.din("wq1", [8, 1024, 128], F32)
        hb["wk1"] = l.din("wk1", [8, 1024, 128], F32)
        hb["wv1"] = l.din("wv1", [8, 1024, 128], F32)
        fconsts = l.din("fconsts", [128, 8], F32)
        consts1 = l.din("consts1", [128, 32], F32)
        consts2 = l.din("consts2", [128, 40], F32)
        consts3 = l.din("consts3", [128, 64], F32)
        onesb = l.din("onesb", [128, 4], BF16)
        indrow = l.din("indrow", [2, 128], BF16)

        xres_o = l.dout("xres", [1024, NT], F32)
        delta_o = l.dout("delta", [16, NT], BF16)
        bt_o = l.dout("bt", [16, NT], BF16)
        gate_o = l.dout("gate", [1024, NT], BF16)
        q1_o = l.dout("q1", [1024, NT], BF16)
        k1_o = l.dout("k1", [1024, NT], BF16)
        v1_o = l.dout("v1", [1024, NT], BF16)

        fcs = l.sbuf("fcs", [128, 8], F32)
        l.inc(nc.sync.dma_start(out=fcs[:], in_=fconsts[:]), l.LD, 16)
        for i, v in enumerate(FCONST_VALS):
            nc.const_aps.register(F32, float(v), fcs[:, i:i + 1])
        x = alloc_fm(l, "x", 1024, NT, F32)
        xn = alloc_fm(l, "xn", 1024, NT, BF16)
        cst = l.sbuf("cst", [128, 32], F32)
        cst2 = l.sbuf("cst2", [128, 40], F32)
        cst3 = l.sbuf("cst3", [128, 64], F32)
        onesb_sb = l.sbuf("onesbsb", [128, 4], BF16)
        ind2_128 = l.sbuf("ind2128", [2, 128], BF16)
        rowones = l.sbuf("rowones", [1, 128], BF16)
        stage = l.sbuf("stage", [128, 4 * 512], BF16)

        load_fm(l, x, x_in)
        l.inc(nc.sync.dma_start(out=cst[:], in_=consts1[:]), l.LD, 16)
        l.inc(nc.sync.dma_start(out=cst2[:], in_=consts2[:]), l.LD, 16)
        l.inc(nc.sync.dma_start(out=cst3[:], in_=consts3[:]), l.LD, 16)
        l.inc(nc.sync.dma_start(out=onesb_sb[:], in_=onesb[:]), l.LD, 16)
        l.inc(nc.sync.dma_start(out=ind2_128[:], in_=indrow[:]), l.LD, 16)
        l.inc(nc.gpsimd.memset(rowones[:], 1.0), l.WD, 1)
        l.c_ones128_b = onesb_sb[:, 0:1]
        l.c_ones1_128 = rowones[0:1, 0:128]
        l.c_ones1_64 = rowones[0:1, 0:64]
        l.c_ind2_128 = ind2_128[:]
        l.c_bf_fm = cst[:, 8:16]
        l.c_bi_fm = cst[:, 16:24]
        l.c_bc_fm = cst[:, 24:32]
        l.c_bo_fm = cst2[:, 0:8]
        l.c_inc_fm = cst2[:, 8:16]
        for eng in (nc.tensor, nc.scalar, nc.vector):
            l.wait(eng, l.LD)
            l.wait(eng, l.WD)

        with contextlib.ExitStack() as st3:
            q_sb = FM(st3.enter_context(nc.sbuf_tensor("qown", [128, 8 * NT], BF16)),
                      1024, NT)
            load_fm(l, q_sb, qb_in)
            if upto >= 1:
                emit_l0ab(l, x, q_sb, kt_all, vt_all, scale=1.0 / 32.0)
            l.phase_barrier()
        if upto >= 2:
            do_ln(l, x, xn, C=NT)
            emit_l0ffn(l, x, xn, hb, pfx="f1")
        if upto >= 3:
            xln = alloc_fm(l, "xln", 1024, NT, BF16)
            do_ln(l, x, xln, C=NT)
            xn_a = xn
            do_ln(l, xln, xn_a, C=NT, g_fm=cst3[:, 0:8], b_fm=cst3[:, 8:16])
            ws1 = WStream(l, "wd16", 1024, ow=16)
            emit_project_store(l, hb["wdelta"], ws1, xn_a, 1, MAIN4, delta_o,
                               stage=stage, act=AF.Sigmoid,
                               bias_ap=cst2[0:16, 32:33])
            emit_project_store(l, hb["win"], ws1, xn_a, 1, MAIN4, bt_o,
                               stage=stage, bias_ap=cst2[0:16, 33:34])
            ws8 = WStream(l, "w8", 1024)
            emit_project_store(l, hb["wgate"], ws8, xn_a, 8, MAIN4, gate_o,
                               bias_fm=cst3[:, 32:40], stage=stage,
                               act=AF.Sigmoid)
            do_ln(l, xln, xn_a, C=NT, g_fm=cst3[:, 16:24], b_fm=cst3[:, 24:32])
            emit_project_store(l, hb["wq1"], ws8, xn_a, 8, MAIN4, q1_o,
                               bias_fm=cst3[:, 40:48], stage=stage)
            emit_project_store(l, hb["wk1"], ws8, xn_a, 8, MAIN4, k1_o,
                               bias_fm=cst3[:, 48:56], stage=stage)
            emit_project_store(l, hb["wv1"], ws8, xn_a, 8, MAIN4, v1_o,
                               bias_fm=cst3[:, 56:64], stage=stage)

        l.wait(nc.sync, l.ED)
        l.wait(nc.sync, l.TD)
        for p in range(8):
            l.inc(nc.sync.dma_start(out=xres_o[p * 128:(p + 1) * 128, :],
                                    in_=x.t(p, 0, NT)), l.ST, 16)
        l.wait(nc.sync, l.ST)
    return nc, l





def _tile_w2(w_eff, ow=128):
    din, dout = w_eff.shape
    return np.ascontiguousarray(
        w_eff.reshape(din, dout // ow, ow).transpose(1, 0, 2))


def prep_launch2(inp, l1outs):
    base = {}
    for nm, wk, uk in (("wfuf", "L0_ffn_wf", "L0_ffn_uf"),
                       ("wiui", "L0_ffn_wi", "L0_ffn_ui"),
                       ("wcuc", "L0_ffn_wc", "L0_ffn_uc"),
                       ("wouo", "L0_ffn_wo", "L0_ffn_uo")):
        base[nm] = np.ascontiguousarray(np.concatenate(
            [_tile_w(inp[wk].T), _tile_w(inp[uk].T)], axis=1))
    base["wdelta"] = _tile_w2(inp["L1_attn_a_Wdelta"], ow=16)
    base["win"] = _tile_w2(inp["L1_attn_a_Win"], ow=16)
    base["wgate"] = _tile_w(inp["L1_attn_a_Wgate"])
    base["wq1"] = _tile_w(inp["L1_attn_a_Wq"])
    base["wk1"] = _tile_w(inp["L1_attn_a_Wk"])
    base["wv1"] = _tile_w(inp["L1_attn_a_Wv"])
    base["fconsts"] = np.tile(np.array([FCONST_VALS], np.float32), (128, 1))
    c1 = np.zeros((128, 32), np.float32)
    c1[:, 8:16] = _fmcols(inp["L0_ffn_bf"])
    c1[:, 16:24] = _fmcols(inp["L0_ffn_bi"])
    c1[:, 24:32] = _fmcols(inp["L0_ffn_bc"])
    base["consts1"] = c1
    c2 = np.zeros((128, 40), np.float32)
    c2[:, 0:8] = _fmcols(inp["L0_ffn_bo"])
    c2[:, 8:16] = _fmcols(inp["L0_ffn_in_c"])
    c2[0:16, 32] = inp["L1_attn_a_bdelta"]
    c2[0:16, 33] = inp["L1_attn_a_b_in"]
    base["consts2"] = c2
    c3 = np.zeros((128, 64), np.float32)
    c3[:, 0:8] = _fmcols(inp["L1_attn_a_ln_g"])
    c3[:, 8:16] = _fmcols(inp["L1_attn_a_ln_b"])
    c3[:, 16:24] = _fmcols(inp["L1_attn_a_ln1_g"])
    c3[:, 24:32] = _fmcols(inp["L1_attn_a_ln1_b"])
    c3[:, 32:40] = _fmcols(inp["L1_attn_a_bgate"])
    c3[:, 40:48] = _fmcols(inp["L1_attn_a_bq"])
    c3[:, 48:56] = _fmcols(inp["L1_attn_a_bk"])
    c3[:, 56:64] = _fmcols(inp["L1_attn_a_bv"])
    base["consts3"] = c3
    ob = np.zeros((128, 4), BF16NP)
    ob[:, 0] = 1
    base["onesb"] = ob
    ir = np.zeros((2, 128), BF16NP)
    ir[0, 0:64] = 1
    ir[1, 64:128] = 1
    base["indrow"] = ir

    kt_all = np.zeros((B, 1024, S), BF16NP)
    vt_all = np.zeros((B, S, 1024), BF16NP)
    for r in range(NCORES):
        kb = l1outs[r]["kb"]
        vb = l1outs[r]["vb"]
        for b in range(B):
            kt_all[b][:, r * SL:(r + 1) * SL] = kb[:, b * SL:(b + 1) * SL]
            vt_all[b][r * SL:(r + 1) * SL, :] = vb[:, b * SL:(b + 1) * SL].T
    base["kt_all"] = kt_all
    base["vt_all"] = vt_all

    in_maps = []
    for r in range(NCORES):
        m = dict(base)
        m["x_in"] = l1outs[r]["xres"]
        m["qb_in"] = l1outs[r]["qb"]
        in_maps.append(m)
    return in_maps


NSWEEP = 12


def emit_scan(l, delta_sb, bt_sb, h_a, h_b, abig, i64, h_bf):
    nc = l.nc
    l.phase_barrier()
    l.inc(nc.gpsimd.memset(h_a[:], 0.0), l.WD, 1)
    l.inc(nc.gpsimd.memset(h_b[:], 0.0), l.WD, 1)
    t1b = l.scan_t1b
    src, dst = h_a, h_b
    for sw in range(NSWEEP):
        l.wait(nc.scalar, l.ED)
        l.wait(nc.scalar, l.TD)
        l.wait(nc.scalar, l.WD)
        l.inc(nc.scalar.copy(src[:, 0:1], src[:, S:S + 1]), l.ED)
        for stile in range(8):
            c0 = stile * 512
            l.wait(nc.vector, l.ED)
            l.wait(nc.vector, l.WD)
            l.inc(nc.vector.tensor_tensor(t1b[:, 0:512], src[:, c0:c0 + 512],
                                          delta_sb[:, c0:c0 + 512], op=ALU.mult),
                  l.ED)
            bi = l.nb()
            l.acquire(bi)
            l.wait(nc.tensor, l.ED)
            nc.tensor.matmul(l.bank(bi)[0:64, 0:512], abig, t1b[:, 0:512],
                             start=True, stop=False)
            l.inc(nc.tensor.matmul(l.bank(bi)[0:64, 0:512], i64,
                                   bt_sb[:, c0:c0 + 512], start=False, stop=True),
                  l.TD)
            l.written(bi)
            l.consume(nc.scalar, bi)
            l.inc(nc.scalar.activation(dst[:, 1 + c0:1 + c0 + 512],
                                       l.bank(bi)[0:64, 0:512], AF.Tanh), l.ED)
            l.freed(bi)
        src, dst = dst, src
    l.wait(nc.scalar, l.ED)
    l.inc(nc.scalar.copy(h_bf[:], src[0:64, 1:513]), l.ED)


def emit_l1a_attn(l, ctx_fm, kt1h, vt1h, scale):
    nc = l.nc
    l.phase_barrier()
    NKT = S // 128
    with contextlib.ExitStack() as st:
        sb = lambda n, s_, d: st.enter_context(nc.sbuf_tensor(n, s_, d))
        kslot = [sb(f"l1k{i}", [128, S], BF16) for i in range(2)]
        vslot = [sb(f"l1v{i}", [128, 32 * 128], BF16) for i in range(2)]
        esb = sb("l1e", [128, 2 * 512], BF16)
        avs = sb("l1av", [128, 512], BF16)
        dens = sb("l1den", [1, 512], F32)
        dinvf = sb("l1dinvf", [1, 512], F32)
        dinvb = sb("l1dinv", [1, 512], BF16)
        slot_user = [-1, -1]

        def issue(idx, b, h):
            s_ = idx % 2
            l.wait(nc.gpsimd, l.TD,
                   slot_user[s_] if slot_user[s_] >= 0 else l.val(l.TD))
            l.inc(nc.gpsimd.dma_start(out=kslot[s_][:],
                                      in_=kt1h[b][h * 128:(h + 1) * 128, :]),
                  l.WD, 16)
            l.inc(nc.gpsimd.dma_start(
                out=vslot[s_][:].rearrange("p (a o) -> p a o", o=128),
                in_=vt1h[b, h].rearrange("(a p) o -> p a o", p=128)), l.WD, 16)
            return s_, l.val(l.WD)

        eslu = [0, 0]
        pairs = [(b, h) for b in range(B) for h in range(8)]
        cur = issue(0, *pairs[0])
        for pi, (b, h) in enumerate(pairs):
            nxt = issue(pi + 1, *pairs[pi + 1]) if pi + 1 < len(pairs) else None
            s_, rdy = cur
            l.wait(nc.tensor, l.WD, rdy)
            for bk in (2, 3):
                l.acquire(bk)
            for kt in range(NKT):
                sbk = kt % 2
                l.wait(nc.tensor, l.ED, l.bank_free[sbk])
                l.inc(nc.tensor.matmul(l.bank(sbk)[0:128, 0:512],
                                       kslot[s_][:, kt * 128:(kt + 1) * 128],
                                       ctx_fm.t(h, b * SL, SL),
                                       start=True, stop=True), l.TD)
                l.written(sbk)
                l.consume(nc.scalar, sbk)
                l.wait(nc.scalar, l.TD, eslu[kt % 2])
                l.inc(nc.scalar.activation(esb[:, (kt % 2) * 512:(kt % 2 + 1) * 512],
                                           l.bank(sbk)[0:128, 0:512], AF.Exp,
                                           scale=scale), l.ED)
                l.freed(sbk)
                l.wait(nc.tensor, l.ED)
                nc.tensor.matmul(l.bank(3)[0:1, 0:512], l.c_ones128_b,
                                 esb[:, (kt % 2) * 512:(kt % 2 + 1) * 512],
                                 start=(kt == 0), stop=(kt == NKT - 1))
                l.inc(nc.tensor.matmul(l.bank(2)[0:128, 0:512],
                                       vslot[s_][:, kt * 128:(kt + 1) * 128],
                                       esb[:, (kt % 2) * 512:(kt % 2 + 1) * 512],
                                       start=(kt == 0), stop=(kt == NKT - 1)),
                      l.TD)
                eslu[kt % 2] = l.val(l.TD)
                slot_user[s_] = l.val(l.TD)
            for bk in (2, 3):
                l.written(bk)
            l.consume(nc.scalar, 3)
            l.inc(nc.scalar.copy(dens[:], l.bank(3)[0:1, 0:512]), l.ED)
            l.freed(3)
            l.wait(nc.vector, l.ED)
            l.inc(nc.vector.reciprocal(dinvf[:], dens[:]), l.ED)
            l.wait(nc.scalar, l.ED)
            l.inc(nc.scalar.copy(dinvb[:], dinvf[:]), l.ED)
            l.consume(nc.scalar, 2)
            l.inc(nc.scalar.copy(avs[:], l.bank(2)[0:128, 0:512]), l.ED)
            l.freed(2)
            l.acquire(4)
            l.wait(nc.tensor, l.ED)
            l.inc(nc.tensor.matmul(l.bank(4)[0:128, 0:512], l.c_ones1_128,
                                   dinvb[:], start=True, stop=True), l.TD)
            l.written(4)
            l.consume(nc.vector, 4)
            l.wait(nc.vector, l.ED)
            l.inc(nc.vector.tensor_tensor(ctx_fm.t(h, b * SL, SL), avs[:],
                                          l.bank(4)[0:128, 0:512], op=ALU.mult),
                  l.ED)
            l.freed(4)
            cur = nxt


def build_launch3(upto=99):
    nc = bass.Bass("TRN2", target_bir_lowering=False)
    ctx = contextlib.ExitStack()
    with ctx:
        l = L(nc, ctx)
        x_in = l.din("x_in", [1024, NT], F32)
        gate_in = l.din("gate_in", [1024, NT], BF16)
        q1_in = l.din("q1_in", [1024, NT], BF16)
        delta_in = l.din("delta_in", [64, S], BF16)
        bt_in = l.din("bt_in", [64, S], BF16)
        kt1_all = l.din("kt1_all", [B, 1024, S], BF16)
        vt1h_all = l.din("vt1h_all", [B, 8, S, 128], BF16)
        abig_in = l.din("abig_in", [64, 64], F32)
        i64_in = l.din("i64_in", [64, 64], BF16)
        cmask_in = l.din("cmask_in", [64, B * 8 * 128], F32)
        hb = {}
        hb["wo1"] = l.din("wo1", [8, 1024, 128], F32)
        hb["w1"] = l.din("w1", [32, 1024, 128], F32)
        hb["w2"] = l.din("w2", [8, 4096, 128], F32)
        hb["wfa"] = l.din("wfa", [8, 1024, 128], F32)
        hb["wfb"] = l.din("wfb", [8, 1024, 128], F32)
        hb["wqkv2"] = l.din("wqkv2", [24, 1024, 128], F32)
        g1t_in = l.din("g1t_in", [32, 2 * 4 * 32], F32)
        g2t_in = l.din("g2t_in", [16, 2 * 4 * 16], F32)
        indd_in = l.din("indd_in", [32, 4 * 128], BF16)
        indg_in = l.din("indg_in", [16, 128], BF16)
        fconsts = l.din("fconsts", [128, 8], F32)
        consts4 = l.din("consts4", [128, 96], F32)
        b1fm_in = l.din("b1fm", [128, 32], F32)
        onesb = l.din("onesb", [128, 4], BF16)

        xres_o = l.dout("xres", [1024, NT], F32)
        q2_o = l.dout("q2", [1024, NT], BF16)
        k2_o = l.dout("k2", [1024, NT], BF16)
        v2_o = l.dout("v2", [1024, NT], BF16)

        fcs = l.sbuf("fcs", [128, 8], F32)
        l.inc(nc.sync.dma_start(out=fcs[:], in_=fconsts[:]), l.LD, 16)
        for i, v in enumerate(FCONST_VALS):
            nc.const_aps.register(F32, float(v), fcs[:, i:i + 1])
        cst4 = l.sbuf("cst4", [128, 96], F32)
        b1fm = l.sbuf("b1fm_sb", [128, 32], F32)
        onesb_sb = l.sbuf("onesbsb", [128, 4], BF16)
        rowones = l.sbuf("rowones", [1, 128], BF16)
        stage = l.sbuf("stage", [128, 4 * 512], BF16)
        l.inc(nc.sync.dma_start(out=cst4[:], in_=consts4[:]), l.LD, 16)
        l.inc(nc.sync.dma_start(out=b1fm[:], in_=b1fm_in[:]), l.LD, 16)
        l.inc(nc.sync.dma_start(out=onesb_sb[:], in_=onesb[:]), l.LD, 16)
        l.inc(nc.gpsimd.memset(rowones[:], 1.0), l.WD, 1)
        l.c_ones128_b = onesb_sb[:, 0:1]
        l.c_ones1_128 = rowones[0:1, 0:128]
        l.c_ln2g, l.c_ln2b = cst4[:, 0:8], cst4[:, 8:16]
        l.c_lnfg, l.c_lnfb = cst4[:, 16:24], cst4[:, 24:32]
        l.c_bo1 = cst4[:, 32:40]
        l.c_b2 = cst4[:, 40:48]
        l.c_bfw = cst4[:, 48:56]
        l.c_bqkv2 = cst4[:, 56:80]
        l.c_bri = cst4[:, 80:88]
        for eng in (nc.tensor, nc.scalar, nc.vector):
            l.wait(eng, l.LD)
            l.wait(eng, l.WD)

        fused_d = nc.dram_tensor("fused_d", [1024, NT], BF16, kind="Internal").ap()
        big = contextlib.ExitStack()
        ctx.enter_context(big)
        xln = FM(big.enter_context(nc.sbuf_tensor("xln", [128, 8 * NT], BF16)),
                 1024, NT)
        mo = FM(big.enter_context(nc.sbuf_tensor("mo", [128, 8 * NT], BF16)),
                1024, NT)

        with contextlib.ExitStack() as st0:
            x0 = FM(st0.enter_context(nc.sbuf_tensor("x0", [128, 8 * NT], F32)),
                    1024, NT)
            load_fm(l, x0, x_in)
            do_ln(l, x0, xln, C=NT)
            l.phase_barrier()

        with contextlib.ExitStack() as st1:
            sb1 = lambda n, s_, d: st1.enter_context(nc.sbuf_tensor(n, s_, d))
            delta_sb = sb1("deltasb", [64, S], BF16)
            bt_sb = sb1("btsb", [64, S], BF16)
            h_a = sb1("ha", [64, S + 1], F32)
            h_b = sb1("hbb", [64, S + 1], F32)
            abig_sb = sb1("abig", [64, 64], BF16)
            i64_sb = sb1("i64", [64, 64], BF16)
            h_bf = sb1("hbf", [64, 512], BF16)
            cmask_sb = sb1("cmask", [64, B * 8 * 128], BF16)
            gate_fm = FM(sb1("gate", [128, 8 * NT], BF16), 1024, NT)
            l.scan_t1b = sb1("t1b", [64, 512], BF16)
            l.inc(nc.sync.dma_start(out=delta_sb[:], in_=delta_in[:]), l.LD, 16)
            l.inc(nc.sync.dma_start(out=bt_sb[:], in_=bt_in[:]), l.LD, 16)
            l.inc(nc.gpsimd.dma_start(out=abig_sb[:], in_=abig_in[:]), l.WD, 16)
            l.inc(nc.sync.dma_start(out=i64_sb[:], in_=i64_in[:]), l.LD, 16)
            l.inc(nc.gpsimd.dma_start(out=cmask_sb[:], in_=cmask_in[:]), l.WD, 16)
            load_fm(l, gate_fm, gate_in)
            if upto >= 1:
                emit_scan(l, delta_sb, bt_sb, h_a, h_b, abig_sb[:], i64_sb[:], h_bf)
                for b in range(B):
                    for ot in range(8):
                        bi = l.nb()
                        l.acquire(bi)
                        l.wait(nc.tensor, l.ED)
                        l.wait(nc.tensor, l.WD)
                        l.inc(nc.tensor.matmul(
                            l.bank(bi)[0:128, 0:512],
                            cmask_sb[:, (b * 8 + ot) * 128:(b * 8 + ot + 1) * 128],
                            h_bf[:], start=True, stop=True), l.TD)
                        l.written(bi)
                        l.consume(nc.vector, bi)
                        l.wait(nc.vector, l.ED)
                        l.inc(nc.vector.tensor_tensor(
                            mo.t(ot, b * SL, SL), l.bank(bi)[0:128, 0:512],
                            gate_fm.t(ot, b * SL, SL), op=ALU.mult), l.ED)
                        l.inc(nc.vector.tensor_tensor(
                            mo.t(ot, b * SL, SL), mo.t(ot, b * SL, SL),
                            xln.t(ot, b * SL, SL), op=ALU.add), l.ED)
                        l.freed(bi)
            l.phase_barrier()

        with contextlib.ExitStack() as st3:
            ctxq = FM(st3.enter_context(nc.sbuf_tensor("ctxq", [128, 8 * NT], BF16)),
                      1024, NT)
            load_fm(l, ctxq, q1_in)
            if upto >= 2:
                emit_l1a_attn(l, ctxq, kt1_all, vt1h_all, scale=1.0 / (128 ** 0.5))
                wso = WStream(l, "wso1", 1024, stack=st3)

                def x2_evac(ot, c0, w, tag, ps, bi):
                    l.consume(nc.vector, bi)
                    l.wait(nc.vector, l.ED)
                    l.inc(nc.vector.tensor_tensor(xln.t(ot, c0, w), xln.t(ot, c0, w),
                                                  ps, op=ALU.add), l.ED)
                    l.freed(bi)
                    l.wait(nc.scalar, l.ED)
                    l.inc(nc.scalar.add(xln.t(ot, c0, w), xln.t(ot, c0, w),
                                        l.c_bo1[:, ot:ot + 1]), l.ED)
                emit_project(l, hb["wo1"], wso, ctxq, 8, MAIN4, x2_evac)
            l.phase_barrier()

        if upto >= 3:
            with contextlib.ExitStack() as st5:
                sb5 = lambda n, s_, d: st5.enter_context(nc.sbuf_tensor(n, s_, d))
                fused = FM(sb5("fusedsb", [128, 8 * 512], BF16), 1024, 512)
                xn2 = FM(sb5("xn2", [128, 8 * 512], BF16), 1024, 512)
                t1f = FM(sb5("t1f", [128, 32 * 512], BF16), 4096, 512)
                tout = FM(sb5("tout", [128, 8 * 512], BF16), 1024, 512)
                fpre = FM(sb5("fpre", [128, 8 * 512], BF16), 1024, 512)
                ws_w1 = WStream(l, "wsw1", 1024, stack=st5)
                ws_w2 = WStream(l, "wsw2", 4096, stack=st5)
                ws_fa = WStream(l, "wsfa", 1024, stack=st5)
                ws_fb = WStream(l, "wsfb", 1024, stack=st5)
                for nt in range(4):
                    c0 = nt * 512
                    do_ln(l, xln, xn2, C=512, g_fm=l.c_ln2g, b_fm=l.c_ln2b,
                          window=(c0, 512))

                    def t1_evac(ot, cc0, w, tag, ps, bi):
                        l.consume(nc.scalar, bi)
                        l.wait(nc.scalar, l.ED)
                        l.inc(nc.scalar.activation(t1f.t(ot, 0, 512), ps, AF.Relu,
                                                   bias=b1fm[:, ot:ot + 1]), l.ED)
                        l.freed(bi)
                    emit_project(l, hb["w1"], ws_w1, xn2, 32, [(0, 512, 0)], t1_evac)

                    def t2_evac(ot, cc0, w, tag, ps, bi):
                        l.consume(nc.scalar, bi)
                        l.wait(nc.scalar, l.ED)
                        l.inc(nc.scalar.activation(tout.t(ot, 0, 512), ps,
                                                   AF.Identity,
                                                   bias=l.c_b2[:, ot:ot + 1]), l.ED)
                        l.freed(bi)
                        l.wait(nc.vector, l.ED)
                        l.inc(nc.vector.tensor_tensor(tout.t(ot, 0, 512),
                                                      tout.t(ot, 0, 512),
                                                      xln.t(ot, c0, 512),
                                                      op=ALU.add), l.ED)
                    emit_project(l, hb["w2"], ws_w2, t1f, 8, [(0, 512, 0)], t2_evac)

                    sfa = ws_fa.issue(hb["wfa"], 0)
                    sfb = ws_fb.issue(hb["wfb"], 0)
                    for ot in range(8):
                        nfa = ws_fa.issue(hb["wfa"], ot + 1) if ot < 7 else None
                        nfb = ws_fb.issue(hb["wfb"], ot + 1) if ot < 7 else None
                        ws_fa.wait_ready(sfa)
                        ws_fb.wait_ready(sfb)
                        bi = l.nb()
                        l.acquire(bi)
                        l.wait(nc.tensor, l.ED)
                        for pt in range(8):
                            nc.tensor.matmul(l.bank(bi)[0:128, 0:512],
                                             ws_fa.chunk(sfa, pt),
                                             mo.t(pt, c0, 512),
                                             start=(pt == 0), stop=False)
                        last = None
                        for pt in range(8):
                            last = nc.tensor.matmul(l.bank(bi)[0:128, 0:512],
                                                    ws_fb.chunk(sfb, pt),
                                                    tout.t(pt, 0, 512),
                                                    start=False, stop=(pt == 7))
                        l.inc(last, l.TD)
                        l.written(bi)
                        ws_fa.use(sfa)
                        ws_fb.use(sfb)
                        l.consume(nc.scalar, bi)
                        l.wait(nc.scalar, l.ED)
                        l.inc(nc.scalar.activation(fpre.t(ot, 0, 512), ps_ := l.bank(bi)[0:128, 0:512],
                                                   AF.Identity,
                                                   bias=l.c_bfw[:, ot:ot + 1]), l.ED)
                        l.freed(bi)
                        sfa, sfb = nfa, nfb
                    do_ln_to(l, fpre, fused, out_c0=0, g_fm=l.c_lnfg, b_fm=l.c_lnfb)
                    l.wait(nc.sync, l.ED)
                    for p in range(8):
                        l.inc(nc.sync.dma_start(
                            out=fused_d[p * 128:(p + 1) * 128, c0:c0 + 512],
                            in_=fused.t(p)), l.ST, 16)
        big.close()
        l.phase_barrier()

        with contextlib.ExitStack() as st6:
            x6 = FM(st6.enter_context(nc.sbuf_tensor("x6", [128, 8 * NT], F32)),
                    1024, NT)
            xnb = FM(st6.enter_context(nc.sbuf_tensor("xnb", [128, 8 * NT], BF16)),
                     1024, NT)
            fscr = st6.enter_context(nc.sbuf_tensor("fscr", [128, NT], BF16))
            load_fm(l, x6, x_in)
            if upto >= 3:
                for ot in range(8):
                    l.wait(nc.sync, l.ST)
                    l.wait(nc.sync, l.ED)
                    l.inc(nc.sync.dma_start(out=fscr[:],
                                            in_=fused_d[ot * 128:(ot + 1) * 128, :]),
                          l.LD, 16)
                    l.wait(nc.vector, l.LD)
                    l.inc(nc.vector.tensor_tensor(x6.t(ot), x6.t(ot), fscr[:],
                                                  op=ALU.add), l.ED)
            if upto >= 4:
                do_ln(l, x6, xnb, C=NT)
                emit_l1ffn(l, x6, xnb, g1t_in, g2t_in, indd_in, indg_in, st6)
            if upto >= 5:
                do_ln(l, x6, xnb, C=NT)
                ws7 = WStream(l, "ws7", 1024, stack=st6)
                emit_project_store(l, hb["wqkv2"], ws7, xnb, 24, MAIN4,
                                   None, bias_fm=l.c_bqkv2, stage=stage,
                                   hbm_outs=(q2_o, k2_o, v2_o))
            l.wait(nc.sync, l.ED)
            l.wait(nc.sync, l.TD)
            for p in range(8):
                l.inc(nc.sync.dma_start(out=xres_o[p * 128:(p + 1) * 128, :],
                                        in_=x6.t(p, 0, NT)), l.ST, 16)
            l.wait(nc.sync, l.ST)
    return nc, l


def do_ln_to(l, xsrc, xdst, out_c0, g_fm=None, b_fm=None):
    view = FM(xdst.sb, xdst.rows, xdst.C)
    class _V:
        np_ = xdst.np_
        C = xdst.C

        def t(self, p, c0=0, w=None):
            w = 512 - c0 if w is None else w
            return xdst.sb[:, p * xdst.C + out_c0 + c0:
                           p * xdst.C + out_c0 + c0 + w]
    do_ln(l, xsrc, _V(), C=512, g_fm=g_fm, b_fm=b_fm)


def emit_l1ffn(l, x, xn, g1t_in, g2t_in, indd_in, indg_in, stack):
    nc = l.nc
    l.phase_barrier()
    sb = lambda n, s_, d: stack.enter_context(nc.sbuf_tensor(n, s_, d))
    g1t = sb("g1t", [32, 256], BF16)
    g2t = sb("g2t", [16, 128], BF16)
    indd = sb("indd", [32, 4 * 128], BF16)
    indg = sb("indg", [16, 128], BF16)
    g1rep = sb("g1rep", [32, 512], BF16)
    g2rep = sb("g2rep", [16, 512], BF16)
    wacc = sb("wacc", [128, 512], F32)
    wtmp = sb("wtmp", [128, 512], F32)
    a_sb = sb("a_sb", [128, 512], BF16)
    wrT = sb("wrT", [128, 2048], BF16)
    wiT = sb("wiT", [128, 2048], BF16)
    wnT = sb("wnT", [128, 2048], BF16)
    stg = sb("l1fstg", [128, 512], BF16)
    l.inc(nc.gpsimd.dma_start(out=g1t[:], in_=g1t_in[:]), l.WD, 16)
    l.inc(nc.gpsimd.dma_start(out=g2t[:], in_=g2t_in[:]), l.WD, 16)
    l.inc(nc.sync.dma_start(out=indd[:], in_=indd_in[:]), l.LD, 16)
    l.inc(nc.sync.dma_start(out=indg[:], in_=indg_in[:]), l.LD, 16)
    l.phase_barrier()
    for wi_ in range(2):
        dst = (wrT, wiT)[wi_]
        for pt in range(4):
            for c in range(4):
                l.wait(nc.vector, l.ED)
                l.wait(nc.vector, l.WD)
                l.wait(nc.vector, l.TD)
                l.inc(nc.vector.tensor_copy(
                    g1rep[:].rearrange("p (b e) -> p b e", e=16),
                    g1t[:, (wi_ * 4 + c) * 32:(wi_ * 4 + c) * 32 + 32]
                    .unsqueeze(2).broadcast_to([32, 32, 16])), l.ED)
                l.inc(nc.vector.tensor_copy(
                    g2rep[:].rearrange("p (b e) -> p b e", e=16),
                    g2t[:, (wi_ * 4 + c) * 16:(wi_ * 4 + c) * 16 + 16]
                    .unsqueeze(1).broadcast_to([16, 32, 16])), l.ED)
                bA, bB = l.nb(), l.nb()
                l.acquire(bA)
                l.acquire(bB)
                l.wait(nc.tensor, l.ED)
                l.inc(nc.tensor.matmul(l.bank(bA)[0:128, 0:512],
                                       indd[:, pt * 128:(pt + 1) * 128],
                                       g1rep[:], start=True, stop=True), l.TD)
                l.inc(nc.tensor.matmul(l.bank(bB)[0:128, 0:512],
                                       indg[:], g2rep[:], start=True, stop=True),
                      l.TD)
                l.written(bA)
                l.written(bB)
                l.consume(nc.scalar, bA)
                l.inc(nc.scalar.copy(a_sb[:], l.bank(bA)[0:128, 0:512]), l.ED)
                l.freed(bA)
                l.consume(nc.vector, bB)
                l.wait(nc.vector, l.ED)
                l.inc(nc.vector.tensor_tensor(wtmp[:], a_sb[:],
                                              l.bank(bB)[0:128, 0:512],
                                              op=ALU.mult), l.ED)
                l.freed(bB)
                if c == 0:
                    l.inc(nc.vector.tensor_copy(wacc[:], wtmp[:]), l.ED)
                else:
                    l.inc(nc.vector.tensor_tensor(wacc[:], wacc[:], wtmp[:],
                                                  op=ALU.add), l.ED)
            l.wait(nc.scalar, l.ED)
            l.inc(nc.scalar.copy(dst[:, pt * 512:(pt + 1) * 512], wacc[:]), l.ED)
            if wi_ == 1:
                l.inc(nc.scalar.activation(wnT[:, pt * 512:(pt + 1) * 512],
                                           wacc[:], AF.Identity, scale=-1.0), l.ED)
    def chunk(buf, pt, ot):
        return buf[:, pt * 512 + ot * 128: pt * 512 + ot * 128 + 128]
    for half in range(2):
        for ot in range(4):
            for nt in range(4):
                c0 = nt * 512
                bi = l.nb()
                l.acquire(bi)
                l.wait(nc.tensor, l.ED)
                last = None
                lA = (wrT, wiT)[half]
                lB = (wnT, wrT)[half]
                for pt in range(4):
                    nc.tensor.matmul(l.bank(bi)[0:128, 0:512], chunk(lA, pt, ot),
                                     xn.t(pt, c0, 512), start=(pt == 0), stop=False)
                for pt in range(4):
                    last = nc.tensor.matmul(l.bank(bi)[0:128, 0:512],
                                            chunk(lB, pt, ot),
                                            xn.t(4 + pt, c0, 512),
                                            start=False, stop=(pt == 3))
                l.inc(last, l.TD)
                l.written(bi)
                l.consume(nc.scalar, bi)
                l.wait(nc.scalar, l.ST)
                l.inc(nc.scalar.activation(stg[:], l.bank(bi)[0:128, 0:512],
                                           AF.Identity,
                                           bias=l.c_bri[:, half * 4 + ot:
                                                        half * 4 + ot + 1]), l.ED)
                l.freed(bi)
                l.wait(nc.vector, l.ED)
                l.inc(nc.vector.tensor_tensor(x.t(half * 4 + ot, c0, 512),
                                              x.t(half * 4 + ot, c0, 512),
                                              stg[:], op=ALU.add), l.ED)


def prep_launch3(inp, l2outs):
    base = {}
    A = inp["L1_attn_a_A"].astype(np.float32)
    abig = np.zeros((64, 64), np.float32)
    for b in range(B):
        abig[b * 16:(b + 1) * 16, b * 16:(b + 1) * 16] = A
    base["abig_in"] = abig
    base["i64_in"] = np.eye(64, dtype=BF16NP)
    C_ = inp["L1_attn_a_C"].astype(np.float32)
    cm = np.zeros((64, B * 8 * 128), np.float32)
    for b in range(B):
        for ot in range(8):
            cm[b * 16:(b + 1) * 16, (b * 8 + ot) * 128:(b * 8 + ot + 1) * 128] = \
                C_[:, ot * 128:(ot + 1) * 128]
    base["cmask_in"] = cm
    base["wo1"] = _tile_w(inp["L1_attn_a_Wo"])
    base["w1"] = _tile_w(inp["L1_attn_a_W1"])
    base["w2"] = _tile_w(inp["L1_attn_a_W2"])
    base["wfa"] = _tile_w(inp["L1_attn_a_Wf"][0:1024])
    base["wfb"] = _tile_w(inp["L1_attn_a_Wf"][1024:2048])
    base["wqkv2"] = _tile_w(inp["L1_attn_b_w_qkv"].T)
    g1 = np.zeros((32, 256), np.float32)
    g2 = np.zeros((16, 128), np.float32)
    for wi_, (k1, k2) in enumerate((("L1_ffn_G1_R", "L1_ffn_G2_R"),
                                    ("L1_ffn_G1_I", "L1_ffn_G2_I"))):
        G1 = inp[k1].astype(np.float32)[0]
        G2 = inp[k2].astype(np.float32)
        for c in range(4):
            g1[:, wi_ * 128 + c * 32: wi_ * 128 + (c + 1) * 32] = G1[:, c, :].T
            g2[:, wi_ * 64 + c * 16: wi_ * 64 + (c + 1) * 16] = G2[c, :, 0, :].T
    base["g1t_in"] = g1
    base["g2t_in"] = g2
    indd = np.zeros((32, 4 * 128), BF16NP)
    for p in range(4):
        for loc in range(128):
            indd[8 * p + loc // 16, p * 128 + loc] = 1
    base["indd_in"] = indd
    indg = np.zeros((16, 128), BF16NP)
    for loc in range(128):
        indg[loc % 16, loc] = 1
    base["indg_in"] = indg
    base["fconsts"] = np.tile(np.array([FCONST_VALS], np.float32), (128, 1))
    c4 = np.zeros((128, 96), np.float32)
    c4[:, 0:8] = _fmcols(inp["L1_attn_a_ln2_g"])
    c4[:, 8:16] = _fmcols(inp["L1_attn_a_ln2_b"])
    c4[:, 16:24] = _fmcols(inp["L1_attn_a_lnf_g"])
    c4[:, 24:32] = _fmcols(inp["L1_attn_a_lnf_b"])
    c4[:, 32:40] = _fmcols(inp["L1_attn_a_bo"])
    c4[:, 40:48] = _fmcols(inp["L1_attn_a_b2"])
    c4[:, 48:56] = _fmcols(inp["L1_attn_a_bf"])
    c4[:, 56:80] = np.ascontiguousarray(
        inp["L1_attn_b_b_qkv"].astype(np.float32).reshape(24, 128).T)
    c4[:, 80:84] = np.ascontiguousarray(
        inp["L1_ffn_bias_real"].astype(np.float32).reshape(4, 128).T)
    c4[:, 84:88] = np.ascontiguousarray(
        inp["L1_ffn_bias_imag"].astype(np.float32).reshape(4, 128).T)
    base["consts4"] = c4
    base["b1fm"] = np.ascontiguousarray(
        inp["L1_attn_a_b1"].astype(np.float32).reshape(32, 128).T)
    ob = np.zeros((128, 4), BF16NP)
    ob[:, 0] = 1
    base["onesb"] = ob

    delta_g = np.zeros((64, S), BF16NP)
    bt_g = np.zeros((64, S), BF16NP)
    kt1_all = np.zeros((B, 1024, S), BF16NP)
    vt1h_all = np.zeros((B, 8, S, 128), BF16NP)
    for r in range(NCORES):
        d_r, b_r = l2outs[r]["delta"], l2outs[r]["bt"]
        k_r, v_r = l2outs[r]["k1"], l2outs[r]["v1"]
        for b in range(B):
            delta_g[b * 16:(b + 1) * 16, r * SL:(r + 1) * SL] = d_r[:, b * SL:(b + 1) * SL]
            bt_g[b * 16:(b + 1) * 16, r * SL:(r + 1) * SL] = b_r[:, b * SL:(b + 1) * SL]
            kt1_all[b][:, r * SL:(r + 1) * SL] = k_r[:, b * SL:(b + 1) * SL]
            for h in range(8):
                vt1h_all[b, h, r * SL:(r + 1) * SL, :] = \
                    v_r[h * 128:(h + 1) * 128, b * SL:(b + 1) * SL].T
    base["kt1_all"] = kt1_all
    base["vt1h_all"] = vt1h_all

    in_maps = []
    for r in range(NCORES):
        m = dict(base)
        dr = np.roll(delta_g, -r * SL, axis=1).copy()
        br = np.roll(bt_g, -r * SL, axis=1).copy()
        cut = (S - r * SL) % S
        dr[:, cut] = 0
        m["delta_in"] = dr
        m["bt_in"] = br
        m["x_in"] = l2outs[r]["xres"]
        m["gate_in"] = l2outs[r]["gate"]
        m["q1_in"] = l2outs[r]["q1"]
        in_maps.append(m)
    return in_maps


def emit_l1ab(l, ctxq, kt2h, vt2h, qext_sb, btab_sb, rtri_sb, kext_sb, dbg=None):
    nc = l.nc
    l.phase_barrier()
    NKT = S // 128
    with contextlib.ExitStack() as st:
        sb = lambda n, s_, d: st.enter_context(nc.sbuf_tensor(n, s_, d))
        kslot = [sb(f"l2k{i}", [64, S], BF16) for i in range(2)]
        qtmp = sb("l2qt", [64, 512], BF16)
        vslot = [sb(f"l2v{i}", [128, 32 * 65], BF16) for i in range(2)]
        esb = sb("l2e", [128, 2 * 512], BF16)
        avs = sb("l2av", [64, 512], BF16)
        dens = sb("l2den", [1, 512], F32)
        dinvf = sb("l2dinvf", [1, 512], F32)
        dinvb = sb("l2dinv", [1, 512], BF16)
        slot_user = [-1, -1]

        def issue(idx, b, h):
            s_ = idx % 2
            l.wait(nc.gpsimd, l.TD,
                   slot_user[s_] if slot_user[s_] >= 0 else l.val(l.TD))
            l.inc(nc.gpsimd.dma_start(out=kslot[s_][:], in_=kt2h[b, h]), l.WD, 16)
            l.inc(nc.gpsimd.dma_start(
                out=vslot[s_][:].rearrange("p (a o) -> p a o", o=65),
                in_=vt2h[b, h].rearrange("(a p) o -> p a o", p=128)), l.WD, 16)
            return s_, l.val(l.WD)

        eslu = [0, 0]
        pairs = [(b, h) for b in range(B) for h in range(16)]
        cur = issue(0, *pairs[0])
        for pi, (b, h) in enumerate(pairs):
            nxt = issue(pi + 1, *pairs[pi + 1]) if pi + 1 < len(pairs) else None
            s_, rdy = cur
            l.wait(nc.tensor, l.WD, rdy)
            l.acquire(2)
            l.wait(nc.scalar, l.TD)
            l.wait(nc.scalar, l.ED)
            l.inc(nc.scalar.copy(qtmp[:], ctxq.rs(h // 2, 64 * (h % 2), 64,
                                                  b * SL, SL)), l.ED)
            for kt in range(NKT):
                sbk = kt % 2
                l.wait(nc.tensor, l.ED, l.bank_free[sbk])
                l.wait(nc.tensor, l.ED)
                nc.tensor.matmul(l.bank(sbk)[0:128, 0:512],
                                 kslot[s_][0:64, kt * 128:(kt + 1) * 128],
                                 qtmp[:], start=True, stop=False)
                l.inc(nc.tensor.matmul(l.bank(sbk)[0:128, 0:512],
                                       kext_sb[:], qext_sb[:],
                                       start=False, stop=True), l.TD)
                l.written(sbk)
                if kt < 4:
                    l.consume(nc.vector, sbk)
                    l.wait(nc.vector, l.TD, eslu[kt % 2])
                    l.wait(nc.vector, l.ED)
                    l.inc(nc.vector.scalar_tensor_tensor(
                        esb[:, (kt % 2) * 512:(kt % 2 + 1) * 512],
                        l.bank(sbk)[0:128, 0:512], 1.0 / 8.0,
                        rtri_sb[:, kt * 512:(kt + 1) * 512],
                        op0=ALU.mult, op1=ALU.add), l.ED)
                    l.freed(sbk)
                    l.wait(nc.scalar, l.ED)
                    l.inc(nc.scalar.activation(
                        esb[:, (kt % 2) * 512:(kt % 2 + 1) * 512],
                        esb[:, (kt % 2) * 512:(kt % 2 + 1) * 512], AF.Exp), l.ED)
                else:
                    l.consume(nc.scalar, sbk)
                    l.wait(nc.scalar, l.TD, eslu[kt % 2])
                    l.inc(nc.scalar.activation(
                        esb[:, (kt % 2) * 512:(kt % 2 + 1) * 512],
                        l.bank(sbk)[0:128, 0:512], AF.Exp, scale=1.0 / 8.0,
                        bias=btab_sb[:, kt:kt + 1]), l.ED)
                    l.freed(sbk)
                if dbg is not None and pi == 0 and kt < 4:
                    l.wait(nc.sync, l.ED)
                    l.inc(nc.sync.dma_start(
                        out=dbg["esb"][kt * 128:(kt + 1) * 128, :],
                        in_=esb[:, (kt % 2) * 512:(kt % 2 + 1) * 512]), l.ST, 16)
                l.wait(nc.tensor, l.ED)
                l.wait(nc.tensor, l.ST)
                l.inc(nc.tensor.matmul(l.bank(2)[0:65, 0:512],
                                       vslot[s_][:, kt * 65:(kt + 1) * 65],
                                       esb[:, (kt % 2) * 512:(kt % 2 + 1) * 512],
                                       start=(kt == 0), stop=(kt == NKT - 1)),
                      l.TD)
                eslu[kt % 2] = l.val(l.TD)
                slot_user[s_] = l.val(l.TD)
            l.written(2)
            l.consume(nc.scalar, 2)
            l.inc(nc.scalar.copy(dens[:], l.bank(2)[64:65, 0:512]), l.ED)
            l.inc(nc.scalar.copy(avs[:], l.bank(2)[0:64, 0:512]), l.ED)
            l.freed(2)
            l.wait(nc.vector, l.ED)
            l.inc(nc.vector.reciprocal(dinvf[:], dens[:]), l.ED)
            l.wait(nc.scalar, l.ED)
            l.inc(nc.scalar.copy(dinvb[:], dinvf[:]), l.ED)
            l.acquire(3)
            l.wait(nc.tensor, l.ED)
            l.inc(nc.tensor.matmul(l.bank(3)[0:64, 0:512], l.c_ones1_64,
                                   dinvb[:], start=True, stop=True), l.TD)
            l.written(3)
            l.consume(nc.vector, 3)
            l.wait(nc.vector, l.ED)
            l.inc(nc.vector.tensor_tensor(
                ctxq.rs(h // 2, 64 * (h % 2), 64, b * SL, SL), avs[:],
                l.bank(3)[0:64, 0:512], op=ALU.mult), l.ED)
            l.freed(3)
            cur = nxt


def build_launch4(upto=99):
    nc = bass.Bass("TRN2", target_bir_lowering=False)
    ctx = contextlib.ExitStack()
    with ctx:
        l = L(nc, ctx)
        x_in = l.din("x_in", [1024, NT], F32)
        q2_in = l.din("q2_in", [1024, NT], BF16)
        kt2h_all = l.din("kt2h_all", [B, 16, 64, S], BF16)
        kext_in = l.din("kext_in", [3, 128], BF16)
        vt2h_all = l.din("vt2h_all", [B, 16, S, 65], BF16)
        qext_in = l.din("qext_in", [3, 512], BF16)
        btab_in = l.din("btab_in", [128, 32], F32)
        rtri_in = l.din("rtri_in", [128, 4 * 512], F32)
        hb = {}
        hb["wout2"] = l.din("wout2", [8, 1024, 128], F32)
        g1t_in = l.din("g1t_in", [32, 2 * 4 * 32], F32)
        g2t_in = l.din("g2t_in", [16, 2 * 4 * 16], F32)
        indd_in = l.din("indd_in", [32, 4 * 128], BF16)
        indg_in = l.din("indg_in", [16, 128], BF16)
        fconsts = l.din("fconsts", [128, 8], F32)
        consts5 = l.din("consts5", [128, 24], F32)
        onesb = l.din("onesb", [128, 4], BF16)

        xres_o = l.dout("xres", [1024, NT], F32)

        fcs = l.sbuf("fcs", [128, 8], F32)
        l.inc(nc.sync.dma_start(out=fcs[:], in_=fconsts[:]), l.LD, 16)
        for i, v in enumerate(FCONST_VALS):
            nc.const_aps.register(F32, float(v), fcs[:, i:i + 1])
        cst5 = l.sbuf("cst5", [128, 24], F32)
        onesb_sb = l.sbuf("onesbsb", [128, 4], BF16)
        rowones = l.sbuf("rowones", [1, 128], BF16)
        qext_sb = l.sbuf("qext", [3, 512], BF16)
        kext_sb = l.sbuf("kext", [3, 128], BF16)
        btab_sb = l.sbuf("btab", [128, 32], F32)
        rtri_sb = l.sbuf("rtri", [128, 4 * 512], F32)
        l.inc(nc.sync.dma_start(out=cst5[:], in_=consts5[:]), l.LD, 16)
        l.inc(nc.sync.dma_start(out=onesb_sb[:], in_=onesb[:]), l.LD, 16)
        l.inc(nc.sync.dma_start(out=qext_sb[:], in_=qext_in[:]), l.LD, 16)
        l.inc(nc.sync.dma_start(out=kext_sb[:], in_=kext_in[:]), l.LD, 16)
        l.inc(nc.sync.dma_start(out=btab_sb[:], in_=btab_in[:]), l.LD, 16)
        l.inc(nc.sync.dma_start(out=rtri_sb[:], in_=rtri_in[:]), l.LD, 16)
        l.inc(nc.gpsimd.memset(rowones[:], 1.0), l.WD, 1)
        l.c_ones128_b = onesb_sb[:, 0:1]
        l.c_ones1_128 = rowones[0:1, 0:128]
        l.c_ones1_64 = rowones[0:1, 0:64]
        l.c_bout = cst5[:, 0:8]
        l.c_bri = cst5[:, 8:16]
        for eng in (nc.tensor, nc.scalar, nc.vector):
            l.wait(eng, l.LD)
            l.wait(eng, l.WD)

        x = alloc_fm(l, "x", 1024, NT, F32)
        xn = alloc_fm(l, "xn", 1024, NT, BF16)
        load_fm(l, x, x_in)

        with contextlib.ExitStack() as st7:
            ctxq = FM(st7.enter_context(nc.sbuf_tensor("ctxq", [128, 8 * NT], BF16)),
                      1024, NT)
            load_fm(l, ctxq, q2_in)
            if upto >= 1:
                emit_l1ab(l, ctxq, kt2h_all, vt2h_all, qext_sb[:], btab_sb, rtri_sb,
                          kext_sb[:])
                wso = WStream(l, "wso2", 1024, stack=st7)

                def out_evac(ot, c0, w, tag, ps, bi):
                    l.consume(nc.scalar, bi)
                    l.wait(nc.scalar, l.ED)
                    l.inc(nc.scalar.activation(l.stage4[:, 0:w], ps, AF.Identity,
                                               bias=l.c_bout[:, ot:ot + 1]), l.ED)
                    l.freed(bi)
                    l.wait(nc.vector, l.ED)
                    l.inc(nc.vector.tensor_tensor(x.t(ot, c0, w), x.t(ot, c0, w),
                                                  l.stage4[:, 0:w], op=ALU.add),
                          l.ED)
                l.stage4 = st7.enter_context(nc.sbuf_tensor("stg4", [128, 512], BF16))
                emit_project(l, hb["wout2"], wso, ctxq, 8, MAIN4, out_evac)
            l.phase_barrier()

        if upto >= 2:
            with contextlib.ExitStack() as st8:
                do_ln(l, x, xn, C=NT)
                emit_l1ffn(l, x, xn, g1t_in, g2t_in, indd_in, indg_in, st8)

        l.wait(nc.sync, l.ED)
        l.wait(nc.sync, l.TD)
        for p in range(8):
            l.inc(nc.sync.dma_start(out=xres_o[p * 128:(p + 1) * 128, :],
                                    in_=x.t(p, 0, NT)), l.ST, 16)
        l.wait(nc.sync, l.ST)
    return nc, l


def prep_launch4(inp, l3outs):
    base = {}
    base["wout2"] = _tile_w(inp["L1_attn_b_w_out"].T)
    base["fconsts"] = np.tile(np.array([FCONST_VALS], np.float32), (128, 1))
    c5 = np.zeros((128, 24), np.float32)
    c5[:, 0:8] = _fmcols(inp["L1_attn_b_b_out"])
    c5[:, 8:12] = np.ascontiguousarray(
        inp["L1_ffn_bias_real"].astype(np.float32).reshape(4, 128).T)
    c5[:, 12:16] = np.ascontiguousarray(
        inp["L1_ffn_bias_imag"].astype(np.float32).reshape(4, 128).T)
    base["consts5"] = c5
    g1 = np.zeros((32, 256), np.float32)
    g2 = np.zeros((16, 128), np.float32)
    for wi_, (k1, k2) in enumerate((("L1_ffn_G1_R", "L1_ffn_G2_R"),
                                    ("L1_ffn_G1_I", "L1_ffn_G2_I"))):
        G1 = inp[k1].astype(np.float32)[0]
        G2 = inp[k2].astype(np.float32)
        for c in range(4):
            g1[:, wi_ * 128 + c * 32: wi_ * 128 + (c + 1) * 32] = G1[:, c, :].T
            g2[:, wi_ * 64 + c * 16: wi_ * 64 + (c + 1) * 16] = G2[c, :, 0, :].T
    base["g1t_in"] = g1
    base["g2t_in"] = g2
    indd = np.zeros((32, 4 * 128), BF16NP)
    for p in range(4):
        for loc in range(128):
            indd[8 * p + loc // 16, p * 128 + loc] = 1
    base["indd_in"] = indd
    indg = np.zeros((16, 128), BF16NP)
    for loc in range(128):
        indg[loc % 16, loc] = 1
    base["indg_in"] = indg
    ob = np.zeros((128, 4), BF16NP)
    ob[:, 0] = 1
    base["onesb"] = ob
    qext = np.zeros((3, 512), np.float32)
    sl = np.arange(512)
    qext[0] = 1.0
    qext[1] = -1024.0 * (sl // 128)
    qext[2] = -8.0 * (sl % 128)
    base["qext_in"] = qext.astype(BF16NP)
    kk = np.arange(128)[:, None]
    qq = np.arange(512)[None, :]
    rtri = np.zeros((128, 4 * 512), np.float32)
    for kt in range(4):
        rtri[:, kt * 512:(kt + 1) * 512] = np.where(128 * kt + kk > qq, NEG,
                                                    128.0 * kt)
    base["rtri_in"] = rtri

    k2g = np.zeros((B, 16, 64, S), BF16NP)
    v2g = np.zeros((B, 16, S, 65), BF16NP)
    for r in range(NCORES):
        k_r, v_r = l3outs[r]["k2"], l3outs[r]["v2"]
        for b in range(B):
            for h in range(16):
                k2g[b, h, 0:64, r * SL:(r + 1) * SL] = \
                    k_r[h * 64:(h + 1) * 64, b * SL:(b + 1) * SL]
                v2g[b, h, r * SL:(r + 1) * SL, 0:64] = \
                    v_r[h * 64:(h + 1) * 64, b * SL:(b + 1) * SL].T
    v2g[:, :, :, 64] = np.asarray(1.0, BF16NP)
    kext = np.zeros((3, 128), np.float32)
    kext[0] = 8.0 * np.arange(128)
    kext[1] = 1.0
    kext[2] = 1.0
    base["kext_in"] = kext.astype(BF16NP)

    in_maps = []
    for r in range(NCORES):
        m = dict(base)
        order = [(r + j) % NCORES for j in range(NCORES)]
        kperm = np.concatenate([k2g[:, :, :, g * SL:(g + 1) * SL] for g in order],
                               axis=3)
        vperm = np.concatenate([v2g[:, :, g * SL:(g + 1) * SL, :] for g in order],
                               axis=2)
        m["kt2h_all"] = np.ascontiguousarray(kperm)
        m["vt2h_all"] = np.ascontiguousarray(vperm)
        btab = np.zeros((128, 32), np.float32)
        for j, g in enumerate(order):
            for t4 in range(4):
                kt = j * 4 + t4
                if j == 0:
                    btab[:, kt] = 0.0
                elif g < r:
                    btab[:, kt] = float(128 * (g * 4 + t4) - r * SL)
                else:
                    btab[:, kt] = NEG
        for t4 in range(4):
            btab[:, t4] = float(128 * t4)
        m["btab_in"] = btab
        m["x_in"] = l3outs[r]["xres"]
        m["q2_in"] = l3outs[r]["q2"]
        in_maps.append(m)
    return in_maps


def kernel(**inputs):
    inp = {k: np.asarray(v) for k, v in inputs.items()}

    in_maps1 = prep_launch1(inp)
    nc1, _ = build_launch1()
    r1 = run_bass_kernel_spmd(nc1, in_maps1, core_ids=list(range(NCORES)))
    l1outs = [r1.results[r] for r in range(NCORES)]

    in_maps2 = prep_launch2(inp, l1outs)
    nc2, _ = build_launch2()
    r2 = run_bass_kernel_spmd(nc2, in_maps2, core_ids=list(range(NCORES)))
    l2outs = [r2.results[r] for r in range(NCORES)]

    in_maps3 = prep_launch3(inp, l2outs)
    nc3, _ = build_launch3()
    r3 = run_bass_kernel_spmd(nc3, in_maps3, core_ids=list(range(NCORES)))
    l3outs = [r3.results[r] for r in range(NCORES)]

    in_maps4 = prep_launch4(inp, l3outs)
    nc4, _ = build_launch4()
    r4 = run_bass_kernel_spmd(nc4, in_maps4, core_ids=list(range(NCORES)))

    out = np.zeros((B, S, D), np.float32)
    for r in range(NCORES):
        xr = np.asarray(r4.results[r]["xres"])
        out[:, r * SL:(r + 1) * SL, :] = \
            xr.reshape(D, B, SL).transpose(1, 2, 0)
    return out


# version 05
# speedup vs baseline: 1.2624x; 1.2624x over previous
import contextlib
import numpy as np
import ml_dtypes
import concourse.bass as bass
import concourse.mybir as mybir
from concourse.bass_utils import run_bass_kernel_spmd

F32 = mybir.dt.float32
BF16 = mybir.dt.bfloat16
AF = mybir.ActivationFunctionType
ALU = mybir.AluOpType
AX = mybir.AxisListType
BF16NP = ml_dtypes.bfloat16

NCORES = 8
B = 4
S = 4096
D = 1024
SL = S // NCORES
NT = B * SL
HALO = 16
NTH = NT + B * HALO
PADB = 640
NEG = -1e30
FCONST_VALS = [1e-5, 0.01, 1.0, 0.0, -1e30, 0.5, 2.0, 3.0]


class L:
    def __init__(self, nc, ctx):
        self.nc = nc
        self.ctx = ctx
        self.counts = {}
        self.last_wait = {}
        self.LD = self._sem("LD")
        self.WD = self._sem("WD")
        self.TD = self._sem("TD")
        self.ES = self._sem("ES")
        self.EV = self._sem("EV")
        self.ST = self._sem("ST")
        self.pm = [ctx.enter_context(nc.psum_tensor(f"pm{i}", [128, 512], F32))
                   for i in range(7)]
        self.pmb = ctx.enter_context(nc.psum_tensor("pmb", [128, 1024], BF16))
        self.bank_free = [(0, 0)] * 8
        self.bank_ready = [0] * 8
        self._rr = 0
        self.inputs = {}
        self.outputs = {}

    def _sem(self, name):
        return self.ctx.enter_context(self.nc.semaphore(name))

    def inc(self, instr, sem, n=1):
        instr.then_inc(sem, n)
        self.counts[sem] = self.counts.get(sem, 0) + n
        return self.counts[sem]

    def val(self, sem):
        return self.counts.get(sem, 0)

    def wait(self, eng, sem, value=None):
        v = self.val(sem) if value is None else value
        if v <= 0:
            return
        key = (id(eng), sem)
        if self.last_wait.get(key, -1) >= v:
            return
        eng.wait_ge(sem, v)
        self.last_wait[key] = v

    def waitE(self, eng, es=None, ev=None):
        self.wait(eng, self.ES, es)
        self.wait(eng, self.EV, ev)

    def eval_(self):
        return (self.val(self.ES), self.val(self.EV))

    def phase_barrier(self):
        for eng in (self.nc.scalar, self.nc.vector, self.nc.gpsimd, self.nc.sync):
            self.wait(eng, self.TD)
            self.waitE(eng)
        for eng in (self.nc.tensor, self.nc.scalar, self.nc.vector):
            self.wait(eng, self.LD)
            self.wait(eng, self.WD)

    def bank(self, i):
        return self.pm[i] if i < 7 else self.pmb

    def acquire(self, i):
        es, ev = self.bank_free[i] if isinstance(self.bank_free[i], tuple) else (None, None)
        if es is not None:
            self.waitE(self.nc.tensor, es, ev)

    def written(self, i):
        self.bank_ready[i] = self.val(self.TD)

    def consume(self, eng, i):
        self.wait(eng, self.TD, self.bank_ready[i])

    def freed(self, i):
        self.bank_free[i] = self.eval_()

    def nb(self):
        i = self._rr % 7
        self._rr += 1
        return i

    def din(self, name, shape, dt):
        ap = self.nc.dram_tensor(name, list(shape), dt, kind="ExternalInput").ap()
        self.inputs[name] = ap
        return ap

    def dout(self, name, shape, dt):
        ap = self.nc.dram_tensor(name, list(shape), dt, kind="ExternalOutput").ap()
        self.outputs[name] = ap
        return ap

    def sbuf(self, name, shape, dt):
        return self.ctx.enter_context(self.nc.sbuf_tensor(name, list(shape), dt))


class FM:
    def __init__(self, sb, rows, C):
        self.sb = sb
        self.rows = rows
        self.C = C
        self.np_ = rows // 128

    def t(self, p, c0=0, w=None):
        w = self.C - c0 if w is None else w
        return self.sb[:, p * self.C + c0: p * self.C + c0 + w]

    def rs(self, p, r0, nr, c0=0, w=None):
        w = self.C - c0 if w is None else w
        return self.sb[r0:r0 + nr, p * self.C + c0: p * self.C + c0 + w]


def alloc_fm(l, name, rows, C, dt):
    sb = l.sbuf(name, [128, (rows // 128) * C], dt)
    return FM(sb, rows, C)


def load_fm(l, fmbuf, hbm_ap, cols=None, cast=False):
    nc = l.nc
    C = cols or fmbuf.C
    eng = nc.gpsimd if cast else nc.sync
    sem = l.WD if cast else l.LD
    for p in range(fmbuf.np_):
        l.inc(eng.dma_start(out=fmbuf.t(p, 0, C),
                            in_=hbm_ap[p * 128:(p + 1) * 128, 0:C]), sem, 16)


def store_fm(l, hbm_ap, fmbuf):
    nc = l.nc
    l.waitE(nc.sync)
    l.wait(nc.sync, l.TD)
    for p in range(fmbuf.np_):
        l.inc(nc.sync.dma_start(out=hbm_ap[p * 128:(p + 1) * 128, :],
                                in_=fmbuf.t(p)), l.ST, 16)


class WStream:

    def __init__(self, l, name, din, nslots=2, stack=None, ow=128):
        self.l = l
        self.din = din
        self.ow = ow
        self.npt = din // 128
        alloc = (stack.enter_context if stack is not None else
                 (lambda cm: l.ctx.enter_context(cm)))
        self.slots = [alloc(l.nc.sbuf_tensor(f"{name}{i}", [128, self.npt * ow], BF16))
                      for i in range(nslots)]
        self.nslots = nslots
        self.cur = 0
        self.slot_ready = [0] * nslots
        self.slot_user = [-1] * nslots

    def issue(self, hbm_ap, ot):
        l, nc = self.l, self.l.nc
        s = self.cur % self.nslots
        self.cur += 1
        l.wait(nc.gpsimd, l.TD,
               self.slot_user[s] if self.slot_user[s] >= 0 else l.val(l.TD))
        l.inc(nc.gpsimd.dma_start(out=self.slots[s][:], in_=hbm_ap[ot]), l.WD, 16)
        self.slot_ready[s] = l.val(l.WD)
        return s

    def chunk(self, s, pt, ow=None):
        ow = ow or self.ow
        return self.slots[s][:, pt * self.ow: pt * self.ow + ow]

    def use(self, s):
        self.slot_user[s] = self.l.val(self.l.TD)

    def wait_ready(self, s):
        self.l.wait(self.l.nc.tensor, self.l.WD, self.slot_ready[s])


def emit_ln(l, x, xn, C=None, g_fm=None, b_fm=None, eps=1e-5, consts=None,
            window=None):
    nc = l.nc
    l.phase_barrier()
    C = C or x.C
    cs = consts
    if window is not None:
        nts = [window]
    else:
        nts = [(i * 512, min(512, C - i * 512)) for i in range((C + 511) // 512)]
    NPT = x.np_
    for (c0, w) in nts:
        rof = 0 if window is not None else c0
        bs, bq = l.nb(), l.nb()
        l.acquire(bs)
        l.acquire(bq)
        slot_use = [0, 0]
        for pt in range(NPT):
            sl = pt % 2
            l.wait(nc.scalar, l.TD, slot_use[sl])
            l.inc(nc.scalar.copy(cs["cast_t"][sl][:, 0:w], x.t(pt, c0, w)), l.ES)
            l.inc(nc.scalar.activation(cs["sq_t"][sl][:, 0:w], x.t(pt, c0, w),
                                       AF.Square), l.ES)
            l.waitE(nc.tensor)
            nc.tensor.matmul(l.bank(bs)[0:1, 0:w], cs["ones128_b"],
                             cs["cast_t"][sl][:, 0:w],
                             start=(pt == 0), stop=(pt == NPT - 1))
            l.inc(nc.tensor.matmul(l.bank(bq)[0:1, 0:w], cs["ones128_b"],
                                   cs["sq_t"][sl][:, 0:w],
                                   start=(pt == 0), stop=(pt == NPT - 1)), l.TD)
            slot_use[sl] = l.val(l.TD)
        l.written(bs)
        l.written(bq)
        l.consume(nc.scalar, bs)
        l.inc(nc.scalar.copy(cs["sums"][0:1, rof:rof + w], l.bank(bs)[0:1, 0:w]), l.ES)
        l.inc(nc.scalar.copy(cs["sqsums"][0:1, rof:rof + w], l.bank(bq)[0:1, 0:w]), l.ES)
        l.freed(bs)
        l.freed(bq)
        bm, bv = l.nb(), l.nb()
        l.acquire(bm)
        l.acquire(bv)
        l.waitE(nc.tensor)
        l.inc(nc.tensor.matmul(l.bank(bm)[:, 0:w], cs["ones1_128"],
                               cs["sums"][0:1, rof:rof + w], start=True, stop=True), l.TD)
        l.inc(nc.tensor.matmul(l.bank(bv)[:, 0:w], cs["ones1_128"],
                               cs["sqsums"][0:1, rof:rof + w], start=True, stop=True), l.TD)
        l.written(bm)
        l.written(bv)
        l.consume(nc.scalar, bm)
        l.inc(nc.scalar.activation(cs["t1"][:, 0:w], l.bank(bm)[:, 0:w], AF.Square,
                                   scale=1.0 / D), l.ES)
        l.consume(nc.vector, bv)
        l.waitE(nc.vector)
        l.inc(nc.vector.scalar_tensor_tensor(
            cs["t2"][:, 0:w], l.bank(bv)[:, 0:w], 1.0 / D, cs["t1"][:, 0:w],
            op0=ALU.mult, op1=ALU.subtract), l.EV)
        l.freed(bv)
        l.waitE(nc.scalar)
        l.inc(nc.scalar.activation(cs["t2"][:, 0:w], cs["t2"][:, 0:w], AF.Ln,
                                   bias=float(eps)), l.ES)
        l.inc(nc.scalar.activation(cs["rrep"][:, rof:rof + w], cs["t2"][:, 0:w],
                                   AF.Exp, scale=-0.5), l.ES)
        l.waitE(nc.vector)
        l.inc(nc.vector.scalar_tensor_tensor(
            cs["mrrep"][:, rof:rof + w], l.bank(bm)[:, 0:w], 1.0 / D,
            cs["rrep"][:, rof:rof + w], op0=ALU.mult, op1=ALU.mult), l.EV)
        l.freed(bm)
    for pt in range(NPT):
        for (c0, w) in nts:
            rof = 0 if window is not None else c0
            oc0 = 0 if window is not None else c0
            l.waitE(nc.vector)
            l.inc(nc.vector.tensor_tensor(xn.t(pt, oc0, w), x.t(pt, c0, w),
                                          cs["rrep"][:, rof:rof + w], op=ALU.mult), l.EV)
            l.inc(nc.vector.tensor_tensor(xn.t(pt, oc0, w), xn.t(pt, oc0, w),
                                          cs["mrrep"][:, rof:rof + w],
                                          op=ALU.subtract), l.EV)
            if g_fm is not None:
                l.waitE(nc.scalar)
                l.inc(nc.scalar.mul(xn.t(pt, oc0, w), xn.t(pt, oc0, w),
                                    g_fm[:, pt:pt + 1]), l.ES)
                l.inc(nc.scalar.add(xn.t(pt, oc0, w), xn.t(pt, oc0, w),
                                    b_fm[:, pt:pt + 1]), l.ES)


def emit_project(l, whbm, wstream, xn, douts, n_tiles, out_fn, extra=None,
                 rhs_fn=None):
    nc = l.nc
    npt = wstream.npt
    s = wstream.issue(whbm, 0)
    for ot in range(douts):
        s_next = wstream.issue(whbm, ot + 1) if ot + 1 < douts else None
        wstream.wait_ready(s)
        for (c0, w, tag) in n_tiles:
            bi = l.nb()
            l.acquire(bi)
            ps = l.bank(bi)[0:wstream.ow, 0:w]
            l.waitE(nc.tensor)
            last = None
            for pt in range(npt):
                rhs = rhs_fn(pt, c0, w, tag) if rhs_fn else xn.t(pt, c0, w)
                stop = (pt == npt - 1) and not extra
                last = nc.tensor.matmul(ps, wstream.chunk(s, pt), rhs,
                                        start=(pt == 0), stop=stop)
            if extra:
                for i, (lf, rf) in enumerate(extra):
                    last = nc.tensor.matmul(ps, lf(ot), rf(c0, w, tag),
                                            start=False, stop=(i == len(extra) - 1))
            l.inc(last, l.TD)
            l.written(bi)
            wstream.use(s)
            out_fn(ot, c0, w, tag, ps, bi)
        s = s_next


_LNCTR = [0]


def ln_scratch(l, stack, C):
    nc = l.nc
    _LNCTR[0] += 1
    u = f"_{_LNCTR[0]}"
    cs = {}
    cs["cast_t"] = [stack.enter_context(nc.sbuf_tensor(f"lncast{i}{u}", [128, 512], BF16))
                    for i in range(2)]
    cs["sq_t"] = [stack.enter_context(nc.sbuf_tensor(f"lnsq{i}{u}", [128, 512], BF16))
                  for i in range(2)]
    cs["sums"] = stack.enter_context(nc.sbuf_tensor(f"lnsums{u}", [1, C], BF16))
    cs["sqsums"] = stack.enter_context(nc.sbuf_tensor(f"lnsqsums{u}", [1, C], BF16))
    cs["t1"] = stack.enter_context(nc.sbuf_tensor(f"lnt1{u}", [128, 512], F32))
    cs["t2"] = stack.enter_context(nc.sbuf_tensor(f"lnt2{u}", [128, 512], F32))
    cs["rrep"] = stack.enter_context(nc.sbuf_tensor(f"lnr{u}", [128, C], F32))
    cs["mrrep"] = stack.enter_context(nc.sbuf_tensor(f"lnmr{u}", [128, C], F32))
    return cs


def do_ln(l, x, xn, C=None, g_fm=None, b_fm=None, eps=1e-5, window=None):
    with contextlib.ExitStack() as st:
        cs = ln_scratch(l, st, 512 if window is not None else (C or x.C))
        cs["ones128_b"] = l.c_ones128_b
        cs["ones1_128"] = l.c_ones1_128
        emit_ln(l, x, xn, C=C, g_fm=g_fm, b_fm=b_fm, eps=eps, consts=cs,
                window=window)


MAIN4 = [(b * SL, SL, b) for b in range(B)]
MAIN4H = MAIN4 + [(NT, B * HALO, "h")]


def emit_l0aa(l, x, xn, hb):
    nc = l.nc
    l.phase_barrier()
    with contextlib.ExitStack() as st:
        sb = lambda n, s, d: st.enter_context(nc.sbuf_tensor(n, s, d))
        q = FM(sb("aaq", [128, 8 * NT], BF16), 1024, NT)
        k_pad = FM(sb("aak", [128, 2 * B * PADB], BF16), 256, B * PADB)
        v_pad = FM(sb("aav", [128, 2 * B * PADB], BF16), 256, B * PADB)
        ws = WStream(l, "aaw", 1024, stack=st)
        t_bf = sb("aat", [128, 2 * 512], BF16)
        rr_sb = sb("aarr", [2, 512], BF16)

        def mk_evac(dest, tag_map):
            def fn(ot, c0, w, tag, ps, bi):
                l.consume(nc.scalar, bi)
                dst = tag_map(dest, ot, c0, w, tag)
                l.inc(nc.scalar.copy(dst, ps), l.ES)
                l.freed(bi)
            return fn

        def q_map(dest, ot, c0, w, tag):
            return dest.t(ot, c0, w)
        emit_project(l, hb["aa_wq"], ws, xn, 8, MAIN4, mk_evac(q, q_map))

        def kv_map(dest, ot, c0, w, tag):
            if tag == "h":
                base = dest.sb[:, ot * (B * PADB): (ot + 1) * (B * PADB)]
                return base.rearrange("p (b c) -> p b c", c=PADB)[:, :, 112:128]
            b = tag
            return dest.t(ot, b * PADB + 128, w)

        def kv_evac(dest):
            def fn(ot, c0, w, tag, ps, bi):
                l.consume(nc.scalar, bi)
                if tag == "h":
                    dst = kv_map(dest, ot, c0, w, tag)
                    l.inc(nc.scalar.copy(dst, ps.rearrange("p (b c) -> p b c", c=16)),
                          l.ES)
                else:
                    l.inc(nc.scalar.copy(kv_map(dest, ot, c0, w, tag), ps), l.ES)
                l.freed(bi)
            return fn
        emit_project(l, hb["aa_wk"], ws, xn, 2, MAIN4H, kv_evac(k_pad))
        emit_project(l, hb["aa_wv"], ws, xn, 2, MAIN4H, kv_evac(v_pad))
        for dest in (k_pad, v_pad):
            for ot in range(2):
                ap = dest.sb[:, ot * (B * PADB): (ot + 1) * (B * PADB)].rearrange(
                    "p (b c) -> p b c", c=PADB)[:, :, 0:112]
                l.inc(nc.gpsimd.memset(ap, 0.0), l.WD, 1)

        def rms(dest, rrbuf, tiles, wn_ap):
            for pt in range(dest.np_):
                for (c0, w, tag) in tiles:
                    l.wait(nc.scalar, l.TD)
                    l.waitE(nc.scalar)
                    l.inc(nc.scalar.activation(t_bf[:, 0:w], dest.t(pt, c0, w),
                                               AF.Square), l.ES)
                    bi = l.nb()
                    l.acquire(bi)
                    l.waitE(nc.tensor)
                    l.inc(nc.tensor.matmul(l.bank(bi)[0:2, 0:w], l.c_ind128_2,
                                           t_bf[:, 0:w], start=True, stop=True), l.TD)
                    l.written(bi)
                    l.consume(nc.scalar, bi)
                    l.inc(nc.scalar.activation(l.bank(bi)[0:2, 0:w],
                                               l.bank(bi)[0:2, 0:w], AF.Ln,
                                               scale=1.0 / 64, bias=0.01), l.ES)
                    l.inc(nc.scalar.activation(rrbuf[0:2, 0:w],
                                               l.bank(bi)[0:2, 0:w], AF.Exp,
                                               scale=-0.5), l.ES)
                    l.freed(bi)
                    bi2 = l.nb()
                    l.acquire(bi2)
                    l.waitE(nc.tensor)
                    l.inc(nc.tensor.matmul(
                        l.bank(bi2)[:, 0:w], l.c_ind2_128,
                        rrbuf[0:2, 0:w], start=True, stop=True), l.TD)
                    l.written(bi2)
                    l.consume(nc.vector, bi2)
                    l.waitE(nc.vector)
                    l.inc(nc.vector.tensor_tensor(dest.t(pt, c0, w), dest.t(pt, c0, w),
                                                  l.bank(bi2)[:, 0:w], op=ALU.mult),
                          l.EV)
                    l.freed(bi2)
                    l.waitE(nc.scalar)
                    l.inc(nc.scalar.mul(dest.t(pt, c0, w), dest.t(pt, c0, w), wn_ap),
                          l.ES)
        rms(q, rr_sb, MAIN4, hb_ap_qnw(l))
        KTILES = ([(b * PADB + 128, 512, b) for b in range(B)]
                  + [(b * PADB + 112, 16, b) for b in range(B)])
        rms(k_pad, rr_sb, KTILES, hb_ap_knw(l))

        v_tok = sb("aavtok", [128, (B * PADB // 128) * 260], BF16)
        ident = l.c_ident[:]
        NTOK = B * PADB // 128
        for vp in range(2):
            for tt in range(NTOK):
                l.waitE(nc.tensor)
                l.wait(nc.tensor, l.WD)
                l.acquire(7)
                l.inc(nc.tensor.transpose(l.pmb[0:128, 0:128],
                                          v_pad.t(vp, tt * 128, 128), ident), l.TD)
                l.written(7)
                l.consume(nc.scalar, 7)
                for gg in range(2):
                    g = vp * 2 + gg
                    l.inc(nc.scalar.copy(
                        v_tok[:, tt * 260 + g * 65: tt * 260 + g * 65 + 64],
                        l.pmb[0:128, gg * 64: gg * 64 + 64]), l.ES)
                l.freed(7)
        for g in range(4):
            ap = v_tok[:].rearrange("p (t c) -> p t c", c=260)[:, :, g * 65 + 64:g * 65 + 65]
            l.inc(nc.gpsimd.memset(ap, 1.0), l.WD, 1)

        ctx = FM(xn.sb, 1024, xn.C)
        att_e = sb("aae", [128, 4 * 512], BF16)
        av_sb = sb("aaav", [64, 512], BF16)
        den_sb = sb("aaden", [1, 512], F32)
        dinvf_sb = sb("aadinvf", [1, 512], F32)
        dinv_sb = sb("aadinv", [1, 512], BF16)
        eslot_use = [0] * 4
        esl = 0
        R4 = hb["aa_R4"]
        R4f = hb["aa_R4f"]
        r_sb = sb("aar4", [128, 4 * 2 * 512], F32)
        rf_sb = sb("aar4f", [128, 4 * 512], F32)
        for g in range(4):
            for c in range(2):
                l.inc(nc.sync.dma_start(out=r_sb[:, (g * 2 + c) * 512:(g * 2 + c + 1) * 512],
                                        in_=R4[g, c]), l.LD, 16)
            l.inc(nc.sync.dma_start(out=rf_sb[:, g * 512:(g + 1) * 512],
                                    in_=R4f[g]), l.LD, 16)

        for b in range(B):
            for qt in range(4):
                for g in range(4):
                    bA, bB = l.nb(), l.nb()
                    for cls, bi in ((0, bA), (1, bB)):
                        kt = qt + cls
                        l.acquire(bi)
                        l.waitE(nc.tensor)
                        l.wait(nc.tensor, l.WD)
                        last = None
                        for j in range(4):
                            h = g + 4 * j
                            lhs = k_pad.rs(g // 2, 64 * (g % 2), 64,
                                           b * PADB + kt * 128, 128)
                            rhs = q.rs(h // 2, 64 * (h % 2), 64,
                                       b * SL + qt * 128, 128)
                            last = nc.tensor.matmul(
                                l.bank(bi)[0:128, j * 128:(j + 1) * 128],
                                lhs, rhs, start=True, stop=True)
                        l.inc(last, l.TD)
                        l.written(bi)
                    for cls, bi in ((0, bA), (1, bB)):
                        l.consume(nc.vector, bi)
                        l.wait(nc.vector, l.TD, eslot_use[esl])
                        rtile = (rf_sb[:, g * 512:(g + 1) * 512]
                                 if (cls == 0 and qt == 0)
                                 else r_sb[:, (g * 2 + cls) * 512:(g * 2 + cls + 1) * 512])
                        l.inc(nc.vector.scalar_tensor_tensor(
                            att_e[:, esl * 512:(esl + 1) * 512],
                            l.bank(bi)[:, 0:512], 1.0 / 8.0, rtile,
                            op0=ALU.mult, op1=ALU.add), l.EV)
                        l.freed(bi)
                        l.waitE(nc.scalar)
                        l.inc(nc.scalar.activation(
                            att_e[:, esl * 512:(esl + 1) * 512],
                            att_e[:, esl * 512:(esl + 1) * 512], AF.Exp), l.ES)
                        if cls == 0:
                            esl0 = esl
                        esl = (esl + 1) % 4
                    bC = l.nb()
                    l.acquire(bC)
                    l.waitE(nc.tensor)
                    i1 = nc.tensor.matmul(
                        l.bank(bC)[0:65, 0:512],
                        v_tok[:, (b * 5 + qt) * 260 + g * 65:
                              (b * 5 + qt) * 260 + g * 65 + 65],
                        att_e[:, esl0 * 512:(esl0 + 1) * 512], start=True, stop=False)
                    i2 = nc.tensor.matmul(
                        l.bank(bC)[0:65, 0:512],
                        v_tok[:, (b * 5 + qt + 1) * 260 + g * 65:
                              (b * 5 + qt + 1) * 260 + g * 65 + 65],
                        att_e[:, ((esl0 + 1) % 4) * 512:((esl0 + 1) % 4 + 1) * 512],
                        start=False, stop=True)
                    l.inc(i2, l.TD)
                    l.written(bC)
                    eslot_use[esl0] = l.val(l.TD)
                    eslot_use[(esl0 + 1) % 4] = l.val(l.TD)
                    l.consume(nc.scalar, bC)
                    l.waitE(nc.scalar)
                    l.inc(nc.scalar.copy(den_sb[:], l.bank(bC)[64:65, 0:512]), l.ES)
                    l.inc(nc.scalar.copy(av_sb[:], l.bank(bC)[0:64, 0:512]), l.ES)
                    l.freed(bC)
                    l.waitE(nc.vector)
                    l.inc(nc.vector.reciprocal(dinvf_sb[:], den_sb[:]), l.EV)
                    l.waitE(nc.scalar)
                    l.inc(nc.scalar.copy(dinv_sb[:], dinvf_sb[:]), l.ES)
                    bD = l.nb()
                    l.acquire(bD)
                    l.waitE(nc.tensor)
                    l.inc(nc.tensor.matmul(l.bank(bD)[0:64, 0:512], l.c_ones1_64,
                                           dinv_sb[:], start=True, stop=True), l.TD)
                    l.written(bD)
                    l.consume(nc.vector, bD)
                    for j in range(4):
                        h = g + 4 * j
                        l.inc(nc.vector.tensor_tensor(
                            ctx.rs(h // 2, 64 * (h % 2), 64, b * SL + qt * 128, 128),
                            av_sb[:, j * 128:(j + 1) * 128],
                            l.bank(bD)[0:64, j * 128:(j + 1) * 128], op=ALU.mult),
                            l.EV)
                    l.freed(bD)

        def wo_evac(ot, c0, w, tag, ps, bi):
            l.consume(nc.vector, bi)
            l.waitE(nc.vector)
            l.inc(nc.vector.tensor_tensor(x.t(ot, c0, w), x.t(ot, c0, w), ps,
                                          op=ALU.add), l.EV)
            l.freed(bi)
        emit_project(l, hb["aa_wo"], ws, ctx, 8, MAIN4, wo_evac)


def hb_ap_qnw(l):
    return l.c_qnw[:, 0:1]


def hb_ap_knw(l):
    return l.c_knw[:, 0:1]


def emit_l0ffn(l, x, xn, hb, pfx="f0", dbg=None):
    nc = l.nc
    l.phase_barrier()
    with contextlib.ExitStack() as st:
        sb = lambda n, s, d: st.enter_context(nc.sbuf_tensor(n, s, d))
        c_fm = FM(sb(pfx + "c", [128, 8 * 512], F32), 1024, 512)
        cb_fm = FM(sb(pfx + "cb", [128, 8 * 512], BF16), 1024, 512)
        c2_fm = FM(sb(pfx + "c2", [128, 8 * 512], F32), 1024, 512)
        fig = sb(pfx + "fig", [128, 3 * 512], F32)
        tmp = sb(pfx + "tmp", [128, 512], F32)
        h_t = sb(pfx + "h", [128, 512], BF16)
        wsg = [WStream(l, pfx + f"wg{i}", 2048, stack=st) for i in range(3)]
        wso = WStream(l, pfx + "wo", 2048, stack=st)
        for ot in range(8):
            l.inc(nc.scalar.activation(c_fm.t(ot), x.t(0, 0, 512), AF.Identity,
                                       bias=l.c_inc_fm[:, ot:ot + 1], scale=0.0),
                  l.ES)
            l.inc(nc.scalar.copy(cb_fm.t(ot), c_fm.t(ot)), l.ES)
        gates = [("wfuf", AF.Sigmoid, l.c_bf_fm), ("wiui", AF.Sigmoid, l.c_bi_fm),
                 ("wcuc", AF.Tanh, l.c_bc_fm)]
        slots = [wsg[gi].issue(hb[nm], 0) for gi, (nm, _, _) in enumerate(gates)]
        for t in range(B):
            for ot in range(8):
                nslots = []
                for gi, (nm, act, bias) in enumerate(gates):
                    if not (t == B - 1 and ot == 7):
                        nslots.append(wsg[gi].issue(
                            hb[nm], (ot + 1) % 8))
                    else:
                        nslots.append(None)
                for gi, (nm, act, bias) in enumerate(gates):
                    s = slots[gi]
                    wsg[gi].wait_ready(s)
                    if dbg is not None and t == 0 and ot == 0 and gi == 0:
                        l.wait(nc.sync, l.WD)
                        l.inc(nc.sync.dma_start(out=dbg["wslot"],
                                                in_=wsg[gi].slots[s][:]), l.ST, 16)
                        l.waitE(nc.sync)
                        l.inc(nc.sync.dma_start(out=dbg["xnt"],
                                                in_=xn.t(0, 0, SL)), l.ST, 16)
                    bi = l.nb()
                    l.acquire(bi)
                    l.waitE(nc.tensor)
                    last = None
                    for pt in range(16):
                        rhs = (xn.t(pt, t * SL, SL) if pt < 8
                               else cb_fm.t(pt - 8))
                        last = nc.tensor.matmul(l.bank(bi)[0:128, 0:SL],
                                                wsg[gi].chunk(s, pt), rhs,
                                                start=(pt == 0), stop=(pt == 15))
                    l.inc(last, l.TD)
                    l.written(bi)
                    wsg[gi].use(s)
                    l.consume(nc.scalar, bi)
                    l.waitE(nc.scalar)
                    l.inc(nc.scalar.activation(fig[:, gi * 512:(gi + 1) * 512],
                                               l.bank(bi)[:, 0:SL], act,
                                               bias=bias[:, ot:ot + 1]), l.ES)
                    l.freed(bi)
                l.waitE(nc.vector)
                l.inc(nc.vector.tensor_tensor(tmp[:], fig[:, 512:1024],
                                              fig[:, 1024:1536], op=ALU.mult), l.EV)
                l.inc(nc.vector.tensor_tensor(c2_fm.t(ot), fig[:, 0:512],
                                              c_fm.t(ot), op=ALU.mult), l.EV)
                l.inc(nc.vector.tensor_tensor(c2_fm.t(ot), c2_fm.t(ot), tmp[:],
                                              op=ALU.add), l.EV)
                if dbg is not None and t == 0 and ot == 0:
                    l.waitE(nc.sync)
                    l.inc(nc.sync.dma_start(out=dbg["fig0"], in_=fig[:]), l.ST, 16)
                slots = nslots
            c_fm, c2_fm = c2_fm, c_fm
            for ot_ in range(8):
                l.waitE(nc.scalar)
                l.wait(nc.scalar, l.TD)
                l.inc(nc.scalar.copy(cb_fm.t(ot_), c_fm.t(ot_)), l.ES)
            if dbg is not None and t == 0:
                l.waitE(nc.sync)
                for p_ in range(8):
                    l.inc(nc.sync.dma_start(out=dbg["c1"][p_ * 128:(p_ + 1) * 128, :],
                                            in_=c_fm.t(p_)), l.ST, 16)
        so = wso.issue(hb["wouo"], 0)
        for ot in range(8):
            so_next = wso.issue(hb["wouo"], ot + 1) if ot < 7 else None
            wso.wait_ready(so)
            for t in range(B):
                bi = l.nb()
                l.acquire(bi)
                l.waitE(nc.tensor)
                last = None
                for pt in range(16):
                    rhs = (xn.t(pt, t * SL, SL) if pt < 8 else cb_fm.t(pt - 8))
                    last = nc.tensor.matmul(l.bank(bi)[0:128, 0:SL],
                                            wso.chunk(so, pt), rhs,
                                            start=(pt == 0), stop=(pt == 15))
                l.inc(last, l.TD)
                l.written(bi)
                wso.use(so)
                l.consume(nc.scalar, bi)
                l.waitE(nc.scalar)
                l.inc(nc.scalar.activation(h_t[:], l.bank(bi)[:, 0:SL], AF.Tanh,
                                           bias=l.c_bo_fm[:, ot:ot + 1]), l.ES)
                l.freed(bi)
                l.waitE(nc.vector)
                l.inc(nc.vector.tensor_tensor(x.t(ot, t * SL, SL),
                                              x.t(ot, t * SL, SL), h_t[:],
                                              op=ALU.add), l.EV)
            so = so_next


def emit_project_store(l, whbm, wstream, xn, douts, n_tiles, hbm_out, bias_fm=None,
                       bias_ap=None, act=None, stage=None, stage_n=4,
                       hbm_outs=None):
    nc = l.nc
    ow = wstream.ow
    stage_use = [0] * stage_n
    ctr = [0]

    def fn(ot, c0, w, tag, ps, bi):
        sl = ctr[0] % stage_n
        ctr[0] += 1
        l.consume(nc.scalar, bi)
        l.wait(nc.scalar, l.ST, stage_use[sl])
        bias = bias_fm[:, ot:ot + 1] if bias_fm is not None else bias_ap
        if bias is not None:
            l.inc(nc.scalar.activation(stage[0:ow, sl * 512: sl * 512 + w], ps,
                                       act or AF.Identity, bias=bias), l.ES)
        else:
            l.inc(nc.scalar.activation(stage[0:ow, sl * 512: sl * 512 + w], ps,
                                       act or AF.Copy), l.ES)
        l.freed(bi)
        l.waitE(nc.sync)
        if hbm_outs is not None:
            dest, orow = hbm_outs[ot // 8], (ot % 8) * ow
        else:
            dest, orow = hbm_out, ot * ow
        l.inc(nc.sync.dma_start(out=dest[orow:orow + ow, c0:c0 + w],
                                in_=stage[0:ow, sl * 512: sl * 512 + w]), l.ST, 16)
        stage_use[sl] = l.val(l.ST)
    emit_project(l, whbm, wstream, xn, douts, n_tiles, fn)


def build_launch1(upto=99):
    nc = bass.Bass("TRN2", target_bir_lowering=False)
    ctx = contextlib.ExitStack()
    with ctx:
        l = L(nc, ctx)
        x_in = l.din("x_in", [1024, NTH], F32)
        hb = {}
        hb["aa_wq"] = l.din("aa_wq", [8, 128, 1024], F32)
        hb["aa_wk"] = l.din("aa_wk", [2, 128, 1024], F32)
        hb["aa_wv"] = l.din("aa_wv", [2, 128, 1024], F32)
        hb["aa_wo"] = l.din("aa_wo", [8, 128, 1024], F32)
        hb["aa_R4"] = l.din("aa_R4", [4, 2, 128, 512], F32)
        hb["aa_R4f"] = l.din("aa_R4f", [4, 128, 512], F32)
        for nm in ("wfuf", "wiui", "wcuc", "wouo"):
            hb[nm] = l.din(nm, [8, 2048, 128], F32)
        hb["ab_wq"] = l.din("ab_wq", [8, 128, 1024], F32)
        hb["ab_wk"] = l.din("ab_wk", [8, 128, 1024], F32)
        hb["ab_wv"] = l.din("ab_wv", [8, 128, 1024], F32)
        consts = l.din("consts1", [128, 32], F32)
        identb = l.din("identb", [128, 128], BF16)
        fconsts = l.din("fconsts", [128, 8], F32)
        onesb = l.din("onesb", [128, 4], BF16)
        xres_o = l.dout("xres", [1024, NT], F32)
        qb_o = l.dout("qb", [1024, NT], BF16)
        kb_o = l.dout("kb", [1024, NT], BF16)
        vb_o = l.dout("vb", [1024, NT], BF16)

        fcs = l.sbuf("fcs", [128, 8], F32)
        l.inc(nc.sync.dma_start(out=fcs[:], in_=fconsts[:]), l.LD, 16)
        for i, v in enumerate(FCONST_VALS):
            nc.const_aps.register(F32, float(v), fcs[:, i:i + 1])
        x = alloc_fm(l, "x", 1024, NTH, F32)
        xn = alloc_fm(l, "xn", 1024, NTH, BF16)
        cst = l.sbuf("cst", [128, 32], F32)
        l.c_ident = l.sbuf("identsb", [128, 128], BF16)
        onesb_sb = l.sbuf("onesbsb", [128, 4], BF16)
        stage = l.sbuf("stage", [128, 4 * 512], BF16)

        load_fm(l, x, x_in)
        l.inc(nc.sync.dma_start(out=cst[:], in_=consts[:]), l.LD, 16)
        l.inc(nc.sync.dma_start(out=l.c_ident[:], in_=identb[:]), l.LD, 16)
        l.inc(nc.sync.dma_start(out=onesb_sb[:], in_=onesb[:]), l.LD, 16)
        l.c_ones128_b = onesb_sb[:, 0:1]
        rowones = l.sbuf("rowones", [1, 128], BF16)
        l.inc(nc.gpsimd.memset(rowones[:], 1.0), l.WD, 1)
        l.c_ones1_128 = rowones[0:1, 0:128]
        l.c_ones1_64 = rowones[0:1, 0:64]
        ind128_2 = l.sbuf("ind1282", [128, 2], BF16)
        l.inc(nc.gpsimd.memset(ind128_2[0:64, 0:1], 1.0), l.WD, 1)
        l.inc(nc.gpsimd.memset(ind128_2[0:64, 1:2], 0.0), l.WD, 1)
        l.inc(nc.gpsimd.memset(ind128_2[64:128, 0:1], 0.0), l.WD, 1)
        l.inc(nc.gpsimd.memset(ind128_2[64:128, 1:2], 1.0), l.WD, 1)
        l.c_ind128_2 = ind128_2[:]
        indrow = l.din("indrow", [2, 128], BF16)
        ind2_128 = l.sbuf("ind2128", [2, 128], BF16)
        l.inc(nc.sync.dma_start(out=ind2_128[:], in_=indrow[:]), l.LD, 16)
        l.c_ind2_128 = ind2_128[:]
        l.c_qnw = cst[:, 0:1]
        l.c_knw = cst[:, 1:2]
        l.c_bf_fm = cst[:, 8:16]
        l.c_bi_fm = cst[:, 16:24]
        l.c_bc_fm = cst[:, 24:32]
        consts2 = l.din("consts2", [128, 40], F32)
        cst2 = l.sbuf("cst2", [128, 40], F32)
        l.inc(nc.sync.dma_start(out=cst2[:], in_=consts2[:]), l.LD, 16)
        l.c_bo_fm = cst2[:, 0:8]
        l.c_inc_fm = cst2[:, 8:16]
        l.c_bqb_fm = cst2[:, 16:24]
        l.c_bkb_fm = cst2[:, 24:32]
        l.c_bvb_fm = cst2[:, 32:40]

        for eng in (nc.tensor, nc.scalar, nc.vector):
            l.wait(eng, l.LD)
            l.wait(eng, l.WD)

        do_ln(l, x, xn, C=NTH)
        if upto >= 1:
            emit_l0aa(l, x, xn, hb)
        if upto >= 2:
            do_ln(l, x, xn, C=NT)
        if upto >= 3:
            emit_l0ffn(l, x, xn, hb, pfx="f0")
        if upto >= 4:
            do_ln(l, x, xn, C=NT)
        ws2 = WStream(l, "abw", 1024)
        if upto >= 4:
            emit_project_store(l, hb["ab_wq"], ws2, xn, 8, MAIN4, qb_o,
                               bias_fm=l.c_bqb_fm, stage=stage)
            emit_project_store(l, hb["ab_wk"], ws2, xn, 8, MAIN4, kb_o,
                               bias_fm=l.c_bkb_fm, stage=stage)
            emit_project_store(l, hb["ab_wv"], ws2, xn, 8, MAIN4, vb_o,
                               bias_fm=l.c_bvb_fm, stage=stage)
        l.waitE(nc.sync)
        l.wait(nc.sync, l.TD)
        for p in range(8):
            l.inc(nc.sync.dma_start(out=xres_o[p * 128:(p + 1) * 128, :],
                                    in_=x.t(p, 0, NT)), l.ST, 16)
        l.wait(nc.sync, l.ST)
    return nc, l


def _tile_w(w_eff, ow=128):
    din, dout = w_eff.shape
    npt = din // 128
    arr = w_eff.reshape(npt, 128, dout // ow, ow)
    return np.ascontiguousarray(arr.transpose(2, 1, 0, 3).reshape(dout // ow, 128,
                                                                  npt * ow))


def _alibi_slopes16():
    import math
    base = 2.0 ** (-2.0 ** (-(math.log2(16) - 3)))
    return (base ** np.arange(1, 17)).astype(np.float32)


def _fmcols(vec):
    return np.ascontiguousarray(vec.reshape(8, 128).T)


def prep_launch1(inp):
    slopes = _alibi_slopes16()
    x = inp["L0_attn_a_x"].astype(np.float32)
    base = {}
    base["aa_wq"] = _tile_w(inp["L0_attn_a_wq"].T)
    base["aa_wk"] = _tile_w(inp["L0_attn_a_wk"].T)
    base["aa_wv"] = _tile_w(inp["L0_attn_a_wv"].T)
    base["aa_wo"] = _tile_w(inp["L0_attn_a_wo"].T)
    for nm, wk, uk in (("wfuf", "L0_ffn_wf", "L0_ffn_uf"),
                       ("wiui", "L0_ffn_wi", "L0_ffn_ui"),
                       ("wcuc", "L0_ffn_wc", "L0_ffn_uc"),
                       ("wouo", "L0_ffn_wo", "L0_ffn_uo")):
        base[nm] = np.ascontiguousarray(np.concatenate(
            [_tile_w(inp[wk].T), _tile_w(inp[uk].T)], axis=2))
    base["ab_wq"] = _tile_w(inp["L0_attn_b_wq"].T)
    base["ab_wk"] = _tile_w(inp["L0_attn_b_wk"].T)
    base["ab_wv"] = _tile_w(inp["L0_attn_b_wv"].T)
    c1 = np.zeros((128, 32), np.float32)
    qnw = inp["L0_attn_a_q_norm_w"]
    knw = inp["L0_attn_a_k_norm_w"]
    c1[:, 0] = np.concatenate([qnw, qnw])
    c1[:, 1] = np.concatenate([knw, knw])
    c1[:, 8:16] = _fmcols(inp["L0_ffn_bf"])
    c1[:, 16:24] = _fmcols(inp["L0_ffn_bi"])
    c1[:, 24:32] = _fmcols(inp["L0_ffn_bc"])
    base["consts1"] = c1
    c2 = np.zeros((128, 40), np.float32)
    c2[:, 0:8] = _fmcols(inp["L0_ffn_bo"])
    c2[:, 8:16] = _fmcols(inp["L0_ffn_in_c"])
    c2[:, 16:24] = _fmcols(inp["L0_attn_b_bq"])
    c2[:, 24:32] = _fmcols(inp["L0_attn_b_bk"])
    c2[:, 32:40] = _fmcols(inp["L0_attn_b_bv"])
    base["consts2"] = c2
    base["identb"] = np.eye(128, dtype=BF16NP)
    ob = np.zeros((128, 4), BF16NP)
    ob[:, 0] = 1
    base["onesb"] = ob
    base["fconsts"] = np.tile(np.array([FCONST_VALS], np.float32), (128, 1))
    ir = np.zeros((2, 128), BF16NP)
    ir[0, 0:64] = 1
    ir[1, 64:128] = 1
    base["indrow"] = ir

    i_idx = np.arange(128)[:, None]
    q_idx = np.arange(128)[None, :]
    R4 = np.zeros((4, 2, 128, 512), np.float32)
    R4f_main = np.zeros((4, 128, 512), np.float32)
    for g in range(4):
        for cls in range(2):
            rel = (i_idx - q_idx) - (128 if cls == 0 else 0)
            ok = (rel <= 0) & (rel >= -HALO)
            for j in range(4):
                h = g + 4 * j
                tile = np.where(ok, slopes[h] * rel, NEG).astype(np.float32)
                R4[g, cls, :, j * 128:(j + 1) * 128] = tile
        R4f_main[g] = R4[g, 0]
    R4f_core0 = np.full((4, 128, 512), NEG, np.float32)

    in_maps = []
    for r in range(NCORES):
        m = dict(base)
        xm = x[:, r * SL:(r + 1) * SL, :].transpose(2, 0, 1).reshape(1024, NT)
        if r == 0:
            xh = np.zeros((1024, B * HALO), np.float32)
        else:
            xh = x[:, r * SL - HALO: r * SL, :].transpose(2, 0, 1).reshape(1024, B * HALO)
        m["x_in"] = np.ascontiguousarray(np.concatenate([xm, xh], axis=1))
        m["aa_R4"] = R4
        m["aa_R4f"] = R4f_core0 if r == 0 else R4f_main
        in_maps.append(m)
    return in_maps


def emit_l0ab(l, x, q_fm, kt_hbm, vt_hbm, scale, qt_w=256):
    nc = l.nc
    l.phase_barrier()
    NKT = S // 128
    with contextlib.ExitStack() as st:
        sb = lambda n, s_, d: st.enter_context(nc.sbuf_tensor(n, s_, d))
        kslot = [sb(f"abk{i}", [128, 8 * 128], BF16) for i in range(2)]
        vslot = [sb(f"abv{i}", [128, 1024], BF16) for i in range(2)]
        esb = sb("abe", [128, 2 * qt_w], BF16)
        avs = sb("abav", [128, qt_w], BF16)
        dens = sb("abden", [1, qt_w], F32)
        dinvf = sb("abdinvf", [1, qt_w], F32)
        dinvb = sb("abdinv", [1, qt_w], BF16)
        slot_user = [-1, -1]

        def issue_kv(idx, b, kt):
            s_ = idx % 2
            l.wait(nc.gpsimd, l.TD,
                   slot_user[s_] if slot_user[s_] >= 0 else l.val(l.TD))
            l.inc(nc.gpsimd.dma_start(out=kslot[s_][:], in_=kt_hbm[b, kt]),
                  l.WD, 16)
            l.inc(nc.gpsimd.dma_start(
                out=vslot[s_][:], in_=vt_hbm[b][kt * 128:(kt + 1) * 128, :]),
                l.WD, 16)
            return s_, l.val(l.WD)

        eslu = [0, 0]
        nqt = SL // qt_w
        for b in range(B):
            for qt in range(nqt):
                q0 = b * SL + qt * qt_w
                for bk in range(7):
                    l.acquire(bk)
                cur = issue_kv(0, b, 0)
                q_ld = l.val(l.LD)
                exp_es0 = [0] * NKT
                kv = [None] * (NKT + 1)
                kv[0] = cur

                def scores0(kt):
                    s_, rdy = kv[kt]
                    sb_bank = kt % 2
                    l.wait(nc.tensor, l.WD, rdy)
                    _es, _ev = l.bank_free[sb_bank]
                    l.waitE(nc.tensor, _es, _ev)
                    l.wait(nc.tensor, l.LD, q_ld)
                    last = None
                    for pt in range(8):
                        last = nc.tensor.matmul(
                            l.bank(sb_bank)[0:128, 0:qt_w],
                            kslot[s_][:, pt * 128:(pt + 1) * 128],
                            q_fm.t(pt, q0, qt_w),
                            start=(pt == 0), stop=(pt == 7))
                    l.inc(last, l.TD)
                    l.written(sb_bank)

                def exp0(kt):
                    sb_bank = kt % 2
                    l.consume(nc.scalar, sb_bank)
                    l.wait(nc.scalar, l.TD, eslu[kt % 2])
                    l.inc(nc.scalar.activation(
                        esb[:, (kt % 2) * qt_w:(kt % 2 + 1) * qt_w],
                        l.bank(sb_bank)[0:128, 0:qt_w],
                        AF.Exp, scale=scale), l.ES)
                    l.freed(sb_bank)
                    exp_es0[kt] = l.val(l.ES)

                def av0(kt):
                    s_, rdy = kv[kt]
                    l.wait(nc.tensor, l.ES, exp_es0[kt])
                    nc.tensor.matmul(l.bank(6)[0:1, 0:qt_w], l.c_ones128_b,
                                     esb[:, (kt % 2) * qt_w:(kt % 2 + 1) * qt_w],
                                     start=(kt == 0), stop=(kt == NKT - 1))
                    last = None
                    for c in range(8):
                        last = nc.tensor.matmul(
                            l.bank(2 + c // 2)[0:128, (c % 2) * qt_w:(c % 2 + 1) * qt_w],
                            vslot[s_][:, c * 128:(c + 1) * 128],
                            esb[:, (kt % 2) * qt_w:(kt % 2 + 1) * qt_w],
                            start=(kt == 0), stop=(kt == NKT - 1))
                    l.inc(last, l.TD)
                    slot_user[s_] = l.val(l.TD)
                    eslu[kt % 2] = l.val(l.TD)

                kv[1] = issue_kv(1, b, 1)
                scores0(0)
                exp0(0)
                for kt in range(NKT):
                    if kt + 1 < NKT:
                        scores0(kt + 1)
                    av0(kt)
                    if kt + 2 <= NKT - 1:
                        kv[kt + 2] = issue_kv(kt + 2, b, kt + 2)
                    if kt + 1 < NKT:
                        exp0(kt + 1)
                for bk in range(7):
                    l.written(bk)
                l.consume(nc.scalar, 6)
                l.inc(nc.scalar.copy(dens[:], l.bank(6)[0:1, 0:qt_w]), l.ES)
                l.freed(6)
                l.waitE(nc.vector)
                l.inc(nc.vector.reciprocal(dinvf[:], dens[:]), l.EV)
                l.waitE(nc.scalar)
                l.inc(nc.scalar.copy(dinvb[:], dinvf[:]), l.ES)
                l.acquire(0)
                l.waitE(nc.tensor)
                l.inc(nc.tensor.matmul(l.bank(0)[0:128, 0:qt_w], l.c_ones1_128,
                                       dinvb[:], start=True, stop=True), l.TD)
                l.written(0)
                for c in range(8):
                    bk = 2 + c // 2
                    l.consume(nc.scalar, bk)
                    l.waitE(nc.scalar)
                    l.inc(nc.scalar.copy(
                        avs[:], l.bank(bk)[0:128, (c % 2) * qt_w:(c % 2 + 1) * qt_w]),
                        l.ES)
                    l.consume(nc.vector, 0)
                    l.waitE(nc.vector)
                    l.inc(nc.vector.tensor_tensor(avs[:], avs[:],
                                                  l.bank(0)[0:128, 0:qt_w],
                                                  op=ALU.mult), l.EV)
                    l.inc(nc.vector.tensor_tensor(x.t(c, q0, qt_w), x.t(c, q0, qt_w),
                                                  avs[:], op=ALU.add), l.EV)
                for bk in range(7):
                    l.freed(bk)


def build_launch2(upto=99):
    nc = bass.Bass("TRN2", target_bir_lowering=False)
    ctx = contextlib.ExitStack()
    with ctx:
        l = L(nc, ctx)
        x_in = l.din("x_in", [1024, NT], F32)
        qb_in = l.din("qb_in", [1024, NT], BF16)
        kt_all = l.din("kt_all", [B, S // 128, 128, 8 * 128], BF16)
        vt_all = l.din("vt_all", [B, S, 1024], BF16)
        hb = {}
        for nm in ("wfuf", "wiui", "wcuc", "wouo"):
            hb[nm] = l.din(nm, [8, 2048, 128], F32)
        hb["wdelta"] = l.din("wdelta", [1, 128, 128], F32)
        hb["win"] = l.din("win", [1, 128, 128], F32)
        hb["wgate"] = l.din("wgate", [8, 128, 1024], F32)
        hb["wq1"] = l.din("wq1", [8, 128, 1024], F32)
        hb["wk1"] = l.din("wk1", [8, 128, 1024], F32)
        hb["wv1"] = l.din("wv1", [8, 128, 1024], F32)
        fconsts = l.din("fconsts", [128, 8], F32)
        consts1 = l.din("consts1", [128, 32], F32)
        consts2 = l.din("consts2", [128, 40], F32)
        consts3 = l.din("consts3", [128, 64], F32)
        onesb = l.din("onesb", [128, 4], BF16)
        indrow = l.din("indrow", [2, 128], BF16)

        xres_o = l.dout("xres", [1024, NT], F32)
        delta_o = l.dout("delta", [16, NT], BF16)
        bt_o = l.dout("bt", [16, NT], BF16)
        gate_o = l.dout("gate", [1024, NT], BF16)
        q1_o = l.dout("q1", [1024, NT], BF16)
        k1_o = l.dout("k1", [1024, NT], BF16)
        v1_o = l.dout("v1", [1024, NT], BF16)

        fcs = l.sbuf("fcs", [128, 8], F32)
        l.inc(nc.sync.dma_start(out=fcs[:], in_=fconsts[:]), l.LD, 16)
        for i, v in enumerate(FCONST_VALS):
            nc.const_aps.register(F32, float(v), fcs[:, i:i + 1])
        x = alloc_fm(l, "x", 1024, NT, F32)
        xn = alloc_fm(l, "xn", 1024, NT, BF16)
        cst = l.sbuf("cst", [128, 32], F32)
        cst2 = l.sbuf("cst2", [128, 40], F32)
        cst3 = l.sbuf("cst3", [128, 64], F32)
        onesb_sb = l.sbuf("onesbsb", [128, 4], BF16)
        ind2_128 = l.sbuf("ind2128", [2, 128], BF16)
        rowones = l.sbuf("rowones", [1, 128], BF16)
        stage = l.sbuf("stage", [128, 4 * 512], BF16)

        load_fm(l, x, x_in)
        l.inc(nc.sync.dma_start(out=cst[:], in_=consts1[:]), l.LD, 16)
        l.inc(nc.sync.dma_start(out=cst2[:], in_=consts2[:]), l.LD, 16)
        l.inc(nc.sync.dma_start(out=cst3[:], in_=consts3[:]), l.LD, 16)
        l.inc(nc.sync.dma_start(out=onesb_sb[:], in_=onesb[:]), l.LD, 16)
        l.inc(nc.sync.dma_start(out=ind2_128[:], in_=indrow[:]), l.LD, 16)
        l.inc(nc.gpsimd.memset(rowones[:], 1.0), l.WD, 1)
        l.c_ones128_b = onesb_sb[:, 0:1]
        l.c_ones1_128 = rowones[0:1, 0:128]
        l.c_ones1_64 = rowones[0:1, 0:64]
        l.c_ind2_128 = ind2_128[:]
        l.c_bf_fm = cst[:, 8:16]
        l.c_bi_fm = cst[:, 16:24]
        l.c_bc_fm = cst[:, 24:32]
        l.c_bo_fm = cst2[:, 0:8]
        l.c_inc_fm = cst2[:, 8:16]
        for eng in (nc.tensor, nc.scalar, nc.vector):
            l.wait(eng, l.LD)
            l.wait(eng, l.WD)

        with contextlib.ExitStack() as st3:
            q_sb = FM(st3.enter_context(nc.sbuf_tensor("qown", [128, 8 * NT], BF16)),
                      1024, NT)
            load_fm(l, q_sb, qb_in)
            if upto >= 1:
                emit_l0ab(l, x, q_sb, kt_all, vt_all, scale=1.0 / 32.0)
            l.phase_barrier()
        if upto >= 2:
            do_ln(l, x, xn, C=NT)
            emit_l0ffn(l, x, xn, hb, pfx="f1")
        if upto >= 3:
            xln = alloc_fm(l, "xln", 1024, NT, BF16)
            do_ln(l, x, xln, C=NT)
            xn_a = xn
            do_ln(l, xln, xn_a, C=NT, g_fm=cst3[:, 0:8], b_fm=cst3[:, 8:16])
            ws1 = WStream(l, "wd16", 1024, ow=16)
            emit_project_store(l, hb["wdelta"], ws1, xn_a, 1, MAIN4, delta_o,
                               stage=stage, act=AF.Sigmoid,
                               bias_ap=cst2[0:16, 32:33])
            emit_project_store(l, hb["win"], ws1, xn_a, 1, MAIN4, bt_o,
                               stage=stage, bias_ap=cst2[0:16, 33:34])
            ws8 = WStream(l, "w8", 1024)
            emit_project_store(l, hb["wgate"], ws8, xn_a, 8, MAIN4, gate_o,
                               bias_fm=cst3[:, 32:40], stage=stage,
                               act=AF.Sigmoid)
            do_ln(l, xln, xn_a, C=NT, g_fm=cst3[:, 16:24], b_fm=cst3[:, 24:32])
            emit_project_store(l, hb["wq1"], ws8, xn_a, 8, MAIN4, q1_o,
                               bias_fm=cst3[:, 40:48], stage=stage)
            emit_project_store(l, hb["wk1"], ws8, xn_a, 8, MAIN4, k1_o,
                               bias_fm=cst3[:, 48:56], stage=stage)
            emit_project_store(l, hb["wv1"], ws8, xn_a, 8, MAIN4, v1_o,
                               bias_fm=cst3[:, 56:64], stage=stage)

        l.waitE(nc.sync)
        l.wait(nc.sync, l.TD)
        for p in range(8):
            l.inc(nc.sync.dma_start(out=xres_o[p * 128:(p + 1) * 128, :],
                                    in_=x.t(p, 0, NT)), l.ST, 16)
        l.wait(nc.sync, l.ST)
    return nc, l





def _tile_w2(w_eff, ow=128):
    return _tile_w(w_eff, ow=ow)


def prep_launch2(inp, l1outs):
    base = {}
    for nm, wk, uk in (("wfuf", "L0_ffn_wf", "L0_ffn_uf"),
                       ("wiui", "L0_ffn_wi", "L0_ffn_ui"),
                       ("wcuc", "L0_ffn_wc", "L0_ffn_uc"),
                       ("wouo", "L0_ffn_wo", "L0_ffn_uo")):
        base[nm] = np.ascontiguousarray(np.concatenate(
            [_tile_w(inp[wk].T), _tile_w(inp[uk].T)], axis=2))
    base["wdelta"] = _tile_w2(inp["L1_attn_a_Wdelta"], ow=16)
    base["win"] = _tile_w2(inp["L1_attn_a_Win"], ow=16)
    base["wgate"] = _tile_w(inp["L1_attn_a_Wgate"])
    base["wq1"] = _tile_w(inp["L1_attn_a_Wq"])
    base["wk1"] = _tile_w(inp["L1_attn_a_Wk"])
    base["wv1"] = _tile_w(inp["L1_attn_a_Wv"])
    base["fconsts"] = np.tile(np.array([FCONST_VALS], np.float32), (128, 1))
    c1 = np.zeros((128, 32), np.float32)
    c1[:, 8:16] = _fmcols(inp["L0_ffn_bf"])
    c1[:, 16:24] = _fmcols(inp["L0_ffn_bi"])
    c1[:, 24:32] = _fmcols(inp["L0_ffn_bc"])
    base["consts1"] = c1
    c2 = np.zeros((128, 40), np.float32)
    c2[:, 0:8] = _fmcols(inp["L0_ffn_bo"])
    c2[:, 8:16] = _fmcols(inp["L0_ffn_in_c"])
    c2[0:16, 32] = inp["L1_attn_a_bdelta"]
    c2[0:16, 33] = inp["L1_attn_a_b_in"]
    base["consts2"] = c2
    c3 = np.zeros((128, 64), np.float32)
    c3[:, 0:8] = _fmcols(inp["L1_attn_a_ln_g"])
    c3[:, 8:16] = _fmcols(inp["L1_attn_a_ln_b"])
    c3[:, 16:24] = _fmcols(inp["L1_attn_a_ln1_g"])
    c3[:, 24:32] = _fmcols(inp["L1_attn_a_ln1_b"])
    c3[:, 32:40] = _fmcols(inp["L1_attn_a_bgate"])
    c3[:, 40:48] = _fmcols(inp["L1_attn_a_bq"])
    c3[:, 48:56] = _fmcols(inp["L1_attn_a_bk"])
    c3[:, 56:64] = _fmcols(inp["L1_attn_a_bv"])
    base["consts3"] = c3
    ob = np.zeros((128, 4), BF16NP)
    ob[:, 0] = 1
    base["onesb"] = ob
    ir = np.zeros((2, 128), BF16NP)
    ir[0, 0:64] = 1
    ir[1, 64:128] = 1
    base["indrow"] = ir

    kt_flat = np.zeros((B, 1024, S), BF16NP)
    vt_all = np.zeros((B, S, 1024), BF16NP)
    for r in range(NCORES):
        kb = l1outs[r]["kb"]
        vb = l1outs[r]["vb"]
        for b in range(B):
            kt_flat[b][:, r * SL:(r + 1) * SL] = kb[:, b * SL:(b + 1) * SL]
            vt_all[b][r * SL:(r + 1) * SL, :] = vb[:, b * SL:(b + 1) * SL].T
    kt_all = np.ascontiguousarray(
        kt_flat.reshape(B, 8, 128, S // 128, 128).transpose(0, 3, 2, 1, 4)
        .reshape(B, S // 128, 128, 8 * 128))
    base["kt_all"] = kt_all
    base["vt_all"] = vt_all

    in_maps = []
    for r in range(NCORES):
        m = dict(base)
        m["x_in"] = l1outs[r]["xres"]
        m["qb_in"] = l1outs[r]["qb"]
        in_maps.append(m)
    return in_maps


NSWEEP = 12


def emit_scan(l, delta_sb, bt_sb, h_a, h_b, abig, i64, h_bf):
    nc = l.nc
    l.phase_barrier()
    l.inc(nc.gpsimd.memset(h_a[:], 0.0), l.WD, 1)
    l.inc(nc.gpsimd.memset(h_b[:], 0.0), l.WD, 1)
    t1b = l.scan_t1b
    src, dst = h_a, h_b
    for sw in range(NSWEEP):
        l.waitE(nc.scalar)
        l.wait(nc.scalar, l.TD)
        l.wait(nc.scalar, l.WD)
        l.inc(nc.scalar.copy(src[:, 0:1], src[:, S:S + 1]), l.ES)
        for stile in range(8):
            c0 = stile * 512
            l.waitE(nc.vector)
            l.wait(nc.vector, l.WD)
            l.inc(nc.vector.tensor_tensor(t1b[:, 0:512], src[:, c0:c0 + 512],
                                          delta_sb[:, c0:c0 + 512], op=ALU.mult),
                  l.EV)
            bi = l.nb()
            l.acquire(bi)
            l.waitE(nc.tensor)
            nc.tensor.matmul(l.bank(bi)[0:64, 0:512], abig, t1b[:, 0:512],
                             start=True, stop=False)
            l.inc(nc.tensor.matmul(l.bank(bi)[0:64, 0:512], i64,
                                   bt_sb[:, c0:c0 + 512], start=False, stop=True),
                  l.TD)
            l.written(bi)
            l.consume(nc.scalar, bi)
            l.inc(nc.scalar.activation(dst[:, 1 + c0:1 + c0 + 512],
                                       l.bank(bi)[0:64, 0:512], AF.Tanh), l.ES)
            l.freed(bi)
        src, dst = dst, src
    l.waitE(nc.scalar)
    l.inc(nc.scalar.copy(h_bf[:], src[0:64, 1:513]), l.ES)


def emit_l1a_attn(l, ctx_fm, kt1h, vt1h, scale):
    nc = l.nc
    l.phase_barrier()
    NKT = S // 128
    with contextlib.ExitStack() as st:
        sb = lambda n, s_, d: st.enter_context(nc.sbuf_tensor(n, s_, d))
        kslot = [sb(f"l1k{i}", [128, S], BF16) for i in range(2)]
        vslot = [sb(f"l1v{i}", [128, 32 * 128], BF16) for i in range(2)]
        esb = sb("l1e", [128, 2 * 512], BF16)
        avs = sb("l1av", [128, 512], BF16)
        dens = sb("l1den", [1, 512], F32)
        dinvf = sb("l1dinvf", [1, 512], F32)
        dinvb = sb("l1dinv", [1, 512], BF16)
        slot_user = [-1, -1]

        def issue(idx, b, h):
            s_ = idx % 2
            l.wait(nc.gpsimd, l.TD,
                   slot_user[s_] if slot_user[s_] >= 0 else l.val(l.TD))
            l.inc(nc.gpsimd.dma_start(out=kslot[s_][:],
                                      in_=kt1h[b][h * 128:(h + 1) * 128, :]),
                  l.WD, 16)
            l.inc(nc.gpsimd.dma_start(out=vslot[s_][:], in_=vt1h[b, h]), l.WD, 16)
            return s_, l.val(l.WD)

        eslu = [0, 0]
        pairs = [(b, h) for b in range(B) for h in range(8)]
        cur = issue(0, *pairs[0])
        for pi, (b, h) in enumerate(pairs):
            nxt = issue(pi + 1, *pairs[pi + 1]) if pi + 1 < len(pairs) else None
            s_, rdy = cur
            l.wait(nc.tensor, l.WD, rdy)
            for bk in (2, 3):
                l.acquire(bk)
            exp_es2 = [0] * NKT

            def scores1(kt):
                sbk = kt % 2
                _es, _ev = l.bank_free[sbk]
                l.waitE(nc.tensor, _es, _ev)
                l.inc(nc.tensor.matmul(l.bank(sbk)[0:128, 0:512],
                                       kslot[s_][:, kt * 128:(kt + 1) * 128],
                                       ctx_fm.t(h, b * SL, SL),
                                       start=True, stop=True), l.TD)
                l.written(sbk)

            def exp1(kt):
                sbk = kt % 2
                l.consume(nc.scalar, sbk)
                l.wait(nc.scalar, l.TD, eslu[kt % 2])
                l.inc(nc.scalar.activation(esb[:, (kt % 2) * 512:(kt % 2 + 1) * 512],
                                           l.bank(sbk)[0:128, 0:512], AF.Exp,
                                           scale=scale), l.ES)
                l.freed(sbk)
                exp_es2[kt] = l.val(l.ES)

            def av1(kt):
                l.wait(nc.tensor, l.ES, exp_es2[kt])
                nc.tensor.matmul(l.bank(3)[0:1, 0:512], l.c_ones128_b,
                                 esb[:, (kt % 2) * 512:(kt % 2 + 1) * 512],
                                 start=(kt == 0), stop=(kt == NKT - 1))
                l.inc(nc.tensor.matmul(l.bank(2)[0:128, 0:512],
                                       vslot[s_][:, kt * 128:(kt + 1) * 128],
                                       esb[:, (kt % 2) * 512:(kt % 2 + 1) * 512],
                                       start=(kt == 0), stop=(kt == NKT - 1)),
                      l.TD)
                eslu[kt % 2] = l.val(l.TD)
                slot_user[s_] = l.val(l.TD)

            scores1(0)
            exp1(0)
            for kt in range(NKT):
                if kt + 1 < NKT:
                    scores1(kt + 1)
                av1(kt)
                if kt + 1 < NKT:
                    exp1(kt + 1)
            for bk in (2, 3):
                l.written(bk)
            l.consume(nc.scalar, 3)
            l.inc(nc.scalar.copy(dens[:], l.bank(3)[0:1, 0:512]), l.ES)
            l.freed(3)
            l.waitE(nc.vector)
            l.inc(nc.vector.reciprocal(dinvf[:], dens[:]), l.EV)
            l.waitE(nc.scalar)
            l.inc(nc.scalar.copy(dinvb[:], dinvf[:]), l.ES)
            l.consume(nc.scalar, 2)
            l.inc(nc.scalar.copy(avs[:], l.bank(2)[0:128, 0:512]), l.ES)
            l.freed(2)
            l.acquire(4)
            l.waitE(nc.tensor)
            l.inc(nc.tensor.matmul(l.bank(4)[0:128, 0:512], l.c_ones1_128,
                                   dinvb[:], start=True, stop=True), l.TD)
            l.written(4)
            l.consume(nc.vector, 4)
            l.waitE(nc.vector)
            l.inc(nc.vector.tensor_tensor(ctx_fm.t(h, b * SL, SL), avs[:],
                                          l.bank(4)[0:128, 0:512], op=ALU.mult),
                  l.EV)
            l.freed(4)
            cur = nxt


def build_launch3(upto=99):
    nc = bass.Bass("TRN2", target_bir_lowering=False)
    ctx = contextlib.ExitStack()
    with ctx:
        l = L(nc, ctx)
        x_in = l.din("x_in", [1024, NT], F32)
        gate_in = l.din("gate_in", [1024, NT], BF16)
        q1_in = l.din("q1_in", [1024, NT], BF16)
        delta_in = l.din("delta_in", [64, S], BF16)
        bt_in = l.din("bt_in", [64, S], BF16)
        kt1_all = l.din("kt1_all", [B, 1024, S], BF16)
        vt1h_all = l.din("vt1h_all", [B, 8, 128, (S // 128) * 128], BF16)
        abig_in = l.din("abig_in", [64, 64], F32)
        i64_in = l.din("i64_in", [64, 64], BF16)
        cmask_in = l.din("cmask_in", [64, B * 8 * 128], F32)
        hb = {}
        hb["wo1"] = l.din("wo1", [8, 128, 1024], F32)
        hb["w1"] = l.din("w1", [32, 128, 1024], F32)
        hb["w2"] = l.din("w2", [8, 128, 4096], F32)
        hb["wfa"] = l.din("wfa", [8, 128, 1024], F32)
        hb["wfb"] = l.din("wfb", [8, 128, 1024], F32)
        hb["wqkv2"] = l.din("wqkv2", [24, 128, 1024], F32)
        g1t_in = l.din("g1t_in", [32, 2 * 4 * 32], F32)
        g2t_in = l.din("g2t_in", [16, 2 * 4 * 16], F32)
        indd_in = l.din("indd_in", [32, 4 * 128], BF16)
        indg_in = l.din("indg_in", [16, 128], BF16)
        fconsts = l.din("fconsts", [128, 8], F32)
        consts4 = l.din("consts4", [128, 96], F32)
        b1fm_in = l.din("b1fm", [128, 32], F32)
        onesb = l.din("onesb", [128, 4], BF16)

        xres_o = l.dout("xres", [1024, NT], F32)
        q2_o = l.dout("q2", [1024, NT], BF16)
        k2_o = l.dout("k2", [1024, NT], BF16)
        v2_o = l.dout("v2", [1024, NT], BF16)

        fcs = l.sbuf("fcs", [128, 8], F32)
        l.inc(nc.sync.dma_start(out=fcs[:], in_=fconsts[:]), l.LD, 16)
        for i, v in enumerate(FCONST_VALS):
            nc.const_aps.register(F32, float(v), fcs[:, i:i + 1])
        cst4 = l.sbuf("cst4", [128, 96], F32)
        b1fm = l.sbuf("b1fm_sb", [128, 32], F32)
        onesb_sb = l.sbuf("onesbsb", [128, 4], BF16)
        rowones = l.sbuf("rowones", [1, 128], BF16)
        stage = l.sbuf("stage", [128, 4 * 512], BF16)
        l.inc(nc.sync.dma_start(out=cst4[:], in_=consts4[:]), l.LD, 16)
        l.inc(nc.sync.dma_start(out=b1fm[:], in_=b1fm_in[:]), l.LD, 16)
        l.inc(nc.sync.dma_start(out=onesb_sb[:], in_=onesb[:]), l.LD, 16)
        l.inc(nc.gpsimd.memset(rowones[:], 1.0), l.WD, 1)
        l.c_ones128_b = onesb_sb[:, 0:1]
        l.c_ones1_128 = rowones[0:1, 0:128]
        l.c_ln2g, l.c_ln2b = cst4[:, 0:8], cst4[:, 8:16]
        l.c_lnfg, l.c_lnfb = cst4[:, 16:24], cst4[:, 24:32]
        l.c_bo1 = cst4[:, 32:40]
        l.c_b2 = cst4[:, 40:48]
        l.c_bfw = cst4[:, 48:56]
        l.c_bqkv2 = cst4[:, 56:80]
        l.c_bri = cst4[:, 80:88]
        for eng in (nc.tensor, nc.scalar, nc.vector):
            l.wait(eng, l.LD)
            l.wait(eng, l.WD)

        fused_d = nc.dram_tensor("fused_d", [1024, NT], BF16, kind="Internal").ap()
        big = contextlib.ExitStack()
        ctx.enter_context(big)
        xln = FM(big.enter_context(nc.sbuf_tensor("xln", [128, 8 * NT], BF16)),
                 1024, NT)
        mo = FM(big.enter_context(nc.sbuf_tensor("mo", [128, 8 * NT], BF16)),
                1024, NT)

        with contextlib.ExitStack() as st0:
            x0 = FM(st0.enter_context(nc.sbuf_tensor("x0", [128, 8 * NT], F32)),
                    1024, NT)
            load_fm(l, x0, x_in)
            do_ln(l, x0, xln, C=NT)
            l.phase_barrier()

        with contextlib.ExitStack() as st1:
            sb1 = lambda n, s_, d: st1.enter_context(nc.sbuf_tensor(n, s_, d))
            delta_sb = sb1("deltasb", [64, S], BF16)
            bt_sb = sb1("btsb", [64, S], BF16)
            h_a = sb1("ha", [64, S + 1], F32)
            h_b = sb1("hbb", [64, S + 1], F32)
            abig_sb = sb1("abig", [64, 64], BF16)
            i64_sb = sb1("i64", [64, 64], BF16)
            h_bf = sb1("hbf", [64, 512], BF16)
            cmask_sb = sb1("cmask", [64, B * 8 * 128], BF16)
            gate_fm = FM(sb1("gate", [128, 8 * NT], BF16), 1024, NT)
            l.scan_t1b = sb1("t1b", [64, 512], BF16)
            l.inc(nc.sync.dma_start(out=delta_sb[:], in_=delta_in[:]), l.LD, 16)
            l.inc(nc.sync.dma_start(out=bt_sb[:], in_=bt_in[:]), l.LD, 16)
            l.inc(nc.gpsimd.dma_start(out=abig_sb[:], in_=abig_in[:]), l.WD, 16)
            l.inc(nc.sync.dma_start(out=i64_sb[:], in_=i64_in[:]), l.LD, 16)
            l.inc(nc.gpsimd.dma_start(out=cmask_sb[:], in_=cmask_in[:]), l.WD, 16)
            load_fm(l, gate_fm, gate_in)
            if upto >= 1:
                emit_scan(l, delta_sb, bt_sb, h_a, h_b, abig_sb[:], i64_sb[:], h_bf)
                for b in range(B):
                    for ot in range(8):
                        bi = l.nb()
                        l.acquire(bi)
                        l.waitE(nc.tensor)
                        l.wait(nc.tensor, l.WD)
                        l.inc(nc.tensor.matmul(
                            l.bank(bi)[0:128, 0:512],
                            cmask_sb[:, (b * 8 + ot) * 128:(b * 8 + ot + 1) * 128],
                            h_bf[:], start=True, stop=True), l.TD)
                        l.written(bi)
                        l.consume(nc.vector, bi)
                        l.waitE(nc.vector)
                        l.inc(nc.vector.tensor_tensor(
                            mo.t(ot, b * SL, SL), l.bank(bi)[0:128, 0:512],
                            gate_fm.t(ot, b * SL, SL), op=ALU.mult), l.EV)
                        l.inc(nc.vector.tensor_tensor(
                            mo.t(ot, b * SL, SL), mo.t(ot, b * SL, SL),
                            xln.t(ot, b * SL, SL), op=ALU.add), l.EV)
                        l.freed(bi)
            l.phase_barrier()

        with contextlib.ExitStack() as st3:
            ctxq = FM(st3.enter_context(nc.sbuf_tensor("ctxq", [128, 8 * NT], BF16)),
                      1024, NT)
            load_fm(l, ctxq, q1_in)
            if upto >= 2:
                emit_l1a_attn(l, ctxq, kt1_all, vt1h_all, scale=1.0 / (128 ** 0.5))
                wso = WStream(l, "wso1", 1024, stack=st3)

                def x2_evac(ot, c0, w, tag, ps, bi):
                    l.consume(nc.vector, bi)
                    l.waitE(nc.vector)
                    l.inc(nc.vector.tensor_tensor(xln.t(ot, c0, w), xln.t(ot, c0, w),
                                                  ps, op=ALU.add), l.EV)
                    l.freed(bi)
                    l.waitE(nc.scalar)
                    l.inc(nc.scalar.add(xln.t(ot, c0, w), xln.t(ot, c0, w),
                                        l.c_bo1[:, ot:ot + 1]), l.ES)
                emit_project(l, hb["wo1"], wso, ctxq, 8, MAIN4, x2_evac)
            l.phase_barrier()

        if upto >= 3:
            with contextlib.ExitStack() as st5:
                sb5 = lambda n, s_, d: st5.enter_context(nc.sbuf_tensor(n, s_, d))
                fused = FM(sb5("fusedsb", [128, 8 * 512], BF16), 1024, 512)
                xn2 = FM(sb5("xn2", [128, 8 * 512], BF16), 1024, 512)
                t1f = FM(sb5("t1f", [128, 32 * 512], BF16), 4096, 512)
                tout = FM(sb5("tout", [128, 8 * 512], BF16), 1024, 512)
                fpre = FM(sb5("fpre", [128, 8 * 512], BF16), 1024, 512)
                ws_w1 = WStream(l, "wsw1", 1024, stack=st5)
                ws_w2 = WStream(l, "wsw2", 4096, stack=st5)
                ws_fa = WStream(l, "wsfa", 1024, stack=st5)
                ws_fb = WStream(l, "wsfb", 1024, stack=st5)
                for nt in range(4):
                    c0 = nt * 512
                    do_ln(l, xln, xn2, C=512, g_fm=l.c_ln2g, b_fm=l.c_ln2b,
                          window=(c0, 512))

                    def t1_evac(ot, cc0, w, tag, ps, bi):
                        l.consume(nc.scalar, bi)
                        l.waitE(nc.scalar)
                        l.inc(nc.scalar.activation(t1f.t(ot, 0, 512), ps, AF.Relu,
                                                   bias=b1fm[:, ot:ot + 1]), l.ES)
                        l.freed(bi)
                    emit_project(l, hb["w1"], ws_w1, xn2, 32, [(0, 512, 0)], t1_evac)

                    def t2_evac(ot, cc0, w, tag, ps, bi):
                        l.consume(nc.scalar, bi)
                        l.waitE(nc.scalar)
                        l.inc(nc.scalar.activation(tout.t(ot, 0, 512), ps,
                                                   AF.Identity,
                                                   bias=l.c_b2[:, ot:ot + 1]), l.ES)
                        l.freed(bi)
                        l.waitE(nc.vector)
                        l.inc(nc.vector.tensor_tensor(tout.t(ot, 0, 512),
                                                      tout.t(ot, 0, 512),
                                                      xln.t(ot, c0, 512),
                                                      op=ALU.add), l.EV)
                    emit_project(l, hb["w2"], ws_w2, t1f, 8, [(0, 512, 0)], t2_evac)

                    sfa = ws_fa.issue(hb["wfa"], 0)
                    sfb = ws_fb.issue(hb["wfb"], 0)
                    for ot in range(8):
                        nfa = ws_fa.issue(hb["wfa"], ot + 1) if ot < 7 else None
                        nfb = ws_fb.issue(hb["wfb"], ot + 1) if ot < 7 else None
                        ws_fa.wait_ready(sfa)
                        ws_fb.wait_ready(sfb)
                        bi = l.nb()
                        l.acquire(bi)
                        l.waitE(nc.tensor)
                        for pt in range(8):
                            nc.tensor.matmul(l.bank(bi)[0:128, 0:512],
                                             ws_fa.chunk(sfa, pt),
                                             mo.t(pt, c0, 512),
                                             start=(pt == 0), stop=False)
                        last = None
                        for pt in range(8):
                            last = nc.tensor.matmul(l.bank(bi)[0:128, 0:512],
                                                    ws_fb.chunk(sfb, pt),
                                                    tout.t(pt, 0, 512),
                                                    start=False, stop=(pt == 7))
                        l.inc(last, l.TD)
                        l.written(bi)
                        ws_fa.use(sfa)
                        ws_fb.use(sfb)
                        l.consume(nc.scalar, bi)
                        l.waitE(nc.scalar)
                        l.inc(nc.scalar.activation(fpre.t(ot, 0, 512), ps_ := l.bank(bi)[0:128, 0:512],
                                                   AF.Identity,
                                                   bias=l.c_bfw[:, ot:ot + 1]), l.ES)
                        l.freed(bi)
                        sfa, sfb = nfa, nfb
                    do_ln_to(l, fpre, fused, out_c0=0, g_fm=l.c_lnfg, b_fm=l.c_lnfb)
                    l.waitE(nc.sync)
                    for p in range(8):
                        l.inc(nc.sync.dma_start(
                            out=fused_d[p * 128:(p + 1) * 128, c0:c0 + 512],
                            in_=fused.t(p)), l.ST, 16)
        big.close()
        l.phase_barrier()

        with contextlib.ExitStack() as st6:
            x6 = FM(st6.enter_context(nc.sbuf_tensor("x6", [128, 8 * NT], F32)),
                    1024, NT)
            xnb = FM(st6.enter_context(nc.sbuf_tensor("xnb", [128, 8 * NT], BF16)),
                     1024, NT)
            fscr = st6.enter_context(nc.sbuf_tensor("fscr", [128, NT], BF16))
            load_fm(l, x6, x_in)
            if upto >= 3:
                for ot in range(8):
                    l.wait(nc.sync, l.ST)
                    l.waitE(nc.sync)
                    l.inc(nc.sync.dma_start(out=fscr[:],
                                            in_=fused_d[ot * 128:(ot + 1) * 128, :]),
                          l.LD, 16)
                    l.wait(nc.vector, l.LD)
                    l.inc(nc.vector.tensor_tensor(x6.t(ot), x6.t(ot), fscr[:],
                                                  op=ALU.add), l.EV)
            if upto >= 4:
                do_ln(l, x6, xnb, C=NT)
                emit_l1ffn(l, x6, xnb, g1t_in, g2t_in, indd_in, indg_in, st6)
            if upto >= 5:
                do_ln(l, x6, xnb, C=NT)
                ws7 = WStream(l, "ws7", 1024, stack=st6)
                emit_project_store(l, hb["wqkv2"], ws7, xnb, 24, MAIN4,
                                   None, bias_fm=l.c_bqkv2, stage=stage,
                                   hbm_outs=(q2_o, k2_o, v2_o))
            l.waitE(nc.sync)
            l.wait(nc.sync, l.TD)
            for p in range(8):
                l.inc(nc.sync.dma_start(out=xres_o[p * 128:(p + 1) * 128, :],
                                        in_=x6.t(p, 0, NT)), l.ST, 16)
            l.wait(nc.sync, l.ST)
    return nc, l


def do_ln_to(l, xsrc, xdst, out_c0, g_fm=None, b_fm=None):
    view = FM(xdst.sb, xdst.rows, xdst.C)
    class _V:
        np_ = xdst.np_
        C = xdst.C

        def t(self, p, c0=0, w=None):
            w = 512 - c0 if w is None else w
            return xdst.sb[:, p * xdst.C + out_c0 + c0:
                           p * xdst.C + out_c0 + c0 + w]
    do_ln(l, xsrc, _V(), C=512, g_fm=g_fm, b_fm=b_fm)


def emit_l1ffn(l, x, xn, g1t_in, g2t_in, indd_in, indg_in, stack):
    nc = l.nc
    l.phase_barrier()
    sb = lambda n, s_, d: stack.enter_context(nc.sbuf_tensor(n, s_, d))
    g1t = sb("g1t", [32, 256], BF16)
    g2t = sb("g2t", [16, 128], BF16)
    indd = sb("indd", [32, 4 * 128], BF16)
    indg = sb("indg", [16, 128], BF16)
    g1rep = sb("g1rep", [32, 512], BF16)
    g2rep = sb("g2rep", [16, 512], BF16)
    wacc = sb("wacc", [128, 512], F32)
    wtmp = sb("wtmp", [128, 512], F32)
    a_sb = sb("a_sb", [128, 512], BF16)
    wrT = sb("wrT", [128, 2048], BF16)
    wiT = sb("wiT", [128, 2048], BF16)
    wnT = sb("wnT", [128, 2048], BF16)
    stg = sb("l1fstg", [128, 512], BF16)
    l.inc(nc.gpsimd.dma_start(out=g1t[:], in_=g1t_in[:]), l.WD, 16)
    l.inc(nc.gpsimd.dma_start(out=g2t[:], in_=g2t_in[:]), l.WD, 16)
    l.inc(nc.sync.dma_start(out=indd[:], in_=indd_in[:]), l.LD, 16)
    l.inc(nc.sync.dma_start(out=indg[:], in_=indg_in[:]), l.LD, 16)
    l.phase_barrier()
    for wi_ in range(2):
        dst = (wrT, wiT)[wi_]
        for pt in range(4):
            for c in range(4):
                l.waitE(nc.vector)
                l.wait(nc.vector, l.WD)
                l.wait(nc.vector, l.TD)
                l.inc(nc.vector.tensor_copy(
                    g1rep[:].rearrange("p (b e) -> p b e", e=16),
                    g1t[:, (wi_ * 4 + c) * 32:(wi_ * 4 + c) * 32 + 32]
                    .unsqueeze(2).broadcast_to([32, 32, 16])), l.EV)
                l.inc(nc.vector.tensor_copy(
                    g2rep[:].rearrange("p (b e) -> p b e", e=16),
                    g2t[:, (wi_ * 4 + c) * 16:(wi_ * 4 + c) * 16 + 16]
                    .unsqueeze(1).broadcast_to([16, 32, 16])), l.EV)
                bA, bB = l.nb(), l.nb()
                l.acquire(bA)
                l.acquire(bB)
                l.waitE(nc.tensor)
                l.inc(nc.tensor.matmul(l.bank(bA)[0:128, 0:512],
                                       indd[:, pt * 128:(pt + 1) * 128],
                                       g1rep[:], start=True, stop=True), l.TD)
                l.inc(nc.tensor.matmul(l.bank(bB)[0:128, 0:512],
                                       indg[:], g2rep[:], start=True, stop=True),
                      l.TD)
                l.written(bA)
                l.written(bB)
                l.consume(nc.scalar, bA)
                l.inc(nc.scalar.copy(a_sb[:], l.bank(bA)[0:128, 0:512]), l.ES)
                l.freed(bA)
                l.consume(nc.vector, bB)
                l.waitE(nc.vector)
                l.inc(nc.vector.tensor_tensor(wtmp[:], a_sb[:],
                                              l.bank(bB)[0:128, 0:512],
                                              op=ALU.mult), l.EV)
                l.freed(bB)
                if c == 0:
                    l.inc(nc.vector.tensor_copy(wacc[:], wtmp[:]), l.EV)
                else:
                    l.inc(nc.vector.tensor_tensor(wacc[:], wacc[:], wtmp[:],
                                                  op=ALU.add), l.EV)
            l.waitE(nc.scalar)
            l.inc(nc.scalar.copy(dst[:, pt * 512:(pt + 1) * 512], wacc[:]), l.ES)
            if wi_ == 1:
                l.inc(nc.scalar.activation(wnT[:, pt * 512:(pt + 1) * 512],
                                           wacc[:], AF.Identity, scale=-1.0), l.ES)
    def chunk(buf, pt, ot):
        return buf[:, pt * 512 + ot * 128: pt * 512 + ot * 128 + 128]
    for half in range(2):
        for ot in range(4):
            for nt in range(4):
                c0 = nt * 512
                bi = l.nb()
                l.acquire(bi)
                l.waitE(nc.tensor)
                last = None
                lA = (wrT, wiT)[half]
                lB = (wnT, wrT)[half]
                for pt in range(4):
                    nc.tensor.matmul(l.bank(bi)[0:128, 0:512], chunk(lA, pt, ot),
                                     xn.t(pt, c0, 512), start=(pt == 0), stop=False)
                for pt in range(4):
                    last = nc.tensor.matmul(l.bank(bi)[0:128, 0:512],
                                            chunk(lB, pt, ot),
                                            xn.t(4 + pt, c0, 512),
                                            start=False, stop=(pt == 3))
                l.inc(last, l.TD)
                l.written(bi)
                l.consume(nc.scalar, bi)
                l.wait(nc.scalar, l.ST)
                l.inc(nc.scalar.activation(stg[:], l.bank(bi)[0:128, 0:512],
                                           AF.Identity,
                                           bias=l.c_bri[:, half * 4 + ot:
                                                        half * 4 + ot + 1]), l.ES)
                l.freed(bi)
                l.waitE(nc.vector)
                l.inc(nc.vector.tensor_tensor(x.t(half * 4 + ot, c0, 512),
                                              x.t(half * 4 + ot, c0, 512),
                                              stg[:], op=ALU.add), l.EV)


def prep_launch3(inp, l2outs):
    base = {}
    A = inp["L1_attn_a_A"].astype(np.float32)
    abig = np.zeros((64, 64), np.float32)
    for b in range(B):
        abig[b * 16:(b + 1) * 16, b * 16:(b + 1) * 16] = A
    base["abig_in"] = abig
    base["i64_in"] = np.eye(64, dtype=BF16NP)
    C_ = inp["L1_attn_a_C"].astype(np.float32)
    cm = np.zeros((64, B * 8 * 128), np.float32)
    for b in range(B):
        for ot in range(8):
            cm[b * 16:(b + 1) * 16, (b * 8 + ot) * 128:(b * 8 + ot + 1) * 128] = \
                C_[:, ot * 128:(ot + 1) * 128]
    base["cmask_in"] = cm
    base["wo1"] = _tile_w(inp["L1_attn_a_Wo"])
    base["w1"] = _tile_w(inp["L1_attn_a_W1"])
    base["w2"] = _tile_w(inp["L1_attn_a_W2"])
    base["wfa"] = _tile_w(inp["L1_attn_a_Wf"][0:1024])
    base["wfb"] = _tile_w(inp["L1_attn_a_Wf"][1024:2048])
    base["wqkv2"] = _tile_w(inp["L1_attn_b_w_qkv"].T)
    g1 = np.zeros((32, 256), np.float32)
    g2 = np.zeros((16, 128), np.float32)
    for wi_, (k1, k2) in enumerate((("L1_ffn_G1_R", "L1_ffn_G2_R"),
                                    ("L1_ffn_G1_I", "L1_ffn_G2_I"))):
        G1 = inp[k1].astype(np.float32)[0]
        G2 = inp[k2].astype(np.float32)
        for c in range(4):
            g1[:, wi_ * 128 + c * 32: wi_ * 128 + (c + 1) * 32] = G1[:, c, :].T
            g2[:, wi_ * 64 + c * 16: wi_ * 64 + (c + 1) * 16] = G2[c, :, 0, :].T
    base["g1t_in"] = g1
    base["g2t_in"] = g2
    indd = np.zeros((32, 4 * 128), BF16NP)
    for p in range(4):
        for loc in range(128):
            indd[8 * p + loc // 16, p * 128 + loc] = 1
    base["indd_in"] = indd
    indg = np.zeros((16, 128), BF16NP)
    for loc in range(128):
        indg[loc % 16, loc] = 1
    base["indg_in"] = indg
    base["fconsts"] = np.tile(np.array([FCONST_VALS], np.float32), (128, 1))
    c4 = np.zeros((128, 96), np.float32)
    c4[:, 0:8] = _fmcols(inp["L1_attn_a_ln2_g"])
    c4[:, 8:16] = _fmcols(inp["L1_attn_a_ln2_b"])
    c4[:, 16:24] = _fmcols(inp["L1_attn_a_lnf_g"])
    c4[:, 24:32] = _fmcols(inp["L1_attn_a_lnf_b"])
    c4[:, 32:40] = _fmcols(inp["L1_attn_a_bo"])
    c4[:, 40:48] = _fmcols(inp["L1_attn_a_b2"])
    c4[:, 48:56] = _fmcols(inp["L1_attn_a_bf"])
    c4[:, 56:80] = np.ascontiguousarray(
        inp["L1_attn_b_b_qkv"].astype(np.float32).reshape(24, 128).T)
    c4[:, 80:84] = np.ascontiguousarray(
        inp["L1_ffn_bias_real"].astype(np.float32).reshape(4, 128).T)
    c4[:, 84:88] = np.ascontiguousarray(
        inp["L1_ffn_bias_imag"].astype(np.float32).reshape(4, 128).T)
    base["consts4"] = c4
    base["b1fm"] = np.ascontiguousarray(
        inp["L1_attn_a_b1"].astype(np.float32).reshape(32, 128).T)
    ob = np.zeros((128, 4), BF16NP)
    ob[:, 0] = 1
    base["onesb"] = ob

    delta_g = np.zeros((64, S), BF16NP)
    bt_g = np.zeros((64, S), BF16NP)
    kt1_all = np.zeros((B, 1024, S), BF16NP)
    vt1h_flat = np.zeros((B, 8, S, 128), BF16NP)
    for r in range(NCORES):
        d_r, b_r = l2outs[r]["delta"], l2outs[r]["bt"]
        k_r, v_r = l2outs[r]["k1"], l2outs[r]["v1"]
        for b in range(B):
            delta_g[b * 16:(b + 1) * 16, r * SL:(r + 1) * SL] = d_r[:, b * SL:(b + 1) * SL]
            bt_g[b * 16:(b + 1) * 16, r * SL:(r + 1) * SL] = b_r[:, b * SL:(b + 1) * SL]
            kt1_all[b][:, r * SL:(r + 1) * SL] = k_r[:, b * SL:(b + 1) * SL]
            for h in range(8):
                vt1h_flat[b, h, r * SL:(r + 1) * SL, :] = \
                    v_r[h * 128:(h + 1) * 128, b * SL:(b + 1) * SL].T
    base["kt1_all"] = kt1_all
    base["vt1h_all"] = np.ascontiguousarray(
        vt1h_flat.reshape(B, 8, S // 128, 128, 128).transpose(0, 1, 3, 2, 4)
        .reshape(B, 8, 128, (S // 128) * 128))

    in_maps = []
    for r in range(NCORES):
        m = dict(base)
        dr = np.roll(delta_g, -r * SL, axis=1).copy()
        br = np.roll(bt_g, -r * SL, axis=1).copy()
        cut = (S - r * SL) % S
        dr[:, cut] = 0
        m["delta_in"] = dr
        m["bt_in"] = br
        m["x_in"] = l2outs[r]["xres"]
        m["gate_in"] = l2outs[r]["gate"]
        m["q1_in"] = l2outs[r]["q1"]
        in_maps.append(m)
    return in_maps


def emit_l1ab(l, ctxq, kt2h, vt2h, qext_sb, btab_sb, rtri_sb, kext_dram):
    nc = l.nc
    l.phase_barrier()
    NKT = S // 128
    with contextlib.ExitStack() as st:
        sb = lambda n, s_, d: st.enter_context(nc.sbuf_tensor(n, s_, d))
        kslot = [sb(f"l2k{i}", [67, S], BF16) for i in range(2)]
        vslot = [sb(f"l2v{i}", [128, 32 * 65], BF16) for i in range(2)]
        esb = sb("l2e", [128, 2 * 512], BF16)
        avs = sb("l2av", [64, 512], BF16)
        dens = sb("l2den", [1, 512], F32)
        dinvf = sb("l2dinvf", [1, 512], F32)
        dinvb = sb("l2dinv", [1, 512], BF16)
        qtmp = sb("l2qt", [67, 512], BF16)
        slot_user = [-1, -1]
        for i in range(2):
            l.inc(nc.gpsimd.dma_start(
                out=kslot[i][64:67, :].rearrange("p (a o) -> p a o", o=128),
                in_=kext_dram.unsqueeze(1).broadcast_to([3, 32, 128])), l.WD, 16)
        l.waitE(nc.scalar)
        l.wait(nc.scalar, l.LD)
        l.inc(nc.scalar.copy(qtmp[64:67, :], qext_sb), l.ES)

        def issue(idx, b, h):
            s_ = idx % 2
            l.wait(nc.gpsimd, l.TD,
                   slot_user[s_] if slot_user[s_] >= 0 else l.val(l.TD))
            l.inc(nc.gpsimd.dma_start(out=kslot[s_][0:64, :], in_=kt2h[b, h]),
                  l.WD, 16)
            l.inc(nc.gpsimd.dma_start(out=vslot[s_][:], in_=vt2h[b, h]), l.WD, 16)
            return s_, l.val(l.WD)

        eslu = [0, 0]
        exp_es = [0] * NKT
        pairs = [(b, h) for b in range(B) for h in range(16)]
        cur = issue(0, *pairs[0])
        for pi, (b, h) in enumerate(pairs):
            nxt = issue(pi + 1, *pairs[pi + 1]) if pi + 1 < len(pairs) else None
            s_, rdy = cur
            l.wait(nc.tensor, l.WD, rdy)
            l.acquire(2)
            l.waitE(nc.scalar)
            l.wait(nc.scalar, l.TD)
            l.inc(nc.scalar.copy(qtmp[0:64, :], ctxq.rs(h // 2, 64 * (h % 2), 64,
                                                        b * SL, SL)), l.ES)
            q_es = l.val(l.ES)

            def scores(kt):
                sbk = kt % 2
                _es, _ev = l.bank_free[sbk]
                l.waitE(nc.tensor, max(_es, q_es), _ev)
                l.inc(nc.tensor.matmul(l.bank(sbk)[0:128, 0:512],
                                       kslot[s_][0:67, kt * 128:(kt + 1) * 128],
                                       qtmp[:], start=True, stop=True), l.TD)
                l.written(sbk)

            def exp(kt):
                sbk = kt % 2
                l.consume(nc.scalar, sbk)
                l.wait(nc.scalar, l.TD, eslu[kt % 2])
                if kt < 4:
                    l.consume(nc.vector, sbk)
                    l.wait(nc.vector, l.TD, eslu[kt % 2])
                    l.inc(nc.vector.scalar_tensor_tensor(
                        esb[:, (kt % 2) * 512:(kt % 2 + 1) * 512],
                        l.bank(sbk)[0:128, 0:512], 1.0 / 8.0,
                        rtri_sb[:, kt * 512:(kt + 1) * 512],
                        op0=ALU.mult, op1=ALU.add), l.EV)
                    l.wait(nc.scalar, l.EV)
                    l.inc(nc.scalar.activation(
                        esb[:, (kt % 2) * 512:(kt % 2 + 1) * 512],
                        esb[:, (kt % 2) * 512:(kt % 2 + 1) * 512], AF.Exp), l.ES)
                else:
                    l.inc(nc.scalar.activation(
                        esb[:, (kt % 2) * 512:(kt % 2 + 1) * 512],
                        l.bank(sbk)[0:128, 0:512], AF.Exp, scale=1.0 / 8.0,
                        bias=btab_sb[:, kt:kt + 1]), l.ES)
                l.freed(sbk)
                exp_es[kt] = l.val(l.ES)

            def av(kt):
                l.wait(nc.tensor, l.ES, exp_es[kt])
                l.inc(nc.tensor.matmul(l.bank(2)[0:65, 0:512],
                                       vslot[s_][:, kt * 65:(kt + 1) * 65],
                                       esb[:, (kt % 2) * 512:(kt % 2 + 1) * 512],
                                       start=(kt == 0), stop=(kt == NKT - 1)),
                      l.TD)
                eslu[kt % 2] = l.val(l.TD)
                slot_user[s_] = l.val(l.TD)

            scores(0)
            exp(0)
            for kt in range(NKT):
                if kt + 1 < NKT:
                    scores(kt + 1)
                av(kt)
                if kt + 1 < NKT:
                    exp(kt + 1)
            l.written(2)
            l.consume(nc.scalar, 2)
            l.inc(nc.scalar.copy(dens[:], l.bank(2)[64:65, 0:512]), l.ES)
            l.inc(nc.scalar.copy(avs[:], l.bank(2)[0:64, 0:512]), l.ES)
            l.freed(2)
            l.waitE(nc.vector)
            l.inc(nc.vector.reciprocal(dinvf[:], dens[:]), l.EV)
            l.waitE(nc.scalar)
            l.inc(nc.scalar.copy(dinvb[:], dinvf[:]), l.ES)
            l.acquire(3)
            l.waitE(nc.tensor)
            l.inc(nc.tensor.matmul(l.bank(3)[0:64, 0:512], l.c_ones1_64,
                                   dinvb[:], start=True, stop=True), l.TD)
            l.written(3)
            l.consume(nc.vector, 3)
            l.waitE(nc.vector)
            l.inc(nc.vector.tensor_tensor(
                ctxq.rs(h // 2, 64 * (h % 2), 64, b * SL, SL), avs[:],
                l.bank(3)[0:64, 0:512], op=ALU.mult), l.EV)
            l.freed(3)
            cur = nxt


def build_launch4(upto=99):
    nc = bass.Bass("TRN2", target_bir_lowering=False)
    ctx = contextlib.ExitStack()
    with ctx:
        l = L(nc, ctx)
        x_in = l.din("x_in", [1024, NT], F32)
        q2_in = l.din("q2_in", [1024, NT], BF16)
        kt2h_all = l.din("kt2h_all", [B, 16, 64, S], BF16)
        kext_in = l.din("kext_in", [3, 128], BF16)
        vt2h_all = l.din("vt2h_all", [B, 16, 128, (S // 128) * 65], BF16)
        qext_in = l.din("qext_in", [3, 512], BF16)
        btab_in = l.din("btab_in", [128, 32], F32)
        rtri_in = l.din("rtri_in", [128, 4 * 512], F32)
        hb = {}
        hb["wout2"] = l.din("wout2", [8, 128, 1024], F32)
        g1t_in = l.din("g1t_in", [32, 2 * 4 * 32], F32)
        g2t_in = l.din("g2t_in", [16, 2 * 4 * 16], F32)
        indd_in = l.din("indd_in", [32, 4 * 128], BF16)
        indg_in = l.din("indg_in", [16, 128], BF16)
        fconsts = l.din("fconsts", [128, 8], F32)
        consts5 = l.din("consts5", [128, 24], F32)
        onesb = l.din("onesb", [128, 4], BF16)

        xres_o = l.dout("xres", [1024, NT], F32)

        fcs = l.sbuf("fcs", [128, 8], F32)
        l.inc(nc.sync.dma_start(out=fcs[:], in_=fconsts[:]), l.LD, 16)
        for i, v in enumerate(FCONST_VALS):
            nc.const_aps.register(F32, float(v), fcs[:, i:i + 1])
        cst5 = l.sbuf("cst5", [128, 24], F32)
        onesb_sb = l.sbuf("onesbsb", [128, 4], BF16)
        rowones = l.sbuf("rowones", [1, 128], BF16)
        qext_sb = l.sbuf("qext", [3, 512], BF16)
        kext_sb = l.sbuf("kext", [3, 128], BF16)
        btab_sb = l.sbuf("btab", [128, 32], F32)
        rtri_sb = l.sbuf("rtri", [128, 4 * 512], F32)
        l.inc(nc.sync.dma_start(out=cst5[:], in_=consts5[:]), l.LD, 16)
        l.inc(nc.sync.dma_start(out=onesb_sb[:], in_=onesb[:]), l.LD, 16)
        l.inc(nc.sync.dma_start(out=qext_sb[:], in_=qext_in[:]), l.LD, 16)
        l.inc(nc.sync.dma_start(out=kext_sb[:], in_=kext_in[:]), l.LD, 16)
        l.inc(nc.sync.dma_start(out=btab_sb[:], in_=btab_in[:]), l.LD, 16)
        l.inc(nc.sync.dma_start(out=rtri_sb[:], in_=rtri_in[:]), l.LD, 16)
        l.inc(nc.gpsimd.memset(rowones[:], 1.0), l.WD, 1)
        l.c_ones128_b = onesb_sb[:, 0:1]
        l.c_ones1_128 = rowones[0:1, 0:128]
        l.c_ones1_64 = rowones[0:1, 0:64]
        l.c_bout = cst5[:, 0:8]
        l.c_bri = cst5[:, 8:16]
        for eng in (nc.tensor, nc.scalar, nc.vector):
            l.wait(eng, l.LD)
            l.wait(eng, l.WD)

        x = alloc_fm(l, "x", 1024, NT, F32)
        xn = alloc_fm(l, "xn", 1024, NT, BF16)
        load_fm(l, x, x_in)

        with contextlib.ExitStack() as st7:
            ctxq = FM(st7.enter_context(nc.sbuf_tensor("ctxq", [128, 8 * NT], BF16)),
                      1024, NT)
            load_fm(l, ctxq, q2_in)
            if upto >= 1:
                emit_l1ab(l, ctxq, kt2h_all, vt2h_all, qext_sb[:], btab_sb, rtri_sb,
                          kext_in)
                wso = WStream(l, "wso2", 1024, stack=st7)

                def out_evac(ot, c0, w, tag, ps, bi):
                    l.consume(nc.scalar, bi)
                    l.waitE(nc.scalar)
                    l.inc(nc.scalar.activation(l.stage4[:, 0:w], ps, AF.Identity,
                                               bias=l.c_bout[:, ot:ot + 1]), l.ES)
                    l.freed(bi)
                    l.waitE(nc.vector)
                    l.inc(nc.vector.tensor_tensor(x.t(ot, c0, w), x.t(ot, c0, w),
                                                  l.stage4[:, 0:w], op=ALU.add),
                          l.EV)
                l.stage4 = st7.enter_context(nc.sbuf_tensor("stg4", [128, 512], BF16))
                emit_project(l, hb["wout2"], wso, ctxq, 8, MAIN4, out_evac)
            l.phase_barrier()

        if upto >= 2:
            with contextlib.ExitStack() as st8:
                do_ln(l, x, xn, C=NT)
                emit_l1ffn(l, x, xn, g1t_in, g2t_in, indd_in, indg_in, st8)

        l.waitE(nc.sync)
        l.wait(nc.sync, l.TD)
        for p in range(8):
            l.inc(nc.sync.dma_start(out=xres_o[p * 128:(p + 1) * 128, :],
                                    in_=x.t(p, 0, NT)), l.ST, 16)
        l.wait(nc.sync, l.ST)
    return nc, l


def prep_launch4(inp, l3outs):
    base = {}
    base["wout2"] = _tile_w(inp["L1_attn_b_w_out"].T)
    base["fconsts"] = np.tile(np.array([FCONST_VALS], np.float32), (128, 1))
    c5 = np.zeros((128, 24), np.float32)
    c5[:, 0:8] = _fmcols(inp["L1_attn_b_b_out"])
    c5[:, 8:12] = np.ascontiguousarray(
        inp["L1_ffn_bias_real"].astype(np.float32).reshape(4, 128).T)
    c5[:, 12:16] = np.ascontiguousarray(
        inp["L1_ffn_bias_imag"].astype(np.float32).reshape(4, 128).T)
    base["consts5"] = c5
    g1 = np.zeros((32, 256), np.float32)
    g2 = np.zeros((16, 128), np.float32)
    for wi_, (k1, k2) in enumerate((("L1_ffn_G1_R", "L1_ffn_G2_R"),
                                    ("L1_ffn_G1_I", "L1_ffn_G2_I"))):
        G1 = inp[k1].astype(np.float32)[0]
        G2 = inp[k2].astype(np.float32)
        for c in range(4):
            g1[:, wi_ * 128 + c * 32: wi_ * 128 + (c + 1) * 32] = G1[:, c, :].T
            g2[:, wi_ * 64 + c * 16: wi_ * 64 + (c + 1) * 16] = G2[c, :, 0, :].T
    base["g1t_in"] = g1
    base["g2t_in"] = g2
    indd = np.zeros((32, 4 * 128), BF16NP)
    for p in range(4):
        for loc in range(128):
            indd[8 * p + loc // 16, p * 128 + loc] = 1
    base["indd_in"] = indd
    indg = np.zeros((16, 128), BF16NP)
    for loc in range(128):
        indg[loc % 16, loc] = 1
    base["indg_in"] = indg
    ob = np.zeros((128, 4), BF16NP)
    ob[:, 0] = 1
    base["onesb"] = ob
    qext = np.zeros((3, 512), np.float32)
    sl = np.arange(512)
    qext[0] = 1.0
    qext[1] = -1024.0 * (sl // 128)
    qext[2] = -8.0 * (sl % 128)
    base["qext_in"] = qext.astype(BF16NP)
    kk = np.arange(128)[:, None]
    qq = np.arange(512)[None, :]
    rtri = np.zeros((128, 4 * 512), np.float32)
    for kt in range(4):
        rtri[:, kt * 512:(kt + 1) * 512] = np.where(128 * kt + kk > qq, NEG,
                                                    128.0 * kt)
    base["rtri_in"] = rtri

    k2g = np.zeros((B, 16, 64, S), BF16NP)
    v2g = np.zeros((B, 16, S, 65), BF16NP)
    for r in range(NCORES):
        k_r, v_r = l3outs[r]["k2"], l3outs[r]["v2"]
        for b in range(B):
            for h in range(16):
                k2g[b, h, 0:64, r * SL:(r + 1) * SL] = \
                    k_r[h * 64:(h + 1) * 64, b * SL:(b + 1) * SL]
                v2g[b, h, r * SL:(r + 1) * SL, 0:64] = \
                    v_r[h * 64:(h + 1) * 64, b * SL:(b + 1) * SL].T
    v2g[:, :, :, 64] = np.asarray(1.0, BF16NP)
    kext = np.zeros((3, 128), np.float32)
    kext[0] = 8.0 * np.arange(128)
    kext[1] = 1.0
    kext[2] = 1.0
    base["kext_in"] = kext.astype(BF16NP)

    in_maps = []
    for r in range(NCORES):
        m = dict(base)
        order = [(r + j) % NCORES for j in range(NCORES)]
        kperm = np.concatenate([k2g[:, :, :, g * SL:(g + 1) * SL] for g in order],
                               axis=3)
        vperm = np.concatenate([v2g[:, :, g * SL:(g + 1) * SL, :] for g in order],
                               axis=2)
        m["kt2h_all"] = np.ascontiguousarray(kperm)
        m["vt2h_all"] = np.ascontiguousarray(
            vperm.reshape(B, 16, S // 128, 128, 65).transpose(0, 1, 3, 2, 4)
            .reshape(B, 16, 128, (S // 128) * 65))
        btab = np.zeros((128, 32), np.float32)
        for j, g in enumerate(order):
            for t4 in range(4):
                kt = j * 4 + t4
                if j == 0:
                    btab[:, kt] = 0.0
                elif g < r:
                    btab[:, kt] = float(128 * (g * 4 + t4) - r * SL)
                else:
                    btab[:, kt] = NEG
        for t4 in range(4):
            btab[:, t4] = float(128 * t4)
        m["btab_in"] = btab
        m["x_in"] = l3outs[r]["xres"]
        m["q2_in"] = l3outs[r]["q2"]
        in_maps.append(m)
    return in_maps


def kernel(**inputs):
    inp = {k: np.asarray(v) for k, v in inputs.items()}

    in_maps1 = prep_launch1(inp)
    nc1, _ = build_launch1()
    r1 = run_bass_kernel_spmd(nc1, in_maps1, core_ids=list(range(NCORES)))
    l1outs = [r1.results[r] for r in range(NCORES)]

    in_maps2 = prep_launch2(inp, l1outs)
    nc2, _ = build_launch2()
    r2 = run_bass_kernel_spmd(nc2, in_maps2, core_ids=list(range(NCORES)))
    l2outs = [r2.results[r] for r in range(NCORES)]

    in_maps3 = prep_launch3(inp, l2outs)
    nc3, _ = build_launch3()
    r3 = run_bass_kernel_spmd(nc3, in_maps3, core_ids=list(range(NCORES)))
    l3outs = [r3.results[r] for r in range(NCORES)]

    in_maps4 = prep_launch4(inp, l3outs)
    nc4, _ = build_launch4()
    r4 = run_bass_kernel_spmd(nc4, in_maps4, core_ids=list(range(NCORES)))

    out = np.zeros((B, S, D), np.float32)
    for r in range(NCORES):
        xr = np.asarray(r4.results[r]["xres"])
        out[:, r * SL:(r + 1) * SL, :] = \
            xr.reshape(D, B, SL).transpose(1, 2, 0)
    return out


# version 08
# speedup vs baseline: 1.2928x; 1.0241x over previous
import contextlib
import numpy as np
import ml_dtypes
import concourse.bass as bass
import concourse.mybir as mybir
from concourse.bass_utils import run_bass_kernel_spmd

F32 = mybir.dt.float32
BF16 = mybir.dt.bfloat16
AF = mybir.ActivationFunctionType
ALU = mybir.AluOpType
AX = mybir.AxisListType
BF16NP = ml_dtypes.bfloat16

NCORES = 8
B = 4
S = 4096
D = 1024
SL = S // NCORES
NT = B * SL
HALO = 16
NTH = NT + B * HALO
PADB = 640
NEG = -1e30
FCONST_VALS = [1e-5, 0.01, 1.0, 0.0, -1e30, 0.5, 2.0, 3.0]


class L:
    def __init__(self, nc, ctx):
        self.nc = nc
        self.ctx = ctx
        self.counts = {}
        self.last_wait = {}
        self.LD = self._sem("LD")
        self.WD = self._sem("WD")
        self.TD = self._sem("TD")
        self.ES = self._sem("ES")
        self.EV = self._sem("EV")
        self.ST = self._sem("ST")
        self.pm = [ctx.enter_context(nc.psum_tensor(f"pm{i}", [128, 512], F32))
                   for i in range(7)]
        self.pmb = ctx.enter_context(nc.psum_tensor("pmb", [128, 1024], BF16))
        self.bank_free = [(0, 0)] * 8
        self.bank_ready = [0] * 8
        self._rr = 0
        self.inputs = {}
        self.outputs = {}

    def _sem(self, name):
        return self.ctx.enter_context(self.nc.semaphore(name))

    def inc(self, instr, sem, n=1):
        instr.then_inc(sem, n)
        self.counts[sem] = self.counts.get(sem, 0) + n
        return self.counts[sem]

    def val(self, sem):
        return self.counts.get(sem, 0)

    def wait(self, eng, sem, value=None):
        v = self.val(sem) if value is None else value
        if v <= 0:
            return
        key = (id(eng), sem)
        if self.last_wait.get(key, -1) >= v:
            return
        eng.wait_ge(sem, v)
        self.last_wait[key] = v

    def waitE(self, eng, es=None, ev=None):
        self.wait(eng, self.ES, es)
        self.wait(eng, self.EV, ev)

    def eval_(self):
        return (self.val(self.ES), self.val(self.EV))

    def phase_barrier(self):
        for eng in (self.nc.scalar, self.nc.vector, self.nc.gpsimd, self.nc.sync):
            self.wait(eng, self.TD)
            self.waitE(eng)
        for eng in (self.nc.tensor, self.nc.scalar, self.nc.vector):
            self.wait(eng, self.LD)
            self.wait(eng, self.WD)

    def bank(self, i):
        return self.pm[i] if i < 7 else self.pmb

    def acquire(self, i):
        es, ev = self.bank_free[i] if isinstance(self.bank_free[i], tuple) else (None, None)
        if es is not None:
            self.waitE(self.nc.tensor, es, ev)

    def written(self, i):
        self.bank_ready[i] = self.val(self.TD)

    def consume(self, eng, i):
        self.wait(eng, self.TD, self.bank_ready[i])

    def freed(self, i):
        self.bank_free[i] = self.eval_()

    def nb(self):
        i = self._rr % 7
        self._rr += 1
        return i

    def din(self, name, shape, dt):
        ap = self.nc.dram_tensor(name, list(shape), dt, kind="ExternalInput").ap()
        self.inputs[name] = ap
        return ap

    def dout(self, name, shape, dt):
        ap = self.nc.dram_tensor(name, list(shape), dt, kind="ExternalOutput").ap()
        self.outputs[name] = ap
        return ap

    def sbuf(self, name, shape, dt):
        return self.ctx.enter_context(self.nc.sbuf_tensor(name, list(shape), dt))


class FM:
    def __init__(self, sb, rows, C):
        self.sb = sb
        self.rows = rows
        self.C = C
        self.np_ = rows // 128

    def t(self, p, c0=0, w=None):
        w = self.C - c0 if w is None else w
        return self.sb[:, p * self.C + c0: p * self.C + c0 + w]

    def rs(self, p, r0, nr, c0=0, w=None):
        w = self.C - c0 if w is None else w
        return self.sb[r0:r0 + nr, p * self.C + c0: p * self.C + c0 + w]


def alloc_fm(l, name, rows, C, dt):
    sb = l.sbuf(name, [128, (rows // 128) * C], dt)
    return FM(sb, rows, C)


def load_fm(l, fmbuf, hbm_ap, cols=None, cast=False):
    nc = l.nc
    C = cols or fmbuf.C
    eng = nc.gpsimd if cast else nc.sync
    sem = l.WD if cast else l.LD
    for p in range(fmbuf.np_):
        l.inc(eng.dma_start(out=fmbuf.t(p, 0, C),
                            in_=hbm_ap[p * 128:(p + 1) * 128, 0:C]), sem, 16)


def store_fm(l, hbm_ap, fmbuf):
    nc = l.nc
    l.waitE(nc.sync)
    l.wait(nc.sync, l.TD)
    for p in range(fmbuf.np_):
        l.inc(nc.sync.dma_start(out=hbm_ap[p * 128:(p + 1) * 128, :],
                                in_=fmbuf.t(p)), l.ST, 16)


class WStream:

    def __init__(self, l, name, din, nslots=2, stack=None, ow=128):
        self.l = l
        self.din = din
        self.ow = ow
        self.npt = din // 128
        alloc = (stack.enter_context if stack is not None else
                 (lambda cm: l.ctx.enter_context(cm)))
        self.slots = [alloc(l.nc.sbuf_tensor(f"{name}{i}", [128, self.npt * ow], BF16))
                      for i in range(nslots)]
        self.nslots = nslots
        self.cur = 0
        self.slot_ready = [0] * nslots
        self.slot_user = [-1] * nslots

    def issue(self, hbm_ap, ot):
        l, nc = self.l, self.l.nc
        s = self.cur % self.nslots
        self.cur += 1
        l.wait(nc.gpsimd, l.TD,
               self.slot_user[s] if self.slot_user[s] >= 0 else l.val(l.TD))
        l.inc(nc.gpsimd.dma_start(out=self.slots[s][:], in_=hbm_ap[ot]), l.WD, 16)
        self.slot_ready[s] = l.val(l.WD)
        return s

    def chunk(self, s, pt, ow=None):
        ow = ow or self.ow
        return self.slots[s][:, pt * self.ow: pt * self.ow + ow]

    def use(self, s):
        self.slot_user[s] = self.l.val(self.l.TD)

    def wait_ready(self, s):
        self.l.wait(self.l.nc.tensor, self.l.WD, self.slot_ready[s])


def emit_ln(l, x, xn, C=None, g_fm=None, b_fm=None, eps=1e-5, consts=None,
            window=None):
    nc = l.nc
    l.phase_barrier()
    C = C or x.C
    cs = consts
    if window is not None:
        nts = [window]
    else:
        nts = [(i * 512, min(512, C - i * 512)) for i in range((C + 511) // 512)]
    NPT = x.np_
    for (c0, w) in nts:
        rof = 0 if window is not None else c0
        bs, bq = l.nb(), l.nb()
        l.acquire(bs)
        l.acquire(bq)
        slot_use = [0, 0, 0, 0]
        for pt in range(NPT):
            sl = pt % 4
            l.wait(nc.scalar, l.TD, slot_use[sl])
            l.inc(nc.scalar.copy(cs["cast_t"][sl][:, 0:w], x.t(pt, c0, w)), l.ES)
            l.inc(nc.scalar.activation(cs["sq_t"][sl][:, 0:w], x.t(pt, c0, w),
                                       AF.Square), l.ES)
            l.waitE(nc.tensor)
            nc.tensor.matmul(l.bank(bs)[0:1, 0:w], cs["ones128_b"],
                             cs["cast_t"][sl][:, 0:w],
                             start=(pt == 0), stop=(pt == NPT - 1))
            l.inc(nc.tensor.matmul(l.bank(bq)[0:1, 0:w], cs["ones128_b"],
                                   cs["sq_t"][sl][:, 0:w],
                                   start=(pt == 0), stop=(pt == NPT - 1)), l.TD)
            slot_use[sl] = l.val(l.TD)
        l.written(bs)
        l.written(bq)
        l.consume(nc.scalar, bs)
        l.inc(nc.scalar.copy(cs["sums"][0:1, rof:rof + w], l.bank(bs)[0:1, 0:w]), l.ES)
        l.inc(nc.scalar.copy(cs["sqsums"][0:1, rof:rof + w], l.bank(bq)[0:1, 0:w]), l.ES)
        l.freed(bs)
        l.freed(bq)
        bm, bv = l.nb(), l.nb()
        l.acquire(bm)
        l.acquire(bv)
        l.waitE(nc.tensor)
        l.inc(nc.tensor.matmul(l.bank(bm)[:, 0:w], cs["ones1_128"],
                               cs["sums"][0:1, rof:rof + w], start=True, stop=True), l.TD)
        l.inc(nc.tensor.matmul(l.bank(bv)[:, 0:w], cs["ones1_128"],
                               cs["sqsums"][0:1, rof:rof + w], start=True, stop=True), l.TD)
        l.written(bm)
        l.written(bv)
        l.consume(nc.scalar, bm)
        l.inc(nc.scalar.activation(cs["t1"][:, 0:w], l.bank(bm)[:, 0:w], AF.Square,
                                   scale=1.0 / D), l.ES)
        l.consume(nc.vector, bv)
        l.waitE(nc.vector)
        l.inc(nc.vector.scalar_tensor_tensor(
            cs["t2"][:, 0:w], l.bank(bv)[:, 0:w], 1.0 / D, cs["t1"][:, 0:w],
            op0=ALU.mult, op1=ALU.subtract), l.EV)
        l.freed(bv)
        l.waitE(nc.scalar)
        l.inc(nc.scalar.activation(cs["t2"][:, 0:w], cs["t2"][:, 0:w], AF.Ln,
                                   bias=float(eps)), l.ES)
        l.inc(nc.scalar.activation(cs["rrep"][:, rof:rof + w], cs["t2"][:, 0:w],
                                   AF.Exp, scale=-0.5), l.ES)
        l.waitE(nc.vector)
        l.inc(nc.vector.scalar_tensor_tensor(
            cs["mrrep"][:, rof:rof + w], l.bank(bm)[:, 0:w], 1.0 / D,
            cs["rrep"][:, rof:rof + w], op0=ALU.mult, op1=ALU.mult), l.EV)
        l.freed(bm)
    for pt in range(NPT):
        for (c0, w) in nts:
            rof = 0 if window is not None else c0
            oc0 = 0 if window is not None else c0
            l.waitE(nc.vector)
            l.inc(nc.vector.tensor_tensor(xn.t(pt, oc0, w), x.t(pt, c0, w),
                                          cs["rrep"][:, rof:rof + w], op=ALU.mult), l.EV)
            l.inc(nc.vector.tensor_tensor(xn.t(pt, oc0, w), xn.t(pt, oc0, w),
                                          cs["mrrep"][:, rof:rof + w],
                                          op=ALU.subtract), l.EV)
            if g_fm is not None:
                l.waitE(nc.scalar)
                l.inc(nc.scalar.mul(xn.t(pt, oc0, w), xn.t(pt, oc0, w),
                                    g_fm[:, pt:pt + 1]), l.ES)
                l.inc(nc.scalar.add(xn.t(pt, oc0, w), xn.t(pt, oc0, w),
                                    b_fm[:, pt:pt + 1]), l.ES)


def emit_project(l, whbm, wstream, xn, douts, n_tiles, out_fn, extra=None,
                 rhs_fn=None):
    nc = l.nc
    npt = wstream.npt
    s = wstream.issue(whbm, 0)
    for ot in range(douts):
        s_next = wstream.issue(whbm, ot + 1) if ot + 1 < douts else None
        wstream.wait_ready(s)
        for (c0, w, tag) in n_tiles:
            bi = l.nb()
            l.acquire(bi)
            ps = l.bank(bi)[0:wstream.ow, 0:w]
            l.waitE(nc.tensor)
            last = None
            for pt in range(npt):
                rhs = rhs_fn(pt, c0, w, tag) if rhs_fn else xn.t(pt, c0, w)
                stop = (pt == npt - 1) and not extra
                last = nc.tensor.matmul(ps, wstream.chunk(s, pt), rhs,
                                        start=(pt == 0), stop=stop)
            if extra:
                for i, (lf, rf) in enumerate(extra):
                    last = nc.tensor.matmul(ps, lf(ot), rf(c0, w, tag),
                                            start=False, stop=(i == len(extra) - 1))
            l.inc(last, l.TD)
            l.written(bi)
            wstream.use(s)
            out_fn(ot, c0, w, tag, ps, bi)
        s = s_next


_LNCTR = [0]


def ln_scratch(l, stack, C):
    nc = l.nc
    _LNCTR[0] += 1
    u = f"_{_LNCTR[0]}"
    cs = {}
    cs["cast_t"] = [stack.enter_context(nc.sbuf_tensor(f"lncast{i}{u}", [128, 512], BF16))
                    for i in range(4)]
    cs["sq_t"] = [stack.enter_context(nc.sbuf_tensor(f"lnsq{i}{u}", [128, 512], BF16))
                  for i in range(4)]
    cs["sums"] = stack.enter_context(nc.sbuf_tensor(f"lnsums{u}", [1, C], BF16))
    cs["sqsums"] = stack.enter_context(nc.sbuf_tensor(f"lnsqsums{u}", [1, C], BF16))
    cs["t1"] = stack.enter_context(nc.sbuf_tensor(f"lnt1{u}", [128, 512], F32))
    cs["t2"] = stack.enter_context(nc.sbuf_tensor(f"lnt2{u}", [128, 512], F32))
    cs["rrep"] = stack.enter_context(nc.sbuf_tensor(f"lnr{u}", [128, C], F32))
    cs["mrrep"] = stack.enter_context(nc.sbuf_tensor(f"lnmr{u}", [128, C], F32))
    return cs


def do_ln(l, x, xn, C=None, g_fm=None, b_fm=None, eps=1e-5, window=None):
    with contextlib.ExitStack() as st:
        cs = ln_scratch(l, st, 512 if window is not None else (C or x.C))
        cs["ones128_b"] = l.c_ones128_b
        cs["ones1_128"] = l.c_ones1_128
        emit_ln(l, x, xn, C=C, g_fm=g_fm, b_fm=b_fm, eps=eps, consts=cs,
                window=window)


MAIN4 = [(b * SL, SL, b) for b in range(B)]
MAIN4H = MAIN4 + [(NT, B * HALO, "h")]


def emit_l0aa(l, x, xn, hb):
    nc = l.nc
    l.phase_barrier()
    with contextlib.ExitStack() as st:
        sb = lambda n, s, d: st.enter_context(nc.sbuf_tensor(n, s, d))
        q = FM(sb("aaq", [128, 8 * NT], BF16), 1024, NT)
        k_pad = FM(sb("aak", [128, 2 * B * PADB], BF16), 256, B * PADB)
        v_pad = FM(sb("aav", [128, 2 * B * PADB], BF16), 256, B * PADB)
        ws = WStream(l, "aaw", 1024, stack=st)
        t_bf = sb("aat", [128, 2 * 512], BF16)
        rr_sb = sb("aarr", [2, 512], BF16)

        def mk_evac(dest, tag_map):
            def fn(ot, c0, w, tag, ps, bi):
                l.consume(nc.scalar, bi)
                dst = tag_map(dest, ot, c0, w, tag)
                l.inc(nc.scalar.copy(dst, ps), l.ES)
                l.freed(bi)
            return fn

        def q_map(dest, ot, c0, w, tag):
            return dest.t(ot, c0, w)
        emit_project(l, hb["aa_wq"], ws, xn, 8, MAIN4, mk_evac(q, q_map))

        def kv_map(dest, ot, c0, w, tag):
            if tag == "h":
                base = dest.sb[:, ot * (B * PADB): (ot + 1) * (B * PADB)]
                return base.rearrange("p (b c) -> p b c", c=PADB)[:, :, 112:128]
            b = tag
            return dest.t(ot, b * PADB + 128, w)

        def kv_evac(dest):
            def fn(ot, c0, w, tag, ps, bi):
                l.consume(nc.scalar, bi)
                if tag == "h":
                    dst = kv_map(dest, ot, c0, w, tag)
                    l.inc(nc.scalar.copy(dst, ps.rearrange("p (b c) -> p b c", c=16)),
                          l.ES)
                else:
                    l.inc(nc.scalar.copy(kv_map(dest, ot, c0, w, tag), ps), l.ES)
                l.freed(bi)
            return fn
        emit_project(l, hb["aa_wk"], ws, xn, 2, MAIN4H, kv_evac(k_pad))
        emit_project(l, hb["aa_wv"], ws, xn, 2, MAIN4H, kv_evac(v_pad))
        for dest in (k_pad, v_pad):
            for ot in range(2):
                ap = dest.sb[:, ot * (B * PADB): (ot + 1) * (B * PADB)].rearrange(
                    "p (b c) -> p b c", c=PADB)[:, :, 0:112]
                l.inc(nc.gpsimd.memset(ap, 0.0), l.WD, 1)

        def rms(dest, rrbuf, tiles, wn_ap):
            for pt in range(dest.np_):
                for (c0, w, tag) in tiles:
                    l.wait(nc.scalar, l.TD)
                    l.waitE(nc.scalar)
                    l.inc(nc.scalar.activation(t_bf[:, 0:w], dest.t(pt, c0, w),
                                               AF.Square), l.ES)
                    bi = l.nb()
                    l.acquire(bi)
                    l.waitE(nc.tensor)
                    l.inc(nc.tensor.matmul(l.bank(bi)[0:2, 0:w], l.c_ind128_2,
                                           t_bf[:, 0:w], start=True, stop=True), l.TD)
                    l.written(bi)
                    l.consume(nc.scalar, bi)
                    l.inc(nc.scalar.activation(l.bank(bi)[0:2, 0:w],
                                               l.bank(bi)[0:2, 0:w], AF.Ln,
                                               scale=1.0 / 64, bias=0.01), l.ES)
                    l.inc(nc.scalar.activation(rrbuf[0:2, 0:w],
                                               l.bank(bi)[0:2, 0:w], AF.Exp,
                                               scale=-0.5), l.ES)
                    l.freed(bi)
                    bi2 = l.nb()
                    l.acquire(bi2)
                    l.waitE(nc.tensor)
                    l.inc(nc.tensor.matmul(
                        l.bank(bi2)[:, 0:w], l.c_ind2_128,
                        rrbuf[0:2, 0:w], start=True, stop=True), l.TD)
                    l.written(bi2)
                    l.consume(nc.vector, bi2)
                    l.waitE(nc.vector)
                    l.inc(nc.vector.tensor_tensor(dest.t(pt, c0, w), dest.t(pt, c0, w),
                                                  l.bank(bi2)[:, 0:w], op=ALU.mult),
                          l.EV)
                    l.freed(bi2)
                    l.waitE(nc.scalar)
                    l.inc(nc.scalar.mul(dest.t(pt, c0, w), dest.t(pt, c0, w), wn_ap),
                          l.ES)
        rms(q, rr_sb, MAIN4, hb_ap_qnw(l))
        KTILES = ([(b * PADB + 128, 512, b) for b in range(B)]
                  + [(b * PADB + 112, 16, b) for b in range(B)])
        rms(k_pad, rr_sb, KTILES, hb_ap_knw(l))

        v_tok = sb("aavtok", [128, (B * PADB // 128) * 260], BF16)
        ident = l.c_ident[:]
        NTOK = B * PADB // 128
        for vp in range(2):
            for tt in range(NTOK):
                l.waitE(nc.tensor)
                l.wait(nc.tensor, l.WD)
                l.acquire(7)
                l.inc(nc.tensor.transpose(l.pmb[0:128, 0:128],
                                          v_pad.t(vp, tt * 128, 128), ident), l.TD)
                l.written(7)
                l.consume(nc.scalar, 7)
                for gg in range(2):
                    g = vp * 2 + gg
                    l.inc(nc.scalar.copy(
                        v_tok[:, tt * 260 + g * 65: tt * 260 + g * 65 + 64],
                        l.pmb[0:128, gg * 64: gg * 64 + 64]), l.ES)
                l.freed(7)
        for g in range(4):
            ap = v_tok[:].rearrange("p (t c) -> p t c", c=260)[:, :, g * 65 + 64:g * 65 + 65]
            l.inc(nc.gpsimd.memset(ap, 1.0), l.WD, 1)

        ctx = FM(xn.sb, 1024, xn.C)
        att_e = sb("aae", [128, 4 * 512], BF16)
        av_sb = sb("aaav", [64, 512], BF16)
        dinv_sb = sb("aadinv", [1, 512], BF16)
        drep_sb = sb("aadrep", [64, 512], F32)
        eslot_use = [0] * 4
        esl = 0
        R4 = hb["aa_R4"]
        R4f = hb["aa_R4f"]
        r_sb = sb("aar4", [128, 4 * 2 * 512], F32)
        rf_sb = sb("aar4f", [128, 4 * 512], F32)
        for g in range(4):
            for c in range(2):
                l.inc(nc.sync.dma_start(out=r_sb[:, (g * 2 + c) * 512:(g * 2 + c + 1) * 512],
                                        in_=R4[g, c]), l.LD, 16)
            l.inc(nc.sync.dma_start(out=rf_sb[:, g * 512:(g + 1) * 512],
                                    in_=R4f[g]), l.LD, 16)

        for b in range(B):
            for qt in range(4):
                for g in range(4):
                    bA, bB = l.nb(), l.nb()
                    for cls, bi in ((0, bA), (1, bB)):
                        kt = qt + cls
                        l.acquire(bi)
                        l.waitE(nc.tensor)
                        l.wait(nc.tensor, l.WD)
                        last = None
                        for j in range(4):
                            h = g + 4 * j
                            lhs = k_pad.rs(g // 2, 64 * (g % 2), 64,
                                           b * PADB + kt * 128, 128)
                            rhs = q.rs(h // 2, 64 * (h % 2), 64,
                                       b * SL + qt * 128, 128)
                            last = nc.tensor.matmul(
                                l.bank(bi)[0:128, j * 128:(j + 1) * 128],
                                lhs, rhs, start=True, stop=True)
                        l.inc(last, l.TD)
                        l.written(bi)
                    for cls, bi in ((0, bA), (1, bB)):
                        l.consume(nc.vector, bi)
                        l.wait(nc.vector, l.TD, eslot_use[esl])
                        rtile = (rf_sb[:, g * 512:(g + 1) * 512]
                                 if (cls == 0 and qt == 0)
                                 else r_sb[:, (g * 2 + cls) * 512:(g * 2 + cls + 1) * 512])
                        l.inc(nc.vector.scalar_tensor_tensor(
                            att_e[:, esl * 512:(esl + 1) * 512],
                            l.bank(bi)[:, 0:512], 1.0 / 8.0, rtile,
                            op0=ALU.mult, op1=ALU.add), l.EV)
                        l.freed(bi)
                        l.waitE(nc.scalar)
                        l.inc(nc.scalar.activation(
                            att_e[:, esl * 512:(esl + 1) * 512],
                            att_e[:, esl * 512:(esl + 1) * 512], AF.Exp), l.ES)
                        if cls == 0:
                            esl0 = esl
                        esl = (esl + 1) % 4
                    bC = l.nb()
                    l.acquire(bC)
                    l.waitE(nc.tensor)
                    i1 = nc.tensor.matmul(
                        l.bank(bC)[0:65, 0:512],
                        v_tok[:, (b * 5 + qt) * 260 + g * 65:
                              (b * 5 + qt) * 260 + g * 65 + 65],
                        att_e[:, esl0 * 512:(esl0 + 1) * 512], start=True, stop=False)
                    i2 = nc.tensor.matmul(
                        l.bank(bC)[0:65, 0:512],
                        v_tok[:, (b * 5 + qt + 1) * 260 + g * 65:
                              (b * 5 + qt + 1) * 260 + g * 65 + 65],
                        att_e[:, ((esl0 + 1) % 4) * 512:((esl0 + 1) % 4 + 1) * 512],
                        start=False, stop=True)
                    l.inc(i2, l.TD)
                    l.written(bC)
                    eslot_use[esl0] = l.val(l.TD)
                    eslot_use[(esl0 + 1) % 4] = l.val(l.TD)
                    l.consume(nc.scalar, bC)
                    l.waitE(nc.scalar)
                    l.inc(nc.scalar.copy(dinv_sb[:], l.bank(bC)[64:65, 0:512]), l.ES)
                    l.inc(nc.scalar.copy(av_sb[:], l.bank(bC)[0:64, 0:512]), l.ES)
                    l.freed(bC)
                    bD = l.nb()
                    l.acquire(bD)
                    l.wait(nc.tensor, l.ES, l.val(l.ES))
                    l.inc(nc.tensor.matmul(l.bank(bD)[0:64, 0:512], l.c_ones1_64,
                                           dinv_sb[:], start=True, stop=True), l.TD)
                    l.written(bD)
                    l.consume(nc.vector, bD)
                    l.waitE(nc.vector)
                    l.inc(nc.vector.reciprocal(drep_sb[:], l.bank(bD)[0:64, 0:512]),
                          l.EV)
                    l.freed(bD)
                    for j in range(4):
                        h = g + 4 * j
                        l.inc(nc.vector.tensor_tensor(
                            ctx.rs(h // 2, 64 * (h % 2), 64, b * SL + qt * 128, 128),
                            av_sb[:, j * 128:(j + 1) * 128],
                            drep_sb[:, j * 128:(j + 1) * 128], op=ALU.mult),
                            l.EV)

        def wo_evac(ot, c0, w, tag, ps, bi):
            l.consume(nc.vector, bi)
            l.waitE(nc.vector)
            l.inc(nc.vector.tensor_tensor(x.t(ot, c0, w), x.t(ot, c0, w), ps,
                                          op=ALU.add), l.EV)
            l.freed(bi)
        emit_project(l, hb["aa_wo"], ws, ctx, 8, MAIN4, wo_evac)


def hb_ap_qnw(l):
    return l.c_qnw[:, 0:1]


def hb_ap_knw(l):
    return l.c_knw[:, 0:1]


def emit_l0ffn(l, x, xn, hb, pfx="f0", dbg=None):
    nc = l.nc
    l.phase_barrier()
    with contextlib.ExitStack() as st:
        sb = lambda n, s, d: st.enter_context(nc.sbuf_tensor(n, s, d))
        c_fm = FM(sb(pfx + "c", [128, 8 * 512], F32), 1024, 512)
        cb_fm = FM(sb(pfx + "cb", [128, 8 * 512], BF16), 1024, 512)
        c2_fm = FM(sb(pfx + "c2", [128, 8 * 512], F32), 1024, 512)
        fig = sb(pfx + "fig", [128, 3 * 512], F32)
        tmp = sb(pfx + "tmp", [128, 512], F32)
        h_t = sb(pfx + "h", [128, 512], BF16)
        wsg = [WStream(l, pfx + f"wg{i}", 2048, stack=st) for i in range(3)]
        wso = WStream(l, pfx + "wo", 2048, stack=st)
        for ot in range(8):
            l.inc(nc.scalar.activation(c_fm.t(ot), x.t(0, 0, 512), AF.Identity,
                                       bias=l.c_inc_fm[:, ot:ot + 1], scale=0.0),
                  l.ES)
            l.inc(nc.scalar.copy(cb_fm.t(ot), c_fm.t(ot)), l.ES)
        gates = [("wfuf", AF.Sigmoid, l.c_bf_fm), ("wiui", AF.Sigmoid, l.c_bi_fm),
                 ("wcuc", AF.Tanh, l.c_bc_fm)]
        slots = [wsg[gi].issue(hb[nm], 0) for gi, (nm, _, _) in enumerate(gates)]
        for t in range(B):
            for ot in range(8):
                nslots = []
                for gi, (nm, act, bias) in enumerate(gates):
                    if not (t == B - 1 and ot == 7):
                        nslots.append(wsg[gi].issue(
                            hb[nm], (ot + 1) % 8))
                    else:
                        nslots.append(None)
                for gi, (nm, act, bias) in enumerate(gates):
                    s = slots[gi]
                    wsg[gi].wait_ready(s)
                    if dbg is not None and t == 0 and ot == 0 and gi == 0:
                        l.wait(nc.sync, l.WD)
                        l.inc(nc.sync.dma_start(out=dbg["wslot"],
                                                in_=wsg[gi].slots[s][:]), l.ST, 16)
                        l.waitE(nc.sync)
                        l.inc(nc.sync.dma_start(out=dbg["xnt"],
                                                in_=xn.t(0, 0, SL)), l.ST, 16)
                    bi = l.nb()
                    l.acquire(bi)
                    l.waitE(nc.tensor)
                    last = None
                    for pt in range(16):
                        rhs = (xn.t(pt, t * SL, SL) if pt < 8
                               else cb_fm.t(pt - 8))
                        last = nc.tensor.matmul(l.bank(bi)[0:128, 0:SL],
                                                wsg[gi].chunk(s, pt), rhs,
                                                start=(pt == 0), stop=(pt == 15))
                    l.inc(last, l.TD)
                    l.written(bi)
                    wsg[gi].use(s)
                    l.consume(nc.scalar, bi)
                    l.waitE(nc.scalar)
                    l.inc(nc.scalar.activation(fig[:, gi * 512:(gi + 1) * 512],
                                               l.bank(bi)[:, 0:SL], act,
                                               bias=bias[:, ot:ot + 1]), l.ES)
                    l.freed(bi)
                l.waitE(nc.vector)
                l.inc(nc.vector.tensor_tensor(tmp[:], fig[:, 512:1024],
                                              fig[:, 1024:1536], op=ALU.mult), l.EV)
                l.inc(nc.vector.tensor_tensor(c2_fm.t(ot), fig[:, 0:512],
                                              c_fm.t(ot), op=ALU.mult), l.EV)
                l.inc(nc.vector.tensor_tensor(c2_fm.t(ot), c2_fm.t(ot), tmp[:],
                                              op=ALU.add), l.EV)
                if dbg is not None and t == 0 and ot == 0:
                    l.waitE(nc.sync)
                    l.inc(nc.sync.dma_start(out=dbg["fig0"], in_=fig[:]), l.ST, 16)
                slots = nslots
            c_fm, c2_fm = c2_fm, c_fm
            for ot_ in range(8):
                l.waitE(nc.scalar)
                l.wait(nc.scalar, l.TD)
                l.inc(nc.scalar.copy(cb_fm.t(ot_), c_fm.t(ot_)), l.ES)
            if dbg is not None and t == 0:
                l.waitE(nc.sync)
                for p_ in range(8):
                    l.inc(nc.sync.dma_start(out=dbg["c1"][p_ * 128:(p_ + 1) * 128, :],
                                            in_=c_fm.t(p_)), l.ST, 16)
        so = wso.issue(hb["wouo"], 0)
        for ot in range(8):
            so_next = wso.issue(hb["wouo"], ot + 1) if ot < 7 else None
            wso.wait_ready(so)
            for t in range(B):
                bi = l.nb()
                l.acquire(bi)
                l.waitE(nc.tensor)
                last = None
                for pt in range(16):
                    rhs = (xn.t(pt, t * SL, SL) if pt < 8 else cb_fm.t(pt - 8))
                    last = nc.tensor.matmul(l.bank(bi)[0:128, 0:SL],
                                            wso.chunk(so, pt), rhs,
                                            start=(pt == 0), stop=(pt == 15))
                l.inc(last, l.TD)
                l.written(bi)
                wso.use(so)
                l.consume(nc.scalar, bi)
                l.waitE(nc.scalar)
                l.inc(nc.scalar.activation(h_t[:], l.bank(bi)[:, 0:SL], AF.Tanh,
                                           bias=l.c_bo_fm[:, ot:ot + 1]), l.ES)
                l.freed(bi)
                l.waitE(nc.vector)
                l.inc(nc.vector.tensor_tensor(x.t(ot, t * SL, SL),
                                              x.t(ot, t * SL, SL), h_t[:],
                                              op=ALU.add), l.EV)
            so = so_next


def emit_project_store(l, whbm, wstream, xn, douts, n_tiles, hbm_out, bias_fm=None,
                       bias_ap=None, act=None, stage=None, stage_n=4,
                       hbm_outs=None):
    nc = l.nc
    ow = wstream.ow
    stage_use = [0] * stage_n
    ctr = [0]

    def fn(ot, c0, w, tag, ps, bi):
        sl = ctr[0] % stage_n
        ctr[0] += 1
        l.consume(nc.scalar, bi)
        l.wait(nc.scalar, l.ST, stage_use[sl])
        bias = bias_fm[:, ot:ot + 1] if bias_fm is not None else bias_ap
        if bias is not None:
            l.inc(nc.scalar.activation(stage[0:ow, sl * 512: sl * 512 + w], ps,
                                       act or AF.Identity, bias=bias), l.ES)
        else:
            l.inc(nc.scalar.activation(stage[0:ow, sl * 512: sl * 512 + w], ps,
                                       act or AF.Copy), l.ES)
        l.freed(bi)
        l.waitE(nc.sync)
        if hbm_outs is not None:
            dest, orow = hbm_outs[ot // 8], (ot % 8) * ow
        else:
            dest, orow = hbm_out, ot * ow
        l.inc(nc.sync.dma_start(out=dest[orow:orow + ow, c0:c0 + w],
                                in_=stage[0:ow, sl * 512: sl * 512 + w]), l.ST, 16)
        stage_use[sl] = l.val(l.ST)
    emit_project(l, whbm, wstream, xn, douts, n_tiles, fn)


def build_launch1(upto=99):
    nc = bass.Bass("TRN2", target_bir_lowering=False)
    ctx = contextlib.ExitStack()
    with ctx:
        l = L(nc, ctx)
        x_in = l.din("x_in", [1024, NTH], F32)
        hb = {}
        hb["aa_wq"] = l.din("aa_wq", [8, 128, 1024], F32)
        hb["aa_wk"] = l.din("aa_wk", [2, 128, 1024], F32)
        hb["aa_wv"] = l.din("aa_wv", [2, 128, 1024], F32)
        hb["aa_wo"] = l.din("aa_wo", [8, 128, 1024], F32)
        hb["aa_R4"] = l.din("aa_R4", [4, 2, 128, 512], F32)
        hb["aa_R4f"] = l.din("aa_R4f", [4, 128, 512], F32)
        for nm in ("wfuf", "wiui", "wcuc", "wouo"):
            hb[nm] = l.din(nm, [8, 2048, 128], F32)
        hb["ab_wq"] = l.din("ab_wq", [8, 128, 1024], F32)
        hb["ab_wk"] = l.din("ab_wk", [8, 128, 1024], F32)
        hb["ab_wv"] = l.din("ab_wv", [8, 128, 1024], F32)
        consts = l.din("consts1", [128, 32], F32)
        identb = l.din("identb", [128, 128], BF16)
        fconsts = l.din("fconsts", [128, 8], F32)
        onesb = l.din("onesb", [128, 4], BF16)
        xres_o = l.dout("xres", [1024, NT], F32)
        qb_o = l.dout("qb", [1024, NT], BF16)
        kb_o = l.dout("kb", [1024, NT], BF16)
        vb_o = l.dout("vb", [1024, NT], BF16)

        fcs = l.sbuf("fcs", [128, 8], F32)
        l.inc(nc.sync.dma_start(out=fcs[:], in_=fconsts[:]), l.LD, 16)
        for i, v in enumerate(FCONST_VALS):
            nc.const_aps.register(F32, float(v), fcs[:, i:i + 1])
        x = alloc_fm(l, "x", 1024, NTH, F32)
        xn = alloc_fm(l, "xn", 1024, NTH, BF16)
        cst = l.sbuf("cst", [128, 32], F32)
        l.c_ident = l.sbuf("identsb", [128, 128], BF16)
        onesb_sb = l.sbuf("onesbsb", [128, 4], BF16)
        stage = l.sbuf("stage", [128, 4 * 512], BF16)

        load_fm(l, x, x_in)
        l.inc(nc.sync.dma_start(out=cst[:], in_=consts[:]), l.LD, 16)
        l.inc(nc.sync.dma_start(out=l.c_ident[:], in_=identb[:]), l.LD, 16)
        l.inc(nc.sync.dma_start(out=onesb_sb[:], in_=onesb[:]), l.LD, 16)
        l.c_ones128_b = onesb_sb[:, 0:1]
        rowones = l.sbuf("rowones", [1, 128], BF16)
        l.inc(nc.gpsimd.memset(rowones[:], 1.0), l.WD, 1)
        l.c_ones1_128 = rowones[0:1, 0:128]
        l.c_ones1_64 = rowones[0:1, 0:64]
        ind128_2 = l.sbuf("ind1282", [128, 2], BF16)
        l.inc(nc.gpsimd.memset(ind128_2[0:64, 0:1], 1.0), l.WD, 1)
        l.inc(nc.gpsimd.memset(ind128_2[0:64, 1:2], 0.0), l.WD, 1)
        l.inc(nc.gpsimd.memset(ind128_2[64:128, 0:1], 0.0), l.WD, 1)
        l.inc(nc.gpsimd.memset(ind128_2[64:128, 1:2], 1.0), l.WD, 1)
        l.c_ind128_2 = ind128_2[:]
        indrow = l.din("indrow", [2, 128], BF16)
        ind2_128 = l.sbuf("ind2128", [2, 128], BF16)
        l.inc(nc.sync.dma_start(out=ind2_128[:], in_=indrow[:]), l.LD, 16)
        l.c_ind2_128 = ind2_128[:]
        l.c_qnw = cst[:, 0:1]
        l.c_knw = cst[:, 1:2]
        l.c_bf_fm = cst[:, 8:16]
        l.c_bi_fm = cst[:, 16:24]
        l.c_bc_fm = cst[:, 24:32]
        consts2 = l.din("consts2", [128, 40], F32)
        cst2 = l.sbuf("cst2", [128, 40], F32)
        l.inc(nc.sync.dma_start(out=cst2[:], in_=consts2[:]), l.LD, 16)
        l.c_bo_fm = cst2[:, 0:8]
        l.c_inc_fm = cst2[:, 8:16]
        l.c_bqb_fm = cst2[:, 16:24]
        l.c_bkb_fm = cst2[:, 24:32]
        l.c_bvb_fm = cst2[:, 32:40]

        for eng in (nc.tensor, nc.scalar, nc.vector):
            l.wait(eng, l.LD)
            l.wait(eng, l.WD)

        do_ln(l, x, xn, C=NTH)
        if upto >= 1:
            emit_l0aa(l, x, xn, hb)
        if upto >= 2:
            do_ln(l, x, xn, C=NT)
        if upto >= 3:
            emit_l0ffn(l, x, xn, hb, pfx="f0")
        if upto >= 4:
            do_ln(l, x, xn, C=NT)
        ws2 = WStream(l, "abw", 1024)
        if upto >= 4:
            emit_project_store(l, hb["ab_wq"], ws2, xn, 8, MAIN4, qb_o,
                               bias_fm=l.c_bqb_fm, stage=stage)
            emit_project_store(l, hb["ab_wk"], ws2, xn, 8, MAIN4, kb_o,
                               bias_fm=l.c_bkb_fm, stage=stage)
            emit_project_store(l, hb["ab_wv"], ws2, xn, 8, MAIN4, vb_o,
                               bias_fm=l.c_bvb_fm, stage=stage)
        l.waitE(nc.sync)
        l.wait(nc.sync, l.TD)
        for p in range(8):
            l.inc(nc.sync.dma_start(out=xres_o[p * 128:(p + 1) * 128, :],
                                    in_=x.t(p, 0, NT)), l.ST, 16)
        l.wait(nc.sync, l.ST)
    return nc, l


def _tile_w(w_eff, ow=128):
    din, dout = w_eff.shape
    npt = din // 128
    arr = w_eff.reshape(npt, 128, dout // ow, ow)
    return np.ascontiguousarray(arr.transpose(2, 1, 0, 3).reshape(dout // ow, 128,
                                                                  npt * ow))


def _alibi_slopes16():
    import math
    base = 2.0 ** (-2.0 ** (-(math.log2(16) - 3)))
    return (base ** np.arange(1, 17)).astype(np.float32)


def _fmcols(vec):
    return np.ascontiguousarray(vec.reshape(8, 128).T)


def prep_launch1(inp):
    slopes = _alibi_slopes16()
    x = inp["L0_attn_a_x"].astype(np.float32)
    base = {}
    base["aa_wq"] = _tile_w(inp["L0_attn_a_wq"].T)
    base["aa_wk"] = _tile_w(inp["L0_attn_a_wk"].T)
    base["aa_wv"] = _tile_w(inp["L0_attn_a_wv"].T)
    base["aa_wo"] = _tile_w(inp["L0_attn_a_wo"].T)
    for nm, wk, uk in (("wfuf", "L0_ffn_wf", "L0_ffn_uf"),
                       ("wiui", "L0_ffn_wi", "L0_ffn_ui"),
                       ("wcuc", "L0_ffn_wc", "L0_ffn_uc"),
                       ("wouo", "L0_ffn_wo", "L0_ffn_uo")):
        base[nm] = np.ascontiguousarray(np.concatenate(
            [_tile_w(inp[wk].T), _tile_w(inp[uk].T)], axis=2))
    base["ab_wq"] = _tile_w(inp["L0_attn_b_wq"].T)
    base["ab_wk"] = _tile_w(inp["L0_attn_b_wk"].T)
    base["ab_wv"] = _tile_w(inp["L0_attn_b_wv"].T)
    c1 = np.zeros((128, 32), np.float32)
    qnw = inp["L0_attn_a_q_norm_w"]
    knw = inp["L0_attn_a_k_norm_w"]
    c1[:, 0] = np.concatenate([qnw, qnw])
    c1[:, 1] = np.concatenate([knw, knw])
    c1[:, 8:16] = _fmcols(inp["L0_ffn_bf"])
    c1[:, 16:24] = _fmcols(inp["L0_ffn_bi"])
    c1[:, 24:32] = _fmcols(inp["L0_ffn_bc"])
    base["consts1"] = c1
    c2 = np.zeros((128, 40), np.float32)
    c2[:, 0:8] = _fmcols(inp["L0_ffn_bo"])
    c2[:, 8:16] = _fmcols(inp["L0_ffn_in_c"])
    c2[:, 16:24] = _fmcols(inp["L0_attn_b_bq"])
    c2[:, 24:32] = _fmcols(inp["L0_attn_b_bk"])
    c2[:, 32:40] = _fmcols(inp["L0_attn_b_bv"])
    base["consts2"] = c2
    base["identb"] = np.eye(128, dtype=BF16NP)
    ob = np.zeros((128, 4), BF16NP)
    ob[:, 0] = 1
    base["onesb"] = ob
    base["fconsts"] = np.tile(np.array([FCONST_VALS], np.float32), (128, 1))
    ir = np.zeros((2, 128), BF16NP)
    ir[0, 0:64] = 1
    ir[1, 64:128] = 1
    base["indrow"] = ir

    i_idx = np.arange(128)[:, None]
    q_idx = np.arange(128)[None, :]
    R4 = np.zeros((4, 2, 128, 512), np.float32)
    R4f_main = np.zeros((4, 128, 512), np.float32)
    for g in range(4):
        for cls in range(2):
            rel = (i_idx - q_idx) - (128 if cls == 0 else 0)
            ok = (rel <= 0) & (rel >= -HALO)
            for j in range(4):
                h = g + 4 * j
                tile = np.where(ok, slopes[h] * rel, NEG).astype(np.float32)
                R4[g, cls, :, j * 128:(j + 1) * 128] = tile
        R4f_main[g] = R4[g, 0]
    R4f_core0 = np.full((4, 128, 512), NEG, np.float32)

    in_maps = []
    for r in range(NCORES):
        m = dict(base)
        xm = x[:, r * SL:(r + 1) * SL, :].transpose(2, 0, 1).reshape(1024, NT)
        if r == 0:
            xh = np.zeros((1024, B * HALO), np.float32)
        else:
            xh = x[:, r * SL - HALO: r * SL, :].transpose(2, 0, 1).reshape(1024, B * HALO)
        m["x_in"] = np.ascontiguousarray(np.concatenate([xm, xh], axis=1))
        m["aa_R4"] = R4
        m["aa_R4f"] = R4f_core0 if r == 0 else R4f_main
        in_maps.append(m)
    return in_maps


def emit_l0ab(l, x, q_fm, kt_hbm, vt_hbm, scale, qt_w=256):
    nc = l.nc
    l.phase_barrier()
    NKT = S // 128
    with contextlib.ExitStack() as st:
        sb = lambda n, s_, d: st.enter_context(nc.sbuf_tensor(n, s_, d))
        kslot = [sb(f"abk{i}", [128, 8 * 128], BF16) for i in range(2)]
        vslot = [sb(f"abv{i}", [128, 1024], BF16) for i in range(2)]
        esb = sb("abe", [128, 2 * qt_w], BF16)
        avs = sb("abav", [128, 8 * qt_w], BF16)
        dinvb = sb("abdinv", [1, qt_w], BF16)
        drep = sb("abdrep", [128, qt_w], F32)
        slot_user = [-1, -1]

        def issue_kv(idx, b, kt):
            s_ = idx % 2
            l.wait(nc.gpsimd, l.TD,
                   slot_user[s_] if slot_user[s_] >= 0 else l.val(l.TD))
            l.inc(nc.gpsimd.dma_start(out=kslot[s_][:], in_=kt_hbm[b, kt]),
                  l.WD, 16)
            l.inc(nc.gpsimd.dma_start(
                out=vslot[s_][:], in_=vt_hbm[b][kt * 128:(kt + 1) * 128, :]),
                l.WD, 16)
            return s_, l.val(l.WD)

        eslu = [0, 0]
        nqt = SL // qt_w
        for b in range(B):
            for qt in range(nqt):
                q0 = b * SL + qt * qt_w
                for bk in range(7):
                    l.acquire(bk)
                cur = issue_kv(0, b, 0)
                q_ld = l.val(l.LD)
                exp_es0 = [0] * NKT
                kv = [None] * (NKT + 1)
                kv[0] = cur

                def scores0(kt):
                    s_, rdy = kv[kt]
                    sb_bank = kt % 2
                    l.wait(nc.tensor, l.WD, rdy)
                    _es, _ev = l.bank_free[sb_bank]
                    l.waitE(nc.tensor, _es, _ev)
                    l.wait(nc.tensor, l.LD, q_ld)
                    last = None
                    for pt in range(8):
                        last = nc.tensor.matmul(
                            l.bank(sb_bank)[0:128, 0:qt_w],
                            kslot[s_][:, pt * 128:(pt + 1) * 128],
                            q_fm.t(pt, q0, qt_w),
                            start=(pt == 0), stop=(pt == 7))
                    l.inc(last, l.TD)
                    l.written(sb_bank)

                def exp0(kt):
                    sb_bank = kt % 2
                    l.consume(nc.scalar, sb_bank)
                    l.wait(nc.scalar, l.TD, eslu[kt % 2])
                    l.inc(nc.scalar.activation(
                        esb[:, (kt % 2) * qt_w:(kt % 2 + 1) * qt_w],
                        l.bank(sb_bank)[0:128, 0:qt_w],
                        AF.Exp, scale=scale), l.ES)
                    l.freed(sb_bank)
                    exp_es0[kt] = l.val(l.ES)

                def av0(kt):
                    s_, rdy = kv[kt]
                    l.wait(nc.tensor, l.ES, exp_es0[kt])
                    nc.tensor.matmul(l.bank(6)[0:1, 0:qt_w], l.c_ones128_b,
                                     esb[:, (kt % 2) * qt_w:(kt % 2 + 1) * qt_w],
                                     start=(kt == 0), stop=(kt == NKT - 1))
                    last = None
                    for c in range(8):
                        last = nc.tensor.matmul(
                            l.bank(2 + c // 2)[0:128, (c % 2) * qt_w:(c % 2 + 1) * qt_w],
                            vslot[s_][:, c * 128:(c + 1) * 128],
                            esb[:, (kt % 2) * qt_w:(kt % 2 + 1) * qt_w],
                            start=(kt == 0), stop=(kt == NKT - 1))
                    l.inc(last, l.TD)
                    slot_user[s_] = l.val(l.TD)
                    eslu[kt % 2] = l.val(l.TD)

                kv[1] = issue_kv(1, b, 1)
                scores0(0)
                exp0(0)
                for kt in range(NKT):
                    if kt + 1 < NKT:
                        scores0(kt + 1)
                    av0(kt)
                    if kt + 2 <= NKT - 1:
                        kv[kt + 2] = issue_kv(kt + 2, b, kt + 2)
                    if kt + 1 < NKT:
                        exp0(kt + 1)
                for bk in range(7):
                    l.written(bk)
                l.consume(nc.vector, 6)
                l.waitE(nc.vector)
                l.inc(nc.vector.tensor_copy(dinvb[:], l.bank(6)[0:1, 0:qt_w]), l.EV)
                l.freed(6)
                ep_ev = l.val(l.EV)
                l.acquire(0)
                l.wait(nc.tensor, l.EV, ep_ev)
                l.inc(nc.tensor.matmul(l.bank(0)[0:128, 0:qt_w], l.c_ones1_128,
                                       dinvb[:], start=True, stop=True), l.TD)
                l.written(0)
                l.consume(nc.vector, 0)
                l.inc(nc.vector.reciprocal(drep[:, 0:qt_w], l.bank(0)[0:128, 0:qt_w]),
                      l.EV)
                l.waitE(nc.scalar)
                for c in range(8):
                    bk = 2 + c // 2
                    l.consume(nc.scalar, bk)
                    l.inc(nc.scalar.copy(
                        avs[:, c * qt_w:(c + 1) * qt_w],
                        l.bank(bk)[0:128, (c % 2) * qt_w:(c % 2 + 1) * qt_w]),
                        l.ES)
                    l.wait(nc.vector, l.ES, l.val(l.ES))
                    l.inc(nc.vector.tensor_tensor(
                        avs[:, c * qt_w:(c + 1) * qt_w],
                        avs[:, c * qt_w:(c + 1) * qt_w],
                        drep[:, 0:qt_w], op=ALU.mult), l.EV)
                    l.inc(nc.vector.tensor_tensor(x.t(c, q0, qt_w), x.t(c, q0, qt_w),
                                                  avs[:, c * qt_w:(c + 1) * qt_w],
                                                  op=ALU.add), l.EV)
                for bk in range(7):
                    l.freed(bk)


def build_launch2(upto=99):
    nc = bass.Bass("TRN2", target_bir_lowering=False)
    ctx = contextlib.ExitStack()
    with ctx:
        l = L(nc, ctx)
        x_in = l.din("x_in", [1024, NT], F32)
        qb_in = l.din("qb_in", [1024, NT], BF16)
        kt_all = l.din("kt_all", [B, S // 128, 128, 8 * 128], BF16)
        vt_all = l.din("vt_all", [B, S, 1024], BF16)
        hb = {}
        for nm in ("wfuf", "wiui", "wcuc", "wouo"):
            hb[nm] = l.din(nm, [8, 2048, 128], F32)
        hb["wdelta"] = l.din("wdelta", [1, 128, 128], F32)
        hb["win"] = l.din("win", [1, 128, 128], F32)
        hb["wgate"] = l.din("wgate", [8, 128, 1024], F32)
        hb["wq1"] = l.din("wq1", [8, 128, 1024], F32)
        hb["wk1"] = l.din("wk1", [8, 128, 1024], F32)
        hb["wv1"] = l.din("wv1", [8, 128, 1024], F32)
        fconsts = l.din("fconsts", [128, 8], F32)
        consts1 = l.din("consts1", [128, 32], F32)
        consts2 = l.din("consts2", [128, 40], F32)
        consts3 = l.din("consts3", [128, 64], F32)
        onesb = l.din("onesb", [128, 4], BF16)
        indrow = l.din("indrow", [2, 128], BF16)

        xres_o = l.dout("xres", [1024, NT], F32)
        delta_o = l.dout("delta", [16, NT], BF16)
        bt_o = l.dout("bt", [16, NT], BF16)
        gate_o = l.dout("gate", [1024, NT], BF16)
        q1_o = l.dout("q1", [1024, NT], BF16)
        k1_o = l.dout("k1", [1024, NT], BF16)
        v1_o = l.dout("v1", [1024, NT], BF16)

        fcs = l.sbuf("fcs", [128, 8], F32)
        l.inc(nc.sync.dma_start(out=fcs[:], in_=fconsts[:]), l.LD, 16)
        for i, v in enumerate(FCONST_VALS):
            nc.const_aps.register(F32, float(v), fcs[:, i:i + 1])
        x = alloc_fm(l, "x", 1024, NT, F32)
        xn = alloc_fm(l, "xn", 1024, NT, BF16)
        cst = l.sbuf("cst", [128, 32], F32)
        cst2 = l.sbuf("cst2", [128, 40], F32)
        cst3 = l.sbuf("cst3", [128, 64], F32)
        onesb_sb = l.sbuf("onesbsb", [128, 4], BF16)
        ind2_128 = l.sbuf("ind2128", [2, 128], BF16)
        rowones = l.sbuf("rowones", [1, 128], BF16)
        stage = l.sbuf("stage", [128, 4 * 512], BF16)

        load_fm(l, x, x_in)
        l.inc(nc.sync.dma_start(out=cst[:], in_=consts1[:]), l.LD, 16)
        l.inc(nc.sync.dma_start(out=cst2[:], in_=consts2[:]), l.LD, 16)
        l.inc(nc.sync.dma_start(out=cst3[:], in_=consts3[:]), l.LD, 16)
        l.inc(nc.sync.dma_start(out=onesb_sb[:], in_=onesb[:]), l.LD, 16)
        l.inc(nc.sync.dma_start(out=ind2_128[:], in_=indrow[:]), l.LD, 16)
        l.inc(nc.gpsimd.memset(rowones[:], 1.0), l.WD, 1)
        l.c_ones128_b = onesb_sb[:, 0:1]
        l.c_ones1_128 = rowones[0:1, 0:128]
        l.c_ones1_64 = rowones[0:1, 0:64]
        l.c_ind2_128 = ind2_128[:]
        l.c_bf_fm = cst[:, 8:16]
        l.c_bi_fm = cst[:, 16:24]
        l.c_bc_fm = cst[:, 24:32]
        l.c_bo_fm = cst2[:, 0:8]
        l.c_inc_fm = cst2[:, 8:16]
        for eng in (nc.tensor, nc.scalar, nc.vector):
            l.wait(eng, l.LD)
            l.wait(eng, l.WD)

        with contextlib.ExitStack() as st3:
            q_sb = FM(st3.enter_context(nc.sbuf_tensor("qown", [128, 8 * NT], BF16)),
                      1024, NT)
            load_fm(l, q_sb, qb_in)
            if upto >= 1:
                emit_l0ab(l, x, q_sb, kt_all, vt_all, scale=1.0 / 32.0)
            l.phase_barrier()
        if upto >= 2:
            do_ln(l, x, xn, C=NT)
            emit_l0ffn(l, x, xn, hb, pfx="f1")
        if upto >= 3:
            xln = alloc_fm(l, "xln", 1024, NT, BF16)
            do_ln(l, x, xln, C=NT)
            xn_a = xn
            do_ln(l, xln, xn_a, C=NT, g_fm=cst3[:, 0:8], b_fm=cst3[:, 8:16])
            ws1 = WStream(l, "wd16", 1024, ow=16)
            emit_project_store(l, hb["wdelta"], ws1, xn_a, 1, MAIN4, delta_o,
                               stage=stage, act=AF.Sigmoid,
                               bias_ap=cst2[0:16, 32:33])
            emit_project_store(l, hb["win"], ws1, xn_a, 1, MAIN4, bt_o,
                               stage=stage, bias_ap=cst2[0:16, 33:34])
            ws8 = WStream(l, "w8", 1024)
            emit_project_store(l, hb["wgate"], ws8, xn_a, 8, MAIN4, gate_o,
                               bias_fm=cst3[:, 32:40], stage=stage,
                               act=AF.Sigmoid)
            do_ln(l, xln, xn_a, C=NT, g_fm=cst3[:, 16:24], b_fm=cst3[:, 24:32])
            emit_project_store(l, hb["wq1"], ws8, xn_a, 8, MAIN4, q1_o,
                               bias_fm=cst3[:, 40:48], stage=stage)
            emit_project_store(l, hb["wk1"], ws8, xn_a, 8, MAIN4, k1_o,
                               bias_fm=cst3[:, 48:56], stage=stage)
            emit_project_store(l, hb["wv1"], ws8, xn_a, 8, MAIN4, v1_o,
                               bias_fm=cst3[:, 56:64], stage=stage)

        l.waitE(nc.sync)
        l.wait(nc.sync, l.TD)
        for p in range(8):
            l.inc(nc.sync.dma_start(out=xres_o[p * 128:(p + 1) * 128, :],
                                    in_=x.t(p, 0, NT)), l.ST, 16)
        l.wait(nc.sync, l.ST)
    return nc, l





def _tile_w2(w_eff, ow=128):
    return _tile_w(w_eff, ow=ow)


def prep_launch2(inp, l1outs):
    base = {}
    for nm, wk, uk in (("wfuf", "L0_ffn_wf", "L0_ffn_uf"),
                       ("wiui", "L0_ffn_wi", "L0_ffn_ui"),
                       ("wcuc", "L0_ffn_wc", "L0_ffn_uc"),
                       ("wouo", "L0_ffn_wo", "L0_ffn_uo")):
        base[nm] = np.ascontiguousarray(np.concatenate(
            [_tile_w(inp[wk].T), _tile_w(inp[uk].T)], axis=2))
    base["wdelta"] = _tile_w2(inp["L1_attn_a_Wdelta"], ow=16)
    base["win"] = _tile_w2(inp["L1_attn_a_Win"], ow=16)
    base["wgate"] = _tile_w(inp["L1_attn_a_Wgate"])
    base["wq1"] = _tile_w(inp["L1_attn_a_Wq"])
    base["wk1"] = _tile_w(inp["L1_attn_a_Wk"])
    base["wv1"] = _tile_w(inp["L1_attn_a_Wv"])
    base["fconsts"] = np.tile(np.array([FCONST_VALS], np.float32), (128, 1))
    c1 = np.zeros((128, 32), np.float32)
    c1[:, 8:16] = _fmcols(inp["L0_ffn_bf"])
    c1[:, 16:24] = _fmcols(inp["L0_ffn_bi"])
    c1[:, 24:32] = _fmcols(inp["L0_ffn_bc"])
    base["consts1"] = c1
    c2 = np.zeros((128, 40), np.float32)
    c2[:, 0:8] = _fmcols(inp["L0_ffn_bo"])
    c2[:, 8:16] = _fmcols(inp["L0_ffn_in_c"])
    c2[0:16, 32] = inp["L1_attn_a_bdelta"]
    c2[0:16, 33] = inp["L1_attn_a_b_in"]
    base["consts2"] = c2
    c3 = np.zeros((128, 64), np.float32)
    c3[:, 0:8] = _fmcols(inp["L1_attn_a_ln_g"])
    c3[:, 8:16] = _fmcols(inp["L1_attn_a_ln_b"])
    c3[:, 16:24] = _fmcols(inp["L1_attn_a_ln1_g"])
    c3[:, 24:32] = _fmcols(inp["L1_attn_a_ln1_b"])
    c3[:, 32:40] = _fmcols(inp["L1_attn_a_bgate"])
    c3[:, 40:48] = _fmcols(inp["L1_attn_a_bq"])
    c3[:, 48:56] = _fmcols(inp["L1_attn_a_bk"])
    c3[:, 56:64] = _fmcols(inp["L1_attn_a_bv"])
    base["consts3"] = c3
    ob = np.zeros((128, 4), BF16NP)
    ob[:, 0] = 1
    base["onesb"] = ob
    ir = np.zeros((2, 128), BF16NP)
    ir[0, 0:64] = 1
    ir[1, 64:128] = 1
    base["indrow"] = ir

    kt_flat = np.zeros((B, 1024, S), BF16NP)
    vt_all = np.zeros((B, S, 1024), BF16NP)
    for r in range(NCORES):
        kb = l1outs[r]["kb"]
        vb = l1outs[r]["vb"]
        for b in range(B):
            kt_flat[b][:, r * SL:(r + 1) * SL] = kb[:, b * SL:(b + 1) * SL]
            vt_all[b][r * SL:(r + 1) * SL, :] = vb[:, b * SL:(b + 1) * SL].T
    kt_all = np.ascontiguousarray(
        kt_flat.reshape(B, 8, 128, S // 128, 128).transpose(0, 3, 2, 1, 4)
        .reshape(B, S // 128, 128, 8 * 128))
    base["kt_all"] = kt_all
    base["vt_all"] = vt_all

    in_maps = []
    for r in range(NCORES):
        m = dict(base)
        m["x_in"] = l1outs[r]["xres"]
        m["qb_in"] = l1outs[r]["qb"]
        in_maps.append(m)
    return in_maps


NSWEEP = 12


def emit_scan(l, delta_sb, bt_sb, h_a, h_b, abig, i64, h_bf):
    nc = l.nc
    l.phase_barrier()
    l.inc(nc.gpsimd.memset(h_a[:], 0.0), l.WD, 1)
    l.inc(nc.gpsimd.memset(h_b[:], 0.0), l.WD, 1)
    t1b = l.scan_t1b
    src, dst = h_a, h_b
    for sw in range(NSWEEP):
        prev_es = l.val(l.ES)
        prev_ev = l.val(l.EV)
        l.wait(nc.scalar, l.TD)
        l.wait(nc.scalar, l.WD)
        l.wait(nc.scalar, l.EV, prev_ev)
        l.inc(nc.scalar.copy(src[:, 0:1], src[:, S:S + 1]), l.ES)
        wrap_es = l.val(l.ES)
        for stile in range(8):
            c0 = stile * 512
            l.wait(nc.vector, l.ES, wrap_es)
            l.wait(nc.vector, l.WD)
            l.inc(nc.vector.tensor_tensor(t1b[:, stile * 512:(stile + 1) * 512],
                                          src[:, c0:c0 + 512],
                                          delta_sb[:, c0:c0 + 512], op=ALU.mult),
                  l.EV)
            t1_ev = l.val(l.EV)
            bi = l.nb()
            l.acquire(bi)
            l.wait(nc.tensor, l.EV, t1_ev)
            nc.tensor.matmul(l.bank(bi)[0:64, 0:512], abig,
                             t1b[:, stile * 512:(stile + 1) * 512],
                             start=True, stop=False)
            l.inc(nc.tensor.matmul(l.bank(bi)[0:64, 0:512], i64,
                                   bt_sb[:, c0:c0 + 512], start=False, stop=True),
                  l.TD)
            l.written(bi)
            l.consume(nc.scalar, bi)
            l.inc(nc.scalar.activation(dst[:, 1 + c0:1 + c0 + 512],
                                       l.bank(bi)[0:64, 0:512], AF.Tanh), l.ES)
            l.freed(bi)
        src, dst = dst, src
    l.waitE(nc.scalar)
    l.inc(nc.scalar.copy(h_bf[:], src[0:64, 1:513]), l.ES)


def emit_l1a_attn(l, ctx_fm, kt1h, vt1h, scale):
    nc = l.nc
    l.phase_barrier()
    NKT = S // 128
    with contextlib.ExitStack() as st:
        sb = lambda n, s_, d: st.enter_context(nc.sbuf_tensor(n, s_, d))
        kslot = [sb(f"l1k{i}", [128, S], BF16) for i in range(2)]
        vslot = [sb(f"l1v{i}", [128, 32 * 128], BF16) for i in range(2)]
        esb = sb("l1e", [128, 2 * 512], BF16)
        avs = sb("l1av", [128, 512], BF16)
        dinvb = sb("l1dinv", [1, 512], BF16)
        drep = sb("l1drep", [128, 512], F32)
        slot_user = [-1, -1]

        def issue(idx, b, h):
            s_ = idx % 2
            l.wait(nc.gpsimd, l.TD,
                   slot_user[s_] if slot_user[s_] >= 0 else l.val(l.TD))
            l.inc(nc.gpsimd.dma_start(out=kslot[s_][:],
                                      in_=kt1h[b][h * 128:(h + 1) * 128, :]),
                  l.WD, 16)
            l.inc(nc.gpsimd.dma_start(out=vslot[s_][:], in_=vt1h[b, h]), l.WD, 16)
            return s_, l.val(l.WD)

        eslu = [0, 0]
        pairs = [(b, h) for b in range(B) for h in range(8)]
        cur = issue(0, *pairs[0])
        for pi, (b, h) in enumerate(pairs):
            nxt = issue(pi + 1, *pairs[pi + 1]) if pi + 1 < len(pairs) else None
            s_, rdy = cur
            l.wait(nc.tensor, l.WD, rdy)
            for bk in (2, 3):
                l.acquire(bk)
            exp_es2 = [0] * NKT

            def scores1(kt):
                sbk = kt % 2
                _es, _ev = l.bank_free[sbk]
                l.waitE(nc.tensor, _es, _ev)
                l.inc(nc.tensor.matmul(l.bank(sbk)[0:128, 0:512],
                                       kslot[s_][:, kt * 128:(kt + 1) * 128],
                                       ctx_fm.t(h, b * SL, SL),
                                       start=True, stop=True), l.TD)
                l.written(sbk)

            def exp1(kt):
                sbk = kt % 2
                l.consume(nc.scalar, sbk)
                l.wait(nc.scalar, l.TD, eslu[kt % 2])
                l.inc(nc.scalar.activation(esb[:, (kt % 2) * 512:(kt % 2 + 1) * 512],
                                           l.bank(sbk)[0:128, 0:512], AF.Exp,
                                           scale=scale), l.ES)
                l.freed(sbk)
                exp_es2[kt] = l.val(l.ES)

            def av1(kt):
                l.wait(nc.tensor, l.ES, exp_es2[kt])
                nc.tensor.matmul(l.bank(3)[0:1, 0:512], l.c_ones128_b,
                                 esb[:, (kt % 2) * 512:(kt % 2 + 1) * 512],
                                 start=(kt == 0), stop=(kt == NKT - 1))
                l.inc(nc.tensor.matmul(l.bank(2)[0:128, 0:512],
                                       vslot[s_][:, kt * 128:(kt + 1) * 128],
                                       esb[:, (kt % 2) * 512:(kt % 2 + 1) * 512],
                                       start=(kt == 0), stop=(kt == NKT - 1)),
                      l.TD)
                eslu[kt % 2] = l.val(l.TD)
                slot_user[s_] = l.val(l.TD)

            scores1(0)
            exp1(0)
            for kt in range(NKT):
                if kt + 1 < NKT:
                    scores1(kt + 1)
                av1(kt)
                if kt + 1 < NKT:
                    exp1(kt + 1)
            for bk in (2, 3):
                l.written(bk)
            l.consume(nc.vector, 3)
            l.waitE(nc.vector)
            l.inc(nc.vector.tensor_copy(dinvb[:], l.bank(3)[0:1, 0:512]), l.EV)
            l.freed(3)
            l.consume(nc.vector, 2)
            l.inc(nc.vector.tensor_copy(avs[:], l.bank(2)[0:128, 0:512]), l.EV)
            l.freed(2)
            ep_ev = l.val(l.EV)
            l.acquire(4)
            l.wait(nc.tensor, l.EV, ep_ev)
            l.inc(nc.tensor.matmul(l.bank(4)[0:128, 0:512], l.c_ones1_128,
                                   dinvb[:], start=True, stop=True), l.TD)
            l.written(4)
            l.consume(nc.vector, 4)
            l.inc(nc.vector.reciprocal(drep[:], l.bank(4)[0:128, 0:512]), l.EV)
            l.freed(4)
            l.inc(nc.vector.tensor_tensor(ctx_fm.t(h, b * SL, SL), avs[:],
                                          drep[:], op=ALU.mult), l.EV)
            cur = nxt


def build_launch3(upto=99):
    nc = bass.Bass("TRN2", target_bir_lowering=False)
    ctx = contextlib.ExitStack()
    with ctx:
        l = L(nc, ctx)
        x_in = l.din("x_in", [1024, NT], F32)
        gate_in = l.din("gate_in", [1024, NT], BF16)
        q1_in = l.din("q1_in", [1024, NT], BF16)
        delta_in = l.din("delta_in", [64, S], BF16)
        bt_in = l.din("bt_in", [64, S], BF16)
        kt1_all = l.din("kt1_all", [B, 1024, S], BF16)
        vt1h_all = l.din("vt1h_all", [B, 8, 128, (S // 128) * 128], BF16)
        abig_in = l.din("abig_in", [64, 64], F32)
        i64_in = l.din("i64_in", [64, 64], BF16)
        cmask_in = l.din("cmask_in", [64, B * 8 * 128], F32)
        hb = {}
        hb["wo1"] = l.din("wo1", [8, 128, 1024], F32)
        hb["w1"] = l.din("w1", [32, 128, 1024], F32)
        hb["w2"] = l.din("w2", [8, 128, 4096], F32)
        hb["wfa"] = l.din("wfa", [8, 128, 1024], F32)
        hb["wfb"] = l.din("wfb", [8, 128, 1024], F32)
        hb["wqkv2"] = l.din("wqkv2", [24, 128, 1024], F32)
        g1t_in = l.din("g1t_in", [32, 2 * 4 * 32], F32)
        g2t_in = l.din("g2t_in", [16, 2 * 4 * 16], F32)
        indd_in = l.din("indd_in", [32, 4 * 128], BF16)
        indg_in = l.din("indg_in", [16, 128], BF16)
        fconsts = l.din("fconsts", [128, 8], F32)
        consts4 = l.din("consts4", [128, 96], F32)
        b1fm_in = l.din("b1fm", [128, 32], F32)
        onesb = l.din("onesb", [128, 4], BF16)

        xres_o = l.dout("xres", [1024, NT], F32)
        q2_o = l.dout("q2", [1024, NT], BF16)
        k2_o = l.dout("k2", [1024, NT], BF16)
        v2_o = l.dout("v2", [1024, NT], BF16)

        fcs = l.sbuf("fcs", [128, 8], F32)
        l.inc(nc.sync.dma_start(out=fcs[:], in_=fconsts[:]), l.LD, 16)
        for i, v in enumerate(FCONST_VALS):
            nc.const_aps.register(F32, float(v), fcs[:, i:i + 1])
        cst4 = l.sbuf("cst4", [128, 96], F32)
        b1fm = l.sbuf("b1fm_sb", [128, 32], F32)
        onesb_sb = l.sbuf("onesbsb", [128, 4], BF16)
        rowones = l.sbuf("rowones", [1, 128], BF16)
        stage = l.sbuf("stage", [128, 4 * 512], BF16)
        l.inc(nc.sync.dma_start(out=cst4[:], in_=consts4[:]), l.LD, 16)
        l.inc(nc.sync.dma_start(out=b1fm[:], in_=b1fm_in[:]), l.LD, 16)
        l.inc(nc.sync.dma_start(out=onesb_sb[:], in_=onesb[:]), l.LD, 16)
        l.inc(nc.gpsimd.memset(rowones[:], 1.0), l.WD, 1)
        l.c_ones128_b = onesb_sb[:, 0:1]
        l.c_ones1_128 = rowones[0:1, 0:128]
        l.c_ln2g, l.c_ln2b = cst4[:, 0:8], cst4[:, 8:16]
        l.c_lnfg, l.c_lnfb = cst4[:, 16:24], cst4[:, 24:32]
        l.c_bo1 = cst4[:, 32:40]
        l.c_b2 = cst4[:, 40:48]
        l.c_bfw = cst4[:, 48:56]
        l.c_bqkv2 = cst4[:, 56:80]
        l.c_bri = cst4[:, 80:88]
        for eng in (nc.tensor, nc.scalar, nc.vector):
            l.wait(eng, l.LD)
            l.wait(eng, l.WD)

        fused_d = nc.dram_tensor("fused_d", [1024, NT], BF16, kind="Internal").ap()
        big = contextlib.ExitStack()
        ctx.enter_context(big)
        xln = FM(big.enter_context(nc.sbuf_tensor("xln", [128, 8 * NT], BF16)),
                 1024, NT)
        mo = FM(big.enter_context(nc.sbuf_tensor("mo", [128, 8 * NT], BF16)),
                1024, NT)

        with contextlib.ExitStack() as st0:
            x0 = FM(st0.enter_context(nc.sbuf_tensor("x0", [128, 8 * NT], F32)),
                    1024, NT)
            load_fm(l, x0, x_in)
            do_ln(l, x0, xln, C=NT)
            l.phase_barrier()

        with contextlib.ExitStack() as st1:
            sb1 = lambda n, s_, d: st1.enter_context(nc.sbuf_tensor(n, s_, d))
            delta_sb = sb1("deltasb", [64, S], BF16)
            bt_sb = sb1("btsb", [64, S], BF16)
            h_a = sb1("ha", [64, S + 1], F32)
            h_b = sb1("hbb", [64, S + 1], F32)
            abig_sb = sb1("abig", [64, 64], BF16)
            i64_sb = sb1("i64", [64, 64], BF16)
            h_bf = sb1("hbf", [64, 512], BF16)
            cmask_sb = sb1("cmask", [64, B * 8 * 128], BF16)
            gate_fm = FM(sb1("gate", [128, 8 * NT], BF16), 1024, NT)
            l.scan_t1b = sb1("t1b", [64, 8 * 512], BF16)
            l.inc(nc.sync.dma_start(out=delta_sb[:], in_=delta_in[:]), l.LD, 16)
            l.inc(nc.sync.dma_start(out=bt_sb[:], in_=bt_in[:]), l.LD, 16)
            l.inc(nc.gpsimd.dma_start(out=abig_sb[:], in_=abig_in[:]), l.WD, 16)
            l.inc(nc.sync.dma_start(out=i64_sb[:], in_=i64_in[:]), l.LD, 16)
            l.inc(nc.gpsimd.dma_start(out=cmask_sb[:], in_=cmask_in[:]), l.WD, 16)
            load_fm(l, gate_fm, gate_in)
            if upto >= 1:
                emit_scan(l, delta_sb, bt_sb, h_a, h_b, abig_sb[:], i64_sb[:], h_bf)
                for b in range(B):
                    for ot in range(8):
                        bi = l.nb()
                        l.acquire(bi)
                        l.waitE(nc.tensor)
                        l.wait(nc.tensor, l.WD)
                        l.inc(nc.tensor.matmul(
                            l.bank(bi)[0:128, 0:512],
                            cmask_sb[:, (b * 8 + ot) * 128:(b * 8 + ot + 1) * 128],
                            h_bf[:], start=True, stop=True), l.TD)
                        l.written(bi)
                        l.consume(nc.vector, bi)
                        l.waitE(nc.vector)
                        l.inc(nc.vector.tensor_tensor(
                            mo.t(ot, b * SL, SL), l.bank(bi)[0:128, 0:512],
                            gate_fm.t(ot, b * SL, SL), op=ALU.mult), l.EV)
                        l.inc(nc.vector.tensor_tensor(
                            mo.t(ot, b * SL, SL), mo.t(ot, b * SL, SL),
                            xln.t(ot, b * SL, SL), op=ALU.add), l.EV)
                        l.freed(bi)
            l.phase_barrier()

        with contextlib.ExitStack() as st3:
            ctxq = FM(st3.enter_context(nc.sbuf_tensor("ctxq", [128, 8 * NT], BF16)),
                      1024, NT)
            load_fm(l, ctxq, q1_in)
            if upto >= 2:
                emit_l1a_attn(l, ctxq, kt1_all, vt1h_all, scale=1.0 / (128 ** 0.5))
                wso = WStream(l, "wso1", 1024, stack=st3)

                def x2_evac(ot, c0, w, tag, ps, bi):
                    l.consume(nc.vector, bi)
                    l.waitE(nc.vector)
                    l.inc(nc.vector.tensor_tensor(xln.t(ot, c0, w), xln.t(ot, c0, w),
                                                  ps, op=ALU.add), l.EV)
                    l.freed(bi)
                    l.waitE(nc.scalar)
                    l.inc(nc.scalar.add(xln.t(ot, c0, w), xln.t(ot, c0, w),
                                        l.c_bo1[:, ot:ot + 1]), l.ES)
                emit_project(l, hb["wo1"], wso, ctxq, 8, MAIN4, x2_evac)
            l.phase_barrier()

        if upto >= 3:
            with contextlib.ExitStack() as st5:
                sb5 = lambda n, s_, d: st5.enter_context(nc.sbuf_tensor(n, s_, d))
                fused = FM(sb5("fusedsb", [128, 8 * 512], BF16), 1024, 512)
                xn2 = FM(sb5("xn2", [128, 8 * 512], BF16), 1024, 512)
                t1f = FM(sb5("t1f", [128, 32 * 512], BF16), 4096, 512)
                tout = FM(sb5("tout", [128, 8 * 512], BF16), 1024, 512)
                fpre = FM(sb5("fpre", [128, 8 * 512], BF16), 1024, 512)
                ws_w1 = WStream(l, "wsw1", 1024, stack=st5)
                ws_w2 = WStream(l, "wsw2", 4096, stack=st5)
                ws_fa = WStream(l, "wsfa", 1024, stack=st5)
                ws_fb = WStream(l, "wsfb", 1024, stack=st5)
                for nt in range(4):
                    c0 = nt * 512
                    do_ln(l, xln, xn2, C=512, g_fm=l.c_ln2g, b_fm=l.c_ln2b,
                          window=(c0, 512))

                    def t1_evac(ot, cc0, w, tag, ps, bi):
                        l.consume(nc.scalar, bi)
                        l.waitE(nc.scalar)
                        l.inc(nc.scalar.activation(t1f.t(ot, 0, 512), ps, AF.Relu,
                                                   bias=b1fm[:, ot:ot + 1]), l.ES)
                        l.freed(bi)
                    emit_project(l, hb["w1"], ws_w1, xn2, 32, [(0, 512, 0)], t1_evac)

                    def t2_evac(ot, cc0, w, tag, ps, bi):
                        l.consume(nc.scalar, bi)
                        l.waitE(nc.scalar)
                        l.inc(nc.scalar.activation(tout.t(ot, 0, 512), ps,
                                                   AF.Identity,
                                                   bias=l.c_b2[:, ot:ot + 1]), l.ES)
                        l.freed(bi)
                        l.waitE(nc.vector)
                        l.inc(nc.vector.tensor_tensor(tout.t(ot, 0, 512),
                                                      tout.t(ot, 0, 512),
                                                      xln.t(ot, c0, 512),
                                                      op=ALU.add), l.EV)
                    emit_project(l, hb["w2"], ws_w2, t1f, 8, [(0, 512, 0)], t2_evac)

                    sfa = ws_fa.issue(hb["wfa"], 0)
                    sfb = ws_fb.issue(hb["wfb"], 0)
                    for ot in range(8):
                        nfa = ws_fa.issue(hb["wfa"], ot + 1) if ot < 7 else None
                        nfb = ws_fb.issue(hb["wfb"], ot + 1) if ot < 7 else None
                        ws_fa.wait_ready(sfa)
                        ws_fb.wait_ready(sfb)
                        bi = l.nb()
                        l.acquire(bi)
                        l.waitE(nc.tensor)
                        for pt in range(8):
                            nc.tensor.matmul(l.bank(bi)[0:128, 0:512],
                                             ws_fa.chunk(sfa, pt),
                                             mo.t(pt, c0, 512),
                                             start=(pt == 0), stop=False)
                        last = None
                        for pt in range(8):
                            last = nc.tensor.matmul(l.bank(bi)[0:128, 0:512],
                                                    ws_fb.chunk(sfb, pt),
                                                    tout.t(pt, 0, 512),
                                                    start=False, stop=(pt == 7))
                        l.inc(last, l.TD)
                        l.written(bi)
                        ws_fa.use(sfa)
                        ws_fb.use(sfb)
                        l.consume(nc.scalar, bi)
                        l.waitE(nc.scalar)
                        l.inc(nc.scalar.activation(fpre.t(ot, 0, 512), ps_ := l.bank(bi)[0:128, 0:512],
                                                   AF.Identity,
                                                   bias=l.c_bfw[:, ot:ot + 1]), l.ES)
                        l.freed(bi)
                        sfa, sfb = nfa, nfb
                    do_ln_to(l, fpre, fused, out_c0=0, g_fm=l.c_lnfg, b_fm=l.c_lnfb)
                    l.waitE(nc.sync)
                    for p in range(8):
                        l.inc(nc.sync.dma_start(
                            out=fused_d[p * 128:(p + 1) * 128, c0:c0 + 512],
                            in_=fused.t(p)), l.ST, 16)
        big.close()
        l.phase_barrier()

        with contextlib.ExitStack() as st6:
            x6 = FM(st6.enter_context(nc.sbuf_tensor("x6", [128, 8 * NT], F32)),
                    1024, NT)
            xnb = FM(st6.enter_context(nc.sbuf_tensor("xnb", [128, 8 * NT], BF16)),
                     1024, NT)
            fscr = st6.enter_context(nc.sbuf_tensor("fscr", [128, NT], BF16))
            load_fm(l, x6, x_in)
            if upto >= 3:
                for ot in range(8):
                    l.wait(nc.sync, l.ST)
                    l.waitE(nc.sync)
                    l.inc(nc.sync.dma_start(out=fscr[:],
                                            in_=fused_d[ot * 128:(ot + 1) * 128, :]),
                          l.LD, 16)
                    l.wait(nc.vector, l.LD)
                    l.inc(nc.vector.tensor_tensor(x6.t(ot), x6.t(ot), fscr[:],
                                                  op=ALU.add), l.EV)
            if upto >= 4:
                do_ln(l, x6, xnb, C=NT)
                emit_l1ffn(l, x6, xnb, g1t_in, g2t_in, indd_in, indg_in, st6)
            if upto >= 5:
                do_ln(l, x6, xnb, C=NT)
                ws7 = WStream(l, "ws7", 1024, stack=st6)
                emit_project_store(l, hb["wqkv2"], ws7, xnb, 24, MAIN4,
                                   None, bias_fm=l.c_bqkv2, stage=stage,
                                   hbm_outs=(q2_o, k2_o, v2_o))
            l.waitE(nc.sync)
            l.wait(nc.sync, l.TD)
            for p in range(8):
                l.inc(nc.sync.dma_start(out=xres_o[p * 128:(p + 1) * 128, :],
                                        in_=x6.t(p, 0, NT)), l.ST, 16)
            l.wait(nc.sync, l.ST)
    return nc, l


def do_ln_to(l, xsrc, xdst, out_c0, g_fm=None, b_fm=None):
    view = FM(xdst.sb, xdst.rows, xdst.C)
    class _V:
        np_ = xdst.np_
        C = xdst.C

        def t(self, p, c0=0, w=None):
            w = 512 - c0 if w is None else w
            return xdst.sb[:, p * xdst.C + out_c0 + c0:
                           p * xdst.C + out_c0 + c0 + w]
    do_ln(l, xsrc, _V(), C=512, g_fm=g_fm, b_fm=b_fm)


def emit_l1ffn(l, x, xn, g1t_in, g2t_in, indd_in, indg_in, stack):
    nc = l.nc
    l.phase_barrier()
    sb = lambda n, s_, d: stack.enter_context(nc.sbuf_tensor(n, s_, d))
    g1t = sb("g1t", [32, 256], BF16)
    g2t = sb("g2t", [16, 128], BF16)
    indd = sb("indd", [32, 4 * 128], BF16)
    indg = sb("indg", [16, 128], BF16)
    g1rep = sb("g1rep", [32, 512], BF16)
    g2rep = sb("g2rep", [16, 512], BF16)
    wacc = sb("wacc", [128, 512], F32)
    wtmp = sb("wtmp", [128, 512], F32)
    a_sb = sb("a_sb", [128, 512], BF16)
    wrT = sb("wrT", [128, 2048], BF16)
    wiT = sb("wiT", [128, 2048], BF16)
    wnT = sb("wnT", [128, 2048], BF16)
    stg = sb("l1fstg", [128, 512], BF16)
    l.inc(nc.gpsimd.dma_start(out=g1t[:], in_=g1t_in[:]), l.WD, 16)
    l.inc(nc.gpsimd.dma_start(out=g2t[:], in_=g2t_in[:]), l.WD, 16)
    l.inc(nc.sync.dma_start(out=indd[:], in_=indd_in[:]), l.LD, 16)
    l.inc(nc.sync.dma_start(out=indg[:], in_=indg_in[:]), l.LD, 16)
    l.phase_barrier()
    for wi_ in range(2):
        dst = (wrT, wiT)[wi_]
        for pt in range(4):
            for c in range(4):
                l.waitE(nc.vector)
                l.wait(nc.vector, l.WD)
                l.wait(nc.vector, l.TD)
                l.inc(nc.vector.tensor_copy(
                    g1rep[:].rearrange("p (b e) -> p b e", e=16),
                    g1t[:, (wi_ * 4 + c) * 32:(wi_ * 4 + c) * 32 + 32]
                    .unsqueeze(2).broadcast_to([32, 32, 16])), l.EV)
                l.inc(nc.vector.tensor_copy(
                    g2rep[:].rearrange("p (b e) -> p b e", e=16),
                    g2t[:, (wi_ * 4 + c) * 16:(wi_ * 4 + c) * 16 + 16]
                    .unsqueeze(1).broadcast_to([16, 32, 16])), l.EV)
                bA, bB = l.nb(), l.nb()
                l.acquire(bA)
                l.acquire(bB)
                l.waitE(nc.tensor)
                l.inc(nc.tensor.matmul(l.bank(bA)[0:128, 0:512],
                                       indd[:, pt * 128:(pt + 1) * 128],
                                       g1rep[:], start=True, stop=True), l.TD)
                l.inc(nc.tensor.matmul(l.bank(bB)[0:128, 0:512],
                                       indg[:], g2rep[:], start=True, stop=True),
                      l.TD)
                l.written(bA)
                l.written(bB)
                l.consume(nc.scalar, bA)
                l.inc(nc.scalar.copy(a_sb[:], l.bank(bA)[0:128, 0:512]), l.ES)
                l.freed(bA)
                l.consume(nc.vector, bB)
                l.waitE(nc.vector)
                l.inc(nc.vector.tensor_tensor(wtmp[:], a_sb[:],
                                              l.bank(bB)[0:128, 0:512],
                                              op=ALU.mult), l.EV)
                l.freed(bB)
                if c == 0:
                    l.inc(nc.vector.tensor_copy(wacc[:], wtmp[:]), l.EV)
                else:
                    l.inc(nc.vector.tensor_tensor(wacc[:], wacc[:], wtmp[:],
                                                  op=ALU.add), l.EV)
            l.waitE(nc.scalar)
            l.inc(nc.scalar.copy(dst[:, pt * 512:(pt + 1) * 512], wacc[:]), l.ES)
            if wi_ == 1:
                l.inc(nc.scalar.activation(wnT[:, pt * 512:(pt + 1) * 512],
                                           wacc[:], AF.Identity, scale=-1.0), l.ES)
    def chunk(buf, pt, ot):
        return buf[:, pt * 512 + ot * 128: pt * 512 + ot * 128 + 128]
    for half in range(2):
        for ot in range(4):
            for nt in range(4):
                c0 = nt * 512
                bi = l.nb()
                l.acquire(bi)
                l.waitE(nc.tensor)
                last = None
                lA = (wrT, wiT)[half]
                lB = (wnT, wrT)[half]
                for pt in range(4):
                    nc.tensor.matmul(l.bank(bi)[0:128, 0:512], chunk(lA, pt, ot),
                                     xn.t(pt, c0, 512), start=(pt == 0), stop=False)
                for pt in range(4):
                    last = nc.tensor.matmul(l.bank(bi)[0:128, 0:512],
                                            chunk(lB, pt, ot),
                                            xn.t(4 + pt, c0, 512),
                                            start=False, stop=(pt == 3))
                l.inc(last, l.TD)
                l.written(bi)
                l.consume(nc.scalar, bi)
                l.wait(nc.scalar, l.ST)
                l.inc(nc.scalar.activation(stg[:], l.bank(bi)[0:128, 0:512],
                                           AF.Identity,
                                           bias=l.c_bri[:, half * 4 + ot:
                                                        half * 4 + ot + 1]), l.ES)
                l.freed(bi)
                l.waitE(nc.vector)
                l.inc(nc.vector.tensor_tensor(x.t(half * 4 + ot, c0, 512),
                                              x.t(half * 4 + ot, c0, 512),
                                              stg[:], op=ALU.add), l.EV)


def prep_launch3(inp, l2outs):
    base = {}
    A = inp["L1_attn_a_A"].astype(np.float32)
    abig = np.zeros((64, 64), np.float32)
    for b in range(B):
        abig[b * 16:(b + 1) * 16, b * 16:(b + 1) * 16] = A
    base["abig_in"] = abig
    base["i64_in"] = np.eye(64, dtype=BF16NP)
    C_ = inp["L1_attn_a_C"].astype(np.float32)
    cm = np.zeros((64, B * 8 * 128), np.float32)
    for b in range(B):
        for ot in range(8):
            cm[b * 16:(b + 1) * 16, (b * 8 + ot) * 128:(b * 8 + ot + 1) * 128] = \
                C_[:, ot * 128:(ot + 1) * 128]
    base["cmask_in"] = cm
    base["wo1"] = _tile_w(inp["L1_attn_a_Wo"])
    base["w1"] = _tile_w(inp["L1_attn_a_W1"])
    base["w2"] = _tile_w(inp["L1_attn_a_W2"])
    base["wfa"] = _tile_w(inp["L1_attn_a_Wf"][0:1024])
    base["wfb"] = _tile_w(inp["L1_attn_a_Wf"][1024:2048])
    base["wqkv2"] = _tile_w(inp["L1_attn_b_w_qkv"].T)
    g1 = np.zeros((32, 256), np.float32)
    g2 = np.zeros((16, 128), np.float32)
    for wi_, (k1, k2) in enumerate((("L1_ffn_G1_R", "L1_ffn_G2_R"),
                                    ("L1_ffn_G1_I", "L1_ffn_G2_I"))):
        G1 = inp[k1].astype(np.float32)[0]
        G2 = inp[k2].astype(np.float32)
        for c in range(4):
            g1[:, wi_ * 128 + c * 32: wi_ * 128 + (c + 1) * 32] = G1[:, c, :].T
            g2[:, wi_ * 64 + c * 16: wi_ * 64 + (c + 1) * 16] = G2[c, :, 0, :].T
    base["g1t_in"] = g1
    base["g2t_in"] = g2
    indd = np.zeros((32, 4 * 128), BF16NP)
    for p in range(4):
        for loc in range(128):
            indd[8 * p + loc // 16, p * 128 + loc] = 1
    base["indd_in"] = indd
    indg = np.zeros((16, 128), BF16NP)
    for loc in range(128):
        indg[loc % 16, loc] = 1
    base["indg_in"] = indg
    base["fconsts"] = np.tile(np.array([FCONST_VALS], np.float32), (128, 1))
    c4 = np.zeros((128, 96), np.float32)
    c4[:, 0:8] = _fmcols(inp["L1_attn_a_ln2_g"])
    c4[:, 8:16] = _fmcols(inp["L1_attn_a_ln2_b"])
    c4[:, 16:24] = _fmcols(inp["L1_attn_a_lnf_g"])
    c4[:, 24:32] = _fmcols(inp["L1_attn_a_lnf_b"])
    c4[:, 32:40] = _fmcols(inp["L1_attn_a_bo"])
    c4[:, 40:48] = _fmcols(inp["L1_attn_a_b2"])
    c4[:, 48:56] = _fmcols(inp["L1_attn_a_bf"])
    c4[:, 56:80] = np.ascontiguousarray(
        inp["L1_attn_b_b_qkv"].astype(np.float32).reshape(24, 128).T)
    c4[:, 80:84] = np.ascontiguousarray(
        inp["L1_ffn_bias_real"].astype(np.float32).reshape(4, 128).T)
    c4[:, 84:88] = np.ascontiguousarray(
        inp["L1_ffn_bias_imag"].astype(np.float32).reshape(4, 128).T)
    base["consts4"] = c4
    base["b1fm"] = np.ascontiguousarray(
        inp["L1_attn_a_b1"].astype(np.float32).reshape(32, 128).T)
    ob = np.zeros((128, 4), BF16NP)
    ob[:, 0] = 1
    base["onesb"] = ob

    delta_g = np.zeros((64, S), BF16NP)
    bt_g = np.zeros((64, S), BF16NP)
    kt1_all = np.zeros((B, 1024, S), BF16NP)
    vt1h_flat = np.zeros((B, 8, S, 128), BF16NP)
    for r in range(NCORES):
        d_r, b_r = l2outs[r]["delta"], l2outs[r]["bt"]
        k_r, v_r = l2outs[r]["k1"], l2outs[r]["v1"]
        for b in range(B):
            delta_g[b * 16:(b + 1) * 16, r * SL:(r + 1) * SL] = d_r[:, b * SL:(b + 1) * SL]
            bt_g[b * 16:(b + 1) * 16, r * SL:(r + 1) * SL] = b_r[:, b * SL:(b + 1) * SL]
            kt1_all[b][:, r * SL:(r + 1) * SL] = k_r[:, b * SL:(b + 1) * SL]
            for h in range(8):
                vt1h_flat[b, h, r * SL:(r + 1) * SL, :] = \
                    v_r[h * 128:(h + 1) * 128, b * SL:(b + 1) * SL].T
    base["kt1_all"] = kt1_all
    base["vt1h_all"] = np.ascontiguousarray(
        vt1h_flat.reshape(B, 8, S // 128, 128, 128).transpose(0, 1, 3, 2, 4)
        .reshape(B, 8, 128, (S // 128) * 128))

    in_maps = []
    for r in range(NCORES):
        m = dict(base)
        dr = np.roll(delta_g, -r * SL, axis=1).copy()
        br = np.roll(bt_g, -r * SL, axis=1).copy()
        cut = (S - r * SL) % S
        dr[:, cut] = 0
        m["delta_in"] = dr
        m["bt_in"] = br
        m["x_in"] = l2outs[r]["xres"]
        m["gate_in"] = l2outs[r]["gate"]
        m["q1_in"] = l2outs[r]["q1"]
        in_maps.append(m)
    return in_maps


def emit_l1ab(l, ctxq, kt2h, vt2h, qext_sb, btab_sb, rtri_sb, kext_dram):
    nc = l.nc
    l.phase_barrier()
    NKT = S // 128
    with contextlib.ExitStack() as st:
        sb = lambda n, s_, d: st.enter_context(nc.sbuf_tensor(n, s_, d))
        kslot = [sb(f"l2k{i}", [67, S], BF16) for i in range(2)]
        vslot = [sb(f"l2v{i}", [128, 32 * 65], BF16) for i in range(2)]
        esb = sb("l2e", [128, 2 * 512], BF16)
        avs = sb("l2av", [64, 512], BF16)
        dinvb = sb("l2dinv", [1, 512], BF16)
        drep = sb("l2drep", [64, 512], F32)
        qtmp = sb("l2qt", [67, 512], BF16)
        slot_user = [-1, -1]
        for i in range(2):
            l.inc(nc.gpsimd.dma_start(
                out=kslot[i][64:67, :].rearrange("p (a o) -> p a o", o=128),
                in_=kext_dram.unsqueeze(1).broadcast_to([3, 32, 128])), l.WD, 16)
        l.waitE(nc.scalar)
        l.wait(nc.scalar, l.LD)
        l.inc(nc.scalar.copy(qtmp[64:67, :], qext_sb), l.ES)

        def issue(idx, b, h):
            s_ = idx % 2
            l.wait(nc.gpsimd, l.TD,
                   slot_user[s_] if slot_user[s_] >= 0 else l.val(l.TD))
            l.inc(nc.gpsimd.dma_start(out=kslot[s_][0:64, :], in_=kt2h[b, h]),
                  l.WD, 16)
            l.inc(nc.gpsimd.dma_start(out=vslot[s_][:], in_=vt2h[b, h]), l.WD, 16)
            return s_, l.val(l.WD)

        eslu = [0, 0]
        exp_es = [0] * NKT
        pairs = [(b, h) for b in range(B) for h in range(16)]
        cur = issue(0, *pairs[0])
        for pi, (b, h) in enumerate(pairs):
            nxt = issue(pi + 1, *pairs[pi + 1]) if pi + 1 < len(pairs) else None
            s_, rdy = cur
            l.wait(nc.tensor, l.WD, rdy)
            l.acquire(2)
            l.waitE(nc.scalar)
            l.wait(nc.scalar, l.TD)
            l.inc(nc.scalar.copy(qtmp[0:64, :], ctxq.rs(h // 2, 64 * (h % 2), 64,
                                                        b * SL, SL)), l.ES)
            q_es = l.val(l.ES)

            def scores(kt):
                sbk = kt % 2
                _es, _ev = l.bank_free[sbk]
                l.waitE(nc.tensor, max(_es, q_es), _ev)
                l.inc(nc.tensor.matmul(l.bank(sbk)[0:128, 0:512],
                                       kslot[s_][0:67, kt * 128:(kt + 1) * 128],
                                       qtmp[:], start=True, stop=True), l.TD)
                l.written(sbk)

            def exp(kt):
                sbk = kt % 2
                l.consume(nc.scalar, sbk)
                l.wait(nc.scalar, l.TD, eslu[kt % 2])
                if kt < 4:
                    l.consume(nc.vector, sbk)
                    l.wait(nc.vector, l.TD, eslu[kt % 2])
                    l.inc(nc.vector.scalar_tensor_tensor(
                        esb[:, (kt % 2) * 512:(kt % 2 + 1) * 512],
                        l.bank(sbk)[0:128, 0:512], 1.0 / 8.0,
                        rtri_sb[:, kt * 512:(kt + 1) * 512],
                        op0=ALU.mult, op1=ALU.add), l.EV)
                    l.wait(nc.scalar, l.EV)
                    l.inc(nc.scalar.activation(
                        esb[:, (kt % 2) * 512:(kt % 2 + 1) * 512],
                        esb[:, (kt % 2) * 512:(kt % 2 + 1) * 512], AF.Exp), l.ES)
                else:
                    l.inc(nc.scalar.activation(
                        esb[:, (kt % 2) * 512:(kt % 2 + 1) * 512],
                        l.bank(sbk)[0:128, 0:512], AF.Exp, scale=1.0 / 8.0,
                        bias=btab_sb[:, kt:kt + 1]), l.ES)
                l.freed(sbk)
                exp_es[kt] = l.val(l.ES)

            def av(kt):
                l.wait(nc.tensor, l.ES, exp_es[kt])
                l.inc(nc.tensor.matmul(l.bank(2)[0:65, 0:512],
                                       vslot[s_][:, kt * 65:(kt + 1) * 65],
                                       esb[:, (kt % 2) * 512:(kt % 2 + 1) * 512],
                                       start=(kt == 0), stop=(kt == NKT - 1)),
                      l.TD)
                eslu[kt % 2] = l.val(l.TD)
                slot_user[s_] = l.val(l.TD)

            scores(0)
            exp(0)
            for kt in range(NKT):
                if kt + 1 < NKT:
                    scores(kt + 1)
                av(kt)
                if kt + 1 < NKT:
                    exp(kt + 1)
            l.written(2)
            l.consume(nc.vector, 2)
            l.waitE(nc.vector)
            l.inc(nc.vector.tensor_copy(dinvb[:], l.bank(2)[64:65, 0:512]), l.EV)
            l.inc(nc.vector.tensor_copy(avs[:], l.bank(2)[0:64, 0:512]), l.EV)
            l.freed(2)
            ep_ev = l.val(l.EV)
            l.acquire(3)
            l.wait(nc.tensor, l.EV, ep_ev)
            l.inc(nc.tensor.matmul(l.bank(3)[0:64, 0:512], l.c_ones1_64,
                                   dinvb[:], start=True, stop=True), l.TD)
            l.written(3)
            l.consume(nc.vector, 3)
            l.inc(nc.vector.reciprocal(drep[:, 0:512], l.bank(3)[0:64, 0:512]),
                  l.EV)
            l.freed(3)
            l.inc(nc.vector.tensor_tensor(
                ctxq.rs(h // 2, 64 * (h % 2), 64, b * SL, SL), avs[:],
                drep[:, 0:512], op=ALU.mult), l.EV)
            cur = nxt


def build_launch4(upto=99):
    nc = bass.Bass("TRN2", target_bir_lowering=False)
    ctx = contextlib.ExitStack()
    with ctx:
        l = L(nc, ctx)
        x_in = l.din("x_in", [1024, NT], F32)
        q2_in = l.din("q2_in", [1024, NT], BF16)
        kt2h_all = l.din("kt2h_all", [B, 16, 64, S], BF16)
        kext_in = l.din("kext_in", [3, 128], BF16)
        vt2h_all = l.din("vt2h_all", [B, 16, 128, (S // 128) * 65], BF16)
        qext_in = l.din("qext_in", [3, 512], BF16)
        btab_in = l.din("btab_in", [128, 32], F32)
        rtri_in = l.din("rtri_in", [128, 4 * 512], F32)
        hb = {}
        hb["wout2"] = l.din("wout2", [8, 128, 1024], F32)
        g1t_in = l.din("g1t_in", [32, 2 * 4 * 32], F32)
        g2t_in = l.din("g2t_in", [16, 2 * 4 * 16], F32)
        indd_in = l.din("indd_in", [32, 4 * 128], BF16)
        indg_in = l.din("indg_in", [16, 128], BF16)
        fconsts = l.din("fconsts", [128, 8], F32)
        consts5 = l.din("consts5", [128, 24], F32)
        onesb = l.din("onesb", [128, 4], BF16)

        xres_o = l.dout("xres", [1024, NT], F32)

        fcs = l.sbuf("fcs", [128, 8], F32)
        l.inc(nc.sync.dma_start(out=fcs[:], in_=fconsts[:]), l.LD, 16)
        for i, v in enumerate(FCONST_VALS):
            nc.const_aps.register(F32, float(v), fcs[:, i:i + 1])
        cst5 = l.sbuf("cst5", [128, 24], F32)
        onesb_sb = l.sbuf("onesbsb", [128, 4], BF16)
        rowones = l.sbuf("rowones", [1, 128], BF16)
        qext_sb = l.sbuf("qext", [3, 512], BF16)
        kext_sb = l.sbuf("kext", [3, 128], BF16)
        btab_sb = l.sbuf("btab", [128, 32], F32)
        rtri_sb = l.sbuf("rtri", [128, 4 * 512], F32)
        l.inc(nc.sync.dma_start(out=cst5[:], in_=consts5[:]), l.LD, 16)
        l.inc(nc.sync.dma_start(out=onesb_sb[:], in_=onesb[:]), l.LD, 16)
        l.inc(nc.sync.dma_start(out=qext_sb[:], in_=qext_in[:]), l.LD, 16)
        l.inc(nc.sync.dma_start(out=kext_sb[:], in_=kext_in[:]), l.LD, 16)
        l.inc(nc.sync.dma_start(out=btab_sb[:], in_=btab_in[:]), l.LD, 16)
        l.inc(nc.sync.dma_start(out=rtri_sb[:], in_=rtri_in[:]), l.LD, 16)
        l.inc(nc.gpsimd.memset(rowones[:], 1.0), l.WD, 1)
        l.c_ones128_b = onesb_sb[:, 0:1]
        l.c_ones1_128 = rowones[0:1, 0:128]
        l.c_ones1_64 = rowones[0:1, 0:64]
        l.c_bout = cst5[:, 0:8]
        l.c_bri = cst5[:, 8:16]
        for eng in (nc.tensor, nc.scalar, nc.vector):
            l.wait(eng, l.LD)
            l.wait(eng, l.WD)

        x = alloc_fm(l, "x", 1024, NT, F32)
        xn = alloc_fm(l, "xn", 1024, NT, BF16)
        load_fm(l, x, x_in)

        with contextlib.ExitStack() as st7:
            ctxq = FM(st7.enter_context(nc.sbuf_tensor("ctxq", [128, 8 * NT], BF16)),
                      1024, NT)
            load_fm(l, ctxq, q2_in)
            if upto >= 1:
                emit_l1ab(l, ctxq, kt2h_all, vt2h_all, qext_sb[:], btab_sb, rtri_sb,
                          kext_in)
                wso = WStream(l, "wso2", 1024, stack=st7)

                def out_evac(ot, c0, w, tag, ps, bi):
                    l.consume(nc.scalar, bi)
                    l.waitE(nc.scalar)
                    l.inc(nc.scalar.activation(l.stage4[:, 0:w], ps, AF.Identity,
                                               bias=l.c_bout[:, ot:ot + 1]), l.ES)
                    l.freed(bi)
                    l.waitE(nc.vector)
                    l.inc(nc.vector.tensor_tensor(x.t(ot, c0, w), x.t(ot, c0, w),
                                                  l.stage4[:, 0:w], op=ALU.add),
                          l.EV)
                l.stage4 = st7.enter_context(nc.sbuf_tensor("stg4", [128, 512], BF16))
                emit_project(l, hb["wout2"], wso, ctxq, 8, MAIN4, out_evac)
            l.phase_barrier()

        if upto >= 2:
            with contextlib.ExitStack() as st8:
                do_ln(l, x, xn, C=NT)
                emit_l1ffn(l, x, xn, g1t_in, g2t_in, indd_in, indg_in, st8)

        l.waitE(nc.sync)
        l.wait(nc.sync, l.TD)
        for p in range(8):
            l.inc(nc.sync.dma_start(out=xres_o[p * 128:(p + 1) * 128, :],
                                    in_=x.t(p, 0, NT)), l.ST, 16)
        l.wait(nc.sync, l.ST)
    return nc, l


def prep_launch4(inp, l3outs):
    base = {}
    base["wout2"] = _tile_w(inp["L1_attn_b_w_out"].T)
    base["fconsts"] = np.tile(np.array([FCONST_VALS], np.float32), (128, 1))
    c5 = np.zeros((128, 24), np.float32)
    c5[:, 0:8] = _fmcols(inp["L1_attn_b_b_out"])
    c5[:, 8:12] = np.ascontiguousarray(
        inp["L1_ffn_bias_real"].astype(np.float32).reshape(4, 128).T)
    c5[:, 12:16] = np.ascontiguousarray(
        inp["L1_ffn_bias_imag"].astype(np.float32).reshape(4, 128).T)
    base["consts5"] = c5
    g1 = np.zeros((32, 256), np.float32)
    g2 = np.zeros((16, 128), np.float32)
    for wi_, (k1, k2) in enumerate((("L1_ffn_G1_R", "L1_ffn_G2_R"),
                                    ("L1_ffn_G1_I", "L1_ffn_G2_I"))):
        G1 = inp[k1].astype(np.float32)[0]
        G2 = inp[k2].astype(np.float32)
        for c in range(4):
            g1[:, wi_ * 128 + c * 32: wi_ * 128 + (c + 1) * 32] = G1[:, c, :].T
            g2[:, wi_ * 64 + c * 16: wi_ * 64 + (c + 1) * 16] = G2[c, :, 0, :].T
    base["g1t_in"] = g1
    base["g2t_in"] = g2
    indd = np.zeros((32, 4 * 128), BF16NP)
    for p in range(4):
        for loc in range(128):
            indd[8 * p + loc // 16, p * 128 + loc] = 1
    base["indd_in"] = indd
    indg = np.zeros((16, 128), BF16NP)
    for loc in range(128):
        indg[loc % 16, loc] = 1
    base["indg_in"] = indg
    ob = np.zeros((128, 4), BF16NP)
    ob[:, 0] = 1
    base["onesb"] = ob
    qext = np.zeros((3, 512), np.float32)
    sl = np.arange(512)
    qext[0] = 1.0
    qext[1] = -1024.0 * (sl // 128)
    qext[2] = -8.0 * (sl % 128)
    base["qext_in"] = qext.astype(BF16NP)
    kk = np.arange(128)[:, None]
    qq = np.arange(512)[None, :]
    rtri = np.zeros((128, 4 * 512), np.float32)
    for kt in range(4):
        rtri[:, kt * 512:(kt + 1) * 512] = np.where(128 * kt + kk > qq, NEG,
                                                    128.0 * kt)
    base["rtri_in"] = rtri

    k2g = np.zeros((B, 16, 64, S), BF16NP)
    v2g = np.zeros((B, 16, S, 65), BF16NP)
    for r in range(NCORES):
        k_r, v_r = l3outs[r]["k2"], l3outs[r]["v2"]
        for b in range(B):
            for h in range(16):
                k2g[b, h, 0:64, r * SL:(r + 1) * SL] = \
                    k_r[h * 64:(h + 1) * 64, b * SL:(b + 1) * SL]
                v2g[b, h, r * SL:(r + 1) * SL, 0:64] = \
                    v_r[h * 64:(h + 1) * 64, b * SL:(b + 1) * SL].T
    v2g[:, :, :, 64] = np.asarray(1.0, BF16NP)
    kext = np.zeros((3, 128), np.float32)
    kext[0] = 8.0 * np.arange(128)
    kext[1] = 1.0
    kext[2] = 1.0
    base["kext_in"] = kext.astype(BF16NP)

    in_maps = []
    for r in range(NCORES):
        m = dict(base)
        order = [(r + j) % NCORES for j in range(NCORES)]
        kperm = np.concatenate([k2g[:, :, :, g * SL:(g + 1) * SL] for g in order],
                               axis=3)
        vperm = np.concatenate([v2g[:, :, g * SL:(g + 1) * SL, :] for g in order],
                               axis=2)
        m["kt2h_all"] = np.ascontiguousarray(kperm)
        m["vt2h_all"] = np.ascontiguousarray(
            vperm.reshape(B, 16, S // 128, 128, 65).transpose(0, 1, 3, 2, 4)
            .reshape(B, 16, 128, (S // 128) * 65))
        btab = np.zeros((128, 32), np.float32)
        for j, g in enumerate(order):
            for t4 in range(4):
                kt = j * 4 + t4
                if j == 0:
                    btab[:, kt] = 0.0
                elif g < r:
                    btab[:, kt] = float(128 * (g * 4 + t4) - r * SL)
                else:
                    btab[:, kt] = NEG
        for t4 in range(4):
            btab[:, t4] = float(128 * t4)
        m["btab_in"] = btab
        m["x_in"] = l3outs[r]["xres"]
        m["q2_in"] = l3outs[r]["q2"]
        in_maps.append(m)
    return in_maps


def kernel(**inputs):
    inp = {k: np.asarray(v) for k, v in inputs.items()}

    in_maps1 = prep_launch1(inp)
    nc1, _ = build_launch1()
    r1 = run_bass_kernel_spmd(nc1, in_maps1, core_ids=list(range(NCORES)))
    l1outs = [r1.results[r] for r in range(NCORES)]

    in_maps2 = prep_launch2(inp, l1outs)
    nc2, _ = build_launch2()
    r2 = run_bass_kernel_spmd(nc2, in_maps2, core_ids=list(range(NCORES)))
    l2outs = [r2.results[r] for r in range(NCORES)]

    in_maps3 = prep_launch3(inp, l2outs)
    nc3, _ = build_launch3()
    r3 = run_bass_kernel_spmd(nc3, in_maps3, core_ids=list(range(NCORES)))
    l3outs = [r3.results[r] for r in range(NCORES)]

    in_maps4 = prep_launch4(inp, l3outs)
    nc4, _ = build_launch4()
    r4 = run_bass_kernel_spmd(nc4, in_maps4, core_ids=list(range(NCORES)))

    out = np.zeros((B, S, D), np.float32)
    for r in range(NCORES):
        xr = np.asarray(r4.results[r]["xres"])
        out[:, r * SL:(r + 1) * SL, :] = \
            xr.reshape(D, B, SL).transpose(1, 2, 0)
    return out


# version 09
# speedup vs baseline: 1.4759x; 1.1416x over previous
import contextlib
import numpy as np
import ml_dtypes
import concourse.bass as bass
import concourse.mybir as mybir
from concourse.bass_utils import run_bass_kernel_spmd

F32 = mybir.dt.float32
BF16 = mybir.dt.bfloat16
AF = mybir.ActivationFunctionType
ALU = mybir.AluOpType
AX = mybir.AxisListType
BF16NP = ml_dtypes.bfloat16

NCORES = 8
B = 4
S = 4096
D = 1024
SL = S // NCORES
NT = B * SL
HALO = 16
NTH = NT + B * HALO
PADB = 640
NEG = -1e30
FCONST_VALS = [1e-5, 0.01, 1.0, 0.0, -1e30, 0.5, 2.0, 3.0]


class L:
    def __init__(self, nc, ctx):
        self.nc = nc
        self.ctx = ctx
        self.counts = {}
        self.last_wait = {}
        self.LD = self._sem("LD")
        self.WD = self._sem("WD")
        self.TD = self._sem("TD")
        self.ES = self._sem("ES")
        self.EV = self._sem("EV")
        self.ST = self._sem("ST")
        self.pm = [ctx.enter_context(nc.psum_tensor(f"pm{i}", [128, 512], F32))
                   for i in range(7)]
        self.pmb = ctx.enter_context(nc.psum_tensor("pmb", [128, 1024], BF16))
        self.bank_free = [(0, 0)] * 8
        self.bank_ready = [0] * 8
        self._rr = 0
        self.inputs = {}
        self.outputs = {}

    def _sem(self, name):
        return self.ctx.enter_context(self.nc.semaphore(name))

    def inc(self, instr, sem, n=1):
        instr.then_inc(sem, n)
        self.counts[sem] = self.counts.get(sem, 0) + n
        return self.counts[sem]

    def val(self, sem):
        return self.counts.get(sem, 0)

    def wait(self, eng, sem, value=None):
        v = self.val(sem) if value is None else value
        if v <= 0:
            return
        key = (id(eng), sem)
        if self.last_wait.get(key, -1) >= v:
            return
        eng.wait_ge(sem, v)
        self.last_wait[key] = v

    def waitE(self, eng, es=None, ev=None):
        self.wait(eng, self.ES, es)
        self.wait(eng, self.EV, ev)

    def eval_(self):
        return (self.val(self.ES), self.val(self.EV))

    def phase_barrier(self):
        for eng in (self.nc.scalar, self.nc.vector, self.nc.gpsimd, self.nc.sync):
            self.wait(eng, self.TD)
            self.waitE(eng)
        for eng in (self.nc.tensor, self.nc.scalar, self.nc.vector):
            self.wait(eng, self.LD)
            self.wait(eng, self.WD)

    def bank(self, i):
        return self.pm[i] if i < 7 else self.pmb

    def acquire(self, i):
        es, ev = self.bank_free[i] if isinstance(self.bank_free[i], tuple) else (None, None)
        if es is not None:
            self.waitE(self.nc.tensor, es, ev)

    def written(self, i):
        self.bank_ready[i] = self.val(self.TD)

    def consume(self, eng, i):
        self.wait(eng, self.TD, self.bank_ready[i])

    def freed(self, i):
        self.bank_free[i] = self.eval_()

    def nb(self):
        i = self._rr % 7
        self._rr += 1
        return i

    def din(self, name, shape, dt):
        ap = self.nc.dram_tensor(name, list(shape), dt, kind="ExternalInput").ap()
        self.inputs[name] = ap
        return ap

    def dout(self, name, shape, dt):
        ap = self.nc.dram_tensor(name, list(shape), dt, kind="ExternalOutput").ap()
        self.outputs[name] = ap
        return ap

    def sbuf(self, name, shape, dt):
        return self.ctx.enter_context(self.nc.sbuf_tensor(name, list(shape), dt))


class FM:
    def __init__(self, sb, rows, C):
        self.sb = sb
        self.rows = rows
        self.C = C
        self.np_ = rows // 128

    def t(self, p, c0=0, w=None):
        w = self.C - c0 if w is None else w
        return self.sb[:, p * self.C + c0: p * self.C + c0 + w]

    def rs(self, p, r0, nr, c0=0, w=None):
        w = self.C - c0 if w is None else w
        return self.sb[r0:r0 + nr, p * self.C + c0: p * self.C + c0 + w]


def alloc_fm(l, name, rows, C, dt):
    sb = l.sbuf(name, [128, (rows // 128) * C], dt)
    return FM(sb, rows, C)


def load_fm(l, fmbuf, hbm_ap, cols=None, cast=False):
    nc = l.nc
    C = cols or fmbuf.C
    eng = nc.gpsimd if cast else nc.sync
    sem = l.WD if cast else l.LD
    for p in range(fmbuf.np_):
        l.inc(eng.dma_start(out=fmbuf.t(p, 0, C),
                            in_=hbm_ap[p * 128:(p + 1) * 128, 0:C]), sem, 16)


def store_fm(l, hbm_ap, fmbuf):
    nc = l.nc
    l.waitE(nc.sync)
    l.wait(nc.sync, l.TD)
    for p in range(fmbuf.np_):
        l.inc(nc.sync.dma_start(out=hbm_ap[p * 128:(p + 1) * 128, :],
                                in_=fmbuf.t(p)), l.ST, 16)


class WStream:

    def __init__(self, l, name, din, nslots=2, stack=None, ow=128):
        self.l = l
        self.din = din
        self.ow = ow
        self.npt = din // 128
        alloc = (stack.enter_context if stack is not None else
                 (lambda cm: l.ctx.enter_context(cm)))
        self.slots = [alloc(l.nc.sbuf_tensor(f"{name}{i}", [128, self.npt * ow], BF16))
                      for i in range(nslots)]
        self.nslots = nslots
        self.cur = 0
        self.slot_ready = [0] * nslots
        self.slot_user = [-1] * nslots

    def issue(self, hbm_ap, ot):
        l, nc = self.l, self.l.nc
        s = self.cur % self.nslots
        self.cur += 1
        l.wait(nc.gpsimd, l.TD,
               self.slot_user[s] if self.slot_user[s] >= 0 else l.val(l.TD))
        l.inc(nc.gpsimd.dma_start(out=self.slots[s][:], in_=hbm_ap[ot]), l.WD, 16)
        self.slot_ready[s] = l.val(l.WD)
        return s

    def chunk(self, s, pt, ow=None):
        ow = ow or self.ow
        return self.slots[s][:, pt * self.ow: pt * self.ow + ow]

    def use(self, s):
        self.slot_user[s] = self.l.val(self.l.TD)

    def wait_ready(self, s):
        self.l.wait(self.l.nc.tensor, self.l.WD, self.slot_ready[s])


def emit_ln(l, x, xn, C=None, g_fm=None, b_fm=None, eps=1e-5, consts=None,
            window=None):
    nc = l.nc
    l.phase_barrier()
    C = C or x.C
    cs = consts
    if window is not None:
        nts = [window]
    else:
        nts = [(i * 512, min(512, C - i * 512)) for i in range((C + 511) // 512)]
    NPT = x.np_
    for (c0, w) in nts:
        rof = 0 if window is not None else c0
        bs, bq = l.nb(), l.nb()
        l.acquire(bs)
        l.acquire(bq)
        slot_use = [0, 0, 0, 0]
        for pt in range(NPT):
            sl = pt % 4
            l.wait(nc.scalar, l.TD, slot_use[sl])
            l.inc(nc.scalar.copy(cs["cast_t"][sl][:, 0:w], x.t(pt, c0, w)), l.ES)
            l.inc(nc.scalar.activation(cs["sq_t"][sl][:, 0:w], x.t(pt, c0, w),
                                       AF.Square), l.ES)
            l.waitE(nc.tensor)
            nc.tensor.matmul(l.bank(bs)[0:1, 0:w], cs["ones128_b"],
                             cs["cast_t"][sl][:, 0:w],
                             start=(pt == 0), stop=(pt == NPT - 1))
            l.inc(nc.tensor.matmul(l.bank(bq)[0:1, 0:w], cs["ones128_b"],
                                   cs["sq_t"][sl][:, 0:w],
                                   start=(pt == 0), stop=(pt == NPT - 1)), l.TD)
            slot_use[sl] = l.val(l.TD)
        l.written(bs)
        l.written(bq)
        l.consume(nc.scalar, bs)
        l.inc(nc.scalar.copy(cs["sums"][0:1, rof:rof + w], l.bank(bs)[0:1, 0:w]), l.ES)
        l.inc(nc.scalar.copy(cs["sqsums"][0:1, rof:rof + w], l.bank(bq)[0:1, 0:w]), l.ES)
        l.freed(bs)
        l.freed(bq)
        bm, bv = l.nb(), l.nb()
        l.acquire(bm)
        l.acquire(bv)
        l.waitE(nc.tensor)
        l.inc(nc.tensor.matmul(l.bank(bm)[:, 0:w], cs["ones1_128"],
                               cs["sums"][0:1, rof:rof + w], start=True, stop=True), l.TD)
        l.inc(nc.tensor.matmul(l.bank(bv)[:, 0:w], cs["ones1_128"],
                               cs["sqsums"][0:1, rof:rof + w], start=True, stop=True), l.TD)
        l.written(bm)
        l.written(bv)
        l.consume(nc.scalar, bm)
        l.inc(nc.scalar.activation(cs["t1"][:, 0:w], l.bank(bm)[:, 0:w], AF.Square,
                                   scale=1.0 / D), l.ES)
        l.consume(nc.vector, bv)
        l.waitE(nc.vector)
        l.inc(nc.vector.scalar_tensor_tensor(
            cs["t2"][:, 0:w], l.bank(bv)[:, 0:w], 1.0 / D, cs["t1"][:, 0:w],
            op0=ALU.mult, op1=ALU.subtract), l.EV)
        l.freed(bv)
        l.waitE(nc.scalar)
        l.inc(nc.scalar.activation(cs["t2"][:, 0:w], cs["t2"][:, 0:w], AF.Ln,
                                   bias=float(eps)), l.ES)
        l.inc(nc.scalar.activation(cs["rrep"][:, rof:rof + w], cs["t2"][:, 0:w],
                                   AF.Exp, scale=-0.5), l.ES)
        l.waitE(nc.vector)
        l.inc(nc.vector.scalar_tensor_tensor(
            cs["mrrep"][:, rof:rof + w], l.bank(bm)[:, 0:w], 1.0 / D,
            cs["rrep"][:, rof:rof + w], op0=ALU.mult, op1=ALU.mult), l.EV)
        l.freed(bm)
    for pt in range(NPT):
        for (c0, w) in nts:
            rof = 0 if window is not None else c0
            oc0 = 0 if window is not None else c0
            l.waitE(nc.vector)
            l.inc(nc.vector.tensor_tensor(xn.t(pt, oc0, w), x.t(pt, c0, w),
                                          cs["rrep"][:, rof:rof + w], op=ALU.mult), l.EV)
            l.inc(nc.vector.tensor_tensor(xn.t(pt, oc0, w), xn.t(pt, oc0, w),
                                          cs["mrrep"][:, rof:rof + w],
                                          op=ALU.subtract), l.EV)
            if g_fm is not None:
                l.waitE(nc.scalar)
                l.inc(nc.scalar.mul(xn.t(pt, oc0, w), xn.t(pt, oc0, w),
                                    g_fm[:, pt:pt + 1]), l.ES)
                l.inc(nc.scalar.add(xn.t(pt, oc0, w), xn.t(pt, oc0, w),
                                    b_fm[:, pt:pt + 1]), l.ES)


def emit_project(l, whbm, wstream, xn, douts, n_tiles, out_fn, extra=None,
                 rhs_fn=None):
    nc = l.nc
    npt = wstream.npt
    s = wstream.issue(whbm, 0)
    for ot in range(douts):
        s_next = wstream.issue(whbm, ot + 1) if ot + 1 < douts else None
        wstream.wait_ready(s)
        for (c0, w, tag) in n_tiles:
            bi = l.nb()
            l.acquire(bi)
            ps = l.bank(bi)[0:wstream.ow, 0:w]
            l.waitE(nc.tensor)
            last = None
            for pt in range(npt):
                rhs = rhs_fn(pt, c0, w, tag) if rhs_fn else xn.t(pt, c0, w)
                stop = (pt == npt - 1) and not extra
                last = nc.tensor.matmul(ps, wstream.chunk(s, pt), rhs,
                                        start=(pt == 0), stop=stop)
            if extra:
                for i, (lf, rf) in enumerate(extra):
                    last = nc.tensor.matmul(ps, lf(ot), rf(c0, w, tag),
                                            start=False, stop=(i == len(extra) - 1))
            l.inc(last, l.TD)
            l.written(bi)
            wstream.use(s)
            out_fn(ot, c0, w, tag, ps, bi)
        s = s_next


_LNCTR = [0]


def ln_scratch(l, stack, C):
    nc = l.nc
    _LNCTR[0] += 1
    u = f"_{_LNCTR[0]}"
    cs = {}
    cs["cast_t"] = [stack.enter_context(nc.sbuf_tensor(f"lncast{i}{u}", [128, 512], BF16))
                    for i in range(4)]
    cs["sq_t"] = [stack.enter_context(nc.sbuf_tensor(f"lnsq{i}{u}", [128, 512], BF16))
                  for i in range(4)]
    cs["sums"] = stack.enter_context(nc.sbuf_tensor(f"lnsums{u}", [1, C], BF16))
    cs["sqsums"] = stack.enter_context(nc.sbuf_tensor(f"lnsqsums{u}", [1, C], BF16))
    cs["t1"] = stack.enter_context(nc.sbuf_tensor(f"lnt1{u}", [128, 512], F32))
    cs["t2"] = stack.enter_context(nc.sbuf_tensor(f"lnt2{u}", [128, 512], F32))
    cs["rrep"] = stack.enter_context(nc.sbuf_tensor(f"lnr{u}", [128, C], F32))
    cs["mrrep"] = stack.enter_context(nc.sbuf_tensor(f"lnmr{u}", [128, C], F32))
    return cs


def do_ln(l, x, xn, C=None, g_fm=None, b_fm=None, eps=1e-5, window=None):
    with contextlib.ExitStack() as st:
        cs = ln_scratch(l, st, 512 if window is not None else (C or x.C))
        cs["ones128_b"] = l.c_ones128_b
        cs["ones1_128"] = l.c_ones1_128
        emit_ln(l, x, xn, C=C, g_fm=g_fm, b_fm=b_fm, eps=eps, consts=cs,
                window=window)


MAIN4 = [(b * SL, SL, b) for b in range(B)]
MAIN4H = MAIN4 + [(NT, B * HALO, "h")]


def emit_l0aa(l, x, xn, hb):
    nc = l.nc
    l.phase_barrier()
    with contextlib.ExitStack() as st:
        sb = lambda n, s, d: st.enter_context(nc.sbuf_tensor(n, s, d))
        q = FM(sb("aaq", [128, 8 * NT], BF16), 1024, NT)
        k_pad = FM(sb("aak", [128, 2 * B * PADB], BF16), 256, B * PADB)
        v_pad = FM(sb("aav", [128, 2 * B * PADB], BF16), 256, B * PADB)
        ws = WStream(l, "aaw", 1024, stack=st)
        t_bf = sb("aat", [128, 2 * 512], BF16)
        rr_sb = sb("aarr", [2, 512], BF16)

        def mk_evac(dest, tag_map):
            def fn(ot, c0, w, tag, ps, bi):
                l.consume(nc.scalar, bi)
                dst = tag_map(dest, ot, c0, w, tag)
                l.inc(nc.scalar.copy(dst, ps), l.ES)
                l.freed(bi)
            return fn

        def q_map(dest, ot, c0, w, tag):
            return dest.t(ot, c0, w)
        emit_project(l, hb["aa_wq"], ws, xn, 8, MAIN4, mk_evac(q, q_map))

        def kv_map(dest, ot, c0, w, tag):
            if tag == "h":
                base = dest.sb[:, ot * (B * PADB): (ot + 1) * (B * PADB)]
                return base.rearrange("p (b c) -> p b c", c=PADB)[:, :, 112:128]
            b = tag
            return dest.t(ot, b * PADB + 128, w)

        def kv_evac(dest):
            def fn(ot, c0, w, tag, ps, bi):
                l.consume(nc.scalar, bi)
                if tag == "h":
                    dst = kv_map(dest, ot, c0, w, tag)
                    l.inc(nc.scalar.copy(dst, ps.rearrange("p (b c) -> p b c", c=16)),
                          l.ES)
                else:
                    l.inc(nc.scalar.copy(kv_map(dest, ot, c0, w, tag), ps), l.ES)
                l.freed(bi)
            return fn
        emit_project(l, hb["aa_wk"], ws, xn, 2, MAIN4H, kv_evac(k_pad))
        emit_project(l, hb["aa_wv"], ws, xn, 2, MAIN4H, kv_evac(v_pad))
        for dest in (k_pad, v_pad):
            for ot in range(2):
                ap = dest.sb[:, ot * (B * PADB): (ot + 1) * (B * PADB)].rearrange(
                    "p (b c) -> p b c", c=PADB)[:, :, 0:112]
                l.inc(nc.gpsimd.memset(ap, 0.0), l.WD, 1)

        def rms(dest, rrbuf, tiles, wn_ap):
            for pt in range(dest.np_):
                for (c0, w, tag) in tiles:
                    l.wait(nc.scalar, l.TD)
                    l.waitE(nc.scalar)
                    l.inc(nc.scalar.activation(t_bf[:, 0:w], dest.t(pt, c0, w),
                                               AF.Square), l.ES)
                    bi = l.nb()
                    l.acquire(bi)
                    l.waitE(nc.tensor)
                    l.inc(nc.tensor.matmul(l.bank(bi)[0:2, 0:w], l.c_ind128_2,
                                           t_bf[:, 0:w], start=True, stop=True), l.TD)
                    l.written(bi)
                    l.consume(nc.scalar, bi)
                    l.inc(nc.scalar.activation(l.bank(bi)[0:2, 0:w],
                                               l.bank(bi)[0:2, 0:w], AF.Ln,
                                               scale=1.0 / 64, bias=0.01), l.ES)
                    l.inc(nc.scalar.activation(rrbuf[0:2, 0:w],
                                               l.bank(bi)[0:2, 0:w], AF.Exp,
                                               scale=-0.5), l.ES)
                    l.freed(bi)
                    bi2 = l.nb()
                    l.acquire(bi2)
                    l.waitE(nc.tensor)
                    l.inc(nc.tensor.matmul(
                        l.bank(bi2)[:, 0:w], l.c_ind2_128,
                        rrbuf[0:2, 0:w], start=True, stop=True), l.TD)
                    l.written(bi2)
                    l.consume(nc.vector, bi2)
                    l.waitE(nc.vector)
                    l.inc(nc.vector.tensor_tensor(dest.t(pt, c0, w), dest.t(pt, c0, w),
                                                  l.bank(bi2)[:, 0:w], op=ALU.mult),
                          l.EV)
                    l.freed(bi2)
                    l.waitE(nc.scalar)
                    l.inc(nc.scalar.mul(dest.t(pt, c0, w), dest.t(pt, c0, w), wn_ap),
                          l.ES)
        rms(q, rr_sb, MAIN4, hb_ap_qnw(l))
        KTILES = ([(b * PADB + 128, 512, b) for b in range(B)]
                  + [(b * PADB + 112, 16, b) for b in range(B)])
        rms(k_pad, rr_sb, KTILES, hb_ap_knw(l))

        v_tok = sb("aavtok", [128, (B * PADB // 128) * 260], BF16)
        ident = l.c_ident[:]
        NTOK = B * PADB // 128
        for vp in range(2):
            for tt in range(NTOK):
                l.waitE(nc.tensor)
                l.wait(nc.tensor, l.WD)
                l.acquire(7)
                l.inc(nc.tensor.transpose(l.pmb[0:128, 0:128],
                                          v_pad.t(vp, tt * 128, 128), ident), l.TD)
                l.written(7)
                l.consume(nc.scalar, 7)
                for gg in range(2):
                    g = vp * 2 + gg
                    l.inc(nc.scalar.copy(
                        v_tok[:, tt * 260 + g * 65: tt * 260 + g * 65 + 64],
                        l.pmb[0:128, gg * 64: gg * 64 + 64]), l.ES)
                l.freed(7)
        for g in range(4):
            ap = v_tok[:].rearrange("p (t c) -> p t c", c=260)[:, :, g * 65 + 64:g * 65 + 65]
            l.inc(nc.gpsimd.memset(ap, 1.0), l.WD, 1)

        ctx = FM(xn.sb, 1024, xn.C)
        att_e = sb("aae", [128, 4 * 512], BF16)
        av_sb = sb("aaav", [64, 512], BF16)
        dinv_sb = sb("aadinv", [1, 512], BF16)
        drep_sb = sb("aadrep", [64, 512], F32)
        eslot_use = [0] * 4
        esl = 0
        R4 = hb["aa_R4"]
        R4f = hb["aa_R4f"]
        r_sb = sb("aar4", [128, 4 * 2 * 512], F32)
        rf_sb = sb("aar4f", [128, 4 * 512], F32)
        for g in range(4):
            for c in range(2):
                l.inc(nc.sync.dma_start(out=r_sb[:, (g * 2 + c) * 512:(g * 2 + c + 1) * 512],
                                        in_=R4[g, c]), l.LD, 16)
            l.inc(nc.sync.dma_start(out=rf_sb[:, g * 512:(g + 1) * 512],
                                    in_=R4f[g]), l.LD, 16)

        for b in range(B):
            for qt in range(4):
                for g in range(4):
                    bA, bB = l.nb(), l.nb()
                    for cls, bi in ((0, bA), (1, bB)):
                        kt = qt + cls
                        l.acquire(bi)
                        l.waitE(nc.tensor)
                        l.wait(nc.tensor, l.WD)
                        last = None
                        for j in range(4):
                            h = g + 4 * j
                            lhs = k_pad.rs(g // 2, 64 * (g % 2), 64,
                                           b * PADB + kt * 128, 128)
                            rhs = q.rs(h // 2, 64 * (h % 2), 64,
                                       b * SL + qt * 128, 128)
                            last = nc.tensor.matmul(
                                l.bank(bi)[0:128, j * 128:(j + 1) * 128],
                                lhs, rhs, start=True, stop=True)
                        l.inc(last, l.TD)
                        l.written(bi)
                    for cls, bi in ((0, bA), (1, bB)):
                        l.consume(nc.vector, bi)
                        l.wait(nc.vector, l.TD, eslot_use[esl])
                        rtile = (rf_sb[:, g * 512:(g + 1) * 512]
                                 if (cls == 0 and qt == 0)
                                 else r_sb[:, (g * 2 + cls) * 512:(g * 2 + cls + 1) * 512])
                        l.inc(nc.vector.scalar_tensor_tensor(
                            att_e[:, esl * 512:(esl + 1) * 512],
                            l.bank(bi)[:, 0:512], 1.0 / 8.0, rtile,
                            op0=ALU.mult, op1=ALU.add), l.EV)
                        l.freed(bi)
                        l.waitE(nc.scalar)
                        l.inc(nc.scalar.activation(
                            att_e[:, esl * 512:(esl + 1) * 512],
                            att_e[:, esl * 512:(esl + 1) * 512], AF.Exp), l.ES)
                        if cls == 0:
                            esl0 = esl
                        esl = (esl + 1) % 4
                    bC = l.nb()
                    l.acquire(bC)
                    l.waitE(nc.tensor)
                    i1 = nc.tensor.matmul(
                        l.bank(bC)[0:65, 0:512],
                        v_tok[:, (b * 5 + qt) * 260 + g * 65:
                              (b * 5 + qt) * 260 + g * 65 + 65],
                        att_e[:, esl0 * 512:(esl0 + 1) * 512], start=True, stop=False)
                    i2 = nc.tensor.matmul(
                        l.bank(bC)[0:65, 0:512],
                        v_tok[:, (b * 5 + qt + 1) * 260 + g * 65:
                              (b * 5 + qt + 1) * 260 + g * 65 + 65],
                        att_e[:, ((esl0 + 1) % 4) * 512:((esl0 + 1) % 4 + 1) * 512],
                        start=False, stop=True)
                    l.inc(i2, l.TD)
                    l.written(bC)
                    eslot_use[esl0] = l.val(l.TD)
                    eslot_use[(esl0 + 1) % 4] = l.val(l.TD)
                    l.consume(nc.scalar, bC)
                    l.waitE(nc.scalar)
                    l.inc(nc.scalar.copy(dinv_sb[:], l.bank(bC)[64:65, 0:512]), l.ES)
                    l.inc(nc.scalar.copy(av_sb[:], l.bank(bC)[0:64, 0:512]), l.ES)
                    l.freed(bC)
                    bD = l.nb()
                    l.acquire(bD)
                    l.wait(nc.tensor, l.ES, l.val(l.ES))
                    l.inc(nc.tensor.matmul(l.bank(bD)[0:64, 0:512], l.c_ones1_64,
                                           dinv_sb[:], start=True, stop=True), l.TD)
                    l.written(bD)
                    l.consume(nc.vector, bD)
                    l.waitE(nc.vector)
                    l.inc(nc.vector.reciprocal(drep_sb[:], l.bank(bD)[0:64, 0:512]),
                          l.EV)
                    l.freed(bD)
                    for j in range(4):
                        h = g + 4 * j
                        l.inc(nc.vector.tensor_tensor(
                            ctx.rs(h // 2, 64 * (h % 2), 64, b * SL + qt * 128, 128),
                            av_sb[:, j * 128:(j + 1) * 128],
                            drep_sb[:, j * 128:(j + 1) * 128], op=ALU.mult),
                            l.EV)

        def wo_evac(ot, c0, w, tag, ps, bi):
            l.consume(nc.vector, bi)
            l.waitE(nc.vector)
            l.inc(nc.vector.tensor_tensor(x.t(ot, c0, w), x.t(ot, c0, w), ps,
                                          op=ALU.add), l.EV)
            l.freed(bi)
        emit_project(l, hb["aa_wo"], ws, ctx, 8, MAIN4, wo_evac)


def hb_ap_qnw(l):
    return l.c_qnw[:, 0:1]


def hb_ap_knw(l):
    return l.c_knw[:, 0:1]


def emit_l0ffn(l, x, xn, hb, pfx="f0", dbg=None):
    nc = l.nc
    l.phase_barrier()
    with contextlib.ExitStack() as st:
        sb = lambda n, s, d: st.enter_context(nc.sbuf_tensor(n, s, d))
        c_fm = FM(sb(pfx + "c", [128, 8 * 512], F32), 1024, 512)
        cb_fm = FM(sb(pfx + "cb", [128, 8 * 512], BF16), 1024, 512)
        c2_fm = FM(sb(pfx + "c2", [128, 8 * 512], F32), 1024, 512)
        fig = sb(pfx + "fig", [128, 3 * 512], F32)
        tmp = sb(pfx + "tmp", [128, 512], F32)
        h_t = sb(pfx + "h", [128, 512], BF16)
        wsg = [WStream(l, pfx + f"wg{i}", 2048, stack=st) for i in range(3)]
        wso = WStream(l, pfx + "wo", 2048, stack=st)
        for ot in range(8):
            l.inc(nc.scalar.activation(c_fm.t(ot), x.t(0, 0, 512), AF.Identity,
                                       bias=l.c_inc_fm[:, ot:ot + 1], scale=0.0),
                  l.ES)
            l.inc(nc.scalar.copy(cb_fm.t(ot), c_fm.t(ot)), l.ES)
        gates = [("wfuf", AF.Sigmoid, l.c_bf_fm), ("wiui", AF.Sigmoid, l.c_bi_fm),
                 ("wcuc", AF.Tanh, l.c_bc_fm)]
        slots = [wsg[gi].issue(hb[nm], 0) for gi, (nm, _, _) in enumerate(gates)]
        for t in range(B):
            for ot in range(8):
                nslots = []
                for gi, (nm, act, bias) in enumerate(gates):
                    if not (t == B - 1 and ot == 7):
                        nslots.append(wsg[gi].issue(
                            hb[nm], (ot + 1) % 8))
                    else:
                        nslots.append(None)
                for gi, (nm, act, bias) in enumerate(gates):
                    s = slots[gi]
                    wsg[gi].wait_ready(s)
                    if dbg is not None and t == 0 and ot == 0 and gi == 0:
                        l.wait(nc.sync, l.WD)
                        l.inc(nc.sync.dma_start(out=dbg["wslot"],
                                                in_=wsg[gi].slots[s][:]), l.ST, 16)
                        l.waitE(nc.sync)
                        l.inc(nc.sync.dma_start(out=dbg["xnt"],
                                                in_=xn.t(0, 0, SL)), l.ST, 16)
                    bi = l.nb()
                    l.acquire(bi)
                    l.waitE(nc.tensor)
                    last = None
                    for pt in range(16):
                        rhs = (xn.t(pt, t * SL, SL) if pt < 8
                               else cb_fm.t(pt - 8))
                        last = nc.tensor.matmul(l.bank(bi)[0:128, 0:SL],
                                                wsg[gi].chunk(s, pt), rhs,
                                                start=(pt == 0), stop=(pt == 15))
                    l.inc(last, l.TD)
                    l.written(bi)
                    wsg[gi].use(s)
                    l.consume(nc.scalar, bi)
                    l.waitE(nc.scalar)
                    l.inc(nc.scalar.activation(fig[:, gi * 512:(gi + 1) * 512],
                                               l.bank(bi)[:, 0:SL], act,
                                               bias=bias[:, ot:ot + 1]), l.ES)
                    l.freed(bi)
                l.waitE(nc.vector)
                l.inc(nc.vector.tensor_tensor(tmp[:], fig[:, 512:1024],
                                              fig[:, 1024:1536], op=ALU.mult), l.EV)
                l.inc(nc.vector.tensor_tensor(c2_fm.t(ot), fig[:, 0:512],
                                              c_fm.t(ot), op=ALU.mult), l.EV)
                l.inc(nc.vector.tensor_tensor(c2_fm.t(ot), c2_fm.t(ot), tmp[:],
                                              op=ALU.add), l.EV)
                if dbg is not None and t == 0 and ot == 0:
                    l.waitE(nc.sync)
                    l.inc(nc.sync.dma_start(out=dbg["fig0"], in_=fig[:]), l.ST, 16)
                slots = nslots
            c_fm, c2_fm = c2_fm, c_fm
            for ot_ in range(8):
                l.waitE(nc.scalar)
                l.wait(nc.scalar, l.TD)
                l.inc(nc.scalar.copy(cb_fm.t(ot_), c_fm.t(ot_)), l.ES)
            if dbg is not None and t == 0:
                l.waitE(nc.sync)
                for p_ in range(8):
                    l.inc(nc.sync.dma_start(out=dbg["c1"][p_ * 128:(p_ + 1) * 128, :],
                                            in_=c_fm.t(p_)), l.ST, 16)
        so = wso.issue(hb["wouo"], 0)
        for ot in range(8):
            so_next = wso.issue(hb["wouo"], ot + 1) if ot < 7 else None
            wso.wait_ready(so)
            for t in range(B):
                bi = l.nb()
                l.acquire(bi)
                l.waitE(nc.tensor)
                last = None
                for pt in range(16):
                    rhs = (xn.t(pt, t * SL, SL) if pt < 8 else cb_fm.t(pt - 8))
                    last = nc.tensor.matmul(l.bank(bi)[0:128, 0:SL],
                                            wso.chunk(so, pt), rhs,
                                            start=(pt == 0), stop=(pt == 15))
                l.inc(last, l.TD)
                l.written(bi)
                wso.use(so)
                l.consume(nc.scalar, bi)
                l.waitE(nc.scalar)
                l.inc(nc.scalar.activation(h_t[:], l.bank(bi)[:, 0:SL], AF.Tanh,
                                           bias=l.c_bo_fm[:, ot:ot + 1]), l.ES)
                l.freed(bi)
                l.waitE(nc.vector)
                l.inc(nc.vector.tensor_tensor(x.t(ot, t * SL, SL),
                                              x.t(ot, t * SL, SL), h_t[:],
                                              op=ALU.add), l.EV)
            so = so_next


def emit_project_store(l, whbm, wstream, xn, douts, n_tiles, hbm_out, bias_fm=None,
                       bias_ap=None, act=None, stage=None, stage_n=4,
                       hbm_outs=None):
    nc = l.nc
    ow = wstream.ow
    stage_use = [0] * stage_n
    ctr = [0]

    def fn(ot, c0, w, tag, ps, bi):
        sl = ctr[0] % stage_n
        ctr[0] += 1
        l.consume(nc.scalar, bi)
        l.wait(nc.scalar, l.ST, stage_use[sl])
        bias = bias_fm[:, ot:ot + 1] if bias_fm is not None else bias_ap
        if bias is not None:
            l.inc(nc.scalar.activation(stage[0:ow, sl * 512: sl * 512 + w], ps,
                                       act or AF.Identity, bias=bias), l.ES)
        else:
            l.inc(nc.scalar.activation(stage[0:ow, sl * 512: sl * 512 + w], ps,
                                       act or AF.Copy), l.ES)
        l.freed(bi)
        l.waitE(nc.sync)
        if hbm_outs is not None:
            dest, orow = hbm_outs[ot // 8], (ot % 8) * ow
        else:
            dest, orow = hbm_out, ot * ow
        l.inc(nc.sync.dma_start(out=dest[orow:orow + ow, c0:c0 + w],
                                in_=stage[0:ow, sl * 512: sl * 512 + w]), l.ST, 16)
        stage_use[sl] = l.val(l.ST)
    emit_project(l, whbm, wstream, xn, douts, n_tiles, fn)


def build_launch1(upto=99):
    nc = bass.Bass("TRN2", target_bir_lowering=False)
    ctx = contextlib.ExitStack()
    with ctx:
        l = L(nc, ctx)
        x_in = l.din("x_in", [1024, NTH], F32)
        hb = {}
        hb["aa_wq"] = l.din("aa_wq", [8, 128, 1024], F32)
        hb["aa_wk"] = l.din("aa_wk", [2, 128, 1024], F32)
        hb["aa_wv"] = l.din("aa_wv", [2, 128, 1024], F32)
        hb["aa_wo"] = l.din("aa_wo", [8, 128, 1024], F32)
        hb["aa_R4"] = l.din("aa_R4", [4, 2, 128, 512], F32)
        hb["aa_R4f"] = l.din("aa_R4f", [4, 128, 512], F32)
        for nm in ("wfuf", "wiui", "wcuc", "wouo"):
            hb[nm] = l.din(nm, [8, 2048, 128], F32)
        hb["ab_wq"] = l.din("ab_wq", [8, 128, 1024], F32)
        hb["ab_wk"] = l.din("ab_wk", [8, 128, 1024], F32)
        hb["ab_wv"] = l.din("ab_wv", [8, 128, 1024], F32)
        consts = l.din("consts1", [128, 32], F32)
        identb = l.din("identb", [128, 128], BF16)
        fconsts = l.din("fconsts", [128, 8], F32)
        onesb = l.din("onesb", [128, 4], BF16)
        xres_o = l.dout("xres", [1024, NT], F32)
        qb_o = l.dout("qb", [1024, NT], BF16)
        kb_o = l.dout("kb", [1024, NT], BF16)
        vb_o = l.dout("vb", [1024, NT], BF16)

        fcs = l.sbuf("fcs", [128, 8], F32)
        l.inc(nc.sync.dma_start(out=fcs[:], in_=fconsts[:]), l.LD, 16)
        for i, v in enumerate(FCONST_VALS):
            nc.const_aps.register(F32, float(v), fcs[:, i:i + 1])
        x = alloc_fm(l, "x", 1024, NTH, F32)
        xn = alloc_fm(l, "xn", 1024, NTH, BF16)
        cst = l.sbuf("cst", [128, 32], F32)
        l.c_ident = l.sbuf("identsb", [128, 128], BF16)
        onesb_sb = l.sbuf("onesbsb", [128, 4], BF16)
        stage = l.sbuf("stage", [128, 4 * 512], BF16)

        load_fm(l, x, x_in)
        l.inc(nc.sync.dma_start(out=cst[:], in_=consts[:]), l.LD, 16)
        l.inc(nc.sync.dma_start(out=l.c_ident[:], in_=identb[:]), l.LD, 16)
        l.inc(nc.sync.dma_start(out=onesb_sb[:], in_=onesb[:]), l.LD, 16)
        l.c_ones128_b = onesb_sb[:, 0:1]
        rowones = l.sbuf("rowones", [1, 128], BF16)
        l.inc(nc.gpsimd.memset(rowones[:], 1.0), l.WD, 1)
        l.c_ones1_128 = rowones[0:1, 0:128]
        l.c_ones1_64 = rowones[0:1, 0:64]
        ind128_2 = l.sbuf("ind1282", [128, 2], BF16)
        l.inc(nc.gpsimd.memset(ind128_2[0:64, 0:1], 1.0), l.WD, 1)
        l.inc(nc.gpsimd.memset(ind128_2[0:64, 1:2], 0.0), l.WD, 1)
        l.inc(nc.gpsimd.memset(ind128_2[64:128, 0:1], 0.0), l.WD, 1)
        l.inc(nc.gpsimd.memset(ind128_2[64:128, 1:2], 1.0), l.WD, 1)
        l.c_ind128_2 = ind128_2[:]
        indrow = l.din("indrow", [2, 128], BF16)
        ind2_128 = l.sbuf("ind2128", [2, 128], BF16)
        l.inc(nc.sync.dma_start(out=ind2_128[:], in_=indrow[:]), l.LD, 16)
        l.c_ind2_128 = ind2_128[:]
        l.c_qnw = cst[:, 0:1]
        l.c_knw = cst[:, 1:2]
        l.c_bf_fm = cst[:, 8:16]
        l.c_bi_fm = cst[:, 16:24]
        l.c_bc_fm = cst[:, 24:32]
        consts2 = l.din("consts2", [128, 40], F32)
        cst2 = l.sbuf("cst2", [128, 40], F32)
        l.inc(nc.sync.dma_start(out=cst2[:], in_=consts2[:]), l.LD, 16)
        l.c_bo_fm = cst2[:, 0:8]
        l.c_inc_fm = cst2[:, 8:16]
        l.c_bqb_fm = cst2[:, 16:24]
        l.c_bkb_fm = cst2[:, 24:32]
        l.c_bvb_fm = cst2[:, 32:40]

        for eng in (nc.tensor, nc.scalar, nc.vector):
            l.wait(eng, l.LD)
            l.wait(eng, l.WD)

        do_ln(l, x, xn, C=NTH)
        if upto >= 1:
            emit_l0aa(l, x, xn, hb)
        if upto >= 2:
            do_ln(l, x, xn, C=NT)
        if upto >= 3:
            emit_l0ffn(l, x, xn, hb, pfx="f0")
        if upto >= 4:
            do_ln(l, x, xn, C=NT)
        ws2 = WStream(l, "abw", 1024)
        if upto >= 4:
            emit_project_store(l, hb["ab_wq"], ws2, xn, 8, MAIN4, qb_o,
                               bias_fm=l.c_bqb_fm, stage=stage)
            emit_project_store(l, hb["ab_wk"], ws2, xn, 8, MAIN4, kb_o,
                               bias_fm=l.c_bkb_fm, stage=stage)
            emit_project_store(l, hb["ab_wv"], ws2, xn, 8, MAIN4, vb_o,
                               bias_fm=l.c_bvb_fm, stage=stage)
        l.waitE(nc.sync)
        l.wait(nc.sync, l.TD)
        for p in range(8):
            l.inc(nc.sync.dma_start(out=xres_o[p * 128:(p + 1) * 128, :],
                                    in_=x.t(p, 0, NT)), l.ST, 16)
        l.wait(nc.sync, l.ST)
    return nc, l


def _tile_w(w_eff, ow=128):
    din, dout = w_eff.shape
    npt = din // 128
    arr = w_eff.reshape(npt, 128, dout // ow, ow)
    return np.ascontiguousarray(arr.transpose(2, 1, 0, 3).reshape(dout // ow, 128,
                                                                  npt * ow))


def _alibi_slopes16():
    import math
    base = 2.0 ** (-2.0 ** (-(math.log2(16) - 3)))
    return (base ** np.arange(1, 17)).astype(np.float32)


def _fmcols(vec):
    return np.ascontiguousarray(vec.reshape(8, 128).T)


def prep_launch1(inp):
    slopes = _alibi_slopes16()
    x = inp["L0_attn_a_x"].astype(np.float32)
    base = {}
    base["aa_wq"] = _tile_w(inp["L0_attn_a_wq"].T)
    base["aa_wk"] = _tile_w(inp["L0_attn_a_wk"].T)
    base["aa_wv"] = _tile_w(inp["L0_attn_a_wv"].T)
    base["aa_wo"] = _tile_w(inp["L0_attn_a_wo"].T)
    for nm, wk, uk in (("wfuf", "L0_ffn_wf", "L0_ffn_uf"),
                       ("wiui", "L0_ffn_wi", "L0_ffn_ui"),
                       ("wcuc", "L0_ffn_wc", "L0_ffn_uc"),
                       ("wouo", "L0_ffn_wo", "L0_ffn_uo")):
        base[nm] = np.ascontiguousarray(np.concatenate(
            [_tile_w(inp[wk].T), _tile_w(inp[uk].T)], axis=2))
    base["ab_wq"] = _tile_w(inp["L0_attn_b_wq"].T)
    base["ab_wk"] = _tile_w(inp["L0_attn_b_wk"].T)
    base["ab_wv"] = _tile_w(inp["L0_attn_b_wv"].T)
    c1 = np.zeros((128, 32), np.float32)
    qnw = inp["L0_attn_a_q_norm_w"]
    knw = inp["L0_attn_a_k_norm_w"]
    c1[:, 0] = np.concatenate([qnw, qnw])
    c1[:, 1] = np.concatenate([knw, knw])
    c1[:, 8:16] = _fmcols(inp["L0_ffn_bf"])
    c1[:, 16:24] = _fmcols(inp["L0_ffn_bi"])
    c1[:, 24:32] = _fmcols(inp["L0_ffn_bc"])
    base["consts1"] = c1
    c2 = np.zeros((128, 40), np.float32)
    c2[:, 0:8] = _fmcols(inp["L0_ffn_bo"])
    c2[:, 8:16] = _fmcols(inp["L0_ffn_in_c"])
    c2[:, 16:24] = _fmcols(inp["L0_attn_b_bq"])
    c2[:, 24:32] = _fmcols(inp["L0_attn_b_bk"])
    c2[:, 32:40] = _fmcols(inp["L0_attn_b_bv"])
    base["consts2"] = c2
    base["identb"] = np.eye(128, dtype=BF16NP)
    ob = np.zeros((128, 4), BF16NP)
    ob[:, 0] = 1
    base["onesb"] = ob
    base["fconsts"] = np.tile(np.array([FCONST_VALS], np.float32), (128, 1))
    ir = np.zeros((2, 128), BF16NP)
    ir[0, 0:64] = 1
    ir[1, 64:128] = 1
    base["indrow"] = ir

    i_idx = np.arange(128)[:, None]
    q_idx = np.arange(128)[None, :]
    R4 = np.zeros((4, 2, 128, 512), np.float32)
    R4f_main = np.zeros((4, 128, 512), np.float32)
    for g in range(4):
        for cls in range(2):
            rel = (i_idx - q_idx) - (128 if cls == 0 else 0)
            ok = (rel <= 0) & (rel >= -HALO)
            for j in range(4):
                h = g + 4 * j
                tile = np.where(ok, slopes[h] * rel, NEG).astype(np.float32)
                R4[g, cls, :, j * 128:(j + 1) * 128] = tile
        R4f_main[g] = R4[g, 0]
    R4f_core0 = np.full((4, 128, 512), NEG, np.float32)

    in_maps = []
    for r in range(NCORES):
        m = dict(base)
        xm = x[:, r * SL:(r + 1) * SL, :].transpose(2, 0, 1).reshape(1024, NT)
        if r == 0:
            xh = np.zeros((1024, B * HALO), np.float32)
        else:
            xh = x[:, r * SL - HALO: r * SL, :].transpose(2, 0, 1).reshape(1024, B * HALO)
        m["x_in"] = np.ascontiguousarray(np.concatenate([xm, xh], axis=1))
        m["aa_R4"] = R4
        m["aa_R4f"] = R4f_core0 if r == 0 else R4f_main
        in_maps.append(m)
    return in_maps


def emit_l0ab(l, x, q_fm, kt_hbm, vh_hbm, scale):
    nc = l.nc
    l.phase_barrier()
    NKT = S // 128
    with contextlib.ExitStack() as st:
        sb = lambda n, s_, d: st.enter_context(nc.sbuf_tensor(n, s_, d))
        kslot = [sb(f"abk{i}", [128, 8 * 128], BF16) for i in range(2)]
        vbuf = [sb(f"abv{i}", [128, 32 * 256], BF16) for i in range(2)]
        esb_all = sb("abe", [128, NKT * 512], BF16)
        avs = sb("abav", [128, 512], BF16)
        dinvb = sb("abdinv", [1, 512], BF16)
        drep = sb("abdrep", [128, 512], F32)
        kslot_user = [-1, -1]
        vbuf_user = [-1, -1]
        vbuf_rdy = [0, 0]

        def issue_k(idx, b, kt):
            s_ = idx % 2
            l.wait(nc.gpsimd, l.TD,
                   kslot_user[s_] if kslot_user[s_] >= 0 else l.val(l.TD))
            l.inc(nc.gpsimd.dma_start(out=kslot[s_][:], in_=kt_hbm[b, kt]),
                  l.WD, 16)
            return s_, l.val(l.WD)

        def issue_v(idx, b, hf):
            s_ = idx % 2
            l.wait(nc.sync, l.TD,
                   vbuf_user[s_] if vbuf_user[s_] >= 0 else l.val(l.TD))
            l.inc(nc.sync.dma_start(out=vbuf[s_][:], in_=vh_hbm[b, hf]), l.LD, 16)
            vbuf_rdy[s_] = l.val(l.LD)
            return s_

        q_es0 = l.val(l.ES)
        q_ld0 = l.val(l.LD)
        vcur = issue_v(0, 0, 0)
        vidx = 1
        exp_es = [0] * NKT
        for b in range(B):
            q0 = b * SL
            l.acquire(6)
            kv = [None] * NKT
            kv[0] = issue_k(b * NKT, b, 0)
            kv[1] = issue_k(b * NKT + 1, b, 1)
            eslu2 = [0, 0]

            def scores0(kt):
                s_, rdy = kv[kt]
                sbk = kt % 2
                l.wait(nc.tensor, l.WD, rdy)
                _es, _ev = l.bank_free[sbk]
                l.waitE(nc.tensor, max(_es, q_es0), _ev)
                l.wait(nc.tensor, l.LD, q_ld0)
                last = None
                for pt in range(8):
                    last = nc.tensor.matmul(
                        l.bank(sbk)[0:128, 0:512],
                        kslot[s_][:, pt * 128:(pt + 1) * 128],
                        q_fm.t(pt, q0, SL),
                        start=(pt == 0), stop=(pt == 7))
                l.inc(last, l.TD)
                l.written(sbk)
                kslot_user[s_] = l.val(l.TD)

            def expden0(kt):
                sbk = kt % 2
                l.consume(nc.scalar, sbk)
                l.wait(nc.scalar, l.TD, eslu2[sbk])
                l.inc(nc.scalar.activation(
                    esb_all[:, kt * 512:(kt + 1) * 512],
                    l.bank(sbk)[0:128, 0:512], AF.Exp, scale=scale), l.ES)
                l.freed(sbk)
                exp_es[kt] = l.val(l.ES)
                l.wait(nc.tensor, l.ES, exp_es[kt])
                l.inc(nc.tensor.matmul(l.bank(6)[0:1, 0:512], l.c_ones128_b,
                                       esb_all[:, kt * 512:(kt + 1) * 512],
                                       start=(kt == 0), stop=(kt == NKT - 1)),
                      l.TD)
                eslu2[sbk] = l.val(l.TD)

            scores0(0)
            for kt in range(NKT):
                if kt + 2 < NKT:
                    kv[kt + 2] = issue_k(b * NKT + kt + 2, b, kt + 2)
                if kt + 1 < NKT:
                    scores0(kt + 1)
                expden0(kt)
            l.written(6)
            for hf in range(4):
                vs = vcur
                if hf < 3:
                    vcur = issue_v(vidx, b, hf + 1)
                    vidx += 1
                elif b + 1 < B:
                    vcur = issue_v(vidx, b + 1, 0)
                    vidx += 1
                l.wait(nc.tensor, l.LD, vbuf_rdy[vs])
                for c in range(2):
                    bk = 2 + c
                    l.acquire(bk)
                    ps = l.bank(bk)[0:128, 0:512]
                    l.wait(nc.tensor, l.ES, exp_es[NKT - 1])
                    last = None
                    for kt in range(NKT):
                        last = nc.tensor.matmul(
                            ps, vbuf[vs][:, kt * 256 + c * 128: kt * 256 + (c + 1) * 128],
                            esb_all[:, kt * 512:(kt + 1) * 512],
                            start=(kt == 0), stop=(kt == NKT - 1))
                    l.inc(last, l.TD)
                    l.written(bk)
                    vbuf_user[vs] = l.val(l.TD)
                if hf == 0:
                    l.consume(nc.vector, 6)
                    l.waitE(nc.vector)
                    l.inc(nc.vector.tensor_copy(dinvb[:], l.bank(6)[0:1, 0:512]),
                          l.EV)
                    l.freed(6)
                    ep_ev = l.val(l.EV)
                    l.acquire(6)
                    l.wait(nc.tensor, l.EV, ep_ev)
                    l.inc(nc.tensor.matmul(l.bank(6)[0:128, 0:512],
                                           l.c_ones1_128, dinvb[:],
                                           start=True, stop=True), l.TD)
                    l.written(6)
                    l.consume(nc.vector, 6)
                    l.inc(nc.vector.reciprocal(drep[:], l.bank(6)[0:128, 0:512]),
                          l.EV)
                    l.freed(6)
                for c in range(2):
                    bk = 2 + c
                    l.consume(nc.scalar, bk)
                    l.waitE(nc.scalar)
                    l.inc(nc.scalar.copy(avs[:], l.bank(bk)[0:128, 0:512]), l.ES)
                    l.wait(nc.vector, l.ES, l.val(l.ES))
                    ot = hf * 2 + c
                    l.inc(nc.vector.tensor_tensor(avs[:], avs[:], drep[:],
                                                  op=ALU.mult), l.EV)
                    l.inc(nc.vector.tensor_tensor(x.t(ot, q0, SL), x.t(ot, q0, SL),
                                                  avs[:], op=ALU.add), l.EV)
                    l.freed(bk)


def build_launch2(upto=99):
    nc = bass.Bass("TRN2", target_bir_lowering=False)
    ctx = contextlib.ExitStack()
    with ctx:
        l = L(nc, ctx)
        x_in = l.din("x_in", [1024, NT], F32)
        qb_in = l.din("qb_in", [1024, NT], BF16)
        kt_all = l.din("kt_all", [B, S // 128, 128, 8 * 128], BF16)
        vt_all = l.din("vt_all", [B, 4, 128, 32 * 256], BF16)
        hb = {}
        for nm in ("wfuf", "wiui", "wcuc", "wouo"):
            hb[nm] = l.din(nm, [8, 2048, 128], F32)
        hb["wdelta"] = l.din("wdelta", [1, 128, 128], F32)
        hb["win"] = l.din("win", [1, 128, 128], F32)
        hb["wgate"] = l.din("wgate", [8, 128, 1024], F32)
        hb["wq1"] = l.din("wq1", [8, 128, 1024], F32)
        hb["wk1"] = l.din("wk1", [8, 128, 1024], F32)
        hb["wv1"] = l.din("wv1", [8, 128, 1024], F32)
        fconsts = l.din("fconsts", [128, 8], F32)
        consts1 = l.din("consts1", [128, 32], F32)
        consts2 = l.din("consts2", [128, 40], F32)
        consts3 = l.din("consts3", [128, 64], F32)
        onesb = l.din("onesb", [128, 4], BF16)
        indrow = l.din("indrow", [2, 128], BF16)

        xres_o = l.dout("xres", [1024, NT], F32)
        delta_o = l.dout("delta", [16, NT], BF16)
        bt_o = l.dout("bt", [16, NT], BF16)
        gate_o = l.dout("gate", [1024, NT], BF16)
        q1_o = l.dout("q1", [1024, NT], BF16)
        k1_o = l.dout("k1", [1024, NT], BF16)
        v1_o = l.dout("v1", [1024, NT], BF16)

        fcs = l.sbuf("fcs", [128, 8], F32)
        l.inc(nc.sync.dma_start(out=fcs[:], in_=fconsts[:]), l.LD, 16)
        for i, v in enumerate(FCONST_VALS):
            nc.const_aps.register(F32, float(v), fcs[:, i:i + 1])
        x = alloc_fm(l, "x", 1024, NT, F32)
        cst = l.sbuf("cst", [128, 32], F32)
        cst2 = l.sbuf("cst2", [128, 40], F32)
        cst3 = l.sbuf("cst3", [128, 64], F32)
        onesb_sb = l.sbuf("onesbsb", [128, 4], BF16)
        ind2_128 = l.sbuf("ind2128", [2, 128], BF16)
        rowones = l.sbuf("rowones", [1, 128], BF16)
        stage = l.sbuf("stage", [128, 4 * 512], BF16)

        load_fm(l, x, x_in)
        l.inc(nc.sync.dma_start(out=cst[:], in_=consts1[:]), l.LD, 16)
        l.inc(nc.sync.dma_start(out=cst2[:], in_=consts2[:]), l.LD, 16)
        l.inc(nc.sync.dma_start(out=cst3[:], in_=consts3[:]), l.LD, 16)
        l.inc(nc.sync.dma_start(out=onesb_sb[:], in_=onesb[:]), l.LD, 16)
        l.inc(nc.sync.dma_start(out=ind2_128[:], in_=indrow[:]), l.LD, 16)
        l.inc(nc.gpsimd.memset(rowones[:], 1.0), l.WD, 1)
        l.c_ones128_b = onesb_sb[:, 0:1]
        l.c_ones1_128 = rowones[0:1, 0:128]
        l.c_ones1_64 = rowones[0:1, 0:64]
        l.c_ind2_128 = ind2_128[:]
        l.c_bf_fm = cst[:, 8:16]
        l.c_bi_fm = cst[:, 16:24]
        l.c_bc_fm = cst[:, 24:32]
        l.c_bo_fm = cst2[:, 0:8]
        l.c_inc_fm = cst2[:, 8:16]
        for eng in (nc.tensor, nc.scalar, nc.vector):
            l.wait(eng, l.LD)
            l.wait(eng, l.WD)

        with contextlib.ExitStack() as st3:
            q_sb = FM(st3.enter_context(nc.sbuf_tensor("qown", [128, 8 * NT], BF16)),
                      1024, NT)
            load_fm(l, q_sb, qb_in)
            if upto >= 1:
                emit_l0ab(l, x, q_sb, kt_all, vt_all, scale=1.0 / 32.0)
            l.phase_barrier()
        xn = alloc_fm(l, "xn", 1024, NT, BF16)
        if upto >= 2:
            do_ln(l, x, xn, C=NT)
            emit_l0ffn(l, x, xn, hb, pfx="f1")
        if upto >= 3:
            xln = alloc_fm(l, "xln", 1024, NT, BF16)
            do_ln(l, x, xln, C=NT)
            xn_a = xn
            do_ln(l, xln, xn_a, C=NT, g_fm=cst3[:, 0:8], b_fm=cst3[:, 8:16])
            ws1 = WStream(l, "wd16", 1024, ow=16)
            emit_project_store(l, hb["wdelta"], ws1, xn_a, 1, MAIN4, delta_o,
                               stage=stage, act=AF.Sigmoid,
                               bias_ap=cst2[0:16, 32:33])
            emit_project_store(l, hb["win"], ws1, xn_a, 1, MAIN4, bt_o,
                               stage=stage, bias_ap=cst2[0:16, 33:34])
            ws8 = WStream(l, "w8", 1024)
            emit_project_store(l, hb["wgate"], ws8, xn_a, 8, MAIN4, gate_o,
                               bias_fm=cst3[:, 32:40], stage=stage,
                               act=AF.Sigmoid)
            do_ln(l, xln, xn_a, C=NT, g_fm=cst3[:, 16:24], b_fm=cst3[:, 24:32])
            emit_project_store(l, hb["wq1"], ws8, xn_a, 8, MAIN4, q1_o,
                               bias_fm=cst3[:, 40:48], stage=stage)
            emit_project_store(l, hb["wk1"], ws8, xn_a, 8, MAIN4, k1_o,
                               bias_fm=cst3[:, 48:56], stage=stage)
            emit_project_store(l, hb["wv1"], ws8, xn_a, 8, MAIN4, v1_o,
                               bias_fm=cst3[:, 56:64], stage=stage)

        l.waitE(nc.sync)
        l.wait(nc.sync, l.TD)
        for p in range(8):
            l.inc(nc.sync.dma_start(out=xres_o[p * 128:(p + 1) * 128, :],
                                    in_=x.t(p, 0, NT)), l.ST, 16)
        l.wait(nc.sync, l.ST)
    return nc, l





def _tile_w2(w_eff, ow=128):
    return _tile_w(w_eff, ow=ow)


def prep_launch2(inp, l1outs):
    base = {}
    for nm, wk, uk in (("wfuf", "L0_ffn_wf", "L0_ffn_uf"),
                       ("wiui", "L0_ffn_wi", "L0_ffn_ui"),
                       ("wcuc", "L0_ffn_wc", "L0_ffn_uc"),
                       ("wouo", "L0_ffn_wo", "L0_ffn_uo")):
        base[nm] = np.ascontiguousarray(np.concatenate(
            [_tile_w(inp[wk].T), _tile_w(inp[uk].T)], axis=2))
    base["wdelta"] = _tile_w2(inp["L1_attn_a_Wdelta"], ow=16)
    base["win"] = _tile_w2(inp["L1_attn_a_Win"], ow=16)
    base["wgate"] = _tile_w(inp["L1_attn_a_Wgate"])
    base["wq1"] = _tile_w(inp["L1_attn_a_Wq"])
    base["wk1"] = _tile_w(inp["L1_attn_a_Wk"])
    base["wv1"] = _tile_w(inp["L1_attn_a_Wv"])
    base["fconsts"] = np.tile(np.array([FCONST_VALS], np.float32), (128, 1))
    c1 = np.zeros((128, 32), np.float32)
    c1[:, 8:16] = _fmcols(inp["L0_ffn_bf"])
    c1[:, 16:24] = _fmcols(inp["L0_ffn_bi"])
    c1[:, 24:32] = _fmcols(inp["L0_ffn_bc"])
    base["consts1"] = c1
    c2 = np.zeros((128, 40), np.float32)
    c2[:, 0:8] = _fmcols(inp["L0_ffn_bo"])
    c2[:, 8:16] = _fmcols(inp["L0_ffn_in_c"])
    c2[0:16, 32] = inp["L1_attn_a_bdelta"]
    c2[0:16, 33] = inp["L1_attn_a_b_in"]
    base["consts2"] = c2
    c3 = np.zeros((128, 64), np.float32)
    c3[:, 0:8] = _fmcols(inp["L1_attn_a_ln_g"])
    c3[:, 8:16] = _fmcols(inp["L1_attn_a_ln_b"])
    c3[:, 16:24] = _fmcols(inp["L1_attn_a_ln1_g"])
    c3[:, 24:32] = _fmcols(inp["L1_attn_a_ln1_b"])
    c3[:, 32:40] = _fmcols(inp["L1_attn_a_bgate"])
    c3[:, 40:48] = _fmcols(inp["L1_attn_a_bq"])
    c3[:, 48:56] = _fmcols(inp["L1_attn_a_bk"])
    c3[:, 56:64] = _fmcols(inp["L1_attn_a_bv"])
    base["consts3"] = c3
    ob = np.zeros((128, 4), BF16NP)
    ob[:, 0] = 1
    base["onesb"] = ob
    ir = np.zeros((2, 128), BF16NP)
    ir[0, 0:64] = 1
    ir[1, 64:128] = 1
    base["indrow"] = ir

    kt_flat = np.zeros((B, 1024, S), BF16NP)
    vt_all = np.zeros((B, S, 1024), BF16NP)
    for r in range(NCORES):
        kb = l1outs[r]["kb"]
        vb = l1outs[r]["vb"]
        for b in range(B):
            kt_flat[b][:, r * SL:(r + 1) * SL] = kb[:, b * SL:(b + 1) * SL]
            vt_all[b][r * SL:(r + 1) * SL, :] = vb[:, b * SL:(b + 1) * SL].T
    kt_all = np.ascontiguousarray(
        kt_flat.reshape(B, 8, 128, S // 128, 128).transpose(0, 3, 2, 1, 4)
        .reshape(B, S // 128, 128, 8 * 128))
    base["kt_all"] = kt_all
    base["vt_all"] = np.ascontiguousarray(
        vt_all.reshape(B, S // 128, 128, 4, 2, 128).transpose(0, 3, 2, 1, 4, 5)
        .reshape(B, 4, 128, 32 * 256))
    in_maps = []
    for r in range(NCORES):
        m = dict(base)
        m["x_in"] = l1outs[r]["xres"]
        m["qb_in"] = l1outs[r]["qb"]
        in_maps.append(m)
    return in_maps


NSWEEP = 12


def emit_scan(l, delta_sb, bt_sb, h_a, h_b, abig, i64, h_bf):
    nc = l.nc
    l.phase_barrier()
    l.inc(nc.gpsimd.memset(h_a[:], 0.0), l.WD, 1)
    l.inc(nc.gpsimd.memset(h_b[:], 0.0), l.WD, 1)
    t1b = l.scan_t1b
    src, dst = h_a, h_b
    for sw in range(NSWEEP):
        prev_es = l.val(l.ES)
        prev_ev = l.val(l.EV)
        l.wait(nc.scalar, l.TD)
        l.wait(nc.scalar, l.WD)
        l.wait(nc.scalar, l.EV, prev_ev)
        l.inc(nc.scalar.copy(src[:, 0:1], src[:, S:S + 1]), l.ES)
        wrap_es = l.val(l.ES)
        for stile in range(8):
            c0 = stile * 512
            l.wait(nc.vector, l.ES, wrap_es)
            l.wait(nc.vector, l.WD)
            l.inc(nc.vector.tensor_tensor(t1b[:, stile * 512:(stile + 1) * 512],
                                          src[:, c0:c0 + 512],
                                          delta_sb[:, c0:c0 + 512], op=ALU.mult),
                  l.EV)
            t1_ev = l.val(l.EV)
            bi = l.nb()
            l.acquire(bi)
            l.wait(nc.tensor, l.EV, t1_ev)
            nc.tensor.matmul(l.bank(bi)[0:64, 0:512], abig,
                             t1b[:, stile * 512:(stile + 1) * 512],
                             start=True, stop=False)
            l.inc(nc.tensor.matmul(l.bank(bi)[0:64, 0:512], i64,
                                   bt_sb[:, c0:c0 + 512], start=False, stop=True),
                  l.TD)
            l.written(bi)
            l.consume(nc.scalar, bi)
            l.inc(nc.scalar.activation(dst[:, 1 + c0:1 + c0 + 512],
                                       l.bank(bi)[0:64, 0:512], AF.Tanh), l.ES)
            l.freed(bi)
        src, dst = dst, src
    l.waitE(nc.scalar)
    l.inc(nc.scalar.copy(h_bf[:], src[0:64, 1:513]), l.ES)


def emit_l1a_attn(l, ctx_fm, kt1h, vt1h, scale):
    nc = l.nc
    l.phase_barrier()
    NKT = S // 128
    with contextlib.ExitStack() as st:
        sb = lambda n, s_, d: st.enter_context(nc.sbuf_tensor(n, s_, d))
        kslot = [sb(f"l1k{i}", [128, S], BF16) for i in range(2)]
        vslot = [sb(f"l1v{i}", [128, 32 * 128], BF16) for i in range(2)]
        esb = sb("l1e", [128, 2 * 512], BF16)
        avs = sb("l1av", [128, 512], BF16)
        dinvb = sb("l1dinv", [1, 512], BF16)
        drep = sb("l1drep", [128, 512], F32)
        slot_user = [-1, -1]

        def issue(idx, b, h):
            s_ = idx % 2
            l.wait(nc.gpsimd, l.TD,
                   slot_user[s_] if slot_user[s_] >= 0 else l.val(l.TD))
            l.inc(nc.gpsimd.dma_start(out=kslot[s_][:],
                                      in_=kt1h[b][h * 128:(h + 1) * 128, :]),
                  l.WD, 16)
            l.inc(nc.gpsimd.dma_start(out=vslot[s_][:], in_=vt1h[b, h]), l.WD, 16)
            return s_, l.val(l.WD)

        eslu = [0, 0]
        pairs = [(b, h) for b in range(B) for h in range(8)]
        cur = issue(0, *pairs[0])
        for pi, (b, h) in enumerate(pairs):
            nxt = issue(pi + 1, *pairs[pi + 1]) if pi + 1 < len(pairs) else None
            s_, rdy = cur
            l.wait(nc.tensor, l.WD, rdy)
            for bk in (2, 3):
                l.acquire(bk)
            exp_es2 = [0] * NKT

            def scores1(kt):
                sbk = kt % 2
                _es, _ev = l.bank_free[sbk]
                l.waitE(nc.tensor, _es, _ev)
                l.inc(nc.tensor.matmul(l.bank(sbk)[0:128, 0:512],
                                       kslot[s_][:, kt * 128:(kt + 1) * 128],
                                       ctx_fm.t(h, b * SL, SL),
                                       start=True, stop=True), l.TD)
                l.written(sbk)

            def exp1(kt):
                sbk = kt % 2
                l.consume(nc.scalar, sbk)
                l.wait(nc.scalar, l.TD, eslu[kt % 2])
                l.inc(nc.scalar.activation(esb[:, (kt % 2) * 512:(kt % 2 + 1) * 512],
                                           l.bank(sbk)[0:128, 0:512], AF.Exp,
                                           scale=scale), l.ES)
                l.freed(sbk)
                exp_es2[kt] = l.val(l.ES)

            def av1(kt):
                l.wait(nc.tensor, l.ES, exp_es2[kt])
                nc.tensor.matmul(l.bank(3)[0:1, 0:512], l.c_ones128_b,
                                 esb[:, (kt % 2) * 512:(kt % 2 + 1) * 512],
                                 start=(kt == 0), stop=(kt == NKT - 1))
                l.inc(nc.tensor.matmul(l.bank(2)[0:128, 0:512],
                                       vslot[s_][:, kt * 128:(kt + 1) * 128],
                                       esb[:, (kt % 2) * 512:(kt % 2 + 1) * 512],
                                       start=(kt == 0), stop=(kt == NKT - 1)),
                      l.TD)
                eslu[kt % 2] = l.val(l.TD)
                slot_user[s_] = l.val(l.TD)

            scores1(0)
            exp1(0)
            for kt in range(NKT):
                if kt + 1 < NKT:
                    scores1(kt + 1)
                av1(kt)
                if kt + 1 < NKT:
                    exp1(kt + 1)
            for bk in (2, 3):
                l.written(bk)
            l.consume(nc.vector, 3)
            l.waitE(nc.vector)
            l.inc(nc.vector.tensor_copy(dinvb[:], l.bank(3)[0:1, 0:512]), l.EV)
            l.freed(3)
            l.consume(nc.vector, 2)
            l.inc(nc.vector.tensor_copy(avs[:], l.bank(2)[0:128, 0:512]), l.EV)
            l.freed(2)
            ep_ev = l.val(l.EV)
            l.acquire(4)
            l.wait(nc.tensor, l.EV, ep_ev)
            l.inc(nc.tensor.matmul(l.bank(4)[0:128, 0:512], l.c_ones1_128,
                                   dinvb[:], start=True, stop=True), l.TD)
            l.written(4)
            l.consume(nc.vector, 4)
            l.inc(nc.vector.reciprocal(drep[:], l.bank(4)[0:128, 0:512]), l.EV)
            l.freed(4)
            l.inc(nc.vector.tensor_tensor(ctx_fm.t(h, b * SL, SL), avs[:],
                                          drep[:], op=ALU.mult), l.EV)
            cur = nxt


def build_launch3(upto=99):
    nc = bass.Bass("TRN2", target_bir_lowering=False)
    ctx = contextlib.ExitStack()
    with ctx:
        l = L(nc, ctx)
        x_in = l.din("x_in", [1024, NT], F32)
        gate_in = l.din("gate_in", [1024, NT], BF16)
        q1_in = l.din("q1_in", [1024, NT], BF16)
        delta_in = l.din("delta_in", [64, S], BF16)
        bt_in = l.din("bt_in", [64, S], BF16)
        kt1_all = l.din("kt1_all", [B, 1024, S], BF16)
        vt1h_all = l.din("vt1h_all", [B, 8, 128, (S // 128) * 128], BF16)
        abig_in = l.din("abig_in", [64, 64], F32)
        i64_in = l.din("i64_in", [64, 64], BF16)
        cmask_in = l.din("cmask_in", [64, B * 8 * 128], F32)
        hb = {}
        hb["wo1"] = l.din("wo1", [8, 128, 1024], F32)
        hb["w1"] = l.din("w1", [32, 128, 1024], F32)
        hb["w2"] = l.din("w2", [8, 128, 4096], F32)
        hb["wfa"] = l.din("wfa", [8, 128, 1024], F32)
        hb["wfb"] = l.din("wfb", [8, 128, 1024], F32)
        hb["wqkv2"] = l.din("wqkv2", [24, 128, 1024], F32)
        g1t_in = l.din("g1t_in", [32, 2 * 4 * 32], F32)
        g2t_in = l.din("g2t_in", [16, 2 * 4 * 16], F32)
        indd_in = l.din("indd_in", [32, 4 * 128], BF16)
        indg_in = l.din("indg_in", [16, 128], BF16)
        fconsts = l.din("fconsts", [128, 8], F32)
        consts4 = l.din("consts4", [128, 96], F32)
        b1fm_in = l.din("b1fm", [128, 32], F32)
        onesb = l.din("onesb", [128, 4], BF16)

        xres_o = l.dout("xres", [1024, NT], F32)
        q2_o = l.dout("q2", [1024, NT], BF16)
        k2_o = l.dout("k2", [1024, NT], BF16)
        v2_o = l.dout("v2", [1024, NT], BF16)

        fcs = l.sbuf("fcs", [128, 8], F32)
        l.inc(nc.sync.dma_start(out=fcs[:], in_=fconsts[:]), l.LD, 16)
        for i, v in enumerate(FCONST_VALS):
            nc.const_aps.register(F32, float(v), fcs[:, i:i + 1])
        cst4 = l.sbuf("cst4", [128, 96], F32)
        b1fm = l.sbuf("b1fm_sb", [128, 32], F32)
        onesb_sb = l.sbuf("onesbsb", [128, 4], BF16)
        rowones = l.sbuf("rowones", [1, 128], BF16)
        stage = l.sbuf("stage", [128, 4 * 512], BF16)
        l.inc(nc.sync.dma_start(out=cst4[:], in_=consts4[:]), l.LD, 16)
        l.inc(nc.sync.dma_start(out=b1fm[:], in_=b1fm_in[:]), l.LD, 16)
        l.inc(nc.sync.dma_start(out=onesb_sb[:], in_=onesb[:]), l.LD, 16)
        l.inc(nc.gpsimd.memset(rowones[:], 1.0), l.WD, 1)
        l.c_ones128_b = onesb_sb[:, 0:1]
        l.c_ones1_128 = rowones[0:1, 0:128]
        l.c_ln2g, l.c_ln2b = cst4[:, 0:8], cst4[:, 8:16]
        l.c_lnfg, l.c_lnfb = cst4[:, 16:24], cst4[:, 24:32]
        l.c_bo1 = cst4[:, 32:40]
        l.c_b2 = cst4[:, 40:48]
        l.c_bfw = cst4[:, 48:56]
        l.c_bqkv2 = cst4[:, 56:80]
        l.c_bri = cst4[:, 80:88]
        for eng in (nc.tensor, nc.scalar, nc.vector):
            l.wait(eng, l.LD)
            l.wait(eng, l.WD)

        fused_d = nc.dram_tensor("fused_d", [1024, NT], BF16, kind="Internal").ap()
        big = contextlib.ExitStack()
        ctx.enter_context(big)
        xln = FM(big.enter_context(nc.sbuf_tensor("xln", [128, 8 * NT], BF16)),
                 1024, NT)
        mo = FM(big.enter_context(nc.sbuf_tensor("mo", [128, 8 * NT], BF16)),
                1024, NT)

        with contextlib.ExitStack() as st0:
            x0 = FM(st0.enter_context(nc.sbuf_tensor("x0", [128, 8 * NT], F32)),
                    1024, NT)
            load_fm(l, x0, x_in)
            do_ln(l, x0, xln, C=NT)
            l.phase_barrier()

        with contextlib.ExitStack() as st1:
            sb1 = lambda n, s_, d: st1.enter_context(nc.sbuf_tensor(n, s_, d))
            delta_sb = sb1("deltasb", [64, S], BF16)
            bt_sb = sb1("btsb", [64, S], BF16)
            h_a = sb1("ha", [64, S + 1], F32)
            h_b = sb1("hbb", [64, S + 1], F32)
            abig_sb = sb1("abig", [64, 64], BF16)
            i64_sb = sb1("i64", [64, 64], BF16)
            h_bf = sb1("hbf", [64, 512], BF16)
            cmask_sb = sb1("cmask", [64, B * 8 * 128], BF16)
            gate_fm = FM(sb1("gate", [128, 8 * NT], BF16), 1024, NT)
            l.scan_t1b = sb1("t1b", [64, 8 * 512], BF16)
            l.inc(nc.sync.dma_start(out=delta_sb[:], in_=delta_in[:]), l.LD, 16)
            l.inc(nc.sync.dma_start(out=bt_sb[:], in_=bt_in[:]), l.LD, 16)
            l.inc(nc.gpsimd.dma_start(out=abig_sb[:], in_=abig_in[:]), l.WD, 16)
            l.inc(nc.sync.dma_start(out=i64_sb[:], in_=i64_in[:]), l.LD, 16)
            l.inc(nc.gpsimd.dma_start(out=cmask_sb[:], in_=cmask_in[:]), l.WD, 16)
            load_fm(l, gate_fm, gate_in)
            if upto >= 1:
                emit_scan(l, delta_sb, bt_sb, h_a, h_b, abig_sb[:], i64_sb[:], h_bf)
                for b in range(B):
                    for ot in range(8):
                        bi = l.nb()
                        l.acquire(bi)
                        l.waitE(nc.tensor)
                        l.wait(nc.tensor, l.WD)
                        l.inc(nc.tensor.matmul(
                            l.bank(bi)[0:128, 0:512],
                            cmask_sb[:, (b * 8 + ot) * 128:(b * 8 + ot + 1) * 128],
                            h_bf[:], start=True, stop=True), l.TD)
                        l.written(bi)
                        l.consume(nc.vector, bi)
                        l.waitE(nc.vector)
                        l.inc(nc.vector.tensor_tensor(
                            mo.t(ot, b * SL, SL), l.bank(bi)[0:128, 0:512],
                            gate_fm.t(ot, b * SL, SL), op=ALU.mult), l.EV)
                        l.inc(nc.vector.tensor_tensor(
                            mo.t(ot, b * SL, SL), mo.t(ot, b * SL, SL),
                            xln.t(ot, b * SL, SL), op=ALU.add), l.EV)
                        l.freed(bi)
            l.phase_barrier()

        with contextlib.ExitStack() as st3:
            ctxq = FM(st3.enter_context(nc.sbuf_tensor("ctxq", [128, 8 * NT], BF16)),
                      1024, NT)
            load_fm(l, ctxq, q1_in)
            if upto >= 2:
                emit_l1a_attn(l, ctxq, kt1_all, vt1h_all, scale=1.0 / (128 ** 0.5))
                wso = WStream(l, "wso1", 1024, stack=st3)

                def x2_evac(ot, c0, w, tag, ps, bi):
                    l.consume(nc.vector, bi)
                    l.waitE(nc.vector)
                    l.inc(nc.vector.tensor_tensor(xln.t(ot, c0, w), xln.t(ot, c0, w),
                                                  ps, op=ALU.add), l.EV)
                    l.freed(bi)
                    l.waitE(nc.scalar)
                    l.inc(nc.scalar.add(xln.t(ot, c0, w), xln.t(ot, c0, w),
                                        l.c_bo1[:, ot:ot + 1]), l.ES)
                emit_project(l, hb["wo1"], wso, ctxq, 8, MAIN4, x2_evac)
            l.phase_barrier()

        if upto >= 3:
            with contextlib.ExitStack() as st5:
                sb5 = lambda n, s_, d: st5.enter_context(nc.sbuf_tensor(n, s_, d))
                fused = FM(sb5("fusedsb", [128, 8 * 512], BF16), 1024, 512)
                xn2 = FM(sb5("xn2", [128, 8 * 512], BF16), 1024, 512)
                t1f = FM(sb5("t1f", [128, 32 * 512], BF16), 4096, 512)
                tout = FM(sb5("tout", [128, 8 * 512], BF16), 1024, 512)
                fpre = FM(sb5("fpre", [128, 8 * 512], BF16), 1024, 512)
                ws_w1 = WStream(l, "wsw1", 1024, stack=st5)
                ws_w2 = WStream(l, "wsw2", 4096, stack=st5)
                ws_fa = WStream(l, "wsfa", 1024, stack=st5)
                ws_fb = WStream(l, "wsfb", 1024, stack=st5)
                for nt in range(4):
                    c0 = nt * 512
                    do_ln(l, xln, xn2, C=512, g_fm=l.c_ln2g, b_fm=l.c_ln2b,
                          window=(c0, 512))

                    def t1_evac(ot, cc0, w, tag, ps, bi):
                        l.consume(nc.scalar, bi)
                        l.waitE(nc.scalar)
                        l.inc(nc.scalar.activation(t1f.t(ot, 0, 512), ps, AF.Relu,
                                                   bias=b1fm[:, ot:ot + 1]), l.ES)
                        l.freed(bi)
                    emit_project(l, hb["w1"], ws_w1, xn2, 32, [(0, 512, 0)], t1_evac)

                    def t2_evac(ot, cc0, w, tag, ps, bi):
                        l.consume(nc.scalar, bi)
                        l.waitE(nc.scalar)
                        l.inc(nc.scalar.activation(tout.t(ot, 0, 512), ps,
                                                   AF.Identity,
                                                   bias=l.c_b2[:, ot:ot + 1]), l.ES)
                        l.freed(bi)
                        l.waitE(nc.vector)
                        l.inc(nc.vector.tensor_tensor(tout.t(ot, 0, 512),
                                                      tout.t(ot, 0, 512),
                                                      xln.t(ot, c0, 512),
                                                      op=ALU.add), l.EV)
                    emit_project(l, hb["w2"], ws_w2, t1f, 8, [(0, 512, 0)], t2_evac)

                    sfa = ws_fa.issue(hb["wfa"], 0)
                    sfb = ws_fb.issue(hb["wfb"], 0)
                    for ot in range(8):
                        nfa = ws_fa.issue(hb["wfa"], ot + 1) if ot < 7 else None
                        nfb = ws_fb.issue(hb["wfb"], ot + 1) if ot < 7 else None
                        ws_fa.wait_ready(sfa)
                        ws_fb.wait_ready(sfb)
                        bi = l.nb()
                        l.acquire(bi)
                        l.waitE(nc.tensor)
                        for pt in range(8):
                            nc.tensor.matmul(l.bank(bi)[0:128, 0:512],
                                             ws_fa.chunk(sfa, pt),
                                             mo.t(pt, c0, 512),
                                             start=(pt == 0), stop=False)
                        last = None
                        for pt in range(8):
                            last = nc.tensor.matmul(l.bank(bi)[0:128, 0:512],
                                                    ws_fb.chunk(sfb, pt),
                                                    tout.t(pt, 0, 512),
                                                    start=False, stop=(pt == 7))
                        l.inc(last, l.TD)
                        l.written(bi)
                        ws_fa.use(sfa)
                        ws_fb.use(sfb)
                        l.consume(nc.scalar, bi)
                        l.waitE(nc.scalar)
                        l.inc(nc.scalar.activation(fpre.t(ot, 0, 512), ps_ := l.bank(bi)[0:128, 0:512],
                                                   AF.Identity,
                                                   bias=l.c_bfw[:, ot:ot + 1]), l.ES)
                        l.freed(bi)
                        sfa, sfb = nfa, nfb
                    do_ln_to(l, fpre, fused, out_c0=0, g_fm=l.c_lnfg, b_fm=l.c_lnfb)
                    l.waitE(nc.sync)
                    for p in range(8):
                        l.inc(nc.sync.dma_start(
                            out=fused_d[p * 128:(p + 1) * 128, c0:c0 + 512],
                            in_=fused.t(p)), l.ST, 16)
        big.close()
        l.phase_barrier()

        with contextlib.ExitStack() as st6:
            x6 = FM(st6.enter_context(nc.sbuf_tensor("x6", [128, 8 * NT], F32)),
                    1024, NT)
            xnb = FM(st6.enter_context(nc.sbuf_tensor("xnb", [128, 8 * NT], BF16)),
                     1024, NT)
            fscr = st6.enter_context(nc.sbuf_tensor("fscr", [128, NT], BF16))
            load_fm(l, x6, x_in)
            if upto >= 3:
                for ot in range(8):
                    l.wait(nc.sync, l.ST)
                    l.waitE(nc.sync)
                    l.inc(nc.sync.dma_start(out=fscr[:],
                                            in_=fused_d[ot * 128:(ot + 1) * 128, :]),
                          l.LD, 16)
                    l.wait(nc.vector, l.LD)
                    l.inc(nc.vector.tensor_tensor(x6.t(ot), x6.t(ot), fscr[:],
                                                  op=ALU.add), l.EV)
            if upto >= 4:
                do_ln(l, x6, xnb, C=NT)
                emit_l1ffn(l, x6, xnb, g1t_in, g2t_in, indd_in, indg_in, st6)
            if upto >= 5:
                do_ln(l, x6, xnb, C=NT)
                ws7 = WStream(l, "ws7", 1024, stack=st6)
                emit_project_store(l, hb["wqkv2"], ws7, xnb, 24, MAIN4,
                                   None, bias_fm=l.c_bqkv2, stage=stage,
                                   hbm_outs=(q2_o, k2_o, v2_o))
            l.waitE(nc.sync)
            l.wait(nc.sync, l.TD)
            for p in range(8):
                l.inc(nc.sync.dma_start(out=xres_o[p * 128:(p + 1) * 128, :],
                                        in_=x6.t(p, 0, NT)), l.ST, 16)
            l.wait(nc.sync, l.ST)
    return nc, l


def do_ln_to(l, xsrc, xdst, out_c0, g_fm=None, b_fm=None):
    view = FM(xdst.sb, xdst.rows, xdst.C)
    class _V:
        np_ = xdst.np_
        C = xdst.C

        def t(self, p, c0=0, w=None):
            w = 512 - c0 if w is None else w
            return xdst.sb[:, p * xdst.C + out_c0 + c0:
                           p * xdst.C + out_c0 + c0 + w]
    do_ln(l, xsrc, _V(), C=512, g_fm=g_fm, b_fm=b_fm)


def emit_l1ffn(l, x, xn, g1t_in, g2t_in, indd_in, indg_in, stack):
    nc = l.nc
    l.phase_barrier()
    sb = lambda n, s_, d: stack.enter_context(nc.sbuf_tensor(n, s_, d))
    g1t = sb("g1t", [32, 256], BF16)
    g2t = sb("g2t", [16, 128], BF16)
    indd = sb("indd", [32, 4 * 128], BF16)
    indg = sb("indg", [16, 128], BF16)
    g1rep = sb("g1rep", [32, 512], BF16)
    g2rep = sb("g2rep", [16, 512], BF16)
    wacc = sb("wacc", [128, 512], F32)
    wtmp = sb("wtmp", [128, 512], F32)
    a_sb = sb("a_sb", [128, 512], BF16)
    wrT = sb("wrT", [128, 2048], BF16)
    wiT = sb("wiT", [128, 2048], BF16)
    wnT = sb("wnT", [128, 2048], BF16)
    stg = sb("l1fstg", [128, 512], BF16)
    l.inc(nc.gpsimd.dma_start(out=g1t[:], in_=g1t_in[:]), l.WD, 16)
    l.inc(nc.gpsimd.dma_start(out=g2t[:], in_=g2t_in[:]), l.WD, 16)
    l.inc(nc.sync.dma_start(out=indd[:], in_=indd_in[:]), l.LD, 16)
    l.inc(nc.sync.dma_start(out=indg[:], in_=indg_in[:]), l.LD, 16)
    l.phase_barrier()
    for wi_ in range(2):
        dst = (wrT, wiT)[wi_]
        for pt in range(4):
            for c in range(4):
                l.waitE(nc.vector)
                l.wait(nc.vector, l.WD)
                l.wait(nc.vector, l.TD)
                l.inc(nc.vector.tensor_copy(
                    g1rep[:].rearrange("p (b e) -> p b e", e=16),
                    g1t[:, (wi_ * 4 + c) * 32:(wi_ * 4 + c) * 32 + 32]
                    .unsqueeze(2).broadcast_to([32, 32, 16])), l.EV)
                l.inc(nc.vector.tensor_copy(
                    g2rep[:].rearrange("p (b e) -> p b e", e=16),
                    g2t[:, (wi_ * 4 + c) * 16:(wi_ * 4 + c) * 16 + 16]
                    .unsqueeze(1).broadcast_to([16, 32, 16])), l.EV)
                bA, bB = l.nb(), l.nb()
                l.acquire(bA)
                l.acquire(bB)
                l.waitE(nc.tensor)
                l.inc(nc.tensor.matmul(l.bank(bA)[0:128, 0:512],
                                       indd[:, pt * 128:(pt + 1) * 128],
                                       g1rep[:], start=True, stop=True), l.TD)
                l.inc(nc.tensor.matmul(l.bank(bB)[0:128, 0:512],
                                       indg[:], g2rep[:], start=True, stop=True),
                      l.TD)
                l.written(bA)
                l.written(bB)
                l.consume(nc.scalar, bA)
                l.inc(nc.scalar.copy(a_sb[:], l.bank(bA)[0:128, 0:512]), l.ES)
                l.freed(bA)
                l.consume(nc.vector, bB)
                l.waitE(nc.vector)
                l.inc(nc.vector.tensor_tensor(wtmp[:], a_sb[:],
                                              l.bank(bB)[0:128, 0:512],
                                              op=ALU.mult), l.EV)
                l.freed(bB)
                if c == 0:
                    l.inc(nc.vector.tensor_copy(wacc[:], wtmp[:]), l.EV)
                else:
                    l.inc(nc.vector.tensor_tensor(wacc[:], wacc[:], wtmp[:],
                                                  op=ALU.add), l.EV)
            l.waitE(nc.scalar)
            l.inc(nc.scalar.copy(dst[:, pt * 512:(pt + 1) * 512], wacc[:]), l.ES)
            if wi_ == 1:
                l.inc(nc.scalar.activation(wnT[:, pt * 512:(pt + 1) * 512],
                                           wacc[:], AF.Identity, scale=-1.0), l.ES)
    def chunk(buf, pt, ot):
        return buf[:, pt * 512 + ot * 128: pt * 512 + ot * 128 + 128]
    for half in range(2):
        for ot in range(4):
            for nt in range(4):
                c0 = nt * 512
                bi = l.nb()
                l.acquire(bi)
                l.waitE(nc.tensor)
                last = None
                lA = (wrT, wiT)[half]
                lB = (wnT, wrT)[half]
                for pt in range(4):
                    nc.tensor.matmul(l.bank(bi)[0:128, 0:512], chunk(lA, pt, ot),
                                     xn.t(pt, c0, 512), start=(pt == 0), stop=False)
                for pt in range(4):
                    last = nc.tensor.matmul(l.bank(bi)[0:128, 0:512],
                                            chunk(lB, pt, ot),
                                            xn.t(4 + pt, c0, 512),
                                            start=False, stop=(pt == 3))
                l.inc(last, l.TD)
                l.written(bi)
                l.consume(nc.scalar, bi)
                l.wait(nc.scalar, l.ST)
                l.inc(nc.scalar.activation(stg[:], l.bank(bi)[0:128, 0:512],
                                           AF.Identity,
                                           bias=l.c_bri[:, half * 4 + ot:
                                                        half * 4 + ot + 1]), l.ES)
                l.freed(bi)
                l.waitE(nc.vector)
                l.inc(nc.vector.tensor_tensor(x.t(half * 4 + ot, c0, 512),
                                              x.t(half * 4 + ot, c0, 512),
                                              stg[:], op=ALU.add), l.EV)


def prep_launch3(inp, l2outs):
    base = {}
    A = inp["L1_attn_a_A"].astype(np.float32)
    abig = np.zeros((64, 64), np.float32)
    for b in range(B):
        abig[b * 16:(b + 1) * 16, b * 16:(b + 1) * 16] = A
    base["abig_in"] = abig
    base["i64_in"] = np.eye(64, dtype=BF16NP)
    C_ = inp["L1_attn_a_C"].astype(np.float32)
    cm = np.zeros((64, B * 8 * 128), np.float32)
    for b in range(B):
        for ot in range(8):
            cm[b * 16:(b + 1) * 16, (b * 8 + ot) * 128:(b * 8 + ot + 1) * 128] = \
                C_[:, ot * 128:(ot + 1) * 128]
    base["cmask_in"] = cm
    base["wo1"] = _tile_w(inp["L1_attn_a_Wo"])
    base["w1"] = _tile_w(inp["L1_attn_a_W1"])
    base["w2"] = _tile_w(inp["L1_attn_a_W2"])
    base["wfa"] = _tile_w(inp["L1_attn_a_Wf"][0:1024])
    base["wfb"] = _tile_w(inp["L1_attn_a_Wf"][1024:2048])
    base["wqkv2"] = _tile_w(inp["L1_attn_b_w_qkv"].T)
    g1 = np.zeros((32, 256), np.float32)
    g2 = np.zeros((16, 128), np.float32)
    for wi_, (k1, k2) in enumerate((("L1_ffn_G1_R", "L1_ffn_G2_R"),
                                    ("L1_ffn_G1_I", "L1_ffn_G2_I"))):
        G1 = inp[k1].astype(np.float32)[0]
        G2 = inp[k2].astype(np.float32)
        for c in range(4):
            g1[:, wi_ * 128 + c * 32: wi_ * 128 + (c + 1) * 32] = G1[:, c, :].T
            g2[:, wi_ * 64 + c * 16: wi_ * 64 + (c + 1) * 16] = G2[c, :, 0, :].T
    base["g1t_in"] = g1
    base["g2t_in"] = g2
    indd = np.zeros((32, 4 * 128), BF16NP)
    for p in range(4):
        for loc in range(128):
            indd[8 * p + loc // 16, p * 128 + loc] = 1
    base["indd_in"] = indd
    indg = np.zeros((16, 128), BF16NP)
    for loc in range(128):
        indg[loc % 16, loc] = 1
    base["indg_in"] = indg
    base["fconsts"] = np.tile(np.array([FCONST_VALS], np.float32), (128, 1))
    c4 = np.zeros((128, 96), np.float32)
    c4[:, 0:8] = _fmcols(inp["L1_attn_a_ln2_g"])
    c4[:, 8:16] = _fmcols(inp["L1_attn_a_ln2_b"])
    c4[:, 16:24] = _fmcols(inp["L1_attn_a_lnf_g"])
    c4[:, 24:32] = _fmcols(inp["L1_attn_a_lnf_b"])
    c4[:, 32:40] = _fmcols(inp["L1_attn_a_bo"])
    c4[:, 40:48] = _fmcols(inp["L1_attn_a_b2"])
    c4[:, 48:56] = _fmcols(inp["L1_attn_a_bf"])
    c4[:, 56:80] = np.ascontiguousarray(
        inp["L1_attn_b_b_qkv"].astype(np.float32).reshape(24, 128).T)
    c4[:, 80:84] = np.ascontiguousarray(
        inp["L1_ffn_bias_real"].astype(np.float32).reshape(4, 128).T)
    c4[:, 84:88] = np.ascontiguousarray(
        inp["L1_ffn_bias_imag"].astype(np.float32).reshape(4, 128).T)
    base["consts4"] = c4
    base["b1fm"] = np.ascontiguousarray(
        inp["L1_attn_a_b1"].astype(np.float32).reshape(32, 128).T)
    ob = np.zeros((128, 4), BF16NP)
    ob[:, 0] = 1
    base["onesb"] = ob

    delta_g = np.zeros((64, S), BF16NP)
    bt_g = np.zeros((64, S), BF16NP)
    kt1_all = np.zeros((B, 1024, S), BF16NP)
    vt1h_flat = np.zeros((B, 8, S, 128), BF16NP)
    for r in range(NCORES):
        d_r, b_r = l2outs[r]["delta"], l2outs[r]["bt"]
        k_r, v_r = l2outs[r]["k1"], l2outs[r]["v1"]
        for b in range(B):
            delta_g[b * 16:(b + 1) * 16, r * SL:(r + 1) * SL] = d_r[:, b * SL:(b + 1) * SL]
            bt_g[b * 16:(b + 1) * 16, r * SL:(r + 1) * SL] = b_r[:, b * SL:(b + 1) * SL]
            kt1_all[b][:, r * SL:(r + 1) * SL] = k_r[:, b * SL:(b + 1) * SL]
            for h in range(8):
                vt1h_flat[b, h, r * SL:(r + 1) * SL, :] = \
                    v_r[h * 128:(h + 1) * 128, b * SL:(b + 1) * SL].T
    base["kt1_all"] = kt1_all
    base["vt1h_all"] = np.ascontiguousarray(
        vt1h_flat.reshape(B, 8, S // 128, 128, 128).transpose(0, 1, 3, 2, 4)
        .reshape(B, 8, 128, (S // 128) * 128))

    in_maps = []
    for r in range(NCORES):
        m = dict(base)
        dr = np.roll(delta_g, -r * SL, axis=1).copy()
        br = np.roll(bt_g, -r * SL, axis=1).copy()
        cut = (S - r * SL) % S
        dr[:, cut] = 0
        m["delta_in"] = dr
        m["bt_in"] = br
        m["x_in"] = l2outs[r]["xres"]
        m["gate_in"] = l2outs[r]["gate"]
        m["q1_in"] = l2outs[r]["q1"]
        in_maps.append(m)
    return in_maps


def emit_l1ab(l, ctxq, kt2h, vt2h, qext_sb, btab_sb, rtri_sb, kext_dram):
    nc = l.nc
    l.phase_barrier()
    NKT = S // 128
    with contextlib.ExitStack() as st:
        sb = lambda n, s_, d: st.enter_context(nc.sbuf_tensor(n, s_, d))
        kslot = [sb(f"l2k{i}", [67, S], BF16) for i in range(2)]
        vslot = [sb(f"l2v{i}", [128, 32 * 65], BF16) for i in range(2)]
        esb = sb("l2e", [128, 2 * 512], BF16)
        avs = sb("l2av", [64, 512], BF16)
        dinvb = sb("l2dinv", [1, 512], BF16)
        drep = sb("l2drep", [64, 512], F32)
        qtmp = sb("l2qt", [67, 512], BF16)
        slot_user = [-1, -1]
        for i in range(2):
            l.inc(nc.gpsimd.dma_start(
                out=kslot[i][64:67, :].rearrange("p (a o) -> p a o", o=128),
                in_=kext_dram.unsqueeze(1).broadcast_to([3, 32, 128])), l.WD, 16)
        l.waitE(nc.scalar)
        l.wait(nc.scalar, l.LD)
        l.inc(nc.scalar.copy(qtmp[64:67, :], qext_sb), l.ES)

        def issue(idx, b, h):
            s_ = idx % 2
            l.wait(nc.gpsimd, l.TD,
                   slot_user[s_] if slot_user[s_] >= 0 else l.val(l.TD))
            l.inc(nc.gpsimd.dma_start(out=kslot[s_][0:64, :], in_=kt2h[b, h]),
                  l.WD, 16)
            l.inc(nc.gpsimd.dma_start(out=vslot[s_][:], in_=vt2h[b, h]), l.WD, 16)
            return s_, l.val(l.WD)

        eslu = [0, 0]
        exp_es = [0] * NKT
        pairs = [(b, h) for b in range(B) for h in range(16)]
        cur = issue(0, *pairs[0])
        for pi, (b, h) in enumerate(pairs):
            nxt = issue(pi + 1, *pairs[pi + 1]) if pi + 1 < len(pairs) else None
            s_, rdy = cur
            l.wait(nc.tensor, l.WD, rdy)
            l.acquire(2)
            l.waitE(nc.scalar)
            l.wait(nc.scalar, l.TD)
            l.inc(nc.scalar.copy(qtmp[0:64, :], ctxq.rs(h // 2, 64 * (h % 2), 64,
                                                        b * SL, SL)), l.ES)
            q_es = l.val(l.ES)

            def scores(kt):
                sbk = kt % 2
                _es, _ev = l.bank_free[sbk]
                l.waitE(nc.tensor, max(_es, q_es), _ev)
                l.inc(nc.tensor.matmul(l.bank(sbk)[0:128, 0:512],
                                       kslot[s_][0:67, kt * 128:(kt + 1) * 128],
                                       qtmp[:], start=True, stop=True), l.TD)
                l.written(sbk)

            def exp(kt):
                sbk = kt % 2
                l.consume(nc.scalar, sbk)
                l.wait(nc.scalar, l.TD, eslu[kt % 2])
                if kt < 4:
                    l.consume(nc.vector, sbk)
                    l.wait(nc.vector, l.TD, eslu[kt % 2])
                    l.inc(nc.vector.scalar_tensor_tensor(
                        esb[:, (kt % 2) * 512:(kt % 2 + 1) * 512],
                        l.bank(sbk)[0:128, 0:512], 1.0 / 8.0,
                        rtri_sb[:, kt * 512:(kt + 1) * 512],
                        op0=ALU.mult, op1=ALU.add), l.EV)
                    l.wait(nc.scalar, l.EV)
                    l.inc(nc.scalar.activation(
                        esb[:, (kt % 2) * 512:(kt % 2 + 1) * 512],
                        esb[:, (kt % 2) * 512:(kt % 2 + 1) * 512], AF.Exp), l.ES)
                else:
                    l.inc(nc.scalar.activation(
                        esb[:, (kt % 2) * 512:(kt % 2 + 1) * 512],
                        l.bank(sbk)[0:128, 0:512], AF.Exp, scale=1.0 / 8.0,
                        bias=btab_sb[:, kt:kt + 1]), l.ES)
                l.freed(sbk)
                exp_es[kt] = l.val(l.ES)

            def av(kt):
                l.wait(nc.tensor, l.ES, exp_es[kt])
                l.inc(nc.tensor.matmul(l.bank(2)[0:65, 0:512],
                                       vslot[s_][:, kt * 65:(kt + 1) * 65],
                                       esb[:, (kt % 2) * 512:(kt % 2 + 1) * 512],
                                       start=(kt == 0), stop=(kt == NKT - 1)),
                      l.TD)
                eslu[kt % 2] = l.val(l.TD)
                slot_user[s_] = l.val(l.TD)

            scores(0)
            exp(0)
            for kt in range(NKT):
                if kt + 1 < NKT:
                    scores(kt + 1)
                av(kt)
                if kt + 1 < NKT:
                    exp(kt + 1)
            l.written(2)
            l.consume(nc.vector, 2)
            l.waitE(nc.vector)
            l.inc(nc.vector.tensor_copy(dinvb[:], l.bank(2)[64:65, 0:512]), l.EV)
            l.inc(nc.vector.tensor_copy(avs[:], l.bank(2)[0:64, 0:512]), l.EV)
            l.freed(2)
            ep_ev = l.val(l.EV)
            l.acquire(3)
            l.wait(nc.tensor, l.EV, ep_ev)
            l.inc(nc.tensor.matmul(l.bank(3)[0:64, 0:512], l.c_ones1_64,
                                   dinvb[:], start=True, stop=True), l.TD)
            l.written(3)
            l.consume(nc.vector, 3)
            l.inc(nc.vector.reciprocal(drep[:, 0:512], l.bank(3)[0:64, 0:512]),
                  l.EV)
            l.freed(3)
            l.inc(nc.vector.tensor_tensor(
                ctxq.rs(h // 2, 64 * (h % 2), 64, b * SL, SL), avs[:],
                drep[:, 0:512], op=ALU.mult), l.EV)
            cur = nxt


def build_launch4(upto=99):
    nc = bass.Bass("TRN2", target_bir_lowering=False)
    ctx = contextlib.ExitStack()
    with ctx:
        l = L(nc, ctx)
        x_in = l.din("x_in", [1024, NT], F32)
        q2_in = l.din("q2_in", [1024, NT], BF16)
        kt2h_all = l.din("kt2h_all", [B, 16, 64, S], BF16)
        kext_in = l.din("kext_in", [3, 128], BF16)
        vt2h_all = l.din("vt2h_all", [B, 16, 128, (S // 128) * 65], BF16)
        qext_in = l.din("qext_in", [3, 512], BF16)
        btab_in = l.din("btab_in", [128, 32], F32)
        rtri_in = l.din("rtri_in", [128, 4 * 512], F32)
        hb = {}
        hb["wout2"] = l.din("wout2", [8, 128, 1024], F32)
        g1t_in = l.din("g1t_in", [32, 2 * 4 * 32], F32)
        g2t_in = l.din("g2t_in", [16, 2 * 4 * 16], F32)
        indd_in = l.din("indd_in", [32, 4 * 128], BF16)
        indg_in = l.din("indg_in", [16, 128], BF16)
        fconsts = l.din("fconsts", [128, 8], F32)
        consts5 = l.din("consts5", [128, 24], F32)
        onesb = l.din("onesb", [128, 4], BF16)

        xres_o = l.dout("xres", [1024, NT], F32)

        fcs = l.sbuf("fcs", [128, 8], F32)
        l.inc(nc.sync.dma_start(out=fcs[:], in_=fconsts[:]), l.LD, 16)
        for i, v in enumerate(FCONST_VALS):
            nc.const_aps.register(F32, float(v), fcs[:, i:i + 1])
        cst5 = l.sbuf("cst5", [128, 24], F32)
        onesb_sb = l.sbuf("onesbsb", [128, 4], BF16)
        rowones = l.sbuf("rowones", [1, 128], BF16)
        qext_sb = l.sbuf("qext", [3, 512], BF16)
        kext_sb = l.sbuf("kext", [3, 128], BF16)
        btab_sb = l.sbuf("btab", [128, 32], F32)
        rtri_sb = l.sbuf("rtri", [128, 4 * 512], F32)
        l.inc(nc.sync.dma_start(out=cst5[:], in_=consts5[:]), l.LD, 16)
        l.inc(nc.sync.dma_start(out=onesb_sb[:], in_=onesb[:]), l.LD, 16)
        l.inc(nc.sync.dma_start(out=qext_sb[:], in_=qext_in[:]), l.LD, 16)
        l.inc(nc.sync.dma_start(out=kext_sb[:], in_=kext_in[:]), l.LD, 16)
        l.inc(nc.sync.dma_start(out=btab_sb[:], in_=btab_in[:]), l.LD, 16)
        l.inc(nc.sync.dma_start(out=rtri_sb[:], in_=rtri_in[:]), l.LD, 16)
        l.inc(nc.gpsimd.memset(rowones[:], 1.0), l.WD, 1)
        l.c_ones128_b = onesb_sb[:, 0:1]
        l.c_ones1_128 = rowones[0:1, 0:128]
        l.c_ones1_64 = rowones[0:1, 0:64]
        l.c_bout = cst5[:, 0:8]
        l.c_bri = cst5[:, 8:16]
        for eng in (nc.tensor, nc.scalar, nc.vector):
            l.wait(eng, l.LD)
            l.wait(eng, l.WD)

        x = alloc_fm(l, "x", 1024, NT, F32)
        xn = alloc_fm(l, "xn", 1024, NT, BF16)
        load_fm(l, x, x_in)

        with contextlib.ExitStack() as st7:
            ctxq = FM(st7.enter_context(nc.sbuf_tensor("ctxq", [128, 8 * NT], BF16)),
                      1024, NT)
            load_fm(l, ctxq, q2_in)
            if upto >= 1:
                emit_l1ab(l, ctxq, kt2h_all, vt2h_all, qext_sb[:], btab_sb, rtri_sb,
                          kext_in)
                wso = WStream(l, "wso2", 1024, stack=st7)

                def out_evac(ot, c0, w, tag, ps, bi):
                    l.consume(nc.scalar, bi)
                    l.waitE(nc.scalar)
                    l.inc(nc.scalar.activation(l.stage4[:, 0:w], ps, AF.Identity,
                                               bias=l.c_bout[:, ot:ot + 1]), l.ES)
                    l.freed(bi)
                    l.waitE(nc.vector)
                    l.inc(nc.vector.tensor_tensor(x.t(ot, c0, w), x.t(ot, c0, w),
                                                  l.stage4[:, 0:w], op=ALU.add),
                          l.EV)
                l.stage4 = st7.enter_context(nc.sbuf_tensor("stg4", [128, 512], BF16))
                emit_project(l, hb["wout2"], wso, ctxq, 8, MAIN4, out_evac)
            l.phase_barrier()

        if upto >= 2:
            with contextlib.ExitStack() as st8:
                do_ln(l, x, xn, C=NT)
                emit_l1ffn(l, x, xn, g1t_in, g2t_in, indd_in, indg_in, st8)

        l.waitE(nc.sync)
        l.wait(nc.sync, l.TD)
        for p in range(8):
            l.inc(nc.sync.dma_start(out=xres_o[p * 128:(p + 1) * 128, :],
                                    in_=x.t(p, 0, NT)), l.ST, 16)
        l.wait(nc.sync, l.ST)
    return nc, l


def prep_launch4(inp, l3outs):
    base = {}
    base["wout2"] = _tile_w(inp["L1_attn_b_w_out"].T)
    base["fconsts"] = np.tile(np.array([FCONST_VALS], np.float32), (128, 1))
    c5 = np.zeros((128, 24), np.float32)
    c5[:, 0:8] = _fmcols(inp["L1_attn_b_b_out"])
    c5[:, 8:12] = np.ascontiguousarray(
        inp["L1_ffn_bias_real"].astype(np.float32).reshape(4, 128).T)
    c5[:, 12:16] = np.ascontiguousarray(
        inp["L1_ffn_bias_imag"].astype(np.float32).reshape(4, 128).T)
    base["consts5"] = c5
    g1 = np.zeros((32, 256), np.float32)
    g2 = np.zeros((16, 128), np.float32)
    for wi_, (k1, k2) in enumerate((("L1_ffn_G1_R", "L1_ffn_G2_R"),
                                    ("L1_ffn_G1_I", "L1_ffn_G2_I"))):
        G1 = inp[k1].astype(np.float32)[0]
        G2 = inp[k2].astype(np.float32)
        for c in range(4):
            g1[:, wi_ * 128 + c * 32: wi_ * 128 + (c + 1) * 32] = G1[:, c, :].T
            g2[:, wi_ * 64 + c * 16: wi_ * 64 + (c + 1) * 16] = G2[c, :, 0, :].T
    base["g1t_in"] = g1
    base["g2t_in"] = g2
    indd = np.zeros((32, 4 * 128), BF16NP)
    for p in range(4):
        for loc in range(128):
            indd[8 * p + loc // 16, p * 128 + loc] = 1
    base["indd_in"] = indd
    indg = np.zeros((16, 128), BF16NP)
    for loc in range(128):
        indg[loc % 16, loc] = 1
    base["indg_in"] = indg
    ob = np.zeros((128, 4), BF16NP)
    ob[:, 0] = 1
    base["onesb"] = ob
    qext = np.zeros((3, 512), np.float32)
    sl = np.arange(512)
    qext[0] = 1.0
    qext[1] = -1024.0 * (sl // 128)
    qext[2] = -8.0 * (sl % 128)
    base["qext_in"] = qext.astype(BF16NP)
    kk = np.arange(128)[:, None]
    qq = np.arange(512)[None, :]
    rtri = np.zeros((128, 4 * 512), np.float32)
    for kt in range(4):
        rtri[:, kt * 512:(kt + 1) * 512] = np.where(128 * kt + kk > qq, NEG,
                                                    128.0 * kt)
    base["rtri_in"] = rtri

    k2g = np.zeros((B, 16, 64, S), BF16NP)
    v2g = np.zeros((B, 16, S, 65), BF16NP)
    for r in range(NCORES):
        k_r, v_r = l3outs[r]["k2"], l3outs[r]["v2"]
        for b in range(B):
            for h in range(16):
                k2g[b, h, 0:64, r * SL:(r + 1) * SL] = \
                    k_r[h * 64:(h + 1) * 64, b * SL:(b + 1) * SL]
                v2g[b, h, r * SL:(r + 1) * SL, 0:64] = \
                    v_r[h * 64:(h + 1) * 64, b * SL:(b + 1) * SL].T
    v2g[:, :, :, 64] = np.asarray(1.0, BF16NP)
    kext = np.zeros((3, 128), np.float32)
    kext[0] = 8.0 * np.arange(128)
    kext[1] = 1.0
    kext[2] = 1.0
    base["kext_in"] = kext.astype(BF16NP)

    in_maps = []
    for r in range(NCORES):
        m = dict(base)
        order = [(r + j) % NCORES for j in range(NCORES)]
        kperm = np.concatenate([k2g[:, :, :, g * SL:(g + 1) * SL] for g in order],
                               axis=3)
        vperm = np.concatenate([v2g[:, :, g * SL:(g + 1) * SL, :] for g in order],
                               axis=2)
        m["kt2h_all"] = np.ascontiguousarray(kperm)
        m["vt2h_all"] = np.ascontiguousarray(
            vperm.reshape(B, 16, S // 128, 128, 65).transpose(0, 1, 3, 2, 4)
            .reshape(B, 16, 128, (S // 128) * 65))
        btab = np.zeros((128, 32), np.float32)
        for j, g in enumerate(order):
            for t4 in range(4):
                kt = j * 4 + t4
                if j == 0:
                    btab[:, kt] = 0.0
                elif g < r:
                    btab[:, kt] = float(128 * (g * 4 + t4) - r * SL)
                else:
                    btab[:, kt] = NEG
        for t4 in range(4):
            btab[:, t4] = float(128 * t4)
        m["btab_in"] = btab
        m["x_in"] = l3outs[r]["xres"]
        m["q2_in"] = l3outs[r]["q2"]
        in_maps.append(m)
    return in_maps


def kernel(**inputs):
    inp = {k: np.asarray(v) for k, v in inputs.items()}

    in_maps1 = prep_launch1(inp)
    nc1, _ = build_launch1()
    r1 = run_bass_kernel_spmd(nc1, in_maps1, core_ids=list(range(NCORES)))
    l1outs = [r1.results[r] for r in range(NCORES)]

    in_maps2 = prep_launch2(inp, l1outs)
    nc2, _ = build_launch2()
    r2 = run_bass_kernel_spmd(nc2, in_maps2, core_ids=list(range(NCORES)))
    l2outs = [r2.results[r] for r in range(NCORES)]

    in_maps3 = prep_launch3(inp, l2outs)
    nc3, _ = build_launch3()
    r3 = run_bass_kernel_spmd(nc3, in_maps3, core_ids=list(range(NCORES)))
    l3outs = [r3.results[r] for r in range(NCORES)]

    in_maps4 = prep_launch4(inp, l3outs)
    nc4, _ = build_launch4()
    r4 = run_bass_kernel_spmd(nc4, in_maps4, core_ids=list(range(NCORES)))

    out = np.zeros((B, S, D), np.float32)
    for r in range(NCORES):
        xr = np.asarray(r4.results[r]["xres"])
        out[:, r * SL:(r + 1) * SL, :] = \
            xr.reshape(D, B, SL).transpose(1, 2, 0)
    return out
